# version 04
# baseline (speedup 1.0000x reference)
import math
import numpy as np
import ml_dtypes
import concourse.bass as bass
import concourse.mybir as mybir
from concourse.bass_utils import run_bass_kernel_spmd
from concourse.bass_core_rust import IndirectOffsetOnAxis

F32 = mybir.dt.float32
BF16 = mybir.dt.bfloat16
U8 = mybir.dt.uint8
I32 = mybir.dt.int32
AF = mybir.ActivationFunctionType
ALU = mybir.AluOpType
AX = mybir.AxisListType
BF = ml_dtypes.bfloat16

DBG = {}

D = 1024
T = 4096
NT = T // 128
NH = 16
HD = 64



class Buf:
    __slots__ = ("name", "w", "rs")

    def __init__(self, name):
        self.name = name
        self.w = None
        self.rs = []


class Op:
    __slots__ = ("kind", "eng", "q", "fn", "deps", "eidx", "qidx", "inc", "tag")

    def __init__(self, kind, eng, q, fn, deps, eidx, qidx, tag):
        self.kind, self.eng, self.q, self.fn = kind, eng, q, fn
        self.deps, self.eidx, self.qidx, self.tag = deps, eidx, qidx, tag
        self.inc = False


class Prog:
    ENGINES = ("t", "v", "s", "g", "y")
    QUEUES = {"qy": "y", "qg": "g", "qs": "s"}

    def __init__(self, nc):
        self.nc = nc
        self.stream = {e: [] for e in self.ENGINES}
        self.ecount = {e: 0 for e in self.ENGINES}
        self.qcount = {q: 0 for q in self.QUEUES}
        self.qops = {q: [] for q in self.QUEUES}
        self.n_same_engine_waits = 0
        self.relax_same_engine = False
        self.trace_plan = None

    def _deps_for(self, r, w):
        deps = set()
        for b in r:
            if b.w is not None:
                deps.add(b.w)
        for b in w:
            if b.w is not None:
                deps.add(b.w)
            deps.update(b.rs)
        return deps

    def _update_bufs(self, op, r, w):
        for b in r:
            b.rs.append(op)
        for b in w:
            b.w = op
            b.rs = []

    def emit(self, eng, fn, r=(), w=(), tag=""):
        op = Op("e", eng, None, fn, self._deps_for(r, w), self.ecount[eng], None, tag)
        self.ecount[eng] += 1
        self.stream[eng].append(op)
        self._update_bufs(op, r, w)
        return op

    def dma(self, q, fn, r=(), w=(), tag=""):
        eng = self.QUEUES[q]
        op = Op("d", eng, q, fn, self._deps_for(r, w), None, self.qcount[q], tag)
        self.qcount[q] += 1
        self.qops[q].append(op)
        self.stream[eng].append(op)
        self._update_bufs(op, r, w)
        return op

    def coll(self, fn, r=(), w=(), tag=""):
        op = Op("c", "g", None, fn, self._deps_for(r, w), None, len(getattr(self, "colls", [])), tag)
        if not hasattr(self, "colls"):
            self.colls = []
        self.colls.append(op)
        self.stream["g"].append(op)
        self._update_bufs(op, r, w)
        return op

    def barrier(self, bufs, tag="barrier"):
        raise NotImplementedError

    NQSEM = 10

    def finalize(self):
        nc = self.nc
        M = self.NQSEM
        for e in self.ENGINES:
            for op in self.stream[e]:
                if op.kind == "d" and op.qidx >= M:
                    op.deps.add(self.qops[op.q][op.qidx - M])
        for e in self.ENGINES:
            for op in self.stream[e]:
                for d in op.deps:
                    if d.kind == "e":
                        d.inc = True
        self.incval = {}
        for e in self.ENGINES:
            c = 0
            for op in self.stream[e]:
                if op.kind == "e":
                    if op.inc:
                        c += 1
                    self.incval[id(op)] = c
        self.sem_e = {e: nc.alloc_semaphore(f"pe_{e}") for e in self.ENGINES}
        self.sem_q = {q: [nc.alloc_semaphore(f"pq_{q}{j}") for j in range(M)]
                      for q in self.QUEUES}
        esec = {"t": "tensor", "v": "vector", "s": "scalar", "g": "gpsimd", "y": "sync"}

        self.sem_c = {}
        for op in getattr(self, "colls", []):
            self.sem_c[id(op)] = nc.alloc_semaphore(f"pc{op.qidx}")

        def resolve(d):
            if d.kind == "e":
                return (self.sem_e[d.eng], self.incval[id(d)])
            if d.kind == "c":
                return (self.sem_c[id(d)], 16)
            return (self.sem_q[d.q][d.qidx % M], 16 * (d.qidx // M + 1))

        M = self.NQSEM
        def body_for(e):
            ops = self.stream[e]
            sems = self.sem_e
            qsems = self.sem_q

            def body(eng):
                known = {}
                for op in ops:
                    trace_waits = []
                    for d in sorted(op.deps, key=lambda d: (d.kind, d.eng, d.q or "", d.eidx or 0, d.qidx or 0)):
                        if self.relax_same_engine and d.kind == "e" and d.eng == e:
                            continue
                        sem, val = resolve(d)
                        if val == 0:
                            continue
                        k = id(sem)
                        if known.get(k, 0) >= val:
                            continue
                        if d.kind == "e" and d.eng == e:
                            self.n_same_engine_waits += 1
                        eng.wait_ge(sem, val)
                        trace_waits.append((d.eng if d.kind == "e" else d.q, val))
                        known[k] = val
                    inst = op.fn(eng)
                    if op.kind == "d":
                        inst.then_inc(qsems[op.q][op.qidx % M], 16)
                    elif op.kind == "c":
                        inst.then_inc(self.sem_c[id(op)], 16)
                    elif op.inc:
                        inst.then_inc(sems[e], 1)
                    if self.trace_plan is not None:
                        self.trace_plan.append((e, op.kind, op.tag, op.inc,
                                                self.incval.get(id(op)), op.qidx, trace_waits))
                if e == "y":
                    for q in self.QUEUES:
                        for j in range(M):
                            gens = (self.qcount[q] - j + M - 1) // M
                            tot = 16 * gens
                            if tot > 0 and known.get(id(qsems[q][j]), 0) < tot:
                                eng.wait_ge(qsems[q][j], tot)
                    for op2 in getattr(self, "colls", []):
                        if known.get(id(self.sem_c[id(op2)]), 0) < 16:
                            eng.wait_ge(self.sem_c[id(op2)], 16)
                    for ee in self.ENGINES:
                        tot = self.incval.get(id(self.stream[ee][-1]), 0) if self.stream[ee] else 0
                        if ee != "y" and tot and known.get(id(sems[ee]), 0) < tot:
                            eng.wait_ge(sems[ee], tot)
            return body

        with nc.Block() as block:
            for e, secname in esec.items():
                if self.stream[e] or e == "y":
                    getattr(block, secname)(body_for(e))


def ceil_div(a, b):
    return (a + b - 1) // b



class Tile:
    __slots__ = ("t", "b", "slot", "pool")

    def __init__(self, t, b, slot, pool):
        self.t, self.b, self.slot, self.pool = t, b, slot, pool

    def __getitem__(self, idx):
        return self.t[idx]

    def free(self):
        self.pool.put(self)


class Pool:

    def __init__(self, nc, name, nbytes, count, prog=None):
        self.prog = prog
        self.nc, self.name, self.nbytes = nc, name, nbytes
        backing = nc.alloc_sbuf_tensor(f"pool_{name}", [128, nbytes * count], U8)
        self.base = nc.lookup_mloc(backing).addr
        self.free_slots = list(range(count))
        self.carry = [[] for _ in range(count)]
        self.n = 0
        self.count = count
        self.watermark = 0

    def get(self, name, shape, dtype):
        assert elem_bytes(shape, dtype) <= self.nbytes, (self.name, name, shape, dtype)
        assert self.free_slots, f"pool {self.name} exhausted ({self.count} slots) at {name}"
        slot = self.free_slots.pop(0)
        self.watermark = max(self.watermark, self.count - len(self.free_slots))
        nc = self.nc
        self.n += 1
        h = nc.virtual_sbuf_tensor(f"{self.name}_{name}_{self.n}", shape, dtype)
        h.manual_base_name = f"pool_{self.name}"
        h.manual_sbuf_range = (slot * self.nbytes, elem_bytes(shape, dtype))
        nc._tensor(h.name, h.shape, h.dtype, type="SB")
        mloc = nc.lookup_mloc(h)
        mloc.allocated = True
        mloc.addr = self.base + slot * self.nbytes
        b = Buf(f"{self.name}_{name}_{self.n}")
        b.rs = list(self.carry[slot])
        tile = Tile(h, b, slot, self)
        if DBG.get("sim_init") and self.prog is not None:
            self.prog.emit("v", lambda v, h=h: v.memset(h[:], 0.0), w=[b], tag=f"simi_{name}")
        return tile

    def put(self, tile):
        ops = list(tile.b.rs)
        if tile.b.w is not None:
            ops.append(tile.b.w)
        self.carry[tile.slot] = ops
        self.free_slots.append(tile.slot)


def elem_bytes(shape, dtype):
    n = 1
    for s in shape[1:]:
        n *= s
    return n * mybir.dt.size(dtype)


class Psum:

    def __init__(self, nc):
        self.nc = nc
        self.banks = []
        self.bufs = []
        for i in range(8):
            t = nc.alloc_psum_tensor(f"bank{i}", [128, 512], F32)
            assert nc.lookup_mloc(t).bank == i
            self.banks.append(t)
            self.bufs.append(Buf(f"psum{i}"))
        self.n = 0

    def view(self, bank, shape, dtype):
        assert elem_bytes(shape, dtype) <= 2048
        nc = self.nc
        self.n += 1
        h = nc.virtual_psum_tensor(f"pb{bank}_{self.n}", shape, dtype)
        h.manual_base_name = f"bank{bank}"
        h.manual_sbuf_range = (0, elem_bytes(shape, dtype))
        nc._tensor(h.name, h.shape, h.dtype, type="PSUM")
        mloc = nc.lookup_mloc(h)
        mloc.allocated = True
        mloc.bank = bank
        return h


C_EPS5 = 0
C_EPS3 = 1
C_INV1024 = 2
C_INV512 = 3
C_S0 = 4
C_S1 = 5
C_S2 = 6
C_HALF = 7
C_ONE = 8
C_SCL0 = 9
C_SCL1 = 10
C_EPS5B = 11
N_CST = 32


class MK:

    def __init__(self, nc, io_names):
        self.nc = nc
        self.P = Prog(nc)
        self.psum = Psum(nc)
        self.io = {}
        self.io_names = io_names

        self.B8 = Pool(nc, "B8", 8448, 12, self.P)
        self.B4 = Pool(nc, "B4", 4096, 8, self.P)
        self.B2 = Pool(nc, "B2", 2048, 18, self.P)
        self.B1 = Pool(nc, "B1", 1024, 16, self.P)
        self.SM = Pool(nc, "SM", 256, 12, self.P)

        self.x32 = nc.dram_tensor("x32", [T, D], F32, kind="Internal").ap()
        self.xbf = nc.dram_tensor("xbf", [T, D], BF16, kind="Internal").ap()
        self.x32_b = [Buf(f"x32_{i}") for i in range(NT)]
        self.xbf_b = [Buf(f"xbf_{i}") for i in range(NT)]

        self.cst = nc.alloc_sbuf_tensor("cstt", [128, N_CST], F32)
        self.cst_b = Buf("cst")
        self.ones = nc.alloc_sbuf_tensor("onest", [1, 128], BF16)
        self.ones_b = Buf("ones")
        self.ones_col = nc.alloc_sbuf_tensor("onesc", [128, 1], BF16)
        self.ones_col_b = Buf("ones_col")
        self.ident = nc.alloc_sbuf_tensor("identt", [128, 128], BF16)
        self.ident_b = Buf("ident")
        self.trash4 = self.B4.get("trash", [128, 1024], F32)
        self.stats = nc.alloc_sbuf_tensor("stats", [128, 16], F32)
        self.stats_b = Buf("stats")

    def inp(self, name, shape, dtype=F32):
        if name not in self.io:
            self.io[name] = self.nc.dram_tensor(name, list(shape), dtype,
                                                kind="ExternalInput").ap()
            self.io_names.add(name)
        return self.io[name]

    def out(self, name, shape, dtype=F32):
        if name not in self.io:
            self.io[name] = self.nc.dram_tensor(name, list(shape), dtype,
                                                kind="ExternalOutput").ap()
        return self.io[name]

    def dram(self, name, shape, dtype):
        return self.nc.dram_tensor(name, list(shape), dtype, kind="Internal").ap()

    def c(self, col):
        return self.cst[:, col:col + 1]

    def prologue(self):
        P = self.P
        cst_in = self.inp("cst", [128, N_CST])
        ident_in = self.inp("ident", [128, 128])
        P.dma("qy", lambda y: y.dma_start(out=self.cst[:], in_=cst_in),
              w=[self.cst_b], tag="ld_cst")
        P.dma("qg", lambda g: g.dma_start(out=self.ident[:], in_=ident_in),
              w=[self.ident_b], tag="ld_ident")
        P.emit("v", lambda v: v.memset(self.ones[:], 1.0), w=[self.ones_b], tag="ones")
        P.emit("v", lambda v: v.memset(self.ones_col[:], 1.0), w=[self.ones_col_b], tag="onesc")

    def load_x(self):
        P = self.P
        x_in = self.inp("x", [T, D])
        for i in range(NT):
            t32 = self.B4.get(f"lx{i}", [128, D], F32)
            P.dma("qy", lambda y, i=i, t=t32: y.dma_start(out=t[:], in_=x_in[i * 128:(i + 1) * 128, :]),
                  w=[t32.b], tag=f"lx{i}")
            P.dma("qy", lambda y, i=i, t=t32: y.dma_start(out=self.x32[i * 128:(i + 1) * 128, :], in_=t[:]),
                  r=[t32.b], w=[self.x32_b[i]], tag=f"sx32_{i}")
            tbf = self.B2.get(f"lxb{i}", [128, D], BF16)
            P.emit("v", lambda v, t=t32, tb=tbf: v.tensor_copy(tb[:], t[:]),
                   r=[t32.b], w=[tbf.b], tag=f"cvt{i}")
            P.dma("qy", lambda y, i=i, tb=tbf: y.dma_start(out=self.xbf[i * 128:(i + 1) * 128, :], in_=tb[:]),
                  r=[tbf.b], w=[self.xbf_b[i]], tag=f"sxbf{i}")
            t32.free()
            tbf.free()

    def store_out(self):
        P = self.P
        o = self.out("o", [T, D])
        for i in range(NT):
            t32 = self.B4.get(f"so{i}", [128, D], F32)
            P.dma("qy", lambda y, i=i, t=t32: y.dma_start(out=t[:], in_=self.x32[i * 128:(i + 1) * 128, :]),
                  r=[self.x32_b[i]], w=[t32.b], tag=f"so_l{i}")
            P.dma("qy", lambda y, i=i, t=t32: y.dma_start(out=o[i * 128:(i + 1) * 128, :], in_=t[:]),
                  r=[t32.b], tag=f"so_s{i}")
            t32.free()

    def make_xT(self, src=None, src_b=None, tag="xT"):
        P = self.P
        if src is None:
            src, src_b = self.xbf, self.xbf_b
        xT = []
        for dc in range(8):
            tl = self.B8.get(f"{tag}{dc}", [128, T], BF16)
            P.dma("qy", lambda y, dc=dc, tl=tl, src=src: y.dma_start(
                out=tl[:], in_=src[:, dc * 128:(dc + 1) * 128], transpose=True),
                r=src_b, w=[tl.b], tag=f"{tag}{dc}")
            xT.append(tl)
        return xT

    def ln_store(self, i, yt, eps_col, gamma_row=None, beta_row=None,
                 dst32=None, dst32_b=None, dstbf=True, second=None):
        P = self.P
        st, st_b = self.stats, self.stats_b
        cst, cst_b = self.cst, self.cst_b
        trash = self.trash4

        def stats_of(src, src_b, ecol):
            P.emit("v", lambda v: v.reduce_sum(st[:, 0:1], src[:, :], AX.X),
                   r=[src_b], w=[st_b], tag=f"ln_sum{i}")
            P.emit("s", lambda s: s.activation(trash[:, :], src[:, :], AF.Square,
                                               accum_out=st[:, 1:2]),
                   r=[src_b], w=[trash.b, st_b], tag=f"ln_sq{i}")
            P.emit("v", lambda v: v.tensor_scalar(st[:, 0:1], st[:, 0:1], cst[:, C_INV1024:C_INV1024 + 1], None, ALU.mult),
                   r=[cst_b], w=[st_b], tag=f"ln_mu{i}")
            P.emit("v", lambda v: v.tensor_scalar(st[:, 2:3], st[:, 0:1], st[:, 0:1], None, ALU.mult),
                   r=[st_b], w=[st_b], tag=f"ln_mu2{i}")
            P.emit("v", lambda v: v.tensor_scalar(st[:, 1:2], st[:, 1:2], cst[:, C_INV1024:C_INV1024 + 1], st[:, 2:3], ALU.mult, ALU.subtract),
                   r=[cst_b], w=[st_b], tag=f"ln_var{i}")
            P.emit("s", lambda s, ecol=ecol: s.activation(st[:, 3:4], st[:, 1:2], AF.Sqrt, bias=cst[:, ecol:ecol + 1]),
                   r=[cst_b, st_b], w=[st_b], tag=f"ln_sd{i}")
            P.emit("v", lambda v: v.reciprocal(st[:, 4:5], st[:, 3:4]),
                   r=[st_b], w=[st_b], tag=f"ln_r{i}")

        src, src_b = yt, yt.b
        stats_of(src, src_b, eps_col)
        if gamma_row is not None:
            z = self.B4.get(f"lnz{i}", [128, D], F32)
            P.emit("v", lambda v, z=z: v.tensor_scalar(z[:, :], src[:, :], st[:, 0:1], st[:, 4:5], ALU.subtract, ALU.mult),
                   r=[src_b, st_b], w=[z.b], tag=f"ln_app{i}")
            P.emit("v", lambda v, z=z: v.tensor_mul(z[:, :], z[:, :], gamma_row[0][:, :]),
                   r=[z.b, gamma_row[1]], w=[z.b], tag=f"ln_g{i}")
            P.emit("v", lambda v, z=z: v.tensor_add(z[:, :], z[:, :], beta_row[0][:, :]),
                   r=[z.b, beta_row[1]], w=[z.b], tag=f"ln_b{i}")
            cur, cur_b = z, z.b
            owned = z
        else:
            z = self.B4.get(f"lnz{i}", [128, D], F32)
            P.emit("v", lambda v, z=z: v.tensor_scalar(z[:, :], src[:, :], st[:, 0:1], st[:, 4:5], ALU.subtract, ALU.mult),
                   r=[src_b, st_b], w=[z.b], tag=f"ln_app{i}")
            cur, cur_b = z, z.b
            owned = z

        if second is not None:
            eps2, x2t, x2b = second
            P.emit("v", lambda v, cur=cur, x2t=x2t: v.tensor_add(cur[:, :], cur[:, :], x2t[:, :]),
                   r=[cur_b, x2b], w=[cur_b], tag=f"ln2_add{i}")
            stats_of(cur, cur_b, eps2)
            z2 = self.B4.get(f"lnz2_{i}", [128, D], F32)
            P.emit("v", lambda v, z2=z2, cur=cur: v.tensor_scalar(z2[:, :], cur[:, :], st[:, 0:1], st[:, 4:5], ALU.subtract, ALU.mult),
                   r=[cur_b, st_b], w=[z2.b], tag=f"ln2_app{i}")
            owned.free()
            cur, cur_b, owned = z2, z2.b, z2

        if dst32 is None:
            P.dma("qy", lambda y, cur=cur: y.dma_start(out=self.x32[i * 128:(i + 1) * 128, :], in_=cur[:, :]),
                  r=[cur_b], w=[self.x32_b[i]], tag=f"st32_{i}")
        else:
            P.dma("qy", lambda y, cur=cur: y.dma_start(out=dst32[i * 128:(i + 1) * 128, :], in_=cur[:, :]),
                  r=[cur_b], w=([dst32_b] if dst32_b is not None else []), tag=f"stout_{i}")
        if dstbf:
            tbf = self.B2.get(f"lnbf{i}", [128, D], BF16)
            P.emit("v", lambda v, tb=tbf, cur=cur: v.tensor_copy(tb[:], cur[:, :]),
                   r=[cur_b], w=[tbf.b], tag=f"lnbfc{i}")
            P.dma("qy", lambda y, tb=tbf: y.dma_start(out=self.xbf[i * 128:(i + 1) * 128, :], in_=tb[:]),
                  r=[tbf.b], w=[self.xbf_b[i]], tag=f"stbf{i}")
            tbf.free()
        owned.free()

    def load_row128(self, name, src_ap, ncols, dtype=F32, pool=None):
        pool = pool or (self.B4 if elem_bytes([128, ncols], dtype) > 2048 else self.B2)
        tl = pool.get(name, [128, ncols], dtype)
        q = "qg" if dtype not in (F32, I32) else "qy"
        self.P.dma(q, lambda y: y.dma_start(out=tl[:], in_=src_ap), w=[tl.b], tag=f"ld_{name}")
        return tl



def mlp_4096(mk, w1_ap, b1p_ap, w2_ap, badd_row, relu_scale_col=None,
             tag="l3f"):
    P, nc, ps = mk.P, mk.nc, mk.psum
    xT = mk.make_xT()
    b1p = mk.load_row128(tag + "b1p", b1p_ap, 32, pool=mk.SM) if b1p_ap is not None else None
    for tc in range(8):
        h8 = [mk.B8.get(f"{tag}h{tc}_{j}", [128, T], BF16) for j in range(4)]
        for ic in range(32):
            w1c = mk.B2.get(f"{tag}w1_{tc}_{ic}", [128, 1024], BF16)
            P.dma("qg", lambda g, ic=ic, w1c=w1c: g.dma_start(
                out=w1c[:].rearrange("p (dc i) -> p dc i", dc=8),
                in_=w1_ap[:, ic * 128:(ic + 1) * 128].rearrange("(dc p) i -> p dc i", p=128)),
                w=[w1c.b], tag=f"{tag}w1l{tc}_{ic}")
            bank = ic % 2
            pb = ps.bufs[bank]
            pp = ps.banks[bank]
            for dc in range(8):
                P.emit("t", lambda t, dc=dc, w1c=w1c, pp=pp, tc=tc: t.matmul(
                    pp[:, :], w1c[:, dc * 128:(dc + 1) * 128],
                    xT[dc][:, tc * 512:(tc + 1) * 512],
                    start=(dc == 0), stop=(dc == 7)),
                    r=[w1c.b, xT[dc].b], w=[pb], tag=f"{tag}hmm{tc}_{ic}_{dc}")
            h_t = h8[ic // 8]
            if b1p is not None:
                P.emit("s", lambda s, ic=ic, pp=pp, h_t=h_t, b1p=b1p: s.activation(
                    h_t[:, (ic % 8) * 512:(ic % 8 + 1) * 512], pp[:, :], AF.Relu,
                    bias=b1p[:, ic:ic + 1]),
                    r=[pb, b1p.b], w=[h_t.b], tag=f"{tag}hev{tc}_{ic}")
            else:
                P.emit("s", lambda s, ic=ic, pp=pp, h_t=h_t: s.activation(
                    h_t[:, (ic % 8) * 512:(ic % 8 + 1) * 512], pp[:, :], AF.Relu),
                    r=[pb], w=[h_t.b], tag=f"{tag}hev{tc}_{ic}")
            w1c.free()
        for ic in range(32):
            w2c = mk.B2.get(f"{tag}w2_{tc}_{ic}", [128, 1024], BF16)
            P.dma("qg", lambda g, ic=ic, w2c=w2c: g.dma_start(
                out=w2c[:], in_=w2_ap[ic * 128:(ic + 1) * 128, :]),
                w=[w2c.b], tag=f"{tag}w2l{tc}_{ic}")
            for tt in range(4):
                h_t = h8[ic // 8]
                for dh in range(2):
                    bank = tt * 2 + dh
                    P.emit("t", lambda t, ic=ic, tt=tt, dh=dh, w2c=w2c, h_t=h_t, bank=bank: t.matmul(
                        ps.banks[bank][:, :],
                        h_t[:, (ic % 8) * 512 + tt * 128:(ic % 8) * 512 + (tt + 1) * 128],
                        w2c[:, dh * 512:(dh + 1) * 512],
                        start=(ic == 0), stop=(ic == 31)),
                        r=[w2c.b, h_t.b], w=[ps.bufs[bank]], tag=f"{tag}ymm{tc}_{ic}_{tt}_{dh}")
            w2c.free()
        for tt in range(4):
            i = tc * 4 + tt
            xt32 = mk.B4.get(f"{tag}x{i}", [128, D], F32)
            P.dma("qy", lambda y, i=i, xt32=xt32: y.dma_start(
                out=xt32[:], in_=mk.x32[i * 128:(i + 1) * 128, :]),
                r=[mk.x32_b[i]], w=[xt32.b], tag=f"{tag}xl{i}")
            if badd_row is not None:
                P.emit("v", lambda v, xt32=xt32: v.tensor_add(xt32[:, :], xt32[:, :], badd_row[:, :]),
                       r=[xt32.b, badd_row.b], w=[xt32.b], tag=f"{tag}xb{i}")
            yt = mk.B4.get(f"{tag}y{i}", [128, D], F32)
            for dh in range(2):
                bank = tt * 2 + dh
                P.emit("v", lambda v, dh=dh, yt=yt, xt32=xt32, bank=bank: v.scalar_tensor_tensor(
                    yt[:, dh * 512:(dh + 1) * 512], ps.banks[bank][:, :], 1.0,
                    xt32[:, dh * 512:(dh + 1) * 512], ALU.mult, ALU.add),
                    r=[ps.bufs[bank], xt32.b], w=[yt.b], tag=f"{tag}res{i}_{dh}")
            xt32.free()
            mk.ln_store(i, yt, C_EPS5)
            yt.free()
        for h_t in h8:
            h_t.free()
    for tl in xT:
        tl.free()


def l3_ffn(mk):
    w1 = mk.inp("l3f_w1", [1024, 4096])
    b1p = mk.inp("l3f_b1p", [128, 32])
    w2 = mk.inp("l3f_w2", [4096, 1024])
    b2r = mk.inp("l3f_b2r", [128, 1024])
    b2b = mk.load_row128("l3f_b2b", b2r, 1024)
    mlp_4096(mk, w1, b1p, w2, b2b, tag="l3f")
    b2b.free()


C_NEG125 = 12
C_NBIG = 13
C_INV64 = 14
C_FOUR = 15
C_TWO = 16
C_NEG1 = 17


def l0_ffn_prep(mk):
    P = mk.P
    w1t = mk.inp("l0_w1t", [8, 1024, 4096])
    w2t = mk.inp("l0_w2t", [8, 4096, 1024])
    mk.w1bf = mk.dram("w1bf", [8, 1024, 4096], BF16)
    mk.w2bf = mk.dram("w2bf", [8, 4096, 1024], BF16)
    mk.w1bf_b = [Buf(f"w1bf{e}") for e in range(8)]
    mk.w2bf_b = [Buf(f"w2bf{e}") for e in range(8)]
    for e in range(8):
        for r in range(0, 1024, 128):
            tl = mk.B8.get(f"mp1_{e}_{r}", [128, 4096], BF16)
            P.dma("qg", lambda g, e=e, r=r, tl=tl: g.dma_start(
                out=tl[:], in_=w1t[e, r:r + 128, :]), w=[tl.b], tag=f"mp1l{e}_{r}")
            P.dma("qy", lambda y, e=e, r=r, tl=tl: y.dma_start(
                out=mk.w1bf[e, r:r + 128, :], in_=tl[:]),
                r=[tl.b], w=[mk.w1bf_b[e]], tag=f"mp1s{e}_{r}")
            tl.free()
        for r in range(0, 4096, 512):
            tl = mk.B8.get(f"mp2_{e}_{r}", [128, 4096], BF16)
            P.dma("qg", lambda g, e=e, r=r, tl=tl: g.dma_start(
                out=tl[:].rearrange("p (q c) -> p q c", q=4),
                in_=w2t[e, r:r + 512, :].rearrange("(q p) c -> p q c", p=128)),
                w=[tl.b], tag=f"mp2l{e}_{r}")
            P.dma("qy", lambda y, e=e, r=r, tl=tl: y.dma_start(
                out=mk.w2bf[e, r:r + 512, :].rearrange("(q p) c -> p q c", p=128),
                in_=tl[:].rearrange("p (q c) -> p q c", q=4)),
                r=[tl.b], w=[mk.w2bf_b[e]], tag=f"mp2s{e}_{r}")
            tl.free()


def l0_ffn(mk):
    P, ps = mk.P, mk.psum
    gate = mk.inp("l0_gate", [1024, 8])
    yacc = mk.dram("l0_yacc", [T, D], F32)
    yacc_b = [Buf(f"yacc{i}") for i in range(NT)]

    xT = mk.make_xT()
    gsb = mk.B1.get("l0gw", [128, 64], BF16)
    P.dma("qg", lambda g: g.dma_start(
        out=gsb[:].rearrange("p (dc e) -> p dc e", dc=8),
        in_=gate.rearrange("(dc p) e -> p dc e", p=128)), w=[gsb.b], tag="l0gwl")
    comb = mk.B1.get("l0comb", [128, 256], F32)
    scr = mk.B1.get("l0gscr", [128, 64], F32)
    P.emit("v", lambda v: v.memset(scr[:, 48:56], 1.0), w=[scr.b], tag="l0gones")
    for i in range(NT):
        pg = ps.view(6, [128, 8], F32)
        for dc in range(8):
            P.emit("t", lambda t, i=i, dc=dc, pg=pg: t.matmul(
                pg[:, :], xT[dc][:, i * 128:(i + 1) * 128], gsb[:, dc * 8:(dc + 1) * 8],
                start=(dc == 0), stop=(dc == 7)),
                r=[xT[dc].b, gsb.b], w=[ps.bufs[6]], tag=f"l0g{i}_{dc}")
        P.emit("s", lambda s, pg=pg: s.activation(scr[:, 0:8], pg[:, :], AF.Identity),
               r=[ps.bufs[6]], w=[scr.b], tag=f"l0gv{i}")
        P.emit("v", lambda v: v.tensor_reduce(scr[:, 32:33], scr[:, 0:8], AX.X, ALU.max),
               r=[scr.b], w=[scr.b], tag=f"l0gm1{i}")
        P.emit("v", lambda v: v.tensor_scalar(scr[:, 8:16], scr[:, 0:8], scr[:, 32:33], None, ALU.is_equal),
               r=[scr.b], w=[scr.b], tag=f"l0geq{i}")
        P.emit("v", lambda v: v.tensor_tensor_scan(scr[:, 40:48], scr[:, 48:56], scr[:, 8:16], 0.0, ALU.mult, ALU.add),
               r=[scr.b], w=[scr.b], tag=f"l0gcum{i}")
        P.emit("v", lambda v: v.tensor_scalar(scr[:, 40:48], scr[:, 40:48], 1.0, None, ALU.is_equal),
               r=[scr.b], w=[scr.b], tag=f"l0gfo{i}")
        P.emit("v", lambda v: v.tensor_mul(scr[:, 8:16], scr[:, 8:16], scr[:, 40:48]),
               r=[scr.b], w=[scr.b], tag=f"l0gf1{i}")
        P.emit("v", lambda v: v.scalar_tensor_tensor(scr[:, 16:24], scr[:, 8:16], mk.cst[:, C_NBIG:C_NBIG + 1], scr[:, 0:8], ALU.mult, ALU.add),
               r=[scr.b, mk.cst_b], w=[scr.b], tag=f"l0gmask{i}")
        P.emit("v", lambda v: v.tensor_reduce(scr[:, 33:34], scr[:, 16:24], AX.X, ALU.max),
               r=[scr.b], w=[scr.b], tag=f"l0gm2{i}")
        P.emit("v", lambda v: v.tensor_scalar(scr[:, 24:32], scr[:, 16:24], scr[:, 33:34], None, ALU.is_equal),
               r=[scr.b], w=[scr.b], tag=f"l0geq2{i}")
        P.emit("v", lambda v: v.tensor_tensor_scan(scr[:, 40:48], scr[:, 48:56], scr[:, 24:32], 0.0, ALU.mult, ALU.add),
               r=[scr.b], w=[scr.b], tag=f"l0gcum2{i}")
        P.emit("v", lambda v: v.tensor_scalar(scr[:, 40:48], scr[:, 40:48], 1.0, None, ALU.is_equal),
               r=[scr.b], w=[scr.b], tag=f"l0gfo2{i}")
        P.emit("v", lambda v: v.tensor_mul(scr[:, 24:32], scr[:, 24:32], scr[:, 40:48]),
               r=[scr.b], w=[scr.b], tag=f"l0gf2{i}")
        P.emit("v", lambda v: v.tensor_sub(scr[:, 34:35], scr[:, 32:33], scr[:, 33:34]),
               r=[scr.b], w=[scr.b], tag=f"l0gd{i}")
        P.emit("s", lambda s: s.activation(scr[:, 35:36], scr[:, 34:35], AF.Sigmoid),
               r=[scr.b], w=[scr.b], tag=f"l0gg1{i}")
        P.emit("s", lambda s: s.activation(scr[:, 36:37], scr[:, 34:35], AF.Sigmoid, scale=-1.0),
               r=[scr.b], w=[scr.b], tag=f"l0gg2{i}")
        P.emit("v", lambda v, i=i: v.tensor_scalar(scr[:, 40:48], scr[:, 8:16], scr[:, 35:36], None, ALU.mult),
               r=[scr.b], w=[scr.b], tag=f"l0gc1{i}")
        P.emit("v", lambda v, i=i: v.scalar_tensor_tensor(comb[:, i * 8:(i + 1) * 8], scr[:, 24:32], scr[:, 36:37], scr[:, 40:48], ALU.mult, ALU.add),
               r=[scr.b], w=[comb.b], tag=f"l0gcomb{i}")
    gsb.free()
    scr.free()

    for e in range(8):
        for tc in range(8):
            h8 = [mk.B8.get(f"l0fh{e}_{tc}_{j}", [128, T], BF16) for j in range(4)]
            for hc in range(32):
                w1c = mk.B2.get(f"l0fw1_{e}_{tc}_{hc}", [128, 1024], BF16)
                P.dma("qg", lambda g, e=e, hc=hc, w1c=w1c: g.dma_start(
                    out=w1c[:].rearrange("p (dc i) -> p dc i", dc=8),
                    in_=mk.w1bf[e, :, hc * 128:(hc + 1) * 128].rearrange("(dc p) i -> p dc i", p=128)),
                    r=[mk.w1bf_b[e]], w=[w1c.b], tag=f"l0fw1l{e}_{tc}_{hc}")
                bank = hc % 2
                pp, pb = ps.banks[bank], ps.bufs[bank]
                for dc in range(8):
                    P.emit("t", lambda t, dc=dc, tc=tc, w1c=w1c, pp=pp: t.matmul(
                        pp[:, :], w1c[:, dc * 128:(dc + 1) * 128],
                        xT[dc][:, tc * 512:(tc + 1) * 512],
                        start=(dc == 0), stop=(dc == 7)),
                        r=[w1c.b, xT[dc].b], w=[pb], tag=f"l0fh{e}_{tc}_{hc}_{dc}")
                h_t = h8[hc // 8]
                P.emit("s", lambda s, hc=hc, pp=pp, h_t=h_t: s.activation(
                    h_t[:, (hc % 8) * 512:(hc % 8 + 1) * 512], pp[:, :], AF.Relu),
                    r=[pb], w=[h_t.b], tag=f"l0fhev{e}_{tc}_{hc}")
                w1c.free()
            for hc in range(32):
                w2c = mk.B2.get(f"l0fw2_{e}_{tc}_{hc}", [128, 1024], BF16)
                P.dma("qg", lambda g, e=e, hc=hc, w2c=w2c: g.dma_start(
                    out=w2c[:], in_=mk.w2bf[e, hc * 128:(hc + 1) * 128, :]),
                    r=[mk.w2bf_b[e]], w=[w2c.b], tag=f"l0fw2l{e}_{tc}_{hc}")
                for tt in range(4):
                    h_t = h8[hc // 8]
                    for dh in range(2):
                        bank = tt * 2 + dh
                        P.emit("t", lambda t, hc=hc, tt=tt, dh=dh, w2c=w2c, h_t=h_t, bank=bank: t.matmul(
                            ps.banks[bank][:, :],
                            h_t[:, (hc % 8) * 512 + tt * 128:(hc % 8) * 512 + (tt + 1) * 128],
                            w2c[:, dh * 512:(dh + 1) * 512],
                            start=(hc == 0), stop=(hc == 31)),
                            r=[w2c.b, h_t.b], w=[ps.bufs[bank]], tag=f"l0fy{e}_{tc}_{hc}_{tt}_{dh}")
                w2c.free()
            for tt in range(4):
                i = tc * 4 + tt
                at = mk.B4.get(f"l0fa{e}_{i}", [128, D], F32)
                if e > 0:
                    P.dma("qy", lambda y, i=i, at=at: y.dma_start(
                        out=at[:], in_=yacc[i * 128:(i + 1) * 128, :]),
                        r=[yacc_b[i]], w=[at.b], tag=f"l0fal{e}_{i}")
                for dh in range(2):
                    bank = tt * 2 + dh
                    if e > 0:
                        P.emit("v", lambda v, dh=dh, at=at, bank=bank, i=i, e=e: v.scalar_tensor_tensor(
                            at[:, dh * 512:(dh + 1) * 512], ps.banks[bank][:, :],
                            comb[:, i * 8 + e:i * 8 + e + 1], at[:, dh * 512:(dh + 1) * 512],
                            ALU.mult, ALU.add),
                            r=[ps.bufs[bank], comb.b, at.b], w=[at.b], tag=f"l0facc{e}_{i}_{dh}")
                    else:
                        P.emit("v", lambda v, dh=dh, at=at, bank=bank, i=i: v.tensor_scalar(
                            at[:, dh * 512:(dh + 1) * 512], ps.banks[bank][:, :],
                            comb[:, i * 8:i * 8 + 1], None, ALU.mult),
                            r=[ps.bufs[bank], comb.b], w=[at.b], tag=f"l0facc0_{i}_{dh}")
                P.dma("qy", lambda y, i=i, at=at: y.dma_start(
                    out=yacc[i * 128:(i + 1) * 128, :], in_=at[:]),
                    r=[at.b], w=[yacc_b[i]], tag=f"l0fas{e}_{i}")
                at.free()
            for h_t in h8:
                h_t.free()
    comb.free()
    for tl in xT:
        tl.free()
    for i in range(NT):
        at = mk.B4.get(f"l0ff{i}", [128, D], F32)
        P.dma("qy", lambda y, i=i, at=at: y.dma_start(
            out=at[:], in_=yacc[i * 128:(i + 1) * 128, :]),
            r=[yacc_b[i]], w=[at.b], tag=f"l0ffl{i}")
        xt32 = mk.B4.get(f"l0fx{i}", [128, D], F32)
        P.dma("qy", lambda y, i=i, xt32=xt32: y.dma_start(
            out=xt32[:], in_=mk.x32[i * 128:(i + 1) * 128, :]),
            r=[mk.x32_b[i]], w=[xt32.b], tag=f"l0fxl{i}")
        P.emit("v", lambda v, at=at, xt32=xt32: v.tensor_add(at[:, :], at[:, :], xt32[:, :]),
               r=[at.b, xt32.b], w=[at.b], tag=f"l0fadd{i}")
        xt32.free()
        mk.ln_store(i, at, C_EPS5)
        at.free()



def proj_to_rows(mk, w_ap, bias_p, dst_d, dst_bufs, xT, ngroups, tag, dst2=None):
    P, ps = mk.P, mk.psum
    for g in range(ngroups):
        wc = mk.B2.get(f"{tag}w{g}", [128, 1024], BF16)
        P.dma("qg", lambda q, g=g, wc=wc: q.dma_start(
            out=wc[:].rearrange("p (dc i) -> p dc i", dc=8),
            in_=w_ap[:, g * 128:(g + 1) * 128].rearrange("(dc p) i -> p dc i", p=128)),
            w=[wc.b], tag=f"{tag}wl{g}")
        ot = mk.B8.get(f"{tag}o{g}", [128, T], BF16)
        for tc in range(8):
            bank = tc % 2
            pp, pb = ps.banks[bank], ps.bufs[bank]
            for dc in range(8):
                P.emit("t", lambda t, dc=dc, tc=tc, wc=wc, pp=pp: t.matmul(
                    pp[:, :], wc[:, dc * 128:(dc + 1) * 128],
                    xT[dc][:, tc * 512:(tc + 1) * 512],
                    start=(dc == 0), stop=(dc == 7)),
                    r=[wc.b, xT[dc].b], w=[pb], tag=f"{tag}mm{g}_{tc}_{dc}")
            if bias_p is not None:
                P.emit("s", lambda s, tc=tc, g=g, pp=pp, ot=ot: s.activation(
                    ot[:, tc * 512:(tc + 1) * 512], pp[:, :], AF.Identity,
                    bias=bias_p[:, g:g + 1]),
                    r=[pb, bias_p.b], w=[ot.b], tag=f"{tag}ev{g}_{tc}")
            else:
                P.emit("s", lambda s, tc=tc, pp=pp, ot=ot: s.activation(
                    ot[:, tc * 512:(tc + 1) * 512], pp[:, :], AF.Identity),
                    r=[pb], w=[ot.b], tag=f"{tag}ev{g}_{tc}")
        if dst2 is not None:
            qd, qb, kd, kb = dst2
            dd, db, gg = (qd, qb, g) if g < 8 else (kd, kb, g - 8)
            P.dma("qy", lambda y, gg=gg, ot=ot, dd=dd: y.dma_start(
                out=dd[gg * 128:(gg + 1) * 128, :], in_=ot[:]),
                r=[ot.b], w=[db[gg]], tag=f"{tag}st{g}")
        else:
            P.dma("qy", lambda y, g=g, ot=ot: y.dma_start(
                out=dst_d[g * 128:(g + 1) * 128, :], in_=ot[:]),
                r=[ot.b], w=[dst_bufs[g]], tag=f"{tag}st{g}")
        wc.free()
        ot.free()


def proj_v(mk, w_ap, bias_row, dst_d, dst_bufs, xT, tag, ncols=1024, natural=False):
    P, ps = mk.P, mk.psum
    nh = ncols // 512
    wcs = []
    for dc in range(8):
        wc = mk.B2.get(f"{tag}vw{dc}", [128, ncols], BF16)
        P.dma("qg", lambda q, dc=dc, wc=wc: q.dma_start(
            out=wc[:], in_=w_ap[dc * 128:(dc + 1) * 128, :]),
            w=[wc.b], tag=f"{tag}vwl{dc}")
        wcs.append(wc)
    for i in range(NT):
        vt = mk.B2.get(f"{tag}v{i}", [128, ncols], BF16)
        for h in range(nh):
            bank = 2 + (i * nh + h) % 2
            pp, pb = ps.banks[bank], ps.bufs[bank]
            for dc in range(8):
                P.emit("t", lambda t, dc=dc, i=i, h=h, pp=pp: t.matmul(
                    pp[:, :], xT[dc][:, i * 128:(i + 1) * 128],
                    wcs[dc][:, h * 512:(h + 1) * 512],
                    start=(dc == 0), stop=(dc == 7)),
                    r=[wcs[dc].b, xT[dc].b], w=[pb], tag=f"{tag}vmm{i}_{h}")
            if bias_row is not None:
                P.emit("v", lambda v, h=h, pp=pp, vt=vt: v.tensor_add(
                    vt[:, h * 512:(h + 1) * 512], pp[:, :], bias_row[:, h * 512:(h + 1) * 512]),
                    r=[pb, bias_row.b], w=[vt.b], tag=f"{tag}vev{i}_{h}")
            else:
                P.emit("v", lambda v, h=h, pp=pp, vt=vt: v.tensor_copy(
                    vt[:, h * 512:(h + 1) * 512], pp[:, :]),
                    r=[pb], w=[vt.b], tag=f"{tag}vev{i}_{h}")
        if natural:
            P.dma("qy", lambda y, i=i, vt=vt: y.dma_start(
                out=dst_d[i * 128:(i + 1) * 128, :], in_=vt[:]),
                r=[vt.b], w=[dst_bufs[i]], tag=f"{tag}vst{i}")
        else:
            P.dma("qy", lambda y, i=i, vt=vt: y.dma_start(
                out=dst_d.rearrange("hp p (tt c) -> tt p hp c", c=128)[i],
                in_=vt[:].rearrange("p (hp c) -> p hp c", c=128)),
                r=[vt.b], w=[dst_bufs[i]], tag=f"{tag}vst{i}")
        vt.free()
    for wc in wcs:
        wc.free()


def neg_cbound64(mk, tile, tag):
    P, ps = mk.P, mk.psum
    sq = mk.B8.get(f"{tag}sq", [64, T], BF16)
    P.emit("s", lambda s: s.activation(sq[:], tile[0:64, :], AF.Square),
           r=[tile.b], w=[sq.b], tag=f"{tag}sq")
    pA = ps.view(6, [128, 32], F32)
    for j in range(32):
        P.emit("t", lambda t, j=j: t.matmul(
            pA[:, j:j + 1], sq[0:64, j * 128:(j + 1) * 128], mk.ones_col[0:64, 0:1],
            start=True, stop=True),
            r=[sq.b, mk.ones_col_b], w=[ps.bufs[6]], tag=f"{tag}n{j}")
    sq.free()
    mx = mk.SM.get(f"{tag}mx", [128, 1], F32)
    P.emit("v", lambda v: v.tensor_reduce(mx[:, 0:1], pA[:, :], AX.X, ALU.max),
           r=[ps.bufs[6]], w=[mx.b], tag=f"{tag}mx")
    mxb = mk.SM.get(f"{tag}mxb", [128, 1], BF16)
    P.emit("v", lambda v: v.tensor_copy(mxb[:], mx[:]), r=[mx.b], w=[mxb.b], tag=f"{tag}cv")
    mx.free()
    ptr = ps.view(7, [1, 128], BF16)
    P.emit("t", lambda t: t.transpose(ptr[:, :], mxb[:, :], mk.ident[:]),
           r=[mxb.b, mk.ident_b], w=[ps.bufs[7]], tag=f"{tag}tr")
    mxb.free()
    out = mk.SM.get(f"{tag}out", [1, 1], F32)
    P.emit("v", lambda v: v.tensor_reduce(out[:, 0:1], ptr[:, :], AX.X, ALU.max),
           r=[ps.bufs[7]], w=[out.b], tag=f"{tag}gm")
    return out


def l0_attn(mk):
    P, nc, ps = mk.P, mk.nc, mk.psum
    wq = mk.inp("l0_wq", [1024, 1024])
    wk = mk.inp("l0_wk", [1024, 1024])
    wv = mk.inp("l0_wv", [1024, 1024])
    wo = mk.inp("l0_wo", [1024, 1024])
    bq_p = mk.load_row128("l0bq", mk.inp("l0_bq_p", [128, 8]), 8, pool=mk.SM)
    bk_p = mk.load_row128("l0bk", mk.inp("l0_bk_p", [128, 8]), 8, pool=mk.SM)
    bv_r = mk.load_row128("l0bv", mk.inp("l0_bv_r", [128, 1024]), 1024)
    bo_r = mk.load_row128("l0bo", mk.inp("l0_bo_r", [128, 1024]), 1024)
    gm_r = mk.load_row128("l0gm", mk.inp("l0_gamma_r", [128, 1024]), 1024)
    bt_r = mk.load_row128("l0bt", mk.inp("l0_beta_r", [128, 1024]), 1024)

    qt_d = mk.dram("l0_qt", [1024, T], BF16)
    kt_d = mk.dram("l0_kt", [1024, T], BF16)
    v_d = mk.dram("l0_v", [8, 128, T], BF16)
    ot_d = mk.dram("l0_ot", [1024, T], BF16)
    qt_db = [Buf(f"l0qt{g}") for g in range(8)]
    kt_db = [Buf(f"l0kt{g}") for g in range(8)]
    v_db = [Buf(f"l0v{i}") for i in range(NT)]
    ot_db = [Buf(f"l0ot{g}") for g in range(8)]

    xT = mk.make_xT()
    proj_to_rows(mk, wq, bq_p, qt_d, qt_db, xT, 8, "l0q")
    proj_to_rows(mk, wk, bk_p, kt_d, kt_db, xT, 8, "l0k")
    proj_v(mk, wv, bv_r, v_d, v_db, xT, "l0v")
    for tl in xT:
        tl.free()
    bq_p.free(); bk_p.free(); bv_r.free()

    attn_core(mk, qt_d, qt_db, kt_d, kt_db, v_d, v_db, ot_d, ot_db, "l0a")

    attn_outproj(mk, ot_d, ot_db, wo, bo_r, "l0o", double=(gm_r, bt_r))
    bo_r.free(); gm_r.free(); bt_r.free()


def attn_outproj(mk, ot_d, ot_db, wo_ap, bo_row, tag, double=None):
    P, ps = mk.P, mk.psum
    wos = []
    for hc in range(8):
        wc = mk.B2.get(f"{tag}wo{hc}", [128, 1024], BF16)
        P.dma("qg", lambda q, hc=hc, wc=wc: q.dma_start(
            out=wc[:], in_=wo_ap[hc * 128:(hc + 1) * 128, :]),
            w=[wc.b], tag=f"{tag}wol{hc}")
        wos.append(wc)
    for i in range(NT):
        otc = mk.B2.get(f"{tag}otc{i}", [128, 1024], BF16)
        P.dma("qy", lambda y, i=i, otc=otc: y.dma_start(
            out=otc[:].rearrange("p (g c) -> p g c", g=8),
            in_=ot_d[:, i * 128:(i + 1) * 128].rearrange("(g p) c -> p g c", p=128)),
            r=ot_db, w=[otc.b], tag=f"{tag}otl{i}")
        for h in range(2):
            bank = 4 + (i * 2 + h) % 2
            pp, pb = ps.banks[bank], ps.bufs[bank]
            for hc in range(8):
                P.emit("t", lambda t, hc=hc, h=h, otc=otc, pp=pp: t.matmul(
                    pp[:, :], otc[:, hc * 128:(hc + 1) * 128],
                    wos[hc][:, h * 512:(h + 1) * 512],
                    start=(hc == 0), stop=(hc == 7)),
                    r=[otc.b, wos[hc].b], w=[pb], tag=f"{tag}omm{i}_{h}_{hc}")
        otc.free()
        xt32 = mk.B4.get(f"{tag}x{i}", [128, D], F32)
        P.dma("qy", lambda y, i=i, xt32=xt32: y.dma_start(
            out=xt32[:], in_=mk.x32[i * 128:(i + 1) * 128, :]),
            r=[mk.x32_b[i]], w=[xt32.b], tag=f"{tag}xl{i}")
        xb = mk.B4.get(f"{tag}xb{i}", [128, D], F32)
        P.emit("v", lambda v, xb=xb, xt32=xt32: v.tensor_add(xb[:, :], xt32[:, :], bo_row[:, :]),
               r=[xt32.b, bo_row.b], w=[xb.b], tag=f"{tag}xb{i}")
        yt = mk.B4.get(f"{tag}y{i}", [128, D], F32)
        for h in range(2):
            bank = 4 + (i * 2 + h) % 2
            P.emit("v", lambda v, h=h, yt=yt, xb=xb, bank=bank: v.scalar_tensor_tensor(
                yt[:, h * 512:(h + 1) * 512], ps.banks[bank][:, :], 1.0,
                xb[:, h * 512:(h + 1) * 512], ALU.mult, ALU.add),
                r=[ps.bufs[bank], xb.b], w=[yt.b], tag=f"{tag}res{i}_{h}")
        xb.free()
        if double is not None:
            gm_r, bt_r = double
            mk.ln_store(i, yt, C_EPS3, gamma_row=(gm_r, gm_r.b), beta_row=(bt_r, bt_r.b),
                        second=(C_EPS5, xt32, xt32.b))
        else:
            mk.ln_store(i, yt, C_EPS5)
        yt.free()
        xt32.free()
    for wc in wos:
        wc.free()


def attn_core(mk, qt_d, qt_db, kt_d, kt_db, v_d, v_db, ot_d, ot_db, tag, hps=None):
    P, ps = mk.P, mk.psum
    for hp in (hps if hps is not None else range(8)):
        qts, kts, negcs = [], [], []
        for hx in range(2):
            qt = mk.B8.get(f"{tag}qt{hp}{hx}", [64, T], BF16)
            P.dma("qy", lambda y, hp=hp, hx=hx, qt=qt: y.dma_start(
                out=qt[:], in_=qt_d[hp * 128 + hx * 64:hp * 128 + (hx + 1) * 64, :]),
                r=[qt_db[hp]], w=[qt.b], tag=f"{tag}qtl{hp}{hx}")
            kt = mk.B8.get(f"{tag}kt{hp}{hx}", [64, T], BF16)
            P.dma("qy", lambda y, hp=hp, hx=hx, kt=kt: y.dma_start(
                out=kt[:], in_=kt_d[hp * 128 + hx * 64:hp * 128 + (hx + 1) * 64, :]),
                r=[kt_db[hp]], w=[kt.b], tag=f"{tag}ktl{hp}{hx}")
            qts.append(qt)
            kts.append(kt)
        vimg = mk.B8.get(f"{tag}vi{hp}", [128, T], BF16)
        P.dma("qy", lambda y, hp=hp, vimg=vimg: y.dma_start(out=vimg[:], in_=v_d[hp]),
              r=v_db, w=[vimg.b], tag=f"{tag}vil{hp}")
        vt = mk.B8.get(f"{tag}vt{hp}", [128, NT * 130], BF16)
        P.emit("v", lambda v, vt=vt: v.memset(vt[:], 1.0), w=[vt.b], tag=f"{tag}vones{hp}")
        P.emit("v", lambda v, vt=vt, vimg=vimg: v.tensor_copy(
            vt[:].rearrange("p (tt c) -> p tt c", tt=NT)[:, :, 0:64],
            vimg[:].rearrange("p (tt c) -> p tt c", tt=NT)[:, :, 0:64]),
            r=[vimg.b], w=[vt.b], tag=f"{tag}vca{hp}")
        P.emit("v", lambda v, vt=vt, vimg=vimg: v.tensor_copy(
            vt[:].rearrange("p (tt c) -> p tt c", tt=NT)[:, :, 65:129],
            vimg[:].rearrange("p (tt c) -> p tt c", tt=NT)[:, :, 64:128]),
            r=[vimg.b], w=[vt.b], tag=f"{tag}vcb{hp}")

        for hx in range(2):
            qn = neg_cbound64(mk, qts[hx], f"{tag}qn{hp}{hx}")
            kn = neg_cbound64(mk, kts[hx], f"{tag}kn{hp}{hx}")
            cc = mk.SM.get(f"{tag}cc{hp}{hx}", [1, 2], F32)
            P.emit("v", lambda v, qn=qn, kn=kn, cc=cc: v.tensor_mul(cc[:, 0:1], qn[:, 0:1], kn[:, 0:1]),
                   r=[qn.b, kn.b], w=[cc.b], tag=f"{tag}cm{hp}{hx}")
            P.emit("s", lambda s, cc=cc: s.activation(cc[:, 1:2], cc[:, 0:1], AF.Sqrt),
                   r=[cc.b], w=[cc.b], tag=f"{tag}cs{hp}{hx}")
            ccb = mk.SM.get(f"{tag}ccb{hp}{hx}", [1, 1], BF16)
            P.emit("v", lambda v, cc=cc, ccb=ccb: v.tensor_scalar(
                ccb[:, 0:1], cc[:, 1:2], mk.cst[0:1, C_NEG125:C_NEG125 + 1], None, ALU.mult),
                r=[cc.b, mk.cst_b], w=[ccb.b], tag=f"{tag}cn{hp}{hx}")
            qn.free(); kn.free(); cc.free()
            negc = mk.SM.get(f"{tag}negc{hp}{hx}", [128, 1], F32)
            pbc = ps.view(6, [128, 1], F32)
            P.emit("t", lambda t, ccb=ccb, pbc=pbc: t.matmul(
                pbc[:, 0:1], mk.ones[:, :], ccb[0:1, 0:1], start=True, stop=True),
                r=[ccb.b, mk.ones_b], w=[ps.bufs[6]], tag=f"{tag}bc{hp}{hx}")
            P.emit("s", lambda s, negc=negc, pbc=pbc: s.activation(negc[:], pbc[:, :], AF.Identity),
                   r=[ps.bufs[6]], w=[negc.b], tag=f"{tag}bce{hp}{hx}")
            ccb.free()
            negcs.append(negc)

        ot = mk.B8.get(f"{tag}ot{hp}", [128, T], BF16)
        pa_t = [mk.B1.get(f"{tag}pa{hp}_{j}", [128, 512], BF16) for j in range(2)]
        pb_t = [mk.B1.get(f"{tag}pb{hp}_{j}", [128, 512], BF16) for j in range(2)]
        for qc in DBG.get("qcs", range(8)):
            oA = ps.view(4, [65, 512], F32)
            oB = ps.view(5, [65, 512], F32)
            for k in range(NT):
                sA = ps.view(k % 2, [128, 512], F32)
                sB = ps.view(2 + k % 2, [128, 512], F32)
                P.emit("t", lambda t, k=k, qc=qc, sA=sA, kA=kts[0], qA=qts[0]: t.matmul(
                    sA[:, :], kA[0:64, k * 128:(k + 1) * 128], qA[0:64, qc * 512:(qc + 1) * 512],
                    start=True, stop=True),
                    r=[kts[0].b, qts[0].b], w=[ps.bufs[k % 2]], tag=f"{tag}sA{hp}_{qc}_{k}")
                P.emit("t", lambda t, k=k, qc=qc, sB=sB, kB=kts[1], qB=qts[1]: t.matmul(
                    sB[:, :], kB[0:64, k * 128:(k + 1) * 128], qB[0:64, qc * 512:(qc + 1) * 512],
                    start=True, stop=True),
                    r=[kts[1].b, qts[1].b], w=[ps.bufs[2 + k % 2]], tag=f"{tag}sB{hp}_{qc}_{k}")
                pa = pa_t[k % 2]
                pb = pb_t[k % 2]
                P.emit("s", lambda s, sA=sA, pa=pa, n0=negcs[0]: s.activation(
                    pa[:, :], sA[:, :], AF.Exp, bias=n0[:, 0:1], scale=0.125),
                    r=[ps.bufs[k % 2], negcs[0].b], w=[pa.b], tag=f"{tag}eA{hp}_{qc}_{k}")
                P.emit("s", lambda s, sB=sB, pb=pb, n1=negcs[1]: s.activation(
                    pb[:, :], sB[:, :], AF.Exp, bias=n1[:, 0:1], scale=0.125),
                    r=[ps.bufs[2 + k % 2], negcs[1].b], w=[pb.b], tag=f"{tag}eB{hp}_{qc}_{k}")
                P.emit("t", lambda t, k=k, pa=pa, oA=oA, vt=vt: t.matmul(
                    oA[:, :], vt[:, k * 130:k * 130 + 65], pa[:, :],
                    start=(k == 0), stop=(k == NT - 1)),
                    r=[vt.b, pa.b], w=[ps.bufs[4]], tag=f"{tag}avA{hp}_{qc}_{k}")
                P.emit("t", lambda t, k=k, pb=pb, oB=oB, vt=vt: t.matmul(
                    oB[:, :], vt[:, k * 130 + 65:k * 130 + 130], pb[:, :],
                    start=(k == 0), stop=(k == NT - 1)),
                    r=[vt.b, pb.b], w=[ps.bufs[5]], tag=f"{tag}avB{hp}_{qc}_{k}")
            for hidx, obank, oview in ((0, 4, oA), (1, 5, oB)):
                zt = mk.B2.get(f"{tag}zt{hp}_{qc}_{hidx}", [1, 512], F32)
                P.emit("s", lambda s, oview=oview, zt=zt: s.activation(zt[:, :], oview[64:65, :], AF.Identity),
                       r=[ps.bufs[obank]], w=[zt.b], tag=f"{tag}zt{hp}_{qc}_{hidx}")
                rz = mk.B2.get(f"{tag}rz{hp}_{qc}_{hidx}", [1, 512], F32)
                P.emit("v", lambda v, zt=zt, rz=rz: v.reciprocal(rz[:, :], zt[:, :]),
                       r=[zt.b], w=[rz.b], tag=f"{tag}rz{hp}_{qc}_{hidx}")
                zt.free()
                rzb = mk.B1.get(f"{tag}rzb{hp}_{qc}_{hidx}", [1, 512], BF16)
                P.emit("v", lambda v, rz=rz, rzb=rzb: v.tensor_copy(rzb[:, :], rz[:, :]),
                       r=[rz.b], w=[rzb.b], tag=f"{tag}rzc{hp}_{qc}_{hidx}")
                rz.free()
                pz = ps.view(6 + hidx, [64, 512], F32)
                P.emit("t", lambda t, rzb=rzb, pz=pz: t.matmul(
                    pz[:, :], mk.ones[:, 0:64], rzb[:, :], start=True, stop=True),
                    r=[rzb.b, mk.ones_b], w=[ps.bufs[6 + hidx]], tag=f"{tag}rzbc{hp}_{qc}_{hidx}")
                rzf = mk.B2.get(f"{tag}rzf{hp}_{qc}_{hidx}", [64, 512], F32)
                P.emit("s", lambda s, pz=pz, rzf=rzf: s.activation(rzf[:, :], pz[:, :], AF.Identity),
                       r=[ps.bufs[6 + hidx]], w=[rzf.b], tag=f"{tag}rze{hp}_{qc}_{hidx}")
                rzb.free()
                P.emit("v", lambda v, oview=oview, rzf=rzf, hidx=hidx, qc=qc, ot=ot: v.tensor_mul(
                    ot[hidx * 64:(hidx + 1) * 64, qc * 512:(qc + 1) * 512],
                    oview[0:64, :], rzf[:, :]),
                    r=[ps.bufs[obank], rzf.b], w=[ot.b], tag=f"{tag}onm{hp}_{qc}_{hidx}")
                rzf.free()
        P.dma("qy", lambda y, hp=hp, ot=ot: y.dma_start(
            out=ot_d[hp * 128:(hp + 1) * 128, :], in_=ot[:]),
            r=[ot.b], w=[ot_db[hp]], tag=f"{tag}ost{hp}")
        for tl in (qts[0], qts[1], kts[0], kts[1], vt, vimg, ot,
                   negcs[0], negcs[1], pa_t[0], pa_t[1], pb_t[0], pb_t[1]):
            tl.free()


def l1_attn(mk):
    P, ps = mk.P, mk.psum
    wqk = mk.inp("l1_wqk", [1024, 2048])
    bqk_p = mk.load_row128("l1bqk", mk.inp("l1_bqk_p", [128, 16]), 16, pool=mk.SM)
    wv = mk.inp("l1_wv", [1024, 1024])
    bv_r = mk.load_row128("l1bv", mk.inp("l1_bv_r", [128, 1024]), 1024)
    wo = mk.inp("l1_wo", [1024, 1024])
    bo_r = mk.load_row128("l1bo", mk.inp("l1_bo_r", [128, 1024]), 1024)

    h2t_d = mk.dram("l1_h2t", [2048, T], BF16)
    v_d = mk.dram("l1_v", [T, 1024], BF16)
    ot_d = mk.dram("l1_ot", [1024, T], BF16)
    h2t_db = [Buf(f"l1h2t{g}") for g in range(16)]
    v_db = [Buf(f"l1v{i}") for i in range(NT)]
    ot_db = [Buf(f"l1ot{g}") for g in range(8)]

    xT = mk.make_xT()
    proj_to_rows(mk, wqk, bqk_p, h2t_d, h2t_db, xT, 16, "l1qk")
    proj_v(mk, wv, bv_r, v_d, v_db, xT, "l1v", natural=True)
    for tl in xT:
        tl.free()
    bqk_p.free(); bv_r.free()

    for n in range(NH):
        qt = mk.B8.get(f"l1qt{n}", [64, T], BF16)
        P.dma("qy", lambda y, n=n, qt=qt: y.dma_start(
            out=qt[:].rearrange("p (r a) -> p r a", r=16),
            in_=h2t_d[:1024, n * 256:(n + 1) * 256].rearrange("(r p) a -> p r a", p=64)),
            r=h2t_db[:8], w=[qt.b], tag=f"l1qtl{n}")
        kt = mk.B8.get(f"l1kt{n}", [64, T], BF16)
        P.dma("qy", lambda y, n=n, kt=kt: y.dma_start(
            out=kt[:].rearrange("p (r a) -> p r a", r=16),
            in_=h2t_d[1024:2048, n * 256:(n + 1) * 256].rearrange("(r p) a -> p r a", p=64)),
            r=h2t_db[8:], w=[kt.b], tag=f"l1ktl{n}")
        vt = mk.B8.get(f"l1vt{n}", [128, NT * 65], BF16)
        P.emit("v", lambda v, vt=vt: v.memset(vt[:], 1.0), w=[vt.b], tag=f"l1vo{n}")
        for ah in range(2):
            P.dma("qy", lambda y, n=n, ah=ah, vt=vt: y.dma_start(
                out=vt[:].rearrange("p (r ah c) -> p r ah c", r=16, ah=2)[:, :, ah, 0:64],
                in_=v_d[n * 256 + ah * 128:n * 256 + (ah + 1) * 128, :]
                    .rearrange("p (r c) -> p r c", r=16)),
                r=[v_db[(n * 256 + ah * 128) // 128]], w=[vt.b], tag=f"l1vl{n}_{ah}")

        qn = neg_cbound64(mk, qt, f"l1qn{n}")
        kn = neg_cbound64(mk, kt, f"l1kn{n}")
        cc = mk.SM.get(f"l1cc{n}", [1, 2], F32)
        P.emit("v", lambda v, qn=qn, kn=kn, cc=cc: v.tensor_mul(cc[:, 0:1], qn[:, 0:1], kn[:, 0:1]),
               r=[qn.b, kn.b], w=[cc.b], tag=f"l1cm{n}")
        P.emit("s", lambda s, cc=cc: s.activation(cc[:, 1:2], cc[:, 0:1], AF.Sqrt),
               r=[cc.b], w=[cc.b], tag=f"l1cs{n}")
        ccb = mk.SM.get(f"l1ccb{n}", [1, 1], BF16)
        P.emit("v", lambda v, cc=cc, ccb=ccb: v.tensor_scalar(
            ccb[:, 0:1], cc[:, 1:2], mk.cst[0:1, C_NEG125:C_NEG125 + 1], None, ALU.mult),
            r=[cc.b, mk.cst_b], w=[ccb.b], tag=f"l1cn{n}")
        qn.free(); kn.free(); cc.free()
        negc = mk.SM.get(f"l1negc{n}", [128, 1], F32)
        pbc = ps.view(6, [128, 1], F32)
        P.emit("t", lambda t, ccb=ccb, pbc=pbc: t.matmul(
            pbc[:, 0:1], mk.ones[:, :], ccb[0:1, 0:1], start=True, stop=True),
            r=[ccb.b, mk.ones_b], w=[ps.bufs[6]], tag=f"l1bc{n}")
        P.emit("s", lambda s, negc=negc, pbc=pbc: s.activation(negc[:], pbc[:, :], AF.Identity),
               r=[ps.bufs[6]], w=[negc.b], tag=f"l1bce{n}")
        ccb.free()

        ot = mk.B8.get(f"l1ot{n}", [64, T], BF16)
        pa_t = [mk.B1.get(f"l1pa{n}_{j}", [128, 512], BF16) for j in range(2)]
        for qc in range(8):
            oA = ps.view(4, [65, 512], F32)
            for k in range(NT):
                sA = ps.view(k % 2, [128, 512], F32)
                P.emit("t", lambda t, k=k, qc=qc, sA=sA, kt=kt, qt=qt: t.matmul(
                    sA[:, :], kt[0:64, k * 128:(k + 1) * 128], qt[0:64, qc * 512:(qc + 1) * 512],
                    start=True, stop=True),
                    r=[kt.b, qt.b], w=[ps.bufs[k % 2]], tag=f"l1s{n}_{qc}_{k}")
                pa = pa_t[k % 2]
                P.emit("s", lambda s, sA=sA, pa=pa, negc=negc: s.activation(
                    pa[:, :], sA[:, :], AF.Exp, bias=negc[:, 0:1], scale=0.125),
                    r=[ps.bufs[k % 2], negc.b], w=[pa.b], tag=f"l1e{n}_{qc}_{k}")
                P.emit("t", lambda t, k=k, pa=pa, oA=oA, vt=vt: t.matmul(
                    oA[:, :], vt[:, k * 65:k * 65 + 65], pa[:, :],
                    start=(k == 0), stop=(k == NT - 1)),
                    r=[vt.b, pa.b], w=[ps.bufs[4]], tag=f"l1av{n}_{qc}_{k}")
            zt = mk.B2.get(f"l1zt{n}_{qc}", [1, 512], F32)
            P.emit("s", lambda s, oA=oA, zt=zt: s.activation(zt[:, :], oA[64:65, :], AF.Identity),
                   r=[ps.bufs[4]], w=[zt.b], tag=f"l1zt{n}_{qc}")
            rz = mk.B2.get(f"l1rz{n}_{qc}", [1, 512], F32)
            P.emit("v", lambda v, zt=zt, rz=rz: v.reciprocal(rz[:, :], zt[:, :]),
                   r=[zt.b], w=[rz.b], tag=f"l1rz{n}_{qc}")
            zt.free()
            rzb = mk.B1.get(f"l1rzb{n}_{qc}", [1, 512], BF16)
            P.emit("v", lambda v, rz=rz, rzb=rzb: v.tensor_copy(rzb[:, :], rz[:, :]),
                   r=[rz.b], w=[rzb.b], tag=f"l1rzc{n}_{qc}")
            rz.free()
            pz = ps.view(6, [64, 512], F32)
            P.emit("t", lambda t, rzb=rzb, pz=pz: t.matmul(
                pz[:, :], mk.ones[:, 0:64], rzb[:, :], start=True, stop=True),
                r=[rzb.b, mk.ones_b], w=[ps.bufs[6]], tag=f"l1rzbc{n}_{qc}")
            rzf = mk.B2.get(f"l1rzf{n}_{qc}", [64, 512], F32)
            P.emit("s", lambda s, pz=pz, rzf=rzf: s.activation(rzf[:, :], pz[:, :], AF.Identity),
                   r=[ps.bufs[6]], w=[rzf.b], tag=f"l1rze{n}_{qc}")
            rzb.free()
            P.emit("v", lambda v, oA=oA, rzf=rzf, qc=qc, ot=ot: v.tensor_mul(
                ot[0:64, :].rearrange("p (a s) -> p s a", s=16)[:, 2 * qc:2 * qc + 2, :],
                oA[0:64, :].rearrange("p (r a) -> p r a", r=2), rzf[:, :].rearrange("p (r a) -> p r a", r=2)),
                r=[ps.bufs[4], rzf.b], w=[ot.b], tag=f"l1onm{n}_{qc}")
            rzf.free()
        P.dma("qy", lambda y, n=n, ot=ot: y.dma_start(
            out=ot_d[n * 64:(n + 1) * 64, :], in_=ot[:]),
            r=[ot.b], w=[ot_db[n // 2]], tag=f"l1ost{n}")
        for tl in (qt, kt, vt, ot, negc, pa_t[0], pa_t[1]):
            tl.free()

    attn_outproj(mk, ot_d, ot_db, wo, bo_r, "l1o")
    bo_r.free()


def l1_ffn_prep(mk):
    P = mk.P
    k0 = mk.inp("l1_k0", [2048, 4096], U8)
    k1 = mk.inp("l1_k1", [4096, 4096], U8)
    k2 = mk.inp("l1_k2", [4096, 1024], U8)
    mk.k0e = mk.dram("k0e", [1024, 4096], BF16)
    mk.k1pm = mk.dram("k1pm", [4096, 4096], BF16)
    mk.k2pm = mk.dram("k2pm", [4096, 1024], BF16)
    mk.k0e_b = Buf("k0e")
    mk.k1pm_b = Buf("k1pm")
    mk.k2pm_b = Buf("k2pm")
    for r in range(8):
        ta = mk.B8.get(f"k0a{r}", [128, 4096], BF16)
        tb = mk.B8.get(f"k0b{r}", [128, 4096], BF16)
        P.dma("qg", lambda g, r=r, ta=ta: g.dma_start(out=ta[:], in_=k0[r * 128:(r + 1) * 128, :]),
              w=[ta.b], tag=f"k0al{r}")
        P.dma("qg", lambda g, r=r, tb=tb: g.dma_start(out=tb[:], in_=k0[1024 + r * 128:1024 + (r + 1) * 128, :]),
              w=[tb.b], tag=f"k0bl{r}")
        P.emit("v", lambda v, ta=ta, tb=tb: v.tensor_sub(ta[:, :], ta[:, :], tb[:, :]),
               r=[ta.b, tb.b], w=[ta.b], tag=f"k0sub{r}")
        P.dma("qy", lambda y, r=r, ta=ta: y.dma_start(out=mk.k0e[r * 128:(r + 1) * 128, :], in_=ta[:]),
              r=[ta.b], w=[mk.k0e_b], tag=f"k0st{r}")
        ta.free(); tb.free()
    for nm, src_ap, dst_ap, dst_b, rows in (("k1", k1, mk.k1pm, mk.k1pm_b, 32),
                                            ("k2", k2, mk.k2pm, mk.k2pm_b, 32)):
        w = 4096 if nm == "k1" else 1024
        for r in range(rows):
            ta = mk.B8.get(f"{nm}a{r}", [128, w], BF16)
            P.dma("qg", lambda g, r=r, ta=ta, src_ap=src_ap, w=w: g.dma_start(
                out=ta[:], in_=src_ap[r * 128:(r + 1) * 128, :]), w=[ta.b], tag=f"{nm}l{r}")
            P.emit("s", lambda s, ta=ta: s.activation(ta[:, :], ta[:, :], AF.Identity,
                                                      bias=mk.cst[:, C_NEG1:C_NEG1 + 1], scale=2.0),
                   r=[ta.b, mk.cst_b], w=[ta.b], tag=f"{nm}pm{r}")
            P.dma("qy", lambda y, r=r, ta=ta, dst_ap=dst_ap: y.dma_start(
                out=dst_ap[r * 128:(r + 1) * 128, :], in_=ta[:]),
                r=[ta.b], w=[dst_b], tag=f"{nm}st{r}")
            ta.free()


def l1_ffn(mk):
    P, ps = mk.P, mk.psum
    P.emit("v", lambda v: v.tensor_scalar(mk.cst[:, C_SCL0:C_SCL0 + 1], mk.cst[:, C_S0:C_S0 + 1],
                                          mk.cst[:, C_FOUR:C_FOUR + 1], None, ALU.mult),
           r=[mk.cst_b], w=[mk.cst_b], tag="l1fscl0")
    P.emit("v", lambda v: v.tensor_scalar(mk.cst[:, C_SCL1:C_SCL1 + 1], mk.cst[:, C_S1:C_S1 + 1],
                                          mk.cst[:, C_TWO:C_TWO + 1], None, ALU.mult),
           r=[mk.cst_b], w=[mk.cst_b], tag="l1fscl1")
    for tc in range(8):
        xTc = []
        for dc in range(8):
            tl = mk.B1.get(f"l1fx{tc}_{dc}", [128, 512], BF16)
            P.dma("qy", lambda y, tc=tc, dc=dc, tl=tl: y.dma_start(
                out=tl[:], in_=mk.xbf[tc * 512:(tc + 1) * 512, dc * 128:(dc + 1) * 128], transpose=True),
                r=mk.xbf_b, w=[tl.b], tag=f"l1fxT{tc}_{dc}")
            xTc.append(tl)
        h8 = [mk.B8.get(f"l1fh{tc}_{j}", [128, T], BF16) for j in range(4)]
        for hc in range(32):
            w1c = mk.B2.get(f"l1fk0_{tc}_{hc}", [128, 1024], BF16)
            P.dma("qg", lambda g, hc=hc, w1c=w1c: g.dma_start(
                out=w1c[:].rearrange("p (dc i) -> p dc i", dc=8),
                in_=mk.k0e[:, hc * 128:(hc + 1) * 128].rearrange("(dc p) i -> p dc i", p=128)),
                r=[mk.k0e_b], w=[w1c.b], tag=f"l1fk0l{tc}_{hc}")
            bank = hc % 2
            pp, pb = ps.banks[bank], ps.bufs[bank]
            for dc in range(8):
                P.emit("t", lambda t, dc=dc, w1c=w1c, pp=pp, xTc=xTc: t.matmul(
                    pp[:, :], w1c[:, dc * 128:(dc + 1) * 128], xTc[dc][:, :],
                    start=(dc == 0), stop=(dc == 7)),
                    r=[w1c.b, xTc[dc].b], w=[pb], tag=f"l1fs1_{tc}_{hc}_{dc}")
            h_t = h8[hc // 8]
            P.emit("s", lambda s, hc=hc, pp=pp, h_t=h_t: s.activation(
                h_t[:, (hc % 8) * 512:(hc % 8 + 1) * 512], pp[:, :], AF.Relu,
                scale=mk.cst[:, C_SCL0:C_SCL0 + 1]),
                r=[pb, mk.cst_b], w=[h_t.b], tag=f"l1fs1e{tc}_{hc}")
            w1c.free()
        for tl in xTc:
            tl.free()
        b8t = [mk.B8.get(f"l1fb2_{tc}_{j}", [128, T], BF16) for j in range(4)]
        for hc2 in range(32):
            k1c = mk.B8.get(f"l1fk1_{tc}_{hc2}", [128, 4096], BF16)
            P.dma("qg", lambda g, hc2=hc2, k1c=k1c: g.dma_start(
                out=k1c[:].rearrange("p (c i) -> p c i", c=32),
                in_=mk.k1pm[:, hc2 * 128:(hc2 + 1) * 128].rearrange("(c p) i -> p c i", p=128)),
                r=[mk.k1pm_b], w=[k1c.b], tag=f"l1fk1l{tc}_{hc2}")
            bank = hc2 % 2
            pp, pb = ps.banks[bank], ps.bufs[bank]
            for c1 in range(32):
                P.emit("t", lambda t, c1=c1, k1c=k1c, pp=pp, h8=h8: t.matmul(
                    pp[:, :], k1c[:, c1 * 128:(c1 + 1) * 128],
                    h8[c1 // 8][:, (c1 % 8) * 512:(c1 % 8 + 1) * 512],
                    start=(c1 == 0), stop=(c1 == 31)),
                    r=[k1c.b, h8[c1 // 8].b], w=[pb], tag=f"l1fs2_{tc}_{hc2}_{c1}")
            b_t = b8t[hc2 // 8]
            P.emit("s", lambda s, hc2=hc2, pp=pp, b_t=b_t: s.activation(
                b_t[:, (hc2 % 8) * 512:(hc2 % 8 + 1) * 512], pp[:, :], AF.Relu,
                scale=mk.cst[:, C_SCL1:C_SCL1 + 1]),
                r=[pb, mk.cst_b], w=[b_t.b], tag=f"l1fs2e{tc}_{hc2}")
            k1c.free()
        for h_t in h8:
            h_t.free()
        for hc2 in range(32):
            k2c = mk.B2.get(f"l1fk2_{tc}_{hc2}", [128, 1024], BF16)
            P.dma("qg", lambda g, hc2=hc2, k2c=k2c: g.dma_start(
                out=k2c[:], in_=mk.k2pm[hc2 * 128:(hc2 + 1) * 128, :]),
                r=[mk.k2pm_b], w=[k2c.b], tag=f"l1fk2l{tc}_{hc2}")
            for tt in range(4):
                b_t = b8t[hc2 // 8]
                for dh in range(2):
                    bank = tt * 2 + dh
                    P.emit("t", lambda t, hc2=hc2, tt=tt, dh=dh, k2c=k2c, b_t=b_t, bank=bank: t.matmul(
                        ps.banks[bank][:, :],
                        b_t[:, (hc2 % 8) * 512 + tt * 128:(hc2 % 8) * 512 + (tt + 1) * 128],
                        k2c[:, dh * 512:(dh + 1) * 512],
                        start=(hc2 == 0), stop=(hc2 == 31)),
                        r=[k2c.b, b_t.b], w=[ps.bufs[bank]], tag=f"l1fs3_{tc}_{hc2}_{tt}_{dh}")
            k2c.free()
        for tt in range(4):
            i = tc * 4 + tt
            xt32 = mk.B4.get(f"l1fxr{i}", [128, D], F32)
            P.dma("qy", lambda y, i=i, xt32=xt32: y.dma_start(
                out=xt32[:], in_=mk.x32[i * 128:(i + 1) * 128, :]),
                r=[mk.x32_b[i]], w=[xt32.b], tag=f"l1fxl{i}")
            yt = mk.B4.get(f"l1fy{i}", [128, D], F32)
            for dh in range(2):
                bank = tt * 2 + dh
                P.emit("v", lambda v, dh=dh, yt=yt, xt32=xt32, bank=bank: v.scalar_tensor_tensor(
                    yt[:, dh * 512:(dh + 1) * 512], ps.banks[bank][:, :],
                    mk.cst[:, C_S2:C_S2 + 1], xt32[:, dh * 512:(dh + 1) * 512],
                    ALU.mult, ALU.add),
                    r=[ps.bufs[bank], xt32.b, mk.cst_b], w=[yt.b], tag=f"l1fres{i}_{dh}")
            xt32.free()
            mk.ln_store(i, yt, C_EPS5)
            yt.free()
        for b_t in b8t:
            b_t.free()


def l2_attn(mk):
    P, ps = mk.P, mk.psum
    wf = mk.inp("l2_wf", [1024, 1024])
    wi = mk.inp("l2_wi", [1024, 1024])
    wh = mk.inp("l2_wh", [1024, 1024])
    bf_p = mk.load_row128("l2bf", mk.inp("l2_bf_p", [128, 8]), 8, pool=mk.SM)
    bi_p = mk.load_row128("l2bi", mk.inp("l2_bi_p", [128, 8]), 8, pool=mk.SM)
    bh_p = mk.load_row128("l2bh", mk.inp("l2_bh_p", [128, 8]), 8, pool=mk.SM)
    h_d = mk.dram("l2_h", [T, D], BF16)
    h_db = [Buf(f"l2h{i}") for i in range(4)]

    for g in range(8):
        init = mk.SM.get(f"l2init{g}", [128, 1], F32)
        for th in range(2):
            xTh = []
            for dc in range(8):
                tl = mk.B8.get(f"l2x{g}_{th}_{dc}", [128, 2048], BF16)
                P.dma("qy", lambda y, th=th, dc=dc, tl=tl: y.dma_start(
                    out=tl[:], in_=mk.xbf[th * 2048:(th + 1) * 2048, dc * 128:(dc + 1) * 128], transpose=True),
                    r=mk.xbf_b, w=[tl.b], tag=f"l2xT{g}_{th}_{dc}")
                xTh.append(tl)
            pre = []
            for wi_ap, bp, nm in ((wf, bf_p, "f"), (wi, bi_p, "i"), (wh, bh_p, "h")):
                wc = mk.B2.get(f"l2w{nm}{g}_{th}", [128, 1024], BF16)
                P.dma("qg", lambda q, g=g, wc=wc, wi_ap=wi_ap: q.dma_start(
                    out=wc[:].rearrange("p (dc i) -> p dc i", dc=8),
                    in_=wi_ap[:, g * 128:(g + 1) * 128].rearrange("(dc p) i -> p dc i", p=128)),
                    w=[wc.b], tag=f"l2wl{nm}{g}_{th}")
                ot = mk.B8.get(f"l2p{nm}{g}_{th}", [128, 2048], F32)
                for q4 in range(4):
                    bank = q4
                    for dc in range(8):
                        P.emit("t", lambda t, q4=q4, dc=dc, wc=wc, bank=bank, xTh=xTh: t.matmul(
                            ps.banks[bank][:, :], wc[:, dc * 128:(dc + 1) * 128],
                            xTh[dc][:, q4 * 512:(q4 + 1) * 512],
                            start=(dc == 0), stop=(dc == 7)),
                            r=[wc.b, xTh[dc].b], w=[ps.bufs[bank]], tag=f"l2p{nm}{g}_{th}_{q4}_{dc}")
                    P.emit("s", lambda s, q4=q4, g=g, ot=ot, bank=bank, bp=bp: s.activation(
                        ot[:, q4 * 512:(q4 + 1) * 512], ps.banks[bank][:, :], AF.Identity,
                        bias=bp[:, g:g + 1]),
                        r=[ps.bufs[bank], bp.b], w=[ot.b], tag=f"l2pe{nm}{g}_{th}_{q4}")
                wc.free()
                pre.append(ot)
            for tl in xTh:
                tl.free()
            A, Bt, H = pre
            S1 = mk.B8.get(f"l2s1_{g}_{th}", [128, 2048], F32)
            S2 = mk.B8.get(f"l2s2_{g}_{th}", [128, 2048], F32)
            m = mk.B2.get(f"l2m{g}_{th}", [128, 2048], U8)
            P.emit("s", lambda s, A=A, S1=S1: s.activation(S1[:, :], A[:, :], AF.Exp, scale=-1.0),
                   r=[A.b], w=[S1.b], tag=f"l2ea{g}_{th}")
            P.emit("s", lambda s, Bt=Bt, S2=S2: s.activation(S2[:, :], Bt[:, :], AF.Exp, scale=-1.0),
                   r=[Bt.b], w=[S2.b], tag=f"l2eb{g}_{th}")
            P.emit("v", lambda v, A=A, S1=S1, S2=S2: v.tensor_add(A[:, :], S1[:, :], S2[:, :]),
                   r=[S1.b, S2.b], w=[A.b], tag=f"l2den{g}_{th}")
            P.emit("v", lambda v, A=A: v.tensor_scalar(A[:, :], A[:, :], mk.cst[:, C_TWO:C_TWO + 1], None, ALU.add),
                   r=[mk.cst_b], w=[A.b], tag=f"l2den2{g}_{th}")
            P.emit("v", lambda v, A=A: v.reciprocal(A[:, :], A[:, :]),
                   r=[A.b], w=[A.b], tag=f"l2rec{g}_{th}")
            P.emit("v", lambda v, S2=S2: v.tensor_scalar(S2[:, :], S2[:, :], 1.0, None, ALU.add),
                   r=[S2.b], w=[S2.b], tag=f"l2f1{g}_{th}")
            P.emit("v", lambda v, S2=S2, A=A: v.tensor_mul(S2[:, :], S2[:, :], A[:, :]),
                   r=[A.b, S2.b], w=[S2.b], tag=f"l2f2{g}_{th}")
            P.emit("v", lambda v, S1=S1: v.tensor_scalar(S1[:, :], S1[:, :], 1.0, None, ALU.add),
                   r=[S1.b], w=[S1.b], tag=f"l2i1{g}_{th}")
            P.emit("v", lambda v, S1=S1, A=A: v.tensor_mul(S1[:, :], S1[:, :], A[:, :]),
                   r=[A.b, S1.b], w=[S1.b], tag=f"l2i2{g}_{th}")
            P.emit("s", lambda s, H=H, Bt=Bt: s.activation(Bt[:, :], H[:, :], AF.Sigmoid),
                   r=[H.b], w=[Bt.b], tag=f"l2sg{g}_{th}")
            P.emit("v", lambda v, H=H, m=m: v.tensor_scalar(m[:, :], H[:, :], 0.0, None, ALU.is_ge),
                   r=[H.b], w=[m.b], tag=f"l2ge{g}_{th}")
            P.emit("v", lambda v, H=H, A=A: v.tensor_scalar(A[:, :], H[:, :], mk.cst[:, C_HALF:C_HALF + 1], None, ALU.add),
                   r=[H.b, mk.cst_b], w=[A.b], tag=f"l2hp{g}_{th}")
            P.emit("v", lambda v, m=m, A=A, Bt=Bt: v.select(Bt[:, :], m[:, :], A[:, :], Bt[:, :]),
                   r=[m.b, A.b, Bt.b], w=[Bt.b], tag=f"l2sel{g}_{th}")
            P.emit("v", lambda v, S1=S1, Bt=Bt: v.tensor_mul(S1[:, :], S1[:, :], Bt[:, :]),
                   r=[Bt.b, S1.b], w=[S1.b], tag=f"l2w{g}_{th}")
            if th == 0:
                P.emit("v", lambda v, A=A, S2=S2, S1=S1: v.tensor_tensor_scan(
                    A[:, :], S2[:, :], S1[:, :], 1.0, ALU.mult, ALU.add),
                    r=[S2.b, S1.b], w=[A.b], tag=f"l2scan{g}_{th}")
            else:
                P.emit("v", lambda v, A=A, S2=S2, S1=S1, init=init: v.tensor_tensor_scan(
                    A[:, :], S2[:, :], S1[:, :], init[:, 0:1], ALU.mult, ALU.add),
                    r=[S2.b, S1.b, init.b], w=[A.b], tag=f"l2scan{g}_{th}")
            if th == 0:
                P.emit("v", lambda v, A=A, init=init: v.tensor_copy(init[:, 0:1], A[:, 2047:2048]),
                       r=[A.b], w=[init.b], tag=f"l2sv{g}")
            hbf = mk.B8.get(f"l2hb{g}_{th}", [128, 2048], BF16)
            P.emit("v", lambda v, A=A, hbf=hbf: v.tensor_copy(hbf[:, :], A[:, :]),
                   r=[A.b], w=[hbf.b], tag=f"l2cbf{g}_{th}")
            for jh in range(2):
                pt = ps.view(6 + jh, [128, 1024], BF16)
                for j in range(8):
                    P.emit("t", lambda t, jh=jh, j=j, hbf=hbf, pt=pt: t.transpose(
                        pt[:, j * 128:(j + 1) * 128],
                        hbf[:, (jh * 8 + j) * 128:(jh * 8 + j + 1) * 128], mk.ident[:]),
                        r=[hbf.b, mk.ident_b], w=[ps.bufs[6 + jh]], tag=f"l2tr{g}_{th}_{jh}_{j}")
                ev = mk.B2.get(f"l2ev{g}_{th}_{jh}", [128, 1024], BF16)
                P.emit("s", lambda s, ev=ev, pt=pt: s.activation(ev[:, :], pt[:, :], AF.Identity),
                       r=[ps.bufs[6 + jh]], w=[ev.b], tag=f"l2tre{g}_{th}_{jh}")
                P.dma("qy", lambda y, g=g, th=th, jh=jh, ev=ev: y.dma_start(
                    out=h_d[th * 2048 + jh * 1024:th * 2048 + (jh + 1) * 1024, g * 128:(g + 1) * 128]
                        .rearrange("(j p) c -> p j c", p=128),
                    in_=ev[:].rearrange("p (j c) -> p j c", j=8)),
                    r=[ev.b], w=[h_db[th * 2 + jh]], tag=f"l2hst{g}_{th}_{jh}")
                ev.free()
            for tl in (A, Bt, H, S1, S2, m, hbf):
                tl.free()
        init.free()
    bf_p.free(); bi_p.free(); bh_p.free()
    for i in range(NT):
        ht = mk.B2.get(f"l2hf{i}", [128, D], BF16)
        P.dma("qy", lambda y, i=i, ht=ht: y.dma_start(
            out=ht[:], in_=h_d[i * 128:(i + 1) * 128, :]),
            r=h_db, w=[ht.b], tag=f"l2hfl{i}")
        xt32 = mk.B4.get(f"l2xf{i}", [128, D], F32)
        P.dma("qy", lambda y, i=i, xt32=xt32: y.dma_start(
            out=xt32[:], in_=mk.x32[i * 128:(i + 1) * 128, :]),
            r=[mk.x32_b[i]], w=[xt32.b], tag=f"l2xfl{i}")
        yt = mk.B4.get(f"l2yf{i}", [128, D], F32)
        P.emit("v", lambda v, ht=ht, xt32=xt32, yt=yt: v.tensor_add(yt[:, :], xt32[:, :], ht[:, :]),
               r=[ht.b, xt32.b], w=[yt.b], tag=f"l2add{i}")
        ht.free(); xt32.free()
        mk.ln_store(i, yt, C_EPS5)
        yt.free()


def ln512(mk, src_t, src_b, gam, bet, dst_t, dst_b, tag):
    P = mk.P
    st, st_b, cst, cst_b = mk.stats, mk.stats_b, mk.cst, mk.cst_b
    trash = mk.trash4
    P.emit("v", lambda v: v.reduce_sum(st[:, 0:1], src_t[:, :], AX.X),
           r=[src_b], w=[st_b], tag=f"{tag}sum")
    P.emit("s", lambda s: s.activation(trash[:, 0:512], src_t[:, :], AF.Square, accum_out=st[:, 1:2]),
           r=[src_b], w=[trash.b, st_b], tag=f"{tag}sq")
    P.emit("v", lambda v: v.tensor_scalar(st[:, 0:1], st[:, 0:1], cst[:, C_INV512:C_INV512 + 1], None, ALU.mult),
           r=[cst_b], w=[st_b], tag=f"{tag}mu")
    P.emit("v", lambda v: v.tensor_scalar(st[:, 2:3], st[:, 0:1], st[:, 0:1], None, ALU.mult),
           r=[st_b], w=[st_b], tag=f"{tag}mu2")
    P.emit("v", lambda v: v.tensor_scalar(st[:, 1:2], st[:, 1:2], cst[:, C_INV512:C_INV512 + 1], st[:, 2:3], ALU.mult, ALU.subtract),
           r=[cst_b], w=[st_b], tag=f"{tag}var")
    P.emit("s", lambda s: s.activation(st[:, 3:4], st[:, 1:2], AF.Sqrt, bias=cst[:, C_EPS5:C_EPS5 + 1]),
           r=[cst_b, st_b], w=[st_b], tag=f"{tag}sd")
    P.emit("v", lambda v: v.reciprocal(st[:, 4:5], st[:, 3:4]),
           r=[st_b], w=[st_b], tag=f"{tag}rc")
    P.emit("v", lambda v: v.tensor_scalar(src_t[:, :], src_t[:, :], st[:, 0:1], st[:, 4:5], ALU.subtract, ALU.mult),
           r=[st_b], w=[src_b], tag=f"{tag}ap")
    P.emit("v", lambda v: v.tensor_mul(src_t[:, :], src_t[:, :], gam[:, :]),
           r=[src_b, gam.b], w=[src_b], tag=f"{tag}g")
    P.emit("v", lambda v: v.tensor_add(dst_t[:, :], src_t[:, :], bet[:, :]),
           r=[src_b, bet.b], w=[dst_b], tag=f"{tag}b")


def l2_ffn(mk):
    P, ps = mk.P, mk.psum
    sw1t = mk.inp("l2f_sw1t", [1024, 512])
    sw2t = mk.inp("l2f_sw2t", [1024, 512])
    sw3t = mk.inp("l2f_sw3t", [512, 1024])
    sb1_r = mk.load_row128("l2fsb1", mk.inp("l2f_sb1_r", [128, 512]), 512, pool=mk.B2)
    sb2_r = mk.load_row128("l2fsb2", mk.inp("l2f_sb2_r", [128, 512]), 512, pool=mk.B2)
    sb3_r = mk.load_row128("l2fsb3", mk.inp("l2f_sb3_r", [128, 1024]), 1024)
    slng = mk.load_row128("l2fslng", mk.inp("l2f_slng_r", [128, 512]), 512, pool=mk.B2)
    slnb = mk.load_row128("l2fslnb", mk.inp("l2f_slnb_r", [128, 512]), 512, pool=mk.B2)
    so_d = mk.dram("l2f_so", [T, D], F32)
    so_db = [Buf(f"l2fso{i}") for i in range(NT)]

    xT = mk.make_xT()
    w1s = [mk.B2.get(f"l2fw1_{j}", [128, 1024], BF16) for j in range(4)]
    w2s = [mk.B2.get(f"l2fw2_{j}", [128, 1024], BF16) for j in range(4)]
    w3s = [mk.B2.get(f"l2fw3_{j}", [128, 1024], BF16) for j in range(4)]
    for j in range(4):
        P.dma("qg", lambda g, j=j, tl=w1s[j]: g.dma_start(
            out=tl[:].rearrange("p (a c) -> p a c", a=2),
            in_=sw1t[j * 256:(j + 1) * 256, :].rearrange("(a p) c -> p a c", p=128)),
            w=[w1s[j].b], tag=f"l2fw1l{j}")
        P.dma("qg", lambda g, j=j, tl=w2s[j]: g.dma_start(
            out=tl[:].rearrange("p (a c) -> p a c", a=2),
            in_=sw2t[j * 256:(j + 1) * 256, :].rearrange("(a p) c -> p a c", p=128)),
            w=[w2s[j].b], tag=f"l2fw2l{j}")
    for j in range(4):
        P.dma("qg", lambda g, j=j, tl=w3s[j]: g.dma_start(
            out=tl[:], in_=sw3t[j * 128:(j + 1) * 128, :]), w=[w3s[j].b], tag=f"l2fw3l{j}")

    for i in range(NT):
        for b, ws in ((0, w1s), (1, w2s)):
            for dc in range(8):
                P.emit("t", lambda t, i=i, dc=dc, b=b, ws=ws: t.matmul(
                    ps.banks[b][:, :], xT[dc][:, i * 128:(i + 1) * 128],
                    ws[dc // 2][:, (dc % 2) * 512:(dc % 2 + 1) * 512],
                    start=(dc == 0), stop=(dc == 7)),
                    r=[xT[dc].b, ws[dc // 2].b], w=[ps.bufs[b]], tag=f"l2fx{b}_{i}_{dc}")
        t1 = mk.B2.get(f"l2ft1_{i}", [128, 512], F32)
        P.emit("v", lambda v, t1=t1: v.tensor_add(t1[:, :], ps.banks[0][:, :], sb1_r[:, :]),
               r=[ps.bufs[0], sb1_r.b], w=[t1.b], tag=f"l2fb1{i}")
        s1 = mk.B2.get(f"l2fs1_{i}", [128, 512], F32)
        P.emit("s", lambda s, t1=t1, s1=s1: s.activation(s1[:, :], t1[:, :], AF.Silu),
               r=[t1.b], w=[s1.b], tag=f"l2fsilu{i}")
        P.emit("v", lambda v, t1=t1: v.tensor_add(t1[:, :], ps.banks[1][:, :], sb2_r[:, :]),
               r=[ps.bufs[1], sb2_r.b], w=[t1.b], tag=f"l2fb2{i}")
        P.emit("v", lambda v, t1=t1, s1=s1: v.tensor_mul(t1[:, :], t1[:, :], s1[:, :]),
               r=[s1.b, t1.b], w=[t1.b], tag=f"l2fprod{i}")
        s1.free()
        shb = mk.B1.get(f"l2fshb{i}", [128, 512], BF16)
        ln512(mk, t1, t1.b, slng, slnb, shb, shb.b, f"l2fln{i}")
        t1.free()
        pt = ps.view(6, [128, 512], BF16)
        for j in range(4):
            P.emit("t", lambda t, j=j, shb=shb, pt=pt: t.transpose(
                pt[:, j * 128:(j + 1) * 128], shb[:, j * 128:(j + 1) * 128], mk.ident[:]),
                r=[shb.b, mk.ident_b], w=[ps.bufs[6]], tag=f"l2ftr{i}_{j}")
        sT = [mk.B1.get(f"l2fsT{i}_{j}", [128, 128], BF16) for j in range(4)]
        for j in range(4):
            P.emit("s", lambda s, j=j, pt=pt, sT=sT: s.activation(sT[j][:, :], pt[:, j * 128:(j + 1) * 128], AF.Identity),
                   r=[ps.bufs[6]], w=[sT[j].b], tag=f"l2ftre{i}_{j}")
        shb.free()
        for dh in range(2):
            bank = 2 + dh
            for j in range(4):
                P.emit("t", lambda t, j=j, dh=dh, bank=bank, sT=sT: t.matmul(
                    ps.banks[bank][:, :], sT[j][:, :], w3s[j][:, dh * 512:(dh + 1) * 512],
                    start=(j == 0), stop=(j == 3)),
                    r=[sT[j].b, w3s[j].b], w=[ps.bufs[bank]], tag=f"l2fo{i}_{dh}_{j}")
        so = mk.B4.get(f"l2fsov{i}", [128, D], F32)
        for dh in range(2):
            P.emit("v", lambda v, dh=dh, so=so: v.tensor_add(
                so[:, dh * 512:(dh + 1) * 512], ps.banks[2 + dh][:, :], sb3_r[:, dh * 512:(dh + 1) * 512]),
                r=[ps.bufs[2 + dh], sb3_r.b], w=[so.b], tag=f"l2fsoe{i}_{dh}")
        P.dma("qy", lambda y, i=i, so=so: y.dma_start(
            out=so_d[i * 128:(i + 1) * 128, :], in_=so[:]),
            r=[so.b], w=[so_db[i]], tag=f"l2fsos{i}")
        for tl in sT:
            tl.free()
        so.free()
    for tl in xT + w1s + w2s + w3s:
        tl.free()
    sb1_r.free(); sb2_r.free(); sb3_r.free(); slng.free(); slnb.free()

    if DBG.get("l2f_nomoe"):
        for i in range(NT):
            so = mk.B4.get(f"l2ffin{i}", [128, D], F32)
            P.dma("qy", lambda y, i=i, so=so: y.dma_start(
                out=so[:], in_=so_d[i * 128:(i + 1) * 128, :]),
                r=so_db, w=[so.b], tag=f"l2ffl{i}")
            xt32 = mk.B4.get(f"l2ffx{i}", [128, D], F32)
            P.dma("qy", lambda y, i=i, xt32=xt32: y.dma_start(
                out=xt32[:], in_=mk.x32[i * 128:(i + 1) * 128, :]),
                r=[mk.x32_b[i]], w=[xt32.b], tag=f"l2ffxl{i}")
            P.emit("v", lambda v, so=so, xt32=xt32: v.tensor_add(so[:, :], so[:, :], xt32[:, :]),
                   r=[so.b, xt32.b], w=[so.b], tag=f"l2ffadd{i}")
            xt32.free()
            mk.ln_store(i, so, C_EPS5)
            so.free()
        return
    M = 2048
    NMT = M // 128
    gidx = mk.load_row128("l2fgidx", mk.inp("l2f_gidx", [128, 16], I32), 16, dtype=I32, pool=mk.SM)
    sidx = mk.load_row128("l2fsidx", mk.inp("l2f_sidx", [128, 16], I32), 16, dtype=I32, pool=mk.SM)
    sgw = mk.load_row128("l2fsgw", mk.inp("l2f_sgw_p", [128, 8]), 8, dtype=BF16, pool=mk.SM)
    sgb = mk.load_row128("l2fsgb", mk.inp("l2f_sgb_p", [128, 1]), 1, pool=mk.SM)
    egw = mk.load_row128("l2fegw", mk.inp("l2f_egw", [128, 64]), 64, dtype=BF16, pool=mk.SM)
    egb = mk.load_row128("l2fegb", mk.inp("l2f_egb_r", [128, 8]), 8, pool=mk.SM)
    expb = mk.load_row128("l2fexpb", mk.inp("l2f_expb_r", [128, 8]), 8, pool=mk.SM)

    xm8 = [mk.B8.get(f"l2fxmT{j}", [128, 4096], BF16) for j in range(4)]
    xmTs = lambda dc, mt: xm8[dc // 2][:, (dc % 2) * 2048 + mt * 128:(dc % 2) * 2048 + (mt + 1) * 128]
    gmeta = mk.B1.get("l2fgmeta", [128, 256], F32)
    scr = mk.B1.get("l2fgscr", [128, 64], F32)
    P.emit("v", lambda v: v.memset(scr[:, 48:56], 1.0), w=[scr.b], tag="l2fones")
    for mt in range(NMT):
        xm = mk.B2.get(f"l2fxm{mt}", [128, 1024], BF16)
        P.dma("qg", lambda g, mt=mt, xm=xm: g.indirect_dma_start(
            xm[:], None, mk.xbf[:, :], IndirectOffsetOnAxis(gidx[:, mt:mt + 1], 0)),
            r=mk.xbf_b + [gidx.b], w=[xm.b], tag=f"l2fgx{mt}")
        pt = ps.view(7, [128, 1024], BF16)
        for j in range(8):
            P.emit("t", lambda t, j=j, xm=xm, pt=pt: t.transpose(
                pt[:, j * 128:(j + 1) * 128], xm[:, j * 128:(j + 1) * 128], mk.ident[:]),
                r=[xm.b, mk.ident_b], w=[ps.bufs[7]], tag=f"l2fxtr{mt}_{j}")
        for j in range(8):
            P.emit("s", lambda s, j=j, mt=mt, pt=pt: s.activation(
                xmTs(j, mt), pt[:, j * 128:(j + 1) * 128], AF.Identity),
                r=[ps.bufs[7]], w=[xm8[j // 2].b], tag=f"l2fxte{mt}_{j}")
        xm.free()
        pg = ps.view(6, [128, 16], F32)
        for dc in range(8):
            P.emit("t", lambda t, mt=mt, dc=dc, pg=pg: t.matmul(
                pg[:, 0:1], xmTs(dc, mt), sgw[:, dc:dc + 1],
                start=(dc == 0), stop=(dc == 7)),
                r=[xm8[dc // 2].b, sgw.b], w=[ps.bufs[6]], tag=f"l2fss{mt}_{dc}")
        for dc in range(8):
            P.emit("t", lambda t, mt=mt, dc=dc, pg=pg: t.matmul(
                pg[:, 8:16], xmTs(dc, mt), egw[:, dc * 8:(dc + 1) * 8],
                start=(dc == 0), stop=(dc == 7)),
                r=[xm8[dc // 2].b, egw.b], w=[ps.bufs[6]], tag=f"l2fes{mt}_{dc}")
        P.emit("s", lambda s, pg=pg: s.activation(scr[:, 32:33], pg[:, 0:1], AF.Sigmoid, bias=sgb[:, 0:1]),
               r=[ps.bufs[6], sgb.b], w=[scr.b], tag=f"l2fssg{mt}")
        P.emit("v", lambda v, pg=pg: v.tensor_add(scr[:, 0:8], pg[:, 8:16], egb[:, :]),
               r=[ps.bufs[6], egb.b], w=[scr.b], tag=f"l2fesb{mt}")
        P.emit("s", lambda s: s.activation(scr[:, 0:8], scr[:, 0:8], AF.Sigmoid),
               r=[scr.b], w=[scr.b], tag=f"l2fesg{mt}")
        P.emit("v", lambda v: v.tensor_add(scr[:, 8:16], scr[:, 0:8], expb[:, :]),
               r=[scr.b, expb.b], w=[scr.b], tag=f"l2fe2{mt}")
        P.emit("v", lambda v: v.tensor_reduce(scr[:, 33:34], scr[:, 8:16], AX.X, ALU.max),
               r=[scr.b], w=[scr.b], tag=f"l2fmx{mt}")
        P.emit("v", lambda v: v.tensor_scalar(scr[:, 16:24], scr[:, 8:16], scr[:, 33:34], None, ALU.is_equal),
               r=[scr.b], w=[scr.b], tag=f"l2feq{mt}")
        P.emit("v", lambda v: v.tensor_tensor_scan(scr[:, 24:32], scr[:, 48:56], scr[:, 16:24], 0.0, ALU.mult, ALU.add),
               r=[scr.b], w=[scr.b], tag=f"l2fcum{mt}")
        P.emit("v", lambda v: v.tensor_scalar(scr[:, 24:32], scr[:, 24:32], 1.0, None, ALU.is_equal),
               r=[scr.b], w=[scr.b], tag=f"l2ffo{mt}")
        P.emit("v", lambda v, mt=mt: v.tensor_mul(gmeta[:, mt * 16:mt * 16 + 8], scr[:, 16:24], scr[:, 24:32]),
               r=[scr.b], w=[gmeta.b], tag=f"l2ff1{mt}")
        P.emit("v", lambda v, mt=mt: v.tensor_mul(scr[:, 16:24], scr[:, 0:8], gmeta[:, mt * 16:mt * 16 + 8]),
               r=[scr.b, gmeta.b], w=[scr.b], tag=f"l2ft1{mt}")
        P.emit("v", lambda v: v.tensor_reduce(scr[:, 34:35], scr[:, 16:24], AX.X, ALU.add),
               r=[scr.b], w=[scr.b], tag=f"l2ft1s{mt}")
        P.emit("v", lambda v: v.tensor_sub(scr[:, 35:36], scr[:, 32:33], scr[:, 34:35]),
               r=[scr.b], w=[scr.b], tag=f"l2fd{mt}")
        P.emit("s", lambda s, mt=mt: s.activation(gmeta[:, mt * 16 + 8:mt * 16 + 9], scr[:, 35:36], AF.Sigmoid),
               r=[scr.b], w=[gmeta.b], tag=f"l2fwsh{mt}")
        P.emit("s", lambda s, mt=mt: s.activation(gmeta[:, mt * 16 + 9:mt * 16 + 10], scr[:, 35:36], AF.Sigmoid, scale=-1.0),
               r=[scr.b], w=[gmeta.b], tag=f"l2fwex{mt}")
    scr.free()

    eacc_d = mk.dram("l2f_eacc", [M, D], F32)
    eacc_db = [Buf(f"l2fe{mt}") for mt in range(NMT)]
    ew1 = mk.inp("l2f_ew1t", [8, 1024, 512])
    ew2 = mk.inp("l2f_ew2t", [8, 1024, 512])
    ew3 = mk.inp("l2f_ew3t", [8, 512, 1024])
    eb1 = mk.inp("l2f_eb1_r", [8, 128, 512])
    eb2 = mk.inp("l2f_eb2_r", [8, 128, 512])
    eb3 = mk.inp("l2f_eb3_r", [8, 128, 1024])
    selg = mk.load_row128("l2fselg", mk.inp("l2f_sellng_r", [128, 512]), 512, pool=mk.B2)
    selb = mk.load_row128("l2fselb", mk.inp("l2f_sellnb_r", [128, 512]), 512, pool=mk.B2)
    for e in range(8):
        we1 = [mk.B2.get(f"l2fe1_{e}_{j}", [128, 1024], BF16) for j in range(4)]
        we2 = [mk.B2.get(f"l2fe2_{e}_{j}", [128, 1024], BF16) for j in range(4)]
        we3 = [mk.B2.get(f"l2fe3_{e}_{j}", [128, 1024], BF16) for j in range(4)]
        for j in range(4):
            P.dma("qg", lambda g, e=e, j=j, tl=we1[j]: g.dma_start(
                out=tl[:].rearrange("p (a c) -> p a c", a=2),
                in_=ew1[e, j * 256:(j + 1) * 256, :].rearrange("(a p) c -> p a c", p=128)),
                w=[we1[j].b], tag=f"l2fe1l{e}_{j}")
            P.dma("qg", lambda g, e=e, j=j, tl=we2[j]: g.dma_start(
                out=tl[:].rearrange("p (a c) -> p a c", a=2),
                in_=ew2[e, j * 256:(j + 1) * 256, :].rearrange("(a p) c -> p a c", p=128)),
                w=[we2[j].b], tag=f"l2fe2l{e}_{j}")
            P.dma("qg", lambda g, e=e, j=j, tl=we3[j]: g.dma_start(
                out=tl[:], in_=ew3[e, j * 128:(j + 1) * 128, :]),
                w=[we3[j].b], tag=f"l2fe3l{e}_{j}")
        b1r = mk.load_row128(f"l2feb1{e}", eb1[e], 512, pool=mk.B2)
        b2r = mk.load_row128(f"l2feb2{e}", eb2[e], 512, pool=mk.B2)
        b3r = mk.load_row128(f"l2feb3{e}", eb3[e], 1024)
        for mt in range(NMT):
            for b, ws in ((0, we1), (1, we2)):
                for dc in range(8):
                    P.emit("t", lambda t, mt=mt, dc=dc, b=b, ws=ws: t.matmul(
                        ps.banks[b][:, :], xmTs(dc, mt),
                        ws[dc // 2][:, (dc % 2) * 512:(dc % 2 + 1) * 512],
                        start=(dc == 0), stop=(dc == 7)),
                        r=[xm8[dc // 2].b, ws[dc // 2].b], w=[ps.bufs[b]], tag=f"l2fh{b}_{e}_{mt}_{dc}")
            t1 = mk.B2.get(f"l2fet1_{e}_{mt}", [128, 512], F32)
            P.emit("v", lambda v, t1=t1, b1r=b1r: v.tensor_add(t1[:, :], ps.banks[0][:, :], b1r[:, :]),
                   r=[ps.bufs[0], b1r.b], w=[t1.b], tag=f"l2feb1a{e}_{mt}")
            s1 = mk.B2.get(f"l2fes1_{e}_{mt}", [128, 512], F32)
            P.emit("s", lambda s, t1=t1, s1=s1: s.activation(s1[:, :], t1[:, :], AF.Silu),
                   r=[t1.b], w=[s1.b], tag=f"l2fesilu{e}_{mt}")
            P.emit("v", lambda v, t1=t1, b2r=b2r: v.tensor_add(t1[:, :], ps.banks[1][:, :], b2r[:, :]),
                   r=[ps.bufs[1], b2r.b], w=[t1.b], tag=f"l2feb2a{e}_{mt}")
            P.emit("v", lambda v, t1=t1, s1=s1: v.tensor_mul(t1[:, :], t1[:, :], s1[:, :]),
                   r=[s1.b, t1.b], w=[t1.b], tag=f"l2feprod{e}_{mt}")
            s1.free()
            if DBG.get("l2f_dump") and e == 0 and mt == 0:
                P.dma("qy", lambda y, t1=t1: y.dma_start(out=DBG["l2f_dump"]["t1"], in_=t1[:]),
                      r=[t1.b], tag="dt1")
            zb = mk.B1.get(f"l2fzb{e}_{mt}", [128, 512], BF16)
            ln512(mk, t1, t1.b, selg, selb, zb, zb.b, f"l2fzln{e}_{mt}")
            t1.free()
            if DBG.get("l2f_dump") and e == 0 and mt == 0:
                P.dma("qy", lambda y, zb=zb: y.dma_start(out=DBG["l2f_dump"]["zb"], in_=zb[:]),
                      r=[zb.b], tag="dzb")
            pt = ps.view(6, [128, 512], BF16)
            for j in range(4):
                P.emit("t", lambda t, j=j, zb=zb, pt=pt: t.transpose(
                    pt[:, j * 128:(j + 1) * 128], zb[:, j * 128:(j + 1) * 128], mk.ident[:]),
                    r=[zb.b, mk.ident_b], w=[ps.bufs[6]], tag=f"l2fztr{e}_{mt}_{j}")
            zT = [mk.B1.get(f"l2fzT{e}_{mt}_{j}", [128, 128], BF16) for j in range(4)]
            for j in range(4):
                P.emit("s", lambda s, j=j, pt=pt, zT=zT: s.activation(zT[j][:, :], pt[:, j * 128:(j + 1) * 128], AF.Identity),
                       r=[ps.bufs[6]], w=[zT[j].b], tag=f"l2fztre{e}_{mt}_{j}")
            zb.free()
            if DBG.get("l2f_dump") and e == 0 and mt == 0:
                P.dma("qy", lambda y, zT=zT: y.dma_start(out=DBG["l2f_dump"]["zT0"], in_=zT[0][:]),
                      r=[zT[0].b], tag="dzT0")
            for dh in range(2):
                bank = 2 + dh
                for j in range(4):
                    P.emit("t", lambda t, j=j, dh=dh, bank=bank, zT=zT, we3=we3: t.matmul(
                        ps.banks[bank][:, :], zT[j][:, :], we3[j][:, dh * 512:(dh + 1) * 512],
                        start=(j == 0), stop=(j == 3)),
                        r=[zT[j].b, we3[j].b], w=[ps.bufs[bank]], tag=f"l2feo{e}_{mt}_{dh}_{j}")
            acc = mk.B4.get(f"l2facc{e}_{mt}", [128, D], F32)
            if e > 0:
                P.dma("qy", lambda y, mt=mt, acc=acc: y.dma_start(
                    out=acc[:], in_=eacc_d[mt * 128:(mt + 1) * 128, :]),
                    r=[eacc_db[mt]], w=[acc.b], tag=f"l2faccl{e}_{mt}")
            tmp = mk.B4.get(f"l2ftmp{e}_{mt}", [128, D], F32)
            for dh in range(2):
                P.emit("v", lambda v, dh=dh, tmp=tmp, b3r=b3r: v.tensor_add(
                    tmp[:, dh * 512:(dh + 1) * 512], ps.banks[2 + dh][:, :], b3r[:, dh * 512:(dh + 1) * 512]),
                    r=[ps.bufs[2 + dh], b3r.b], w=[tmp.b], tag=f"l2feba{e}_{mt}_{dh}")
            if DBG.get("l2f_dump") and e == 0 and mt == 0:
                P.dma("qy", lambda y, tmp=tmp: y.dma_start(out=DBG["l2f_dump"]["tmp"], in_=tmp[:]),
                      r=[tmp.b], tag="dtmp")
            if e > 0:
                P.emit("v", lambda v, e=e, mt=mt, acc=acc, tmp=tmp: v.scalar_tensor_tensor(
                    acc[:, :], tmp[:, :], gmeta[:, mt * 16 + e:mt * 16 + e + 1], acc[:, :],
                    ALU.mult, ALU.add),
                    r=[tmp.b, gmeta.b, acc.b], w=[acc.b], tag=f"l2fsel{e}_{mt}")
            else:
                P.emit("v", lambda v, mt=mt, acc=acc, tmp=tmp: v.tensor_scalar(
                    acc[:, :], tmp[:, :], gmeta[:, mt * 16:mt * 16 + 1], None, ALU.mult),
                    r=[tmp.b, gmeta.b], w=[acc.b], tag=f"l2fsel0_{mt}")
            tmp.free()
            P.dma("qy", lambda y, mt=mt, acc=acc: y.dma_start(
                out=eacc_d[mt * 128:(mt + 1) * 128, :], in_=acc[:]),
                r=[acc.b], w=[eacc_db[mt]], tag=f"l2faccs{e}_{mt}")
            acc.free()
            for tl in zT:
                tl.free()
        for tl in we1 + we2 + we3:
            tl.free()
        b1r.free(); b2r.free(); b3r.free()
    for tl in xm8:
        tl.free()
    selg.free(); selb.free()

    for mt in range(NMT):
        sog = mk.B4.get(f"l2fsog{mt}", [128, D], F32)
        P.dma("qg", lambda g, mt=mt, sog=sog: g.indirect_dma_start(
            sog[:], None, so_d[:, :], IndirectOffsetOnAxis(gidx[:, mt:mt + 1], 0)),
            r=so_db + [gidx.b], w=[sog.b], tag=f"l2fsog{mt}")
        acc = mk.B4.get(f"l2ffacc{mt}", [128, D], F32)
        P.dma("qy", lambda y, mt=mt, acc=acc: y.dma_start(
            out=acc[:], in_=eacc_d[mt * 128:(mt + 1) * 128, :]),
            r=[eacc_db[mt]], w=[acc.b], tag=f"l2ffal{mt}")
        P.emit("v", lambda v, mt=mt, sog=sog: v.tensor_scalar(
            sog[:, :], sog[:, :], gmeta[:, mt * 16 + 8:mt * 16 + 9], None, ALU.mult),
            r=[gmeta.b], w=[sog.b], tag=f"l2fshw{mt}")
        P.emit("v", lambda v, mt=mt, sog=sog, acc=acc: v.scalar_tensor_tensor(
            sog[:, :], acc[:, :], gmeta[:, mt * 16 + 9:mt * 16 + 10], sog[:, :],
            ALU.mult, ALU.add),
            r=[acc.b, gmeta.b, sog.b], w=[sog.b], tag=f"l2fcomb{mt}")
        acc.free()
        P.dma("qg", lambda g, mt=mt, sog=sog: g.indirect_dma_start(
            so_d[:, :], IndirectOffsetOnAxis(sidx[:, mt:mt + 1], 0), sog[:], None,
            bounds_check=T - 1, oob_is_err=False),
            r=[sog.b, sidx.b], w=so_db, tag=f"l2fscat{mt}")
        sog.free()
    gmeta.free()
    gidx.free(); sidx.free(); sgw.free(); sgb.free(); egw.free(); egb.free(); expb.free()

    if DBG.get("l2f_dump"):
        P.dma("qy", lambda y: y.dma_start(out=DBG["l2f_dump"]["so"], in_=so_d), r=so_db, tag="dso")
        P.dma("qy", lambda y: y.dma_start(out=DBG["l2f_dump"]["eacc"], in_=eacc_d), r=eacc_db, tag="deacc")
        gm2 = mk.B1.get("dgm", [128, 256], F32)
        P.emit("v", lambda v, gm2=gm2: v.tensor_copy(gm2[:], gmeta[:]), r=[gmeta.b], w=[gm2.b], tag="dgmc")
        P.dma("qy", lambda y, gm2=gm2: y.dma_start(out=DBG["l2f_dump"]["gmeta"], in_=gm2[:]), r=[gm2.b], tag="dgm")
        gm2.free()
    for i in range(NT):
        so = mk.B4.get(f"l2ffin{i}", [128, D], F32)
        P.dma("qy", lambda y, i=i, so=so: y.dma_start(
            out=so[:], in_=so_d[i * 128:(i + 1) * 128, :]),
            r=so_db, w=[so.b], tag=f"l2ffl{i}")
        xt32 = mk.B4.get(f"l2ffx{i}", [128, D], F32)
        P.dma("qy", lambda y, i=i, xt32=xt32: y.dma_start(
            out=xt32[:], in_=mk.x32[i * 128:(i + 1) * 128, :]),
            r=[mk.x32_b[i]], w=[xt32.b], tag=f"l2ffxl{i}")
        P.emit("v", lambda v, so=so, xt32=xt32: v.tensor_add(so[:, :], so[:, :], xt32[:, :]),
               r=[so.b, xt32.b], w=[so.b], tag=f"l2ffadd{i}")
        xt32.free()
        mk.ln_store(i, so, C_EPS5)
        so.free()


def l3_attn_local_only(mk):
    l3_attn(mk, mode="local")


def l3_attn_global_only(mk):
    l3_attn(mk, mode="global")


def l3_attn(mk, mode="full"):
    P, ps = mk.P, mk.psum
    if mode == "global":
        scr_d = mk.inp("scr", [T, D], BF16)
        scr_db = [Buf("l3scr")]
        _l3_global(mk, scr_d, scr_db)
        return
    winl = mk.inp("l3_winl", [1024, 3072])
    binl = [mk.load_row128(f"l3binl{j}", mk.inp("l3_binl_r", [128, 3072])[:, j * 1024:(j + 1) * 1024], 1024)
            for j in range(3)]
    woutl = mk.inp("l3_woutl", [1024, 1024])
    boutl = mk.load_row128("l3boutl", mk.inp("l3_boutl_r", [128, 1024]), 1024)

    qkv_d = mk.dram("l3_qkv", [T, 3072], BF16)
    qkv_db = [Buf(f"l3qkv{i}") for i in range(NT)]
    if mode == "local":
        contrib_d = mk.out("loc", [T, D], BF16)
    else:
        contrib_d = mk.dram("l3_contrib", [T, D], BF16)
    contrib_db = [Buf(f"l3ct{i}") for i in range(NT)]
    scr_d = mk.dram("l3_scr", [T, D], BF16)
    scr_db = [Buf("l3scr")]

    xT = mk.make_xT(tag="l3xT")
    for g6 in range(6):
        wc = mk.B8.get(f"l3w{g6}", [128, 4096], BF16)
        P.dma("qg", lambda g, g6=g6, wc=wc: g.dma_start(
            out=wc[:].rearrange("p (dc c) -> p dc c", dc=8),
            in_=winl[:, g6 * 512:(g6 + 1) * 512].rearrange("(dc p) c -> p dc c", p=128)),
            w=[wc.b], tag=f"l3wl{g6}")
        for i in range(NT):
            bank = i % 2
            for dc in range(8):
                P.emit("t", lambda t, i=i, dc=dc, bank=bank, wc=wc: t.matmul(
                    ps.banks[bank][:, :], xT[dc][:, i * 128:(i + 1) * 128],
                    wc[:, dc * 512:(dc + 1) * 512],
                    start=(dc == 0), stop=(dc == 7)),
                    r=[xT[dc].b, wc.b], w=[ps.bufs[bank]], tag=f"l3qkv{i}_{g6}_{dc}")
            ot = mk.B1.get(f"l3q{i}_{g6}", [128, 512], BF16)
            P.emit("v", lambda v, g6=g6, bank=bank, ot=ot, binl=binl: v.tensor_add(
                ot[:, :], ps.banks[bank][:, :],
                binl[g6 // 2][:, (g6 % 2) * 512:(g6 % 2 + 1) * 512]),
                r=[ps.bufs[bank], binl[g6 // 2].b], w=[ot.b], tag=f"l3qkve{i}_{g6}")
            P.dma("qy", lambda y, i=i, g6=g6, ot=ot: y.dma_start(
                out=qkv_d[i * 128:(i + 1) * 128, g6 * 512:(g6 + 1) * 512], in_=ot[:]),
                r=[ot.b], w=[qkv_db[i]], tag=f"l3qkvs{i}_{g6}")
            ot.free()
        wc.free()
    for tl in xT:
        tl.free()
    for b in binl:
        b.free()

    wol8 = [mk.B8.get(f"l3wol{j}", [128, 4096], BF16) for j in range(2)]
    for j in range(2):
        P.dma("qg", lambda g, j=j, wol8=wol8: g.dma_start(
            out=wol8[j][:].rearrange("p (a c) -> p a c", a=4),
            in_=woutl[j * 512:(j + 1) * 512, :].rearrange("(a p) c -> p a c", p=128)),
            w=[wol8[j].b], tag=f"l3woll{j}")
    wols = lambda hc: wol8[hc // 4][:, (hc % 4) * 1024:(hc % 4) * 1024 + 1024]
    for pt in range(4):
        qts = [mk.B2.get(f"l3lq{pt}_{c}", [128, 1024], BF16) for c in range(8)]
        kts = [mk.B2.get(f"l3lk{pt}_{c}", [128, 1024], BF16) for c in range(8)]
        for c in range(8):
            P.dma("qy", lambda y, pt=pt, c=c, qts=qts: y.dma_start(
                out=qts[c][:], in_=qkv_d[c * 512 + pt * 128:c * 512 + (pt + 1) * 128, 0:1024]),
                r=[qkv_db[(c * 512 + pt * 128) // 128]], w=[qts[c].b], tag=f"l3lql{pt}_{c}")
            P.dma("qy", lambda y, pt=pt, c=c, kts=kts: y.dma_start(
                out=kts[c][:], in_=qkv_d[c * 512 + pt * 128:c * 512 + (pt + 1) * 128, 1024:2048]),
                r=[qkv_db[(c * 512 + pt * 128) // 128]], w=[kts[c].b], tag=f"l3lkl{pt}_{c}")
        sc = mk.B4.get(f"l3sc{pt}", [128, 1024], F32)
        prodb = mk.B2.get(f"l3pr{pt}", [128, 1024], BF16)
        for i in range(8):
            for j in range(8):
                P.emit("v", lambda v, i=i, j=j, prodb=prodb, qts=qts, kts=kts: v.tensor_mul(
                    prodb[:, :], qts[i][:, :], kts[j][:, :]),
                    r=[qts[i].b, kts[j].b], w=[prodb.b], tag=f"l3m{pt}_{i}_{j}")
                P.emit("v", lambda v, i=i, j=j, prodb=prodb, sc=sc: v.tensor_reduce(
                    sc[:].rearrange("p (h i j) -> p h i j", h=16, i=8)[:, :, i, j],
                    prodb[:].rearrange("p (h c) -> p h c", h=16), AX.X, ALU.add),
                    r=[prodb.b], w=[sc.b], tag=f"l3r{pt}_{i}_{j}")
        prodb.free()
        for c in range(8):
            kts[c].free()
            qts[c].free()
        P.emit("v", lambda v, sc=sc: v.tensor_scalar(sc[:, :], sc[:, :], mk.cst[:, C_NEG125:C_NEG125 + 1], None, ALU.mult),
               r=[mk.cst_b], w=[sc.b], tag=f"l3scl{pt}")
        mx = mk.B4.get(f"l3mx{pt}", [128, 128], F32)
        P.emit("v", lambda v, sc=sc, mx=mx: v.tensor_reduce(
            mx[:, :], sc[:].rearrange("p (hi j) -> p hi j", j=8), AX.X, ALU.min),
            r=[sc.b], w=[mx.b], tag=f"l3mx{pt}")
        for j in range(8):
            P.emit("v", lambda v, j=j, sc=sc, mx=mx: v.tensor_sub(
                sc[:].rearrange("p (hi j) -> p hi j", j=8)[:, :, j],
                sc[:].rearrange("p (hi j) -> p hi j", j=8)[:, :, j], mx[:, :]),
                r=[mx.b], w=[sc.b], tag=f"l3sub{pt}_{j}")
        P.emit("s", lambda s, sc=sc: s.activation(sc[:, :], sc[:, :], AF.Exp, scale=-1.0),
               r=[sc.b], w=[sc.b], tag=f"l3exp{pt}")
        P.emit("v", lambda v, sc=sc, mx=mx: v.tensor_reduce(
            mx[:, :], sc[:].rearrange("p (hi j) -> p hi j", j=8), AX.X, ALU.add),
            r=[sc.b], w=[mx.b], tag=f"l3z{pt}")
        P.emit("v", lambda v, mx=mx: v.reciprocal(mx[:, :], mx[:, :]),
               r=[mx.b], w=[mx.b], tag=f"l3rz{pt}")
        for j in range(8):
            P.emit("v", lambda v, j=j, sc=sc, mx=mx: v.tensor_mul(
                sc[:].rearrange("p (hi j) -> p hi j", j=8)[:, :, j],
                sc[:].rearrange("p (hi j) -> p hi j", j=8)[:, :, j], mx[:, :]),
                r=[mx.b], w=[sc.b], tag=f"l3nm{pt}_{j}")
        mx.free()
        vts = [mk.B2.get(f"l3lv{pt}_{c}", [128, 1024], BF16) for c in range(8)]
        for c in range(8):
            P.dma("qy", lambda y, pt=pt, c=c, vts=vts: y.dma_start(
                out=vts[c][:], in_=qkv_d[c * 512 + pt * 128:c * 512 + (pt + 1) * 128, 2048:3072]),
                r=[qkv_db[(c * 512 + pt * 128) // 128]], w=[vts[c].b], tag=f"l3lvl{pt}_{c}")
        for i in range(8):
            oi = mk.B4.get(f"l3oi{pt}_{i}", [128, 1024], F32)
            for j in range(8):
                for h in range(16):
                    eng = "v"
                    col = h * 64 + i * 8 + j
                    if j == 0:
                        P.emit(eng, lambda v, h=h, j=j, col=col, oi=oi, sc=sc, vts=vts: v.tensor_scalar(
                            oi[:, h * 64:(h + 1) * 64], vts[j][:, h * 64:(h + 1) * 64],
                            sc[:, col:col + 1], None, ALU.mult),
                            r=[vts[j].b, sc.b], w=[oi.b], tag=f"l3av{pt}_{i}_{j}_{h}")
                    else:
                        P.emit(eng, lambda v, h=h, j=j, col=col, oi=oi, sc=sc, vts=vts: v.scalar_tensor_tensor(
                            oi[:, h * 64:(h + 1) * 64], vts[j][:, h * 64:(h + 1) * 64],
                            sc[:, col:col + 1], oi[:, h * 64:(h + 1) * 64], ALU.mult, ALU.add),
                            r=[vts[j].b, sc.b, oi.b], w=[oi.b], tag=f"l3av{pt}_{i}_{j}_{h}")
            ob = mk.B2.get(f"l3ob{pt}_{i}", [128, 1024], BF16)
            P.emit("v", lambda v, oi=oi, ob=ob: v.tensor_copy(ob[:, :], oi[:, :]),
                   r=[oi.b], w=[ob.b], tag=f"l3obc{pt}_{i}")
            oi.free()
            ptp = ps.view(6, [128, 1024], BF16)
            for j8 in range(8):
                P.emit("t", lambda t, j8=j8, ob=ob, ptp=ptp: t.transpose(
                    ptp[:, j8 * 128:(j8 + 1) * 128], ob[:, j8 * 128:(j8 + 1) * 128], mk.ident[:]),
                    r=[ob.b, mk.ident_b], w=[ps.bufs[6]], tag=f"l3otr{pt}_{i}_{j8}")
            otb = mk.B2.get(f"l3otb{pt}_{i}", [128, 1024], BF16)
            P.emit("s", lambda s, otb=otb, ptp=ptp: s.activation(otb[:, :], ptp[:, :], AF.Identity),
                   r=[ps.bufs[6]], w=[otb.b], tag=f"l3otre{pt}_{i}")
            ob.free()
            for dh in range(2):
                bank = 4 + dh
                for hc in range(8):
                    P.emit("t", lambda t, hc=hc, dh=dh, bank=bank, otb=otb: t.matmul(
                        ps.banks[bank][:, :], otb[:, hc * 128:(hc + 1) * 128],
                        wols(hc)[:, dh * 512:(dh + 1) * 512],
                        start=(hc == 0), stop=(hc == 7)),
                        r=[otb.b, wol8[hc // 4].b], w=[ps.bufs[bank]], tag=f"l3oo{pt}_{i}_{dh}_{hc}")
            lo = mk.B2.get(f"l3lo{pt}_{i}", [128, 1024], BF16)
            for dh in range(2):
                P.emit("v", lambda v, dh=dh, lo=lo: v.tensor_add(
                    lo[:, dh * 512:(dh + 1) * 512], ps.banks[4 + dh][:, :],
                    boutl[:, dh * 512:(dh + 1) * 512]),
                    r=[ps.bufs[4 + dh], boutl.b], w=[lo.b], tag=f"l3loe{pt}_{i}_{dh}")
            otb.free()
            P.dma("qy", lambda y, pt=pt, i=i, lo=lo: y.dma_start(
                out=contrib_d[i * 512 + pt * 128:i * 512 + (pt + 1) * 128, :], in_=lo[:]),
                r=[lo.b], w=[contrib_db[(i * 512 + pt * 128) // 128]], tag=f"l3los{pt}_{i}")
            lo.free()
        for c in range(8):
            vts[c].free()
        sc.free()
    for wc in wol8:
        wc.free()
    boutl.free()

    if mode == "local":
        return
    if DBG.get("l3_noa2a"):
        for i in range(NT):
            tl = mk.B2.get(f"l3cp{i}", [128, D], BF16)
            P.dma("qy", lambda y, i=i, tl=tl: y.dma_start(
                out=tl[:], in_=contrib_d[i * 128:(i + 1) * 128, :]),
                r=[contrib_db[i]], w=[tl.b], tag=f"l3cpl{i}")
            P.dma("qy", lambda y, i=i, tl=tl: y.dma_start(
                out=scr_d[i * 128:(i + 1) * 128, :], in_=tl[:]),
                r=[tl.b], w=scr_db, tag=f"l3cps{i}")
            tl.free()
    else:
        from concourse.replica_groups import maybe_share_collective_output_space
        space = maybe_share_collective_output_space("AllGather", [[c for c in range(8)]])
        gat_d = mk.nc.dram_tensor("l3_gat", [8 * T, D], BF16, kind="Internal",
                                  addr_space=space).ap()
        gat_db = [Buf("l3gat")]
        P.coll(lambda g: g.collective_compute(
            "AllGather", ALU.bypass, [[c for c in range(8)]], [contrib_d[:, :]], [gat_d[:, :]]),
            r=contrib_db, w=gat_db, tag="l3ag")
        gidx3 = mk.load_row128("l3gidx", mk.inp("l3_gidx", [128, 32], I32), 32, dtype=I32, pool=mk.SM)
        for i in range(NT):
            tl = mk.B2.get(f"l3sl{i}", [128, D], BF16)
            P.dma("qg", lambda g, i=i, tl=tl, gat_d=gat_d, gidx3=gidx3: g.indirect_dma_start(
                tl[:], None, gat_d[:, :], IndirectOffsetOnAxis(gidx3[:, i:i + 1], 0)),
                r=gat_db + [gidx3.b], w=[tl.b], tag=f"l3slg{i}")
            P.dma("qy", lambda y, i=i, tl=tl: y.dma_start(
                out=scr_d[i * 128:(i + 1) * 128, :], in_=tl[:]),
                r=[tl.b], w=scr_db, tag=f"l3sls{i}")
            tl.free()
        gidx3.free()

    _l3_global(mk, scr_d, scr_db)


def _l3_global(mk, scr_d, scr_db):
    P, ps = mk.P, mk.psum
    wing_qk = mk.inp("l3_wing_qk", [1024, 2048])
    bing_qk = mk.load_row128("l3bingqk", mk.inp("l3_bing_qk_p", [128, 16]), 16, pool=mk.SM)
    wing_v = mk.inp("l3_wing_v", [1024, 1024])
    bing_v = mk.load_row128("l3bingv", mk.inp("l3_bing_v_r", [128, 1024]), 1024)
    woutg = mk.inp("l3_woutg", [1024, 1024])
    boutg = mk.load_row128("l3boutg", mk.inp("l3_boutg_r", [128, 1024]), 1024)
    qt_d = mk.dram("l3g_qt", [1024, T], BF16)
    kt_d = mk.dram("l3g_kt", [1024, T], BF16)
    v_d = mk.dram("l3g_v", [8, 128, T], BF16)
    ot_d = mk.dram("l3g_ot", [1024, T], BF16)
    qt_db = [Buf(f"l3gqt{g}") for g in range(8)]
    kt_db = [Buf(f"l3gkt{g}") for g in range(8)]
    v_db = [Buf(f"l3gv{i}") for i in range(NT)]
    ot_db = [Buf(f"l3got{g}") for g in range(8)]
    sT = mk.make_xT(src=scr_d, src_b=scr_db, tag="l3sT")
    proj_to_rows(mk, wing_qk, bing_qk, None, None, sT, 16, "l3gqk", dst2=(qt_d, qt_db, kt_d, kt_db))
    proj_v(mk, wing_v, bing_v, v_d, v_db, sT, "l3gv")
    for tl in sT:
        tl.free()
    bing_qk.free(); bing_v.free()
    attn_core(mk, qt_d, qt_db, kt_d, kt_db, v_d, v_db, ot_d, ot_db, "l3g")
    attn_outproj(mk, ot_d, ot_db, woutg, boutg, "l3go")
    boutg.free()



def host_consts(inputs):
    cst = np.zeros((128, N_CST), np.float32)
    cst[:, C_EPS5] = 1e-5
    cst[:, C_EPS3] = 1e-3
    cst[:, C_INV1024] = 1.0 / 1024.0
    cst[:, C_INV512] = 1.0 / 512.0
    cst[:, C_S0] = np.float32(inputs["L1_ffn_s0"])
    cst[:, C_S1] = np.float32(inputs["L1_ffn_s1"])
    cst[:, C_S2] = np.float32(inputs["L1_ffn_s2"])
    cst[:, C_HALF] = 0.5
    cst[:, C_ONE] = 1.0
    cst[:, C_NEG125] = -0.125
    cst[:, C_NBIG] = -1e30
    cst[:, C_INV64] = 1.0 / 64.0
    cst[:, C_FOUR] = 4.0
    cst[:, C_TWO] = 2.0
    cst[:, C_NEG1] = -1.0
    return cst


def host_l0a(inputs):
    rep = lambda a: np.tile(np.asarray(a, np.float32).reshape(1, -1), (128, 1))
    perp = lambda a: np.ascontiguousarray(np.asarray(a, np.float32).reshape(-1, 128).T)
    return {
        "l0_wq": np.ascontiguousarray(inputs["L0_attn_Wq"].reshape(1024, 1024)),
        "l0_wk": np.ascontiguousarray(inputs["L0_attn_Wk"].reshape(1024, 1024)),
        "l0_wv": np.ascontiguousarray(inputs["L0_attn_Wv"].reshape(1024, 1024)),
        "l0_wo": np.ascontiguousarray(inputs["L0_attn_Wo"].reshape(1024, 1024)),
        "l0_bq_p": perp(inputs["L0_attn_bq"]),
        "l0_bk_p": perp(inputs["L0_attn_bk"]),
        "l0_bv_r": rep(inputs["L0_attn_bv"]),
        "l0_bo_r": rep(inputs["L0_attn_bo"]),
        "l0_gamma_r": rep(inputs["L0_attn_gamma"]),
        "l0_beta_r": rep(inputs["L0_attn_beta"]),
    }


def host_l0f(inputs):
    return {
        "l0_gate": np.ascontiguousarray(inputs["L0_ffn_gate_w"].T),
        "l0_w1t": np.ascontiguousarray(np.transpose(inputs["L0_ffn_w1"], (0, 2, 1))),
        "l0_w2t": np.ascontiguousarray(np.transpose(inputs["L0_ffn_w2"], (0, 2, 1))),
    }


def host_l1(inputs):
    W = inputs["L1_attn_W_qkv"].T
    qcols = np.concatenate([W[:, r * 192:r * 192 + 64] for r in range(16)], axis=1)
    kcols = np.concatenate([W[:, r * 192 + 64:r * 192 + 128] for r in range(16)], axis=1)
    vcols = np.concatenate([W[:, r * 192 + 128:r * 192 + 192] for r in range(16)], axis=1)
    b = inputs["L1_attn_b_qkv"]
    bq = np.concatenate([b[r * 192:r * 192 + 64] for r in range(16)])
    bk = np.concatenate([b[r * 192 + 64:r * 192 + 128] for r in range(16)])
    bv = np.concatenate([b[r * 192 + 128:r * 192 + 192] for r in range(16)])
    bqk = np.concatenate([bq, bk]).reshape(16, 128).T
    return {
        "l1_wqk": np.ascontiguousarray(np.concatenate([qcols, kcols], axis=1)),
        "l1_bqk_p": np.ascontiguousarray(bqk),
        "l1_wv": np.ascontiguousarray(vcols),
        "l1_bv_r": np.tile(bv[None, :], (128, 1)),
        "l1_wo": np.ascontiguousarray(inputs["L1_attn_W_o"].T),
        "l1_bo_r": np.tile(inputs["L1_attn_b_o"][None, :], (128, 1)),
        "l1_k0": inputs["L1_ffn_k0"].astype(np.uint8),
        "l1_k1": inputs["L1_ffn_k1"].astype(np.uint8),
        "l1_k2": inputs["L1_ffn_k2"].astype(np.uint8),
    }


def host_l2a(inputs):
    perp = lambda a: np.ascontiguousarray(np.asarray(a, np.float32).reshape(-1, 128).T)
    return {
        "l2_wf": np.ascontiguousarray(inputs["L2_attn_W_f"].T),
        "l2_wi": np.ascontiguousarray(inputs["L2_attn_W_i"].T),
        "l2_wh": np.ascontiguousarray(inputs["L2_attn_W_h"].T),
        "l2_bf_p": perp(inputs["L2_attn_b_f"]),
        "l2_bi_p": perp(inputs["L2_attn_b_i"]),
        "l2_bh_p": perp(inputs["L2_attn_b_h"]),
    }


def host_l2f(inputs, core):
    rep = lambda a: np.tile(np.asarray(a, np.float32).reshape(1, -1), (128, 1))
    ib, it = inputs["L2_ffn_index_b"], inputs["L2_ffn_index_t"]
    sel = [int(t) for b, t in zip(ib, it) if int(b) == core]
    gi = np.zeros(2048, np.int32)
    si = np.full(2048, 1 << 20, np.int32)
    gi[:len(sel)] = sel
    si[:len(sel)] = sel
    perp16 = lambda a: np.ascontiguousarray(a.reshape(16, 128).T)
    return {
        "l2f_sw1t": np.ascontiguousarray(inputs["L2_ffn_sw1"].T),
        "l2f_sw2t": np.ascontiguousarray(inputs["L2_ffn_sw2"].T),
        "l2f_sw3t": np.ascontiguousarray(inputs["L2_ffn_sw3"].T),
        "l2f_sb1_r": rep(inputs["L2_ffn_sb1"]),
        "l2f_sb2_r": rep(inputs["L2_ffn_sb2"]),
        "l2f_sb3_r": rep(inputs["L2_ffn_sb3"]),
        "l2f_slng_r": rep(inputs["L2_ffn_s_ln_g"]),
        "l2f_slnb_r": rep(inputs["L2_ffn_s_ln_b"]),
        "l2f_gidx": perp16(gi),
        "l2f_sidx": perp16(si),
        "l2f_sgw_p": np.ascontiguousarray(inputs["L2_ffn_sg_w"].reshape(8, 128).T),
        "l2f_sgb_p": np.tile(np.asarray(inputs["L2_ffn_sg_b"], np.float32).reshape(1, 1), (128, 1)),
        "l2f_egw": np.ascontiguousarray(inputs["L2_ffn_eg_w"].T.reshape(8, 128, 8).transpose(1, 0, 2).reshape(128, 64)),
        "l2f_egb_r": rep(inputs["L2_ffn_eg_b"]),
        "l2f_expb_r": rep(inputs["L2_ffn_exp_bias"]),
        "l2f_ew1t": np.ascontiguousarray(np.transpose(inputs["L2_ffn_ew1"], (0, 2, 1))),
        "l2f_ew2t": np.ascontiguousarray(np.transpose(inputs["L2_ffn_ew2"], (0, 2, 1))),
        "l2f_ew3t": np.ascontiguousarray(np.transpose(inputs["L2_ffn_ew3"], (0, 2, 1))),
        "l2f_eb1_r": np.ascontiguousarray(np.tile(inputs["L2_ffn_eb1"][:, None, :], (1, 128, 1))),
        "l2f_eb2_r": np.ascontiguousarray(np.tile(inputs["L2_ffn_eb2"][:, None, :], (1, 128, 1))),
        "l2f_eb3_r": np.ascontiguousarray(np.tile(inputs["L2_ffn_eb3"][:, None, :], (1, 128, 1))),
        "l2f_sellng_r": rep(inputs["L2_ffn_sel_ln_g"]),
        "l2f_sellnb_r": rep(inputs["L2_ffn_sel_ln_b"]),
    }


def host_l3a(inputs):
    rep = lambda a: np.tile(np.asarray(a, np.float32).reshape(1, -1), (128, 1))
    Wg = inputs["L3_attn_w_in_g"].T
    bg = inputs["L3_attn_b_in_g"]
    return {
        "l3_winl": np.ascontiguousarray(inputs["L3_attn_w_in_l"].T),
        "l3_binl_r": rep(inputs["L3_attn_b_in_l"]),
        "l3_woutl": np.ascontiguousarray(inputs["L3_attn_w_out_l"].T),
        "l3_boutl_r": rep(inputs["L3_attn_b_out_l"]),
        "l3_wing_qk": np.ascontiguousarray(Wg[:, 0:2048]),
        "l3_bing_qk_p": np.ascontiguousarray(bg[0:2048].reshape(16, 128).T),
        "l3_wing_v": np.ascontiguousarray(Wg[:, 2048:3072]),
        "l3_bing_v_r": rep(bg[2048:3072]),
        "l3_woutg": np.ascontiguousarray(inputs["L3_attn_w_out_g"].T),
        "l3_boutg_r": rep(inputs["L3_attn_b_out_g"]),
    }


def host_l3f(inputs):
    return {
        "l3f_w1": np.ascontiguousarray(inputs["L3_ffn_W1"]),
        "l3f_b1p": np.ascontiguousarray(inputs["L3_ffn_b1"].reshape(32, 128).T),
        "l3f_w2": np.ascontiguousarray(inputs["L3_ffn_W2"]),
        "l3f_b2r": np.tile(inputs["L3_ffn_b2"][None, :], (128, 1)),
    }



def build_full():
    nc = bass.Bass("TRN2", target_bir_lowering=False)
    with nc.cleanup_on_exit():
        mk = MK(nc, set())
        mk.prologue()
        l0_ffn_prep(mk)
        l1_ffn_prep(mk)
        mk.load_x()
        l0_attn(mk)
        l0_ffn(mk)
        l1_attn(mk)
        l1_ffn(mk)
        l2_attn(mk)
        l2_ffn(mk)
        l3_attn(mk)
        l3_ffn(mk)
        mk.store_out()
        mk.P.finalize()
        nc.all_engine_barrier()
    return nc, mk


def host_all(inputs):
    m = {"cst": host_consts(inputs), "ident": np.eye(128, dtype=np.float32)}
    m.update(host_l0a(inputs))
    m.update(host_l0f(inputs))
    m.update(host_l1(inputs))
    m.update(host_l2a(inputs))
    m.update(host_l3a(inputs))
    m.update(host_l3f(inputs))
    return m


def build_part(layers, preps, store=True):
    nc = bass.Bass("TRN2", target_bir_lowering=False)
    with nc.cleanup_on_exit():
        mk = MK(nc, set())
        mk.prologue()
        for p in preps:
            p(mk)
        mk.load_x()
        for l in layers:
            l(mk)
        if store:
            mk.store_out()
        mk.P.finalize()
        nc.all_engine_barrier()
    return nc, mk


def kernel_split(**inputs):
    DBG.clear()
    common = host_all(inputs)
    x = np.ascontiguousarray(inputs["L0_attn_x"], dtype=np.float32)
    xc = [np.ascontiguousarray(x[c]) for c in range(8)]
    percore = [host_l2f(inputs, c) for c in range(8)]
    for c in range(8):
        gi3 = np.zeros((128, 32), np.int32)
        for i in range(32):
            b, jj = i // 4, i % 4
            gi3[:, i] = b * T + c * 512 + jj * 128 + np.arange(128)
        percore[c]["l3_gidx"] = gi3
    parts = [([l0_attn, l0_ffn], [l0_ffn_prep], True, "o"),
             ([l1_attn, l1_ffn], [l1_ffn_prep], True, "o"),
             ([l2_attn, l2_ffn], [], True, "o"),
             ([l3_attn_local_only], [], False, "loc"),
             ([l3_attn_global_only, l3_ffn], [], True, "o")]
    scrs = None
    for layers, preps, store, outname in parts:
        nc, mk = build_part(layers, preps, store=store)
        in_maps = []
        for c in range(8):
            m = {k: common[k] for k in mk.io_names if k in common}
            m.update({k: percore[c][k] for k in mk.io_names if k in percore[c]})
            m["x"] = xc[c]
            if "scr" in mk.io_names:
                m["scr"] = scrs[c]
            in_maps.append(m)
        res = run_bass_kernel_spmd(nc, in_maps, core_ids=list(range(8)))
        if outname == "o":
            xc = [np.ascontiguousarray(np.asarray(res.results[c]["o"], np.float32)) for c in range(8)]
        else:
            loc = [np.asarray(res.results[c]["loc"]) for c in range(8)]
            scrs = [np.ascontiguousarray(np.concatenate(
                [loc[b][c * 512:(c + 1) * 512] for b in range(8)], axis=0)) for c in range(8)]
    return np.stack(xc, axis=0)


def kernel(**inputs):
    return kernel_split(**inputs)


def kernel_fused(**inputs):
    DBG.clear()
    nc, mk = build_full()
    common = host_all(inputs)
    x = np.ascontiguousarray(inputs["L0_attn_x"], dtype=np.float32)
    in_maps = []
    for c in range(8):
        m = dict(common)
        m["x"] = np.ascontiguousarray(x[c])
        m.update(host_l2f(inputs, c))
        gi3 = np.zeros((128, 32), np.int32)
        for i in range(32):
            b, jj = i // 4, i % 4
            gi3[:, i] = b * T + c * 512 + jj * 128 + np.arange(128)
        m["l3_gidx"] = gi3
        in_maps.append(m)
    res = run_bass_kernel_spmd(nc, in_maps, core_ids=list(range(8)))
    return np.stack([np.asarray(res.results[c]["o"], np.float32) for c in range(8)], axis=0)



# version 05
# speedup vs baseline: 1.0126x; 1.0126x over previous
import math
import numpy as np
import ml_dtypes
import concourse.bass as bass
import concourse.mybir as mybir
from concourse.bass_utils import run_bass_kernel_spmd
from concourse.bass_core_rust import IndirectOffsetOnAxis

F32 = mybir.dt.float32
BF16 = mybir.dt.bfloat16
U8 = mybir.dt.uint8
I32 = mybir.dt.int32
AF = mybir.ActivationFunctionType
ALU = mybir.AluOpType
AX = mybir.AxisListType
BF = ml_dtypes.bfloat16

DBG = {}

D = 1024
T = 4096
NT = T // 128
NH = 16
HD = 64



class Buf:
    __slots__ = ("name", "w", "rs")

    def __init__(self, name):
        self.name = name
        self.w = None
        self.rs = []


class Op:
    __slots__ = ("kind", "eng", "q", "fn", "deps", "eidx", "qidx", "inc", "tag")

    def __init__(self, kind, eng, q, fn, deps, eidx, qidx, tag):
        self.kind, self.eng, self.q, self.fn = kind, eng, q, fn
        self.deps, self.eidx, self.qidx, self.tag = deps, eidx, qidx, tag
        self.inc = False


class Prog:
    ENGINES = ("t", "v", "s", "g", "y")
    QUEUES = {"qy": "y", "qg": "g", "qs": "s"}

    def __init__(self, nc):
        self.nc = nc
        self.stream = {e: [] for e in self.ENGINES}
        self.ecount = {e: 0 for e in self.ENGINES}
        self.qcount = {q: 0 for q in self.QUEUES}
        self.qops = {q: [] for q in self.QUEUES}
        self.n_same_engine_waits = 0
        self.relax_same_engine = False
        self.trace_plan = None

    def _deps_for(self, r, w):
        deps = set()
        for b in r:
            if b.w is not None:
                deps.add(b.w)
        for b in w:
            if b.w is not None:
                deps.add(b.w)
            deps.update(b.rs)
        return deps

    def _update_bufs(self, op, r, w):
        for b in r:
            b.rs.append(op)
        for b in w:
            b.w = op
            b.rs = []

    def emit(self, eng, fn, r=(), w=(), tag=""):
        op = Op("e", eng, None, fn, self._deps_for(r, w), self.ecount[eng], None, tag)
        self.ecount[eng] += 1
        self.stream[eng].append(op)
        self._update_bufs(op, r, w)
        return op

    def dma(self, q, fn, r=(), w=(), tag=""):
        eng = self.QUEUES[q]
        op = Op("d", eng, q, fn, self._deps_for(r, w), None, self.qcount[q], tag)
        self.qcount[q] += 1
        self.qops[q].append(op)
        self.stream[eng].append(op)
        self._update_bufs(op, r, w)
        return op

    def coll(self, fn, r=(), w=(), tag=""):
        op = Op("c", "g", None, fn, self._deps_for(r, w), None, len(getattr(self, "colls", [])), tag)
        if not hasattr(self, "colls"):
            self.colls = []
        self.colls.append(op)
        self.stream["g"].append(op)
        self._update_bufs(op, r, w)
        return op

    def barrier(self, bufs, tag="barrier"):
        raise NotImplementedError

    NQSEM = 10

    def finalize(self):
        nc = self.nc
        M = self.NQSEM
        for e in self.ENGINES:
            for op in self.stream[e]:
                if op.kind == "d" and op.qidx >= M:
                    op.deps.add(self.qops[op.q][op.qidx - M])
        for e in self.ENGINES:
            for op in self.stream[e]:
                for d in op.deps:
                    if d.kind == "e":
                        d.inc = True
        self.incval = {}
        for e in self.ENGINES:
            c = 0
            for op in self.stream[e]:
                if op.kind == "e":
                    if op.inc:
                        c += 1
                    self.incval[id(op)] = c
        self.sem_e = {e: nc.alloc_semaphore(f"pe_{e}") for e in self.ENGINES}
        self.sem_q = {q: [nc.alloc_semaphore(f"pq_{q}{j}") for j in range(M)]
                      for q in self.QUEUES}
        esec = {"t": "tensor", "v": "vector", "s": "scalar", "g": "gpsimd", "y": "sync"}

        self.sem_c = {}
        for op in getattr(self, "colls", []):
            self.sem_c[id(op)] = nc.alloc_semaphore(f"pc{op.qidx}")

        def resolve(d):
            if d.kind == "e":
                return (self.sem_e[d.eng], self.incval[id(d)])
            if d.kind == "c":
                return (self.sem_c[id(d)], 16)
            return (self.sem_q[d.q][d.qidx % M], 16 * (d.qidx // M + 1))

        M = self.NQSEM
        def body_for(e):
            ops = self.stream[e]
            sems = self.sem_e
            qsems = self.sem_q

            def body(eng):
                known = {}
                for op in ops:
                    trace_waits = []
                    for d in sorted(op.deps, key=lambda d: (d.kind, d.eng, d.q or "", d.eidx or 0, d.qidx or 0)):
                        if self.relax_same_engine and d.kind == "e" and d.eng == e:
                            continue
                        sem, val = resolve(d)
                        if val == 0:
                            continue
                        k = id(sem)
                        if known.get(k, 0) >= val:
                            continue
                        if d.kind == "e" and d.eng == e:
                            self.n_same_engine_waits += 1
                        eng.wait_ge(sem, val)
                        trace_waits.append((d.eng if d.kind == "e" else d.q, val))
                        known[k] = val
                    inst = op.fn(eng)
                    if op.kind == "d":
                        inst.then_inc(qsems[op.q][op.qidx % M], 16)
                    elif op.kind == "c":
                        inst.then_inc(self.sem_c[id(op)], 16)
                    elif op.inc:
                        inst.then_inc(sems[e], 1)
                    if self.trace_plan is not None:
                        self.trace_plan.append((e, op.kind, op.tag, op.inc,
                                                self.incval.get(id(op)), op.qidx, trace_waits))
                if e == "y":
                    for q in self.QUEUES:
                        for j in range(M):
                            gens = (self.qcount[q] - j + M - 1) // M
                            tot = 16 * gens
                            if tot > 0 and known.get(id(qsems[q][j]), 0) < tot:
                                eng.wait_ge(qsems[q][j], tot)
                    for op2 in getattr(self, "colls", []):
                        if known.get(id(self.sem_c[id(op2)]), 0) < 16:
                            eng.wait_ge(self.sem_c[id(op2)], 16)
                    for ee in self.ENGINES:
                        tot = self.incval.get(id(self.stream[ee][-1]), 0) if self.stream[ee] else 0
                        if ee != "y" and tot and known.get(id(sems[ee]), 0) < tot:
                            eng.wait_ge(sems[ee], tot)
            return body

        with nc.Block() as block:
            for e, secname in esec.items():
                if self.stream[e] or e == "y":
                    getattr(block, secname)(body_for(e))


def ceil_div(a, b):
    return (a + b - 1) // b



class Tile:
    __slots__ = ("t", "b", "slot", "pool")

    def __init__(self, t, b, slot, pool):
        self.t, self.b, self.slot, self.pool = t, b, slot, pool

    def __getitem__(self, idx):
        return self.t[idx]

    def free(self):
        self.pool.put(self)


class Pool:

    def __init__(self, nc, name, nbytes, count, prog=None):
        self.prog = prog
        self.nc, self.name, self.nbytes = nc, name, nbytes
        backing = nc.alloc_sbuf_tensor(f"pool_{name}", [128, nbytes * count], U8)
        self.base = nc.lookup_mloc(backing).addr
        self.free_slots = list(range(count))
        self.carry = [[] for _ in range(count)]
        self.n = 0
        self.count = count
        self.watermark = 0

    def get(self, name, shape, dtype):
        assert elem_bytes(shape, dtype) <= self.nbytes, (self.name, name, shape, dtype)
        assert self.free_slots, f"pool {self.name} exhausted ({self.count} slots) at {name}"
        slot = self.free_slots.pop(0)
        self.watermark = max(self.watermark, self.count - len(self.free_slots))
        nc = self.nc
        self.n += 1
        h = nc.virtual_sbuf_tensor(f"{self.name}_{name}_{self.n}", shape, dtype)
        h.manual_base_name = f"pool_{self.name}"
        h.manual_sbuf_range = (slot * self.nbytes, elem_bytes(shape, dtype))
        nc._tensor(h.name, h.shape, h.dtype, type="SB")
        mloc = nc.lookup_mloc(h)
        mloc.allocated = True
        mloc.addr = self.base + slot * self.nbytes
        b = Buf(f"{self.name}_{name}_{self.n}")
        b.rs = list(self.carry[slot])
        tile = Tile(h, b, slot, self)
        if DBG.get("sim_init") and self.prog is not None:
            self.prog.emit("v", lambda v, h=h: v.memset(h[:], 0.0), w=[b], tag=f"simi_{name}")
        return tile

    def put(self, tile):
        ops = list(tile.b.rs)
        if tile.b.w is not None:
            ops.append(tile.b.w)
        self.carry[tile.slot] = ops
        self.free_slots.append(tile.slot)


def elem_bytes(shape, dtype):
    n = 1
    for s in shape[1:]:
        n *= s
    return n * mybir.dt.size(dtype)


class Psum:

    def __init__(self, nc):
        self.nc = nc
        self.banks = []
        self.bufs = []
        for i in range(8):
            t = nc.alloc_psum_tensor(f"bank{i}", [128, 512], F32)
            assert nc.lookup_mloc(t).bank == i
            self.banks.append(t)
            self.bufs.append(Buf(f"psum{i}"))
        self.n = 0

    def view(self, bank, shape, dtype):
        assert elem_bytes(shape, dtype) <= 2048
        nc = self.nc
        self.n += 1
        h = nc.virtual_psum_tensor(f"pb{bank}_{self.n}", shape, dtype)
        h.manual_base_name = f"bank{bank}"
        h.manual_sbuf_range = (0, elem_bytes(shape, dtype))
        nc._tensor(h.name, h.shape, h.dtype, type="PSUM")
        mloc = nc.lookup_mloc(h)
        mloc.allocated = True
        mloc.bank = bank
        return h


C_EPS5 = 0
C_EPS3 = 1
C_INV1024 = 2
C_INV512 = 3
C_S0 = 4
C_S1 = 5
C_S2 = 6
C_HALF = 7
C_ONE = 8
C_SCL0 = 9
C_SCL1 = 10
C_EPS5B = 11
N_CST = 32


class MK:

    def __init__(self, nc, io_names):
        self.nc = nc
        self.P = Prog(nc)
        self.psum = Psum(nc)
        self.io = {}
        self.io_names = io_names

        self.B8 = Pool(nc, "B8", 8448, 12, self.P)
        self.B4 = Pool(nc, "B4", 4096, 8, self.P)
        self.B2 = Pool(nc, "B2", 2048, 18, self.P)
        self.B1 = Pool(nc, "B1", 1024, 16, self.P)
        self.SM = Pool(nc, "SM", 256, 12, self.P)

        self.x32 = nc.dram_tensor("x32", [T, D], F32, kind="Internal").ap()
        self.xbf = nc.dram_tensor("xbf", [T, D], BF16, kind="Internal").ap()
        self.x32_b = [Buf(f"x32_{i}") for i in range(NT)]
        self.xbf_b = [Buf(f"xbf_{i}") for i in range(NT)]

        self.cst = nc.alloc_sbuf_tensor("cstt", [128, N_CST], F32)
        self.cst_b = Buf("cst")
        self.ones = nc.alloc_sbuf_tensor("onest", [1, 128], BF16)
        self.ones_b = Buf("ones")
        self.ones_col = nc.alloc_sbuf_tensor("onesc", [128, 1], BF16)
        self.ones_col_b = Buf("ones_col")
        self.ident = nc.alloc_sbuf_tensor("identt", [128, 128], BF16)
        self.ident_b = Buf("ident")
        self.trash4 = self.B4.get("trash", [128, 1024], F32)
        self.stats = nc.alloc_sbuf_tensor("stats", [128, 16], F32)
        self.stats_b = Buf("stats")

    def inp(self, name, shape, dtype=F32):
        if name not in self.io:
            self.io[name] = self.nc.dram_tensor(name, list(shape), dtype,
                                                kind="ExternalInput").ap()
            self.io_names.add(name)
        return self.io[name]

    def out(self, name, shape, dtype=F32):
        if name not in self.io:
            self.io[name] = self.nc.dram_tensor(name, list(shape), dtype,
                                                kind="ExternalOutput").ap()
        return self.io[name]

    def dram(self, name, shape, dtype):
        return self.nc.dram_tensor(name, list(shape), dtype, kind="Internal").ap()

    def c(self, col):
        return self.cst[:, col:col + 1]

    def prologue(self):
        P = self.P
        cst_in = self.inp("cst", [128, N_CST])
        ident_in = self.inp("ident", [128, 128])
        P.dma("qy", lambda y: y.dma_start(out=self.cst[:], in_=cst_in),
              w=[self.cst_b], tag="ld_cst")
        P.dma("qg", lambda g: g.dma_start(out=self.ident[:], in_=ident_in),
              w=[self.ident_b], tag="ld_ident")
        P.emit("v", lambda v: v.memset(self.ones[:], 1.0), w=[self.ones_b], tag="ones")
        P.emit("v", lambda v: v.memset(self.ones_col[:], 1.0), w=[self.ones_col_b], tag="onesc")

    def load_x(self):
        P = self.P
        x_in = self.inp("x", [T, D])
        for i in range(NT):
            t32 = self.B4.get(f"lx{i}", [128, D], F32)
            P.dma("qy", lambda y, i=i, t=t32: y.dma_start(out=t[:], in_=x_in[i * 128:(i + 1) * 128, :]),
                  w=[t32.b], tag=f"lx{i}")
            P.dma("qy", lambda y, i=i, t=t32: y.dma_start(out=self.x32[i * 128:(i + 1) * 128, :], in_=t[:]),
                  r=[t32.b], w=[self.x32_b[i]], tag=f"sx32_{i}")
            tbf = self.B2.get(f"lxb{i}", [128, D], BF16)
            P.emit("v", lambda v, t=t32, tb=tbf: v.tensor_copy(tb[:], t[:]),
                   r=[t32.b], w=[tbf.b], tag=f"cvt{i}")
            P.dma("qy", lambda y, i=i, tb=tbf: y.dma_start(out=self.xbf[i * 128:(i + 1) * 128, :], in_=tb[:]),
                  r=[tbf.b], w=[self.xbf_b[i]], tag=f"sxbf{i}")
            t32.free()
            tbf.free()

    def store_out(self):
        P = self.P
        o = self.out("o", [T, D])
        for i in range(NT):
            t32 = self.B4.get(f"so{i}", [128, D], F32)
            P.dma("qy", lambda y, i=i, t=t32: y.dma_start(out=t[:], in_=self.x32[i * 128:(i + 1) * 128, :]),
                  r=[self.x32_b[i]], w=[t32.b], tag=f"so_l{i}")
            P.dma("qy", lambda y, i=i, t=t32: y.dma_start(out=o[i * 128:(i + 1) * 128, :], in_=t[:]),
                  r=[t32.b], tag=f"so_s{i}")
            t32.free()

    def make_xT(self, src=None, src_b=None, tag="xT"):
        P = self.P
        if src is None:
            src, src_b = self.xbf, self.xbf_b
        xT = []
        for dc in range(8):
            tl = self.B8.get(f"{tag}{dc}", [128, T], BF16)
            P.dma("qy", lambda y, dc=dc, tl=tl, src=src: y.dma_start(
                out=tl[:], in_=src[:, dc * 128:(dc + 1) * 128], transpose=True),
                r=src_b, w=[tl.b], tag=f"{tag}{dc}")
            xT.append(tl)
        return xT

    def ln_store(self, i, yt, eps_col, gamma_row=None, beta_row=None,
                 dst32=None, dst32_b=None, dstbf=True, second=None):
        P = self.P
        st, st_b = self.stats, self.stats_b
        cst, cst_b = self.cst, self.cst_b
        trash = self.trash4

        def stats_of(src, src_b, ecol):
            P.emit("v", lambda v: v.reduce_sum(st[:, 0:1], src[:, :], AX.X),
                   r=[src_b], w=[st_b], tag=f"ln_sum{i}")
            P.emit("s", lambda s: s.activation(trash[:, :], src[:, :], AF.Square,
                                               accum_out=st[:, 1:2]),
                   r=[src_b], w=[trash.b, st_b], tag=f"ln_sq{i}")
            P.emit("v", lambda v: v.tensor_scalar(st[:, 0:1], st[:, 0:1], cst[:, C_INV1024:C_INV1024 + 1], None, ALU.mult),
                   r=[cst_b], w=[st_b], tag=f"ln_mu{i}")
            P.emit("v", lambda v: v.tensor_scalar(st[:, 2:3], st[:, 0:1], st[:, 0:1], None, ALU.mult),
                   r=[st_b], w=[st_b], tag=f"ln_mu2{i}")
            P.emit("v", lambda v: v.tensor_scalar(st[:, 1:2], st[:, 1:2], cst[:, C_INV1024:C_INV1024 + 1], st[:, 2:3], ALU.mult, ALU.subtract),
                   r=[cst_b], w=[st_b], tag=f"ln_var{i}")
            P.emit("s", lambda s, ecol=ecol: s.activation(st[:, 3:4], st[:, 1:2], AF.Sqrt, bias=cst[:, ecol:ecol + 1]),
                   r=[cst_b, st_b], w=[st_b], tag=f"ln_sd{i}")
            P.emit("v", lambda v: v.reciprocal(st[:, 4:5], st[:, 3:4]),
                   r=[st_b], w=[st_b], tag=f"ln_r{i}")

        src, src_b = yt, yt.b
        stats_of(src, src_b, eps_col)
        if gamma_row is not None:
            z = self.B4.get(f"lnz{i}", [128, D], F32)
            P.emit("v", lambda v, z=z: v.tensor_scalar(z[:, :], src[:, :], st[:, 0:1], st[:, 4:5], ALU.subtract, ALU.mult),
                   r=[src_b, st_b], w=[z.b], tag=f"ln_app{i}")
            P.emit("v", lambda v, z=z: v.tensor_mul(z[:, :], z[:, :], gamma_row[0][:, :]),
                   r=[z.b, gamma_row[1]], w=[z.b], tag=f"ln_g{i}")
            P.emit("v", lambda v, z=z: v.tensor_add(z[:, :], z[:, :], beta_row[0][:, :]),
                   r=[z.b, beta_row[1]], w=[z.b], tag=f"ln_b{i}")
            cur, cur_b = z, z.b
            owned = z
        else:
            z = self.B4.get(f"lnz{i}", [128, D], F32)
            P.emit("v", lambda v, z=z: v.tensor_scalar(z[:, :], src[:, :], st[:, 0:1], st[:, 4:5], ALU.subtract, ALU.mult),
                   r=[src_b, st_b], w=[z.b], tag=f"ln_app{i}")
            cur, cur_b = z, z.b
            owned = z

        if second is not None:
            eps2, x2t, x2b = second
            P.emit("v", lambda v, cur=cur, x2t=x2t: v.tensor_add(cur[:, :], cur[:, :], x2t[:, :]),
                   r=[cur_b, x2b], w=[cur_b], tag=f"ln2_add{i}")
            stats_of(cur, cur_b, eps2)
            z2 = self.B4.get(f"lnz2_{i}", [128, D], F32)
            P.emit("v", lambda v, z2=z2, cur=cur: v.tensor_scalar(z2[:, :], cur[:, :], st[:, 0:1], st[:, 4:5], ALU.subtract, ALU.mult),
                   r=[cur_b, st_b], w=[z2.b], tag=f"ln2_app{i}")
            owned.free()
            cur, cur_b, owned = z2, z2.b, z2

        if dst32 is None:
            P.dma("qy", lambda y, cur=cur: y.dma_start(out=self.x32[i * 128:(i + 1) * 128, :], in_=cur[:, :]),
                  r=[cur_b], w=[self.x32_b[i]], tag=f"st32_{i}")
        else:
            P.dma("qy", lambda y, cur=cur: y.dma_start(out=dst32[i * 128:(i + 1) * 128, :], in_=cur[:, :]),
                  r=[cur_b], w=([dst32_b] if dst32_b is not None else []), tag=f"stout_{i}")
        if dstbf:
            tbf = self.B2.get(f"lnbf{i}", [128, D], BF16)
            P.emit("v", lambda v, tb=tbf, cur=cur: v.tensor_copy(tb[:], cur[:, :]),
                   r=[cur_b], w=[tbf.b], tag=f"lnbfc{i}")
            P.dma("qy", lambda y, tb=tbf: y.dma_start(out=self.xbf[i * 128:(i + 1) * 128, :], in_=tb[:]),
                  r=[tbf.b], w=[self.xbf_b[i]], tag=f"stbf{i}")
            tbf.free()
        owned.free()

    def load_row128(self, name, src_ap, ncols, dtype=F32, pool=None):
        pool = pool or (self.B4 if elem_bytes([128, ncols], dtype) > 2048 else self.B2)
        tl = pool.get(name, [128, ncols], dtype)
        q = "qg" if dtype not in (F32, I32) else "qy"
        self.P.dma(q, lambda y: y.dma_start(out=tl[:], in_=src_ap), w=[tl.b], tag=f"ld_{name}")
        return tl



def mlp_4096(mk, w1_ap, b1p_ap, w2_ap, badd_row, relu_scale_col=None,
             tag="l3f"):
    P, nc, ps = mk.P, mk.nc, mk.psum
    xT = mk.make_xT()
    b1p = mk.load_row128(tag + "b1p", b1p_ap, 32, pool=mk.SM) if b1p_ap is not None else None
    for tc in range(8):
        h8 = [mk.B8.get(f"{tag}h{tc}_{j}", [128, T], BF16) for j in range(4)]
        for ic in range(32):
            w1c = mk.B2.get(f"{tag}w1_{tc}_{ic}", [128, 1024], BF16)
            P.dma("qg", lambda g, ic=ic, w1c=w1c: g.dma_start(
                out=w1c[:].rearrange("p (dc i) -> p dc i", dc=8),
                in_=w1_ap[:, ic * 128:(ic + 1) * 128].rearrange("(dc p) i -> p dc i", p=128)),
                w=[w1c.b], tag=f"{tag}w1l{tc}_{ic}")
            bank = ic % 2
            pb = ps.bufs[bank]
            pp = ps.banks[bank]
            for dc in range(8):
                P.emit("t", lambda t, dc=dc, w1c=w1c, pp=pp, tc=tc: t.matmul(
                    pp[:, :], w1c[:, dc * 128:(dc + 1) * 128],
                    xT[dc][:, tc * 512:(tc + 1) * 512],
                    start=(dc == 0), stop=(dc == 7)),
                    r=[w1c.b, xT[dc].b], w=[pb], tag=f"{tag}hmm{tc}_{ic}_{dc}")
            h_t = h8[ic // 8]
            if b1p is not None:
                P.emit("s", lambda s, ic=ic, pp=pp, h_t=h_t, b1p=b1p: s.activation(
                    h_t[:, (ic % 8) * 512:(ic % 8 + 1) * 512], pp[:, :], AF.Relu,
                    bias=b1p[:, ic:ic + 1]),
                    r=[pb, b1p.b], w=[h_t.b], tag=f"{tag}hev{tc}_{ic}")
            else:
                P.emit("s", lambda s, ic=ic, pp=pp, h_t=h_t: s.activation(
                    h_t[:, (ic % 8) * 512:(ic % 8 + 1) * 512], pp[:, :], AF.Relu),
                    r=[pb], w=[h_t.b], tag=f"{tag}hev{tc}_{ic}")
            w1c.free()
        for ic in range(32):
            w2c = mk.B2.get(f"{tag}w2_{tc}_{ic}", [128, 1024], BF16)
            P.dma("qg", lambda g, ic=ic, w2c=w2c: g.dma_start(
                out=w2c[:], in_=w2_ap[ic * 128:(ic + 1) * 128, :]),
                w=[w2c.b], tag=f"{tag}w2l{tc}_{ic}")
            for tt in range(4):
                h_t = h8[ic // 8]
                for dh in range(2):
                    bank = tt * 2 + dh
                    P.emit("t", lambda t, ic=ic, tt=tt, dh=dh, w2c=w2c, h_t=h_t, bank=bank: t.matmul(
                        ps.banks[bank][:, :],
                        h_t[:, (ic % 8) * 512 + tt * 128:(ic % 8) * 512 + (tt + 1) * 128],
                        w2c[:, dh * 512:(dh + 1) * 512],
                        start=(ic == 0), stop=(ic == 31)),
                        r=[w2c.b, h_t.b], w=[ps.bufs[bank]], tag=f"{tag}ymm{tc}_{ic}_{tt}_{dh}")
            w2c.free()
        for tt in range(4):
            i = tc * 4 + tt
            xt32 = mk.B4.get(f"{tag}x{i}", [128, D], F32)
            P.dma("qy", lambda y, i=i, xt32=xt32: y.dma_start(
                out=xt32[:], in_=mk.x32[i * 128:(i + 1) * 128, :]),
                r=[mk.x32_b[i]], w=[xt32.b], tag=f"{tag}xl{i}")
            if badd_row is not None:
                P.emit("v", lambda v, xt32=xt32: v.tensor_add(xt32[:, :], xt32[:, :], badd_row[:, :]),
                       r=[xt32.b, badd_row.b], w=[xt32.b], tag=f"{tag}xb{i}")
            yt = mk.B4.get(f"{tag}y{i}", [128, D], F32)
            for dh in range(2):
                bank = tt * 2 + dh
                P.emit("v", lambda v, dh=dh, yt=yt, xt32=xt32, bank=bank: v.scalar_tensor_tensor(
                    yt[:, dh * 512:(dh + 1) * 512], ps.banks[bank][:, :], 1.0,
                    xt32[:, dh * 512:(dh + 1) * 512], ALU.mult, ALU.add),
                    r=[ps.bufs[bank], xt32.b], w=[yt.b], tag=f"{tag}res{i}_{dh}")
            xt32.free()
            mk.ln_store(i, yt, C_EPS5)
            yt.free()
        for h_t in h8:
            h_t.free()
    for tl in xT:
        tl.free()


def l3_ffn(mk):
    w1 = mk.inp("l3f_w1", [1024, 4096])
    b1p = mk.inp("l3f_b1p", [128, 32])
    w2 = mk.inp("l3f_w2", [4096, 1024])
    b2r = mk.inp("l3f_b2r", [128, 1024])
    b2b = mk.load_row128("l3f_b2b", b2r, 1024)
    mlp_4096(mk, w1, b1p, w2, b2b, tag="l3f")
    b2b.free()


C_NEG125 = 12
C_NBIG = 13
C_INV64 = 14
C_FOUR = 15
C_TWO = 16
C_NEG1 = 17


def l0_ffn_prep(mk):
    P = mk.P
    w1t = mk.inp("l0_w1t", [8, 1024, 4096])
    w2t = mk.inp("l0_w2t", [8, 4096, 1024])
    mk.w1bf = mk.dram("w1bf", [8, 1024, 4096], BF16)
    mk.w2bf = mk.dram("w2bf", [8, 4096, 1024], BF16)
    mk.w1bf_b = [Buf(f"w1bf{e}") for e in range(8)]
    mk.w2bf_b = [Buf(f"w2bf{e}") for e in range(8)]
    for e in range(8):
        for r in range(0, 1024, 128):
            tl = mk.B8.get(f"mp1_{e}_{r}", [128, 4096], BF16)
            P.dma("qg", lambda g, e=e, r=r, tl=tl: g.dma_start(
                out=tl[:], in_=w1t[e, r:r + 128, :]), w=[tl.b], tag=f"mp1l{e}_{r}")
            P.dma("qy", lambda y, e=e, r=r, tl=tl: y.dma_start(
                out=mk.w1bf[e, r:r + 128, :], in_=tl[:]),
                r=[tl.b], w=[mk.w1bf_b[e]], tag=f"mp1s{e}_{r}")
            tl.free()
        for r in range(0, 4096, 512):
            tl = mk.B8.get(f"mp2_{e}_{r}", [128, 4096], BF16)
            P.dma("qg", lambda g, e=e, r=r, tl=tl: g.dma_start(
                out=tl[:].rearrange("p (q c) -> p q c", q=4),
                in_=w2t[e, r:r + 512, :].rearrange("(q p) c -> p q c", p=128)),
                w=[tl.b], tag=f"mp2l{e}_{r}")
            P.dma("qy", lambda y, e=e, r=r, tl=tl: y.dma_start(
                out=mk.w2bf[e, r:r + 512, :].rearrange("(q p) c -> p q c", p=128),
                in_=tl[:].rearrange("p (q c) -> p q c", q=4)),
                r=[tl.b], w=[mk.w2bf_b[e]], tag=f"mp2s{e}_{r}")
            tl.free()


def l0_ffn(mk):
    P, ps = mk.P, mk.psum
    gate = mk.inp("l0_gate", [1024, 8])
    yacc = mk.dram("l0_yacc", [T, D], F32)
    yacc_b = [Buf(f"yacc{i}") for i in range(NT)]

    xT = mk.make_xT()
    gsb = mk.B1.get("l0gw", [128, 64], BF16)
    P.dma("qg", lambda g: g.dma_start(
        out=gsb[:].rearrange("p (dc e) -> p dc e", dc=8),
        in_=gate.rearrange("(dc p) e -> p dc e", p=128)), w=[gsb.b], tag="l0gwl")
    comb = mk.B1.get("l0comb", [128, 256], F32)
    scr = mk.B1.get("l0gscr", [128, 64], F32)
    P.emit("v", lambda v: v.memset(scr[:, 48:56], 1.0), w=[scr.b], tag="l0gones")
    for i in range(NT):
        pg = ps.view(6, [128, 8], F32)
        for dc in range(8):
            P.emit("t", lambda t, i=i, dc=dc, pg=pg: t.matmul(
                pg[:, :], xT[dc][:, i * 128:(i + 1) * 128], gsb[:, dc * 8:(dc + 1) * 8],
                start=(dc == 0), stop=(dc == 7)),
                r=[xT[dc].b, gsb.b], w=[ps.bufs[6]], tag=f"l0g{i}_{dc}")
        P.emit("s", lambda s, pg=pg: s.activation(scr[:, 0:8], pg[:, :], AF.Identity),
               r=[ps.bufs[6]], w=[scr.b], tag=f"l0gv{i}")
        P.emit("v", lambda v: v.tensor_reduce(scr[:, 32:33], scr[:, 0:8], AX.X, ALU.max),
               r=[scr.b], w=[scr.b], tag=f"l0gm1{i}")
        P.emit("v", lambda v: v.tensor_scalar(scr[:, 8:16], scr[:, 0:8], scr[:, 32:33], None, ALU.is_equal),
               r=[scr.b], w=[scr.b], tag=f"l0geq{i}")
        P.emit("v", lambda v: v.tensor_tensor_scan(scr[:, 40:48], scr[:, 48:56], scr[:, 8:16], 0.0, ALU.mult, ALU.add),
               r=[scr.b], w=[scr.b], tag=f"l0gcum{i}")
        P.emit("v", lambda v: v.tensor_scalar(scr[:, 40:48], scr[:, 40:48], 1.0, None, ALU.is_equal),
               r=[scr.b], w=[scr.b], tag=f"l0gfo{i}")
        P.emit("v", lambda v: v.tensor_mul(scr[:, 8:16], scr[:, 8:16], scr[:, 40:48]),
               r=[scr.b], w=[scr.b], tag=f"l0gf1{i}")
        P.emit("v", lambda v: v.scalar_tensor_tensor(scr[:, 16:24], scr[:, 8:16], mk.cst[:, C_NBIG:C_NBIG + 1], scr[:, 0:8], ALU.mult, ALU.add),
               r=[scr.b, mk.cst_b], w=[scr.b], tag=f"l0gmask{i}")
        P.emit("v", lambda v: v.tensor_reduce(scr[:, 33:34], scr[:, 16:24], AX.X, ALU.max),
               r=[scr.b], w=[scr.b], tag=f"l0gm2{i}")
        P.emit("v", lambda v: v.tensor_scalar(scr[:, 24:32], scr[:, 16:24], scr[:, 33:34], None, ALU.is_equal),
               r=[scr.b], w=[scr.b], tag=f"l0geq2{i}")
        P.emit("v", lambda v: v.tensor_tensor_scan(scr[:, 40:48], scr[:, 48:56], scr[:, 24:32], 0.0, ALU.mult, ALU.add),
               r=[scr.b], w=[scr.b], tag=f"l0gcum2{i}")
        P.emit("v", lambda v: v.tensor_scalar(scr[:, 40:48], scr[:, 40:48], 1.0, None, ALU.is_equal),
               r=[scr.b], w=[scr.b], tag=f"l0gfo2{i}")
        P.emit("v", lambda v: v.tensor_mul(scr[:, 24:32], scr[:, 24:32], scr[:, 40:48]),
               r=[scr.b], w=[scr.b], tag=f"l0gf2{i}")
        P.emit("v", lambda v: v.tensor_sub(scr[:, 34:35], scr[:, 32:33], scr[:, 33:34]),
               r=[scr.b], w=[scr.b], tag=f"l0gd{i}")
        P.emit("s", lambda s: s.activation(scr[:, 35:36], scr[:, 34:35], AF.Sigmoid),
               r=[scr.b], w=[scr.b], tag=f"l0gg1{i}")
        P.emit("s", lambda s: s.activation(scr[:, 36:37], scr[:, 34:35], AF.Sigmoid, scale=-1.0),
               r=[scr.b], w=[scr.b], tag=f"l0gg2{i}")
        P.emit("v", lambda v, i=i: v.tensor_scalar(scr[:, 40:48], scr[:, 8:16], scr[:, 35:36], None, ALU.mult),
               r=[scr.b], w=[scr.b], tag=f"l0gc1{i}")
        P.emit("v", lambda v, i=i: v.scalar_tensor_tensor(comb[:, i * 8:(i + 1) * 8], scr[:, 24:32], scr[:, 36:37], scr[:, 40:48], ALU.mult, ALU.add),
               r=[scr.b], w=[comb.b], tag=f"l0gcomb{i}")
    gsb.free()
    scr.free()

    for e in range(8):
        for tc in range(8):
            h8 = [mk.B8.get(f"l0fh{e}_{tc}_{j}", [128, T], BF16) for j in range(4)]
            for hc in range(32):
                w1c = mk.B2.get(f"l0fw1_{e}_{tc}_{hc}", [128, 1024], BF16)
                P.dma("qg", lambda g, e=e, hc=hc, w1c=w1c: g.dma_start(
                    out=w1c[:].rearrange("p (dc i) -> p dc i", dc=8),
                    in_=mk.w1bf[e, :, hc * 128:(hc + 1) * 128].rearrange("(dc p) i -> p dc i", p=128)),
                    r=[mk.w1bf_b[e]], w=[w1c.b], tag=f"l0fw1l{e}_{tc}_{hc}")
                bank = hc % 2
                pp, pb = ps.banks[bank], ps.bufs[bank]
                for dc in range(8):
                    P.emit("t", lambda t, dc=dc, tc=tc, w1c=w1c, pp=pp: t.matmul(
                        pp[:, :], w1c[:, dc * 128:(dc + 1) * 128],
                        xT[dc][:, tc * 512:(tc + 1) * 512],
                        start=(dc == 0), stop=(dc == 7)),
                        r=[w1c.b, xT[dc].b], w=[pb], tag=f"l0fh{e}_{tc}_{hc}_{dc}")
                h_t = h8[hc // 8]
                P.emit("s", lambda s, hc=hc, pp=pp, h_t=h_t: s.activation(
                    h_t[:, (hc % 8) * 512:(hc % 8 + 1) * 512], pp[:, :], AF.Relu),
                    r=[pb], w=[h_t.b], tag=f"l0fhev{e}_{tc}_{hc}")
                w1c.free()
            for hc in range(32):
                w2c = mk.B2.get(f"l0fw2_{e}_{tc}_{hc}", [128, 1024], BF16)
                P.dma("qg", lambda g, e=e, hc=hc, w2c=w2c: g.dma_start(
                    out=w2c[:], in_=mk.w2bf[e, hc * 128:(hc + 1) * 128, :]),
                    r=[mk.w2bf_b[e]], w=[w2c.b], tag=f"l0fw2l{e}_{tc}_{hc}")
                for tt in range(4):
                    h_t = h8[hc // 8]
                    for dh in range(2):
                        bank = tt * 2 + dh
                        P.emit("t", lambda t, hc=hc, tt=tt, dh=dh, w2c=w2c, h_t=h_t, bank=bank: t.matmul(
                            ps.banks[bank][:, :],
                            h_t[:, (hc % 8) * 512 + tt * 128:(hc % 8) * 512 + (tt + 1) * 128],
                            w2c[:, dh * 512:(dh + 1) * 512],
                            start=(hc == 0), stop=(hc == 31)),
                            r=[w2c.b, h_t.b], w=[ps.bufs[bank]], tag=f"l0fy{e}_{tc}_{hc}_{tt}_{dh}")
                w2c.free()
            for tt in range(4):
                i = tc * 4 + tt
                at = mk.B4.get(f"l0fa{e}_{i}", [128, D], F32)
                if e > 0:
                    P.dma("qy", lambda y, i=i, at=at: y.dma_start(
                        out=at[:], in_=yacc[i * 128:(i + 1) * 128, :]),
                        r=[yacc_b[i]], w=[at.b], tag=f"l0fal{e}_{i}")
                for dh in range(2):
                    bank = tt * 2 + dh
                    if e > 0:
                        P.emit("v", lambda v, dh=dh, at=at, bank=bank, i=i, e=e: v.scalar_tensor_tensor(
                            at[:, dh * 512:(dh + 1) * 512], ps.banks[bank][:, :],
                            comb[:, i * 8 + e:i * 8 + e + 1], at[:, dh * 512:(dh + 1) * 512],
                            ALU.mult, ALU.add),
                            r=[ps.bufs[bank], comb.b, at.b], w=[at.b], tag=f"l0facc{e}_{i}_{dh}")
                    else:
                        P.emit("v", lambda v, dh=dh, at=at, bank=bank, i=i: v.tensor_scalar(
                            at[:, dh * 512:(dh + 1) * 512], ps.banks[bank][:, :],
                            comb[:, i * 8:i * 8 + 1], None, ALU.mult),
                            r=[ps.bufs[bank], comb.b], w=[at.b], tag=f"l0facc0_{i}_{dh}")
                P.dma("qy", lambda y, i=i, at=at: y.dma_start(
                    out=yacc[i * 128:(i + 1) * 128, :], in_=at[:]),
                    r=[at.b], w=[yacc_b[i]], tag=f"l0fas{e}_{i}")
                at.free()
            for h_t in h8:
                h_t.free()
    comb.free()
    for tl in xT:
        tl.free()
    for i in range(NT):
        at = mk.B4.get(f"l0ff{i}", [128, D], F32)
        P.dma("qy", lambda y, i=i, at=at: y.dma_start(
            out=at[:], in_=yacc[i * 128:(i + 1) * 128, :]),
            r=[yacc_b[i]], w=[at.b], tag=f"l0ffl{i}")
        xt32 = mk.B4.get(f"l0fx{i}", [128, D], F32)
        P.dma("qy", lambda y, i=i, xt32=xt32: y.dma_start(
            out=xt32[:], in_=mk.x32[i * 128:(i + 1) * 128, :]),
            r=[mk.x32_b[i]], w=[xt32.b], tag=f"l0fxl{i}")
        P.emit("v", lambda v, at=at, xt32=xt32: v.tensor_add(at[:, :], at[:, :], xt32[:, :]),
               r=[at.b, xt32.b], w=[at.b], tag=f"l0fadd{i}")
        xt32.free()
        mk.ln_store(i, at, C_EPS5)
        at.free()



def proj_to_rows(mk, w_ap, bias_p, dst_d, dst_bufs, xT, ngroups, tag, dst2=None):
    P, ps = mk.P, mk.psum
    for g in range(ngroups):
        wc = mk.B2.get(f"{tag}w{g}", [128, 1024], BF16)
        P.dma("qg", lambda q, g=g, wc=wc: q.dma_start(
            out=wc[:].rearrange("p (dc i) -> p dc i", dc=8),
            in_=w_ap[:, g * 128:(g + 1) * 128].rearrange("(dc p) i -> p dc i", p=128)),
            w=[wc.b], tag=f"{tag}wl{g}")
        ot = mk.B8.get(f"{tag}o{g}", [128, T], BF16)
        for tc in range(8):
            bank = tc % 2
            pp, pb = ps.banks[bank], ps.bufs[bank]
            for dc in range(8):
                P.emit("t", lambda t, dc=dc, tc=tc, wc=wc, pp=pp: t.matmul(
                    pp[:, :], wc[:, dc * 128:(dc + 1) * 128],
                    xT[dc][:, tc * 512:(tc + 1) * 512],
                    start=(dc == 0), stop=(dc == 7)),
                    r=[wc.b, xT[dc].b], w=[pb], tag=f"{tag}mm{g}_{tc}_{dc}")
            if bias_p is not None:
                P.emit("s", lambda s, tc=tc, g=g, pp=pp, ot=ot: s.activation(
                    ot[:, tc * 512:(tc + 1) * 512], pp[:, :], AF.Identity,
                    bias=bias_p[:, g:g + 1]),
                    r=[pb, bias_p.b], w=[ot.b], tag=f"{tag}ev{g}_{tc}")
            else:
                P.emit("s", lambda s, tc=tc, pp=pp, ot=ot: s.activation(
                    ot[:, tc * 512:(tc + 1) * 512], pp[:, :], AF.Identity),
                    r=[pb], w=[ot.b], tag=f"{tag}ev{g}_{tc}")
        if dst2 is not None:
            qd, qb, kd, kb = dst2
            dd, db, gg = (qd, qb, g) if g < 8 else (kd, kb, g - 8)
            P.dma("qy", lambda y, gg=gg, ot=ot, dd=dd: y.dma_start(
                out=dd[gg * 128:(gg + 1) * 128, :], in_=ot[:]),
                r=[ot.b], w=[db[gg]], tag=f"{tag}st{g}")
        else:
            P.dma("qy", lambda y, g=g, ot=ot: y.dma_start(
                out=dst_d[g * 128:(g + 1) * 128, :], in_=ot[:]),
                r=[ot.b], w=[dst_bufs[g]], tag=f"{tag}st{g}")
        wc.free()
        ot.free()


def proj_v(mk, w_ap, bias_row, dst_d, dst_bufs, xT, tag, ncols=1024, natural=False):
    P, ps = mk.P, mk.psum
    nh = ncols // 512
    wcs = []
    for dc in range(8):
        wc = mk.B2.get(f"{tag}vw{dc}", [128, ncols], BF16)
        P.dma("qg", lambda q, dc=dc, wc=wc: q.dma_start(
            out=wc[:], in_=w_ap[dc * 128:(dc + 1) * 128, :]),
            w=[wc.b], tag=f"{tag}vwl{dc}")
        wcs.append(wc)
    for i in range(NT):
        vt = mk.B2.get(f"{tag}v{i}", [128, ncols], BF16)
        for h in range(nh):
            bank = 2 + (i * nh + h) % 2
            pp, pb = ps.banks[bank], ps.bufs[bank]
            for dc in range(8):
                P.emit("t", lambda t, dc=dc, i=i, h=h, pp=pp: t.matmul(
                    pp[:, :], xT[dc][:, i * 128:(i + 1) * 128],
                    wcs[dc][:, h * 512:(h + 1) * 512],
                    start=(dc == 0), stop=(dc == 7)),
                    r=[wcs[dc].b, xT[dc].b], w=[pb], tag=f"{tag}vmm{i}_{h}")
            if bias_row is not None:
                P.emit("v", lambda v, h=h, pp=pp, vt=vt: v.tensor_add(
                    vt[:, h * 512:(h + 1) * 512], pp[:, :], bias_row[:, h * 512:(h + 1) * 512]),
                    r=[pb, bias_row.b], w=[vt.b], tag=f"{tag}vev{i}_{h}")
            else:
                P.emit("v", lambda v, h=h, pp=pp, vt=vt: v.tensor_copy(
                    vt[:, h * 512:(h + 1) * 512], pp[:, :]),
                    r=[pb], w=[vt.b], tag=f"{tag}vev{i}_{h}")
        if natural:
            P.dma("qy", lambda y, i=i, vt=vt: y.dma_start(
                out=dst_d[i * 128:(i + 1) * 128, :], in_=vt[:]),
                r=[vt.b], w=[dst_bufs[i]], tag=f"{tag}vst{i}")
        else:
            P.dma("qy", lambda y, i=i, vt=vt: y.dma_start(
                out=dst_d.rearrange("hp p (tt c) -> tt p hp c", c=128)[i],
                in_=vt[:].rearrange("p (hp c) -> p hp c", c=128)),
                r=[vt.b], w=[dst_bufs[i]], tag=f"{tag}vst{i}")
        vt.free()
    for wc in wcs:
        wc.free()


def neg_cbound64(mk, tile, tag):
    P, ps = mk.P, mk.psum
    sq = mk.B8.get(f"{tag}sq", [64, T], BF16)
    P.emit("s", lambda s: s.activation(sq[:], tile[0:64, :], AF.Square),
           r=[tile.b], w=[sq.b], tag=f"{tag}sq")
    pA = ps.view(6, [128, 32], F32)
    for j in range(32):
        P.emit("t", lambda t, j=j: t.matmul(
            pA[:, j:j + 1], sq[0:64, j * 128:(j + 1) * 128], mk.ones_col[0:64, 0:1],
            start=True, stop=True),
            r=[sq.b, mk.ones_col_b], w=[ps.bufs[6]], tag=f"{tag}n{j}")
    sq.free()
    mx = mk.SM.get(f"{tag}mx", [128, 1], F32)
    P.emit("v", lambda v: v.tensor_reduce(mx[:, 0:1], pA[:, :], AX.X, ALU.max),
           r=[ps.bufs[6]], w=[mx.b], tag=f"{tag}mx")
    mxb = mk.SM.get(f"{tag}mxb", [128, 1], BF16)
    P.emit("v", lambda v: v.tensor_copy(mxb[:], mx[:]), r=[mx.b], w=[mxb.b], tag=f"{tag}cv")
    mx.free()
    ptr = ps.view(7, [1, 128], BF16)
    P.emit("t", lambda t: t.transpose(ptr[:, :], mxb[:, :], mk.ident[:]),
           r=[mxb.b, mk.ident_b], w=[ps.bufs[7]], tag=f"{tag}tr")
    mxb.free()
    out = mk.SM.get(f"{tag}out", [1, 1], F32)
    P.emit("v", lambda v: v.tensor_reduce(out[:, 0:1], ptr[:, :], AX.X, ALU.max),
           r=[ps.bufs[7]], w=[out.b], tag=f"{tag}gm")
    return out


def l0_attn(mk):
    P, nc, ps = mk.P, mk.nc, mk.psum
    wq = mk.inp("l0_wq", [1024, 1024])
    wk = mk.inp("l0_wk", [1024, 1024])
    wv = mk.inp("l0_wv", [1024, 1024])
    wo = mk.inp("l0_wo", [1024, 1024])
    bq_p = mk.load_row128("l0bq", mk.inp("l0_bq_p", [128, 8]), 8, pool=mk.SM)
    bk_p = mk.load_row128("l0bk", mk.inp("l0_bk_p", [128, 8]), 8, pool=mk.SM)
    bv_r = mk.load_row128("l0bv", mk.inp("l0_bv_r", [128, 1024]), 1024)
    bo_r = mk.load_row128("l0bo", mk.inp("l0_bo_r", [128, 1024]), 1024)
    gm_r = mk.load_row128("l0gm", mk.inp("l0_gamma_r", [128, 1024]), 1024)
    bt_r = mk.load_row128("l0bt", mk.inp("l0_beta_r", [128, 1024]), 1024)

    qt_d = mk.dram("l0_qt", [1024, T], BF16)
    kt_d = mk.dram("l0_kt", [1024, T], BF16)
    v_d = mk.dram("l0_v", [8, 128, T], BF16)
    ot_d = mk.dram("l0_ot", [1024, T], BF16)
    qt_db = [Buf(f"l0qt{g}") for g in range(8)]
    kt_db = [Buf(f"l0kt{g}") for g in range(8)]
    v_db = [Buf(f"l0v{i}") for i in range(NT)]
    ot_db = [Buf(f"l0ot{g}") for g in range(8)]

    xT = mk.make_xT()
    proj_to_rows(mk, wq, bq_p, qt_d, qt_db, xT, 8, "l0q")
    proj_to_rows(mk, wk, bk_p, kt_d, kt_db, xT, 8, "l0k")
    proj_v(mk, wv, bv_r, v_d, v_db, xT, "l0v")
    for tl in xT:
        tl.free()
    bq_p.free(); bk_p.free(); bv_r.free()

    attn_core(mk, qt_d, qt_db, kt_d, kt_db, v_d, v_db, ot_d, ot_db, "l0a")

    attn_outproj(mk, ot_d, ot_db, wo, bo_r, "l0o", double=(gm_r, bt_r))
    bo_r.free(); gm_r.free(); bt_r.free()


def attn_outproj(mk, ot_d, ot_db, wo_ap, bo_row, tag, double=None):
    P, ps = mk.P, mk.psum
    wos = []
    for hc in range(8):
        wc = mk.B2.get(f"{tag}wo{hc}", [128, 1024], BF16)
        P.dma("qg", lambda q, hc=hc, wc=wc: q.dma_start(
            out=wc[:], in_=wo_ap[hc * 128:(hc + 1) * 128, :]),
            w=[wc.b], tag=f"{tag}wol{hc}")
        wos.append(wc)
    for i in range(NT):
        otc = mk.B2.get(f"{tag}otc{i}", [128, 1024], BF16)
        P.dma("qy", lambda y, i=i, otc=otc: y.dma_start(
            out=otc[:].rearrange("p (g c) -> p g c", g=8),
            in_=ot_d[:, i * 128:(i + 1) * 128].rearrange("(g p) c -> p g c", p=128)),
            r=ot_db, w=[otc.b], tag=f"{tag}otl{i}")
        for h in range(2):
            bank = 4 + (i * 2 + h) % 2
            pp, pb = ps.banks[bank], ps.bufs[bank]
            for hc in range(8):
                P.emit("t", lambda t, hc=hc, h=h, otc=otc, pp=pp: t.matmul(
                    pp[:, :], otc[:, hc * 128:(hc + 1) * 128],
                    wos[hc][:, h * 512:(h + 1) * 512],
                    start=(hc == 0), stop=(hc == 7)),
                    r=[otc.b, wos[hc].b], w=[pb], tag=f"{tag}omm{i}_{h}_{hc}")
        otc.free()
        xt32 = mk.B4.get(f"{tag}x{i}", [128, D], F32)
        P.dma("qy", lambda y, i=i, xt32=xt32: y.dma_start(
            out=xt32[:], in_=mk.x32[i * 128:(i + 1) * 128, :]),
            r=[mk.x32_b[i]], w=[xt32.b], tag=f"{tag}xl{i}")
        xb = mk.B4.get(f"{tag}xb{i}", [128, D], F32)
        P.emit("v", lambda v, xb=xb, xt32=xt32: v.tensor_add(xb[:, :], xt32[:, :], bo_row[:, :]),
               r=[xt32.b, bo_row.b], w=[xb.b], tag=f"{tag}xb{i}")
        yt = mk.B4.get(f"{tag}y{i}", [128, D], F32)
        for h in range(2):
            bank = 4 + (i * 2 + h) % 2
            P.emit("v", lambda v, h=h, yt=yt, xb=xb, bank=bank: v.scalar_tensor_tensor(
                yt[:, h * 512:(h + 1) * 512], ps.banks[bank][:, :], 1.0,
                xb[:, h * 512:(h + 1) * 512], ALU.mult, ALU.add),
                r=[ps.bufs[bank], xb.b], w=[yt.b], tag=f"{tag}res{i}_{h}")
        xb.free()
        if double is not None:
            gm_r, bt_r = double
            mk.ln_store(i, yt, C_EPS3, gamma_row=(gm_r, gm_r.b), beta_row=(bt_r, bt_r.b),
                        second=(C_EPS5, xt32, xt32.b))
        else:
            mk.ln_store(i, yt, C_EPS5)
        yt.free()
        xt32.free()
    for wc in wos:
        wc.free()


def attn_core(mk, qt_d, qt_db, kt_d, kt_db, v_d, v_db, ot_d, ot_db, tag, hps=None):
    P, ps = mk.P, mk.psum
    for hp in (hps if hps is not None else range(8)):
        qts, kts, negcs = [], [], []
        for hx in range(2):
            qt = mk.B8.get(f"{tag}qt{hp}{hx}", [64, T], BF16)
            P.dma("qy", lambda y, hp=hp, hx=hx, qt=qt: y.dma_start(
                out=qt[:], in_=qt_d[hp * 128 + hx * 64:hp * 128 + (hx + 1) * 64, :]),
                r=[qt_db[hp]], w=[qt.b], tag=f"{tag}qtl{hp}{hx}")
            kt = mk.B8.get(f"{tag}kt{hp}{hx}", [64, T], BF16)
            P.dma("qy", lambda y, hp=hp, hx=hx, kt=kt: y.dma_start(
                out=kt[:], in_=kt_d[hp * 128 + hx * 64:hp * 128 + (hx + 1) * 64, :]),
                r=[kt_db[hp]], w=[kt.b], tag=f"{tag}ktl{hp}{hx}")
            qts.append(qt)
            kts.append(kt)
        vimg = mk.B8.get(f"{tag}vi{hp}", [128, T], BF16)
        P.dma("qy", lambda y, hp=hp, vimg=vimg: y.dma_start(out=vimg[:], in_=v_d[hp]),
              r=v_db, w=[vimg.b], tag=f"{tag}vil{hp}")
        vt = mk.B8.get(f"{tag}vt{hp}", [128, NT * 130], BF16)
        P.emit("v", lambda v, vt=vt: v.memset(vt[:], 1.0), w=[vt.b], tag=f"{tag}vones{hp}")
        P.emit("v", lambda v, vt=vt, vimg=vimg: v.tensor_copy(
            vt[:].rearrange("p (tt c) -> p tt c", tt=NT)[:, :, 0:64],
            vimg[:].rearrange("p (tt c) -> p tt c", tt=NT)[:, :, 0:64]),
            r=[vimg.b], w=[vt.b], tag=f"{tag}vca{hp}")
        P.emit("v", lambda v, vt=vt, vimg=vimg: v.tensor_copy(
            vt[:].rearrange("p (tt c) -> p tt c", tt=NT)[:, :, 65:129],
            vimg[:].rearrange("p (tt c) -> p tt c", tt=NT)[:, :, 64:128]),
            r=[vimg.b], w=[vt.b], tag=f"{tag}vcb{hp}")

        for hx in range(2):
            qn = neg_cbound64(mk, qts[hx], f"{tag}qn{hp}{hx}")
            kn = neg_cbound64(mk, kts[hx], f"{tag}kn{hp}{hx}")
            cc = mk.SM.get(f"{tag}cc{hp}{hx}", [1, 2], F32)
            P.emit("v", lambda v, qn=qn, kn=kn, cc=cc: v.tensor_mul(cc[:, 0:1], qn[:, 0:1], kn[:, 0:1]),
                   r=[qn.b, kn.b], w=[cc.b], tag=f"{tag}cm{hp}{hx}")
            P.emit("s", lambda s, cc=cc: s.activation(cc[:, 1:2], cc[:, 0:1], AF.Sqrt),
                   r=[cc.b], w=[cc.b], tag=f"{tag}cs{hp}{hx}")
            ccb = mk.SM.get(f"{tag}ccb{hp}{hx}", [1, 1], BF16)
            P.emit("v", lambda v, cc=cc, ccb=ccb: v.tensor_scalar(
                ccb[:, 0:1], cc[:, 1:2], mk.cst[0:1, C_NEG125:C_NEG125 + 1], None, ALU.mult),
                r=[cc.b, mk.cst_b], w=[ccb.b], tag=f"{tag}cn{hp}{hx}")
            qn.free(); kn.free(); cc.free()
            negc = mk.SM.get(f"{tag}negc{hp}{hx}", [128, 1], F32)
            pbc = ps.view(6, [128, 1], F32)
            P.emit("t", lambda t, ccb=ccb, pbc=pbc: t.matmul(
                pbc[:, 0:1], mk.ones[:, :], ccb[0:1, 0:1], start=True, stop=True),
                r=[ccb.b, mk.ones_b], w=[ps.bufs[6]], tag=f"{tag}bc{hp}{hx}")
            P.emit("s", lambda s, negc=negc, pbc=pbc: s.activation(negc[:], pbc[:, :], AF.Identity),
                   r=[ps.bufs[6]], w=[negc.b], tag=f"{tag}bce{hp}{hx}")
            ccb.free()
            negcs.append(negc)

        ot = mk.B8.get(f"{tag}ot{hp}", [128, T], BF16)
        pa_t = [mk.B1.get(f"{tag}pa{hp}_{j}", [128, 512], BF16) for j in range(2)]
        pb_t = [mk.B1.get(f"{tag}pb{hp}_{j}", [128, 512], BF16) for j in range(2)]
        for qc in DBG.get("qcs", range(8)):
            oA = ps.view(4, [65, 512], F32)
            oB = ps.view(5, [65, 512], F32)
            for k in range(NT):
                sA = ps.view(k % 2, [128, 512], F32)
                sB = ps.view(2 + k % 2, [128, 512], F32)
                P.emit("t", lambda t, k=k, qc=qc, sA=sA, kA=kts[0], qA=qts[0]: t.matmul(
                    sA[:, :], kA[0:64, k * 128:(k + 1) * 128], qA[0:64, qc * 512:(qc + 1) * 512],
                    start=True, stop=True),
                    r=[kts[0].b, qts[0].b], w=[ps.bufs[k % 2]], tag=f"{tag}sA{hp}_{qc}_{k}")
                P.emit("t", lambda t, k=k, qc=qc, sB=sB, kB=kts[1], qB=qts[1]: t.matmul(
                    sB[:, :], kB[0:64, k * 128:(k + 1) * 128], qB[0:64, qc * 512:(qc + 1) * 512],
                    start=True, stop=True),
                    r=[kts[1].b, qts[1].b], w=[ps.bufs[2 + k % 2]], tag=f"{tag}sB{hp}_{qc}_{k}")
                pa = pa_t[k % 2]
                pb = pb_t[k % 2]
                P.emit("s", lambda s, sA=sA, pa=pa, n0=negcs[0]: s.activation(
                    pa[:, :], sA[:, :], AF.Exp, bias=n0[:, 0:1], scale=0.125),
                    r=[ps.bufs[k % 2], negcs[0].b], w=[pa.b], tag=f"{tag}eA{hp}_{qc}_{k}")
                P.emit("s", lambda s, sB=sB, pb=pb, n1=negcs[1]: s.activation(
                    pb[:, :], sB[:, :], AF.Exp, bias=n1[:, 0:1], scale=0.125),
                    r=[ps.bufs[2 + k % 2], negcs[1].b], w=[pb.b], tag=f"{tag}eB{hp}_{qc}_{k}")
                P.emit("t", lambda t, k=k, pa=pa, oA=oA, vt=vt: t.matmul(
                    oA[:, :], vt[:, k * 130:k * 130 + 65], pa[:, :],
                    start=(k == 0), stop=(k == NT - 1)),
                    r=[vt.b, pa.b], w=[ps.bufs[4]], tag=f"{tag}avA{hp}_{qc}_{k}")
                P.emit("t", lambda t, k=k, pb=pb, oB=oB, vt=vt: t.matmul(
                    oB[:, :], vt[:, k * 130 + 65:k * 130 + 130], pb[:, :],
                    start=(k == 0), stop=(k == NT - 1)),
                    r=[vt.b, pb.b], w=[ps.bufs[5]], tag=f"{tag}avB{hp}_{qc}_{k}")
            for hidx, obank, oview in ((0, 4, oA), (1, 5, oB)):
                zt = mk.B2.get(f"{tag}zt{hp}_{qc}_{hidx}", [1, 512], F32)
                P.emit("s", lambda s, oview=oview, zt=zt: s.activation(zt[:, :], oview[64:65, :], AF.Identity),
                       r=[ps.bufs[obank]], w=[zt.b], tag=f"{tag}zt{hp}_{qc}_{hidx}")
                rz = mk.B2.get(f"{tag}rz{hp}_{qc}_{hidx}", [1, 512], F32)
                P.emit("v", lambda v, zt=zt, rz=rz: v.reciprocal(rz[:, :], zt[:, :]),
                       r=[zt.b], w=[rz.b], tag=f"{tag}rz{hp}_{qc}_{hidx}")
                zt.free()
                rzb = mk.B1.get(f"{tag}rzb{hp}_{qc}_{hidx}", [1, 512], BF16)
                P.emit("v", lambda v, rz=rz, rzb=rzb: v.tensor_copy(rzb[:, :], rz[:, :]),
                       r=[rz.b], w=[rzb.b], tag=f"{tag}rzc{hp}_{qc}_{hidx}")
                rz.free()
                pz = ps.view(6 + hidx, [64, 512], F32)
                P.emit("t", lambda t, rzb=rzb, pz=pz: t.matmul(
                    pz[:, :], mk.ones[:, 0:64], rzb[:, :], start=True, stop=True),
                    r=[rzb.b, mk.ones_b], w=[ps.bufs[6 + hidx]], tag=f"{tag}rzbc{hp}_{qc}_{hidx}")
                rzf = mk.B2.get(f"{tag}rzf{hp}_{qc}_{hidx}", [64, 512], F32)
                P.emit("s", lambda s, pz=pz, rzf=rzf: s.activation(rzf[:, :], pz[:, :], AF.Identity),
                       r=[ps.bufs[6 + hidx]], w=[rzf.b], tag=f"{tag}rze{hp}_{qc}_{hidx}")
                rzb.free()
                P.emit("v", lambda v, oview=oview, rzf=rzf, hidx=hidx, qc=qc, ot=ot: v.tensor_mul(
                    ot[hidx * 64:(hidx + 1) * 64, qc * 512:(qc + 1) * 512],
                    oview[0:64, :], rzf[:, :]),
                    r=[ps.bufs[obank], rzf.b], w=[ot.b], tag=f"{tag}onm{hp}_{qc}_{hidx}")
                rzf.free()
        P.dma("qy", lambda y, hp=hp, ot=ot: y.dma_start(
            out=ot_d[hp * 128:(hp + 1) * 128, :], in_=ot[:]),
            r=[ot.b], w=[ot_db[hp]], tag=f"{tag}ost{hp}")
        for tl in (qts[0], qts[1], kts[0], kts[1], vt, vimg, ot,
                   negcs[0], negcs[1], pa_t[0], pa_t[1], pb_t[0], pb_t[1]):
            tl.free()


def l1_attn(mk):
    P, ps = mk.P, mk.psum
    wqk = mk.inp("l1_wqk", [1024, 2048])
    bqk_p = mk.load_row128("l1bqk", mk.inp("l1_bqk_p", [128, 16]), 16, pool=mk.SM)
    wv = mk.inp("l1_wv", [1024, 1024])
    bv_r = mk.load_row128("l1bv", mk.inp("l1_bv_r", [128, 1024]), 1024)
    wo = mk.inp("l1_wo", [1024, 1024])
    bo_r = mk.load_row128("l1bo", mk.inp("l1_bo_r", [128, 1024]), 1024)

    h2t_d = mk.dram("l1_h2t", [2048, T], BF16)
    v_d = mk.dram("l1_v", [T, 1024], BF16)
    ot_d = mk.dram("l1_ot", [1024, T], BF16)
    h2t_db = [Buf(f"l1h2t{g}") for g in range(16)]
    v_db = [Buf(f"l1v{i}") for i in range(NT)]
    ot_db = [Buf(f"l1ot{g}") for g in range(8)]

    xT = mk.make_xT()
    proj_to_rows(mk, wqk, bqk_p, h2t_d, h2t_db, xT, 16, "l1qk")
    proj_v(mk, wv, bv_r, v_d, v_db, xT, "l1v", natural=True)
    for tl in xT:
        tl.free()
    bqk_p.free(); bv_r.free()

    for n in range(NH):
        qt = mk.B8.get(f"l1qt{n}", [64, T], BF16)
        P.dma("qy", lambda y, n=n, qt=qt: y.dma_start(
            out=qt[:].rearrange("p (r a) -> p r a", r=16),
            in_=h2t_d[:1024, n * 256:(n + 1) * 256].rearrange("(r p) a -> p r a", p=64)),
            r=h2t_db[:8], w=[qt.b], tag=f"l1qtl{n}")
        kt = mk.B8.get(f"l1kt{n}", [64, T], BF16)
        P.dma("qy", lambda y, n=n, kt=kt: y.dma_start(
            out=kt[:].rearrange("p (r a) -> p r a", r=16),
            in_=h2t_d[1024:2048, n * 256:(n + 1) * 256].rearrange("(r p) a -> p r a", p=64)),
            r=h2t_db[8:], w=[kt.b], tag=f"l1ktl{n}")
        vt = mk.B8.get(f"l1vt{n}", [128, NT * 65], BF16)
        P.emit("v", lambda v, vt=vt: v.memset(vt[:], 1.0), w=[vt.b], tag=f"l1vo{n}")
        for ah in range(2):
            P.dma("qy", lambda y, n=n, ah=ah, vt=vt: y.dma_start(
                out=vt[:].rearrange("p (r ah c) -> p r ah c", r=16, ah=2)[:, :, ah, 0:64],
                in_=v_d[n * 256 + ah * 128:n * 256 + (ah + 1) * 128, :]
                    .rearrange("p (r c) -> p r c", r=16)),
                r=[v_db[(n * 256 + ah * 128) // 128]], w=[vt.b], tag=f"l1vl{n}_{ah}")

        qn = neg_cbound64(mk, qt, f"l1qn{n}")
        kn = neg_cbound64(mk, kt, f"l1kn{n}")
        cc = mk.SM.get(f"l1cc{n}", [1, 2], F32)
        P.emit("v", lambda v, qn=qn, kn=kn, cc=cc: v.tensor_mul(cc[:, 0:1], qn[:, 0:1], kn[:, 0:1]),
               r=[qn.b, kn.b], w=[cc.b], tag=f"l1cm{n}")
        P.emit("s", lambda s, cc=cc: s.activation(cc[:, 1:2], cc[:, 0:1], AF.Sqrt),
               r=[cc.b], w=[cc.b], tag=f"l1cs{n}")
        ccb = mk.SM.get(f"l1ccb{n}", [1, 1], BF16)
        P.emit("v", lambda v, cc=cc, ccb=ccb: v.tensor_scalar(
            ccb[:, 0:1], cc[:, 1:2], mk.cst[0:1, C_NEG125:C_NEG125 + 1], None, ALU.mult),
            r=[cc.b, mk.cst_b], w=[ccb.b], tag=f"l1cn{n}")
        qn.free(); kn.free(); cc.free()
        negc = mk.SM.get(f"l1negc{n}", [128, 1], F32)
        pbc = ps.view(6, [128, 1], F32)
        P.emit("t", lambda t, ccb=ccb, pbc=pbc: t.matmul(
            pbc[:, 0:1], mk.ones[:, :], ccb[0:1, 0:1], start=True, stop=True),
            r=[ccb.b, mk.ones_b], w=[ps.bufs[6]], tag=f"l1bc{n}")
        P.emit("s", lambda s, negc=negc, pbc=pbc: s.activation(negc[:], pbc[:, :], AF.Identity),
               r=[ps.bufs[6]], w=[negc.b], tag=f"l1bce{n}")
        ccb.free()

        ot = mk.B8.get(f"l1ot{n}", [64, T], BF16)
        pa_t = [mk.B1.get(f"l1pa{n}_{j}", [128, 512], BF16) for j in range(2)]
        for qc in range(8):
            oA = ps.view(4, [65, 512], F32)
            for k in range(NT):
                sA = ps.view(k % 2, [128, 512], F32)
                P.emit("t", lambda t, k=k, qc=qc, sA=sA, kt=kt, qt=qt: t.matmul(
                    sA[:, :], kt[0:64, k * 128:(k + 1) * 128], qt[0:64, qc * 512:(qc + 1) * 512],
                    start=True, stop=True),
                    r=[kt.b, qt.b], w=[ps.bufs[k % 2]], tag=f"l1s{n}_{qc}_{k}")
                pa = pa_t[k % 2]
                P.emit("s", lambda s, sA=sA, pa=pa, negc=negc: s.activation(
                    pa[:, :], sA[:, :], AF.Exp, bias=negc[:, 0:1], scale=0.125),
                    r=[ps.bufs[k % 2], negc.b], w=[pa.b], tag=f"l1e{n}_{qc}_{k}")
                P.emit("t", lambda t, k=k, pa=pa, oA=oA, vt=vt: t.matmul(
                    oA[:, :], vt[:, k * 65:k * 65 + 65], pa[:, :],
                    start=(k == 0), stop=(k == NT - 1)),
                    r=[vt.b, pa.b], w=[ps.bufs[4]], tag=f"l1av{n}_{qc}_{k}")
            zt = mk.B2.get(f"l1zt{n}_{qc}", [1, 512], F32)
            P.emit("s", lambda s, oA=oA, zt=zt: s.activation(zt[:, :], oA[64:65, :], AF.Identity),
                   r=[ps.bufs[4]], w=[zt.b], tag=f"l1zt{n}_{qc}")
            rz = mk.B2.get(f"l1rz{n}_{qc}", [1, 512], F32)
            P.emit("v", lambda v, zt=zt, rz=rz: v.reciprocal(rz[:, :], zt[:, :]),
                   r=[zt.b], w=[rz.b], tag=f"l1rz{n}_{qc}")
            zt.free()
            rzb = mk.B1.get(f"l1rzb{n}_{qc}", [1, 512], BF16)
            P.emit("v", lambda v, rz=rz, rzb=rzb: v.tensor_copy(rzb[:, :], rz[:, :]),
                   r=[rz.b], w=[rzb.b], tag=f"l1rzc{n}_{qc}")
            rz.free()
            pz = ps.view(6, [64, 512], F32)
            P.emit("t", lambda t, rzb=rzb, pz=pz: t.matmul(
                pz[:, :], mk.ones[:, 0:64], rzb[:, :], start=True, stop=True),
                r=[rzb.b, mk.ones_b], w=[ps.bufs[6]], tag=f"l1rzbc{n}_{qc}")
            rzf = mk.B2.get(f"l1rzf{n}_{qc}", [64, 512], F32)
            P.emit("s", lambda s, pz=pz, rzf=rzf: s.activation(rzf[:, :], pz[:, :], AF.Identity),
                   r=[ps.bufs[6]], w=[rzf.b], tag=f"l1rze{n}_{qc}")
            rzb.free()
            P.emit("v", lambda v, oA=oA, rzf=rzf, qc=qc, ot=ot: v.tensor_mul(
                ot[0:64, :].rearrange("p (a s) -> p s a", s=16)[:, 2 * qc:2 * qc + 2, :],
                oA[0:64, :].rearrange("p (r a) -> p r a", r=2), rzf[:, :].rearrange("p (r a) -> p r a", r=2)),
                r=[ps.bufs[4], rzf.b], w=[ot.b], tag=f"l1onm{n}_{qc}")
            rzf.free()
        P.dma("qy", lambda y, n=n, ot=ot: y.dma_start(
            out=ot_d[n * 64:(n + 1) * 64, :], in_=ot[:]),
            r=[ot.b], w=[ot_db[n // 2]], tag=f"l1ost{n}")
        for tl in (qt, kt, vt, ot, negc, pa_t[0], pa_t[1]):
            tl.free()

    attn_outproj(mk, ot_d, ot_db, wo, bo_r, "l1o")
    bo_r.free()


def l1_ffn_prep(mk):
    P = mk.P
    k0 = mk.inp("l1_k0", [2048, 4096], U8)
    k1 = mk.inp("l1_k1", [4096, 4096], U8)
    k2 = mk.inp("l1_k2", [4096, 1024], U8)
    mk.k0e = mk.dram("k0e", [1024, 4096], BF16)
    mk.k1pm = mk.dram("k1pm", [4096, 4096], BF16)
    mk.k2pm = mk.dram("k2pm", [4096, 1024], BF16)
    mk.k0e_b = Buf("k0e")
    mk.k1pm_b = Buf("k1pm")
    mk.k2pm_b = Buf("k2pm")
    for r in range(8):
        ta = mk.B8.get(f"k0a{r}", [128, 4096], BF16)
        tb = mk.B8.get(f"k0b{r}", [128, 4096], BF16)
        P.dma("qg", lambda g, r=r, ta=ta: g.dma_start(out=ta[:], in_=k0[r * 128:(r + 1) * 128, :]),
              w=[ta.b], tag=f"k0al{r}")
        P.dma("qg", lambda g, r=r, tb=tb: g.dma_start(out=tb[:], in_=k0[1024 + r * 128:1024 + (r + 1) * 128, :]),
              w=[tb.b], tag=f"k0bl{r}")
        P.emit("v", lambda v, ta=ta, tb=tb: v.tensor_sub(ta[:, :], ta[:, :], tb[:, :]),
               r=[ta.b, tb.b], w=[ta.b], tag=f"k0sub{r}")
        P.dma("qy", lambda y, r=r, ta=ta: y.dma_start(out=mk.k0e[r * 128:(r + 1) * 128, :], in_=ta[:]),
              r=[ta.b], w=[mk.k0e_b], tag=f"k0st{r}")
        ta.free(); tb.free()
    for nm, src_ap, dst_ap, dst_b, rows in (("k1", k1, mk.k1pm, mk.k1pm_b, 32),
                                            ("k2", k2, mk.k2pm, mk.k2pm_b, 32)):
        w = 4096 if nm == "k1" else 1024
        for r in range(rows):
            ta = mk.B8.get(f"{nm}a{r}", [128, w], BF16)
            P.dma("qg", lambda g, r=r, ta=ta, src_ap=src_ap, w=w: g.dma_start(
                out=ta[:], in_=src_ap[r * 128:(r + 1) * 128, :]), w=[ta.b], tag=f"{nm}l{r}")
            P.emit("s", lambda s, ta=ta: s.activation(ta[:, :], ta[:, :], AF.Identity,
                                                      bias=mk.cst[:, C_NEG1:C_NEG1 + 1], scale=2.0),
                   r=[ta.b, mk.cst_b], w=[ta.b], tag=f"{nm}pm{r}")
            P.dma("qy", lambda y, r=r, ta=ta, dst_ap=dst_ap: y.dma_start(
                out=dst_ap[r * 128:(r + 1) * 128, :], in_=ta[:]),
                r=[ta.b], w=[dst_b], tag=f"{nm}st{r}")
            ta.free()


def l1_ffn(mk):
    P, ps = mk.P, mk.psum
    P.emit("v", lambda v: v.tensor_scalar(mk.cst[:, C_SCL0:C_SCL0 + 1], mk.cst[:, C_S0:C_S0 + 1],
                                          mk.cst[:, C_FOUR:C_FOUR + 1], None, ALU.mult),
           r=[mk.cst_b], w=[mk.cst_b], tag="l1fscl0")
    P.emit("v", lambda v: v.tensor_scalar(mk.cst[:, C_SCL1:C_SCL1 + 1], mk.cst[:, C_S1:C_S1 + 1],
                                          mk.cst[:, C_TWO:C_TWO + 1], None, ALU.mult),
           r=[mk.cst_b], w=[mk.cst_b], tag="l1fscl1")
    for tc in range(8):
        xTc = []
        for dc in range(8):
            tl = mk.B1.get(f"l1fx{tc}_{dc}", [128, 512], BF16)
            P.dma("qy", lambda y, tc=tc, dc=dc, tl=tl: y.dma_start(
                out=tl[:], in_=mk.xbf[tc * 512:(tc + 1) * 512, dc * 128:(dc + 1) * 128], transpose=True),
                r=mk.xbf_b, w=[tl.b], tag=f"l1fxT{tc}_{dc}")
            xTc.append(tl)
        h8 = [mk.B8.get(f"l1fh{tc}_{j}", [128, T], BF16) for j in range(4)]
        for hc in range(32):
            w1c = mk.B2.get(f"l1fk0_{tc}_{hc}", [128, 1024], BF16)
            P.dma("qg", lambda g, hc=hc, w1c=w1c: g.dma_start(
                out=w1c[:].rearrange("p (dc i) -> p dc i", dc=8),
                in_=mk.k0e[:, hc * 128:(hc + 1) * 128].rearrange("(dc p) i -> p dc i", p=128)),
                r=[mk.k0e_b], w=[w1c.b], tag=f"l1fk0l{tc}_{hc}")
            bank = hc % 2
            pp, pb = ps.banks[bank], ps.bufs[bank]
            for dc in range(8):
                P.emit("t", lambda t, dc=dc, w1c=w1c, pp=pp, xTc=xTc: t.matmul(
                    pp[:, :], w1c[:, dc * 128:(dc + 1) * 128], xTc[dc][:, :],
                    start=(dc == 0), stop=(dc == 7)),
                    r=[w1c.b, xTc[dc].b], w=[pb], tag=f"l1fs1_{tc}_{hc}_{dc}")
            h_t = h8[hc // 8]
            P.emit("s", lambda s, hc=hc, pp=pp, h_t=h_t: s.activation(
                h_t[:, (hc % 8) * 512:(hc % 8 + 1) * 512], pp[:, :], AF.Relu,
                scale=mk.cst[:, C_SCL0:C_SCL0 + 1]),
                r=[pb, mk.cst_b], w=[h_t.b], tag=f"l1fs1e{tc}_{hc}")
            w1c.free()
        for tl in xTc:
            tl.free()
        b8t = [mk.B8.get(f"l1fb2_{tc}_{j}", [128, T], BF16) for j in range(4)]
        for hc2 in range(32):
            k1c = mk.B8.get(f"l1fk1_{tc}_{hc2}", [128, 4096], BF16)
            P.dma("qg", lambda g, hc2=hc2, k1c=k1c: g.dma_start(
                out=k1c[:].rearrange("p (c i) -> p c i", c=32),
                in_=mk.k1pm[:, hc2 * 128:(hc2 + 1) * 128].rearrange("(c p) i -> p c i", p=128)),
                r=[mk.k1pm_b], w=[k1c.b], tag=f"l1fk1l{tc}_{hc2}")
            bank = hc2 % 2
            pp, pb = ps.banks[bank], ps.bufs[bank]
            for c1 in range(32):
                P.emit("t", lambda t, c1=c1, k1c=k1c, pp=pp, h8=h8: t.matmul(
                    pp[:, :], k1c[:, c1 * 128:(c1 + 1) * 128],
                    h8[c1 // 8][:, (c1 % 8) * 512:(c1 % 8 + 1) * 512],
                    start=(c1 == 0), stop=(c1 == 31)),
                    r=[k1c.b, h8[c1 // 8].b], w=[pb], tag=f"l1fs2_{tc}_{hc2}_{c1}")
            b_t = b8t[hc2 // 8]
            P.emit("s", lambda s, hc2=hc2, pp=pp, b_t=b_t: s.activation(
                b_t[:, (hc2 % 8) * 512:(hc2 % 8 + 1) * 512], pp[:, :], AF.Relu,
                scale=mk.cst[:, C_SCL1:C_SCL1 + 1]),
                r=[pb, mk.cst_b], w=[b_t.b], tag=f"l1fs2e{tc}_{hc2}")
            k1c.free()
        for h_t in h8:
            h_t.free()
        for hc2 in range(32):
            k2c = mk.B2.get(f"l1fk2_{tc}_{hc2}", [128, 1024], BF16)
            P.dma("qg", lambda g, hc2=hc2, k2c=k2c: g.dma_start(
                out=k2c[:], in_=mk.k2pm[hc2 * 128:(hc2 + 1) * 128, :]),
                r=[mk.k2pm_b], w=[k2c.b], tag=f"l1fk2l{tc}_{hc2}")
            for tt in range(4):
                b_t = b8t[hc2 // 8]
                for dh in range(2):
                    bank = tt * 2 + dh
                    P.emit("t", lambda t, hc2=hc2, tt=tt, dh=dh, k2c=k2c, b_t=b_t, bank=bank: t.matmul(
                        ps.banks[bank][:, :],
                        b_t[:, (hc2 % 8) * 512 + tt * 128:(hc2 % 8) * 512 + (tt + 1) * 128],
                        k2c[:, dh * 512:(dh + 1) * 512],
                        start=(hc2 == 0), stop=(hc2 == 31)),
                        r=[k2c.b, b_t.b], w=[ps.bufs[bank]], tag=f"l1fs3_{tc}_{hc2}_{tt}_{dh}")
            k2c.free()
        for tt in range(4):
            i = tc * 4 + tt
            xt32 = mk.B4.get(f"l1fxr{i}", [128, D], F32)
            P.dma("qy", lambda y, i=i, xt32=xt32: y.dma_start(
                out=xt32[:], in_=mk.x32[i * 128:(i + 1) * 128, :]),
                r=[mk.x32_b[i]], w=[xt32.b], tag=f"l1fxl{i}")
            yt = mk.B4.get(f"l1fy{i}", [128, D], F32)
            for dh in range(2):
                bank = tt * 2 + dh
                P.emit("v", lambda v, dh=dh, yt=yt, xt32=xt32, bank=bank: v.scalar_tensor_tensor(
                    yt[:, dh * 512:(dh + 1) * 512], ps.banks[bank][:, :],
                    mk.cst[:, C_S2:C_S2 + 1], xt32[:, dh * 512:(dh + 1) * 512],
                    ALU.mult, ALU.add),
                    r=[ps.bufs[bank], xt32.b, mk.cst_b], w=[yt.b], tag=f"l1fres{i}_{dh}")
            xt32.free()
            mk.ln_store(i, yt, C_EPS5)
            yt.free()
        for b_t in b8t:
            b_t.free()


def l2_attn(mk):
    P, ps = mk.P, mk.psum
    wf = mk.inp("l2_wf", [1024, 1024])
    wi = mk.inp("l2_wi", [1024, 1024])
    wh = mk.inp("l2_wh", [1024, 1024])
    bf_p = mk.load_row128("l2bf", mk.inp("l2_bf_p", [128, 8]), 8, pool=mk.SM)
    bi_p = mk.load_row128("l2bi", mk.inp("l2_bi_p", [128, 8]), 8, pool=mk.SM)
    bh_p = mk.load_row128("l2bh", mk.inp("l2_bh_p", [128, 8]), 8, pool=mk.SM)
    h_d = mk.dram("l2_h", [T, D], BF16)
    h_db = [Buf(f"l2h{i}") for i in range(4)]

    for g in range(8):
        init = mk.SM.get(f"l2init{g}", [128, 1], F32)
        for th in range(2):
            xTh = []
            for dc in range(8):
                tl = mk.B8.get(f"l2x{g}_{th}_{dc}", [128, 2048], BF16)
                P.dma("qy", lambda y, th=th, dc=dc, tl=tl: y.dma_start(
                    out=tl[:], in_=mk.xbf[th * 2048:(th + 1) * 2048, dc * 128:(dc + 1) * 128], transpose=True),
                    r=mk.xbf_b, w=[tl.b], tag=f"l2xT{g}_{th}_{dc}")
                xTh.append(tl)
            pre = []
            for wi_ap, bp, nm in ((wf, bf_p, "f"), (wi, bi_p, "i"), (wh, bh_p, "h")):
                wc = mk.B2.get(f"l2w{nm}{g}_{th}", [128, 1024], BF16)
                P.dma("qg", lambda q, g=g, wc=wc, wi_ap=wi_ap: q.dma_start(
                    out=wc[:].rearrange("p (dc i) -> p dc i", dc=8),
                    in_=wi_ap[:, g * 128:(g + 1) * 128].rearrange("(dc p) i -> p dc i", p=128)),
                    w=[wc.b], tag=f"l2wl{nm}{g}_{th}")
                ot = mk.B8.get(f"l2p{nm}{g}_{th}", [128, 2048], F32)
                for q4 in range(4):
                    bank = q4
                    for dc in range(8):
                        P.emit("t", lambda t, q4=q4, dc=dc, wc=wc, bank=bank, xTh=xTh: t.matmul(
                            ps.banks[bank][:, :], wc[:, dc * 128:(dc + 1) * 128],
                            xTh[dc][:, q4 * 512:(q4 + 1) * 512],
                            start=(dc == 0), stop=(dc == 7)),
                            r=[wc.b, xTh[dc].b], w=[ps.bufs[bank]], tag=f"l2p{nm}{g}_{th}_{q4}_{dc}")
                    P.emit("s", lambda s, q4=q4, g=g, ot=ot, bank=bank, bp=bp: s.activation(
                        ot[:, q4 * 512:(q4 + 1) * 512], ps.banks[bank][:, :], AF.Identity,
                        bias=bp[:, g:g + 1]),
                        r=[ps.bufs[bank], bp.b], w=[ot.b], tag=f"l2pe{nm}{g}_{th}_{q4}")
                wc.free()
                pre.append(ot)
            for tl in xTh:
                tl.free()
            A, Bt, H = pre
            S1 = mk.B8.get(f"l2s1_{g}_{th}", [128, 2048], F32)
            S2 = mk.B8.get(f"l2s2_{g}_{th}", [128, 2048], F32)
            m = mk.B2.get(f"l2m{g}_{th}", [128, 2048], U8)
            P.emit("s", lambda s, A=A, S1=S1: s.activation(S1[:, :], A[:, :], AF.Exp, scale=-1.0),
                   r=[A.b], w=[S1.b], tag=f"l2ea{g}_{th}")
            P.emit("s", lambda s, Bt=Bt, S2=S2: s.activation(S2[:, :], Bt[:, :], AF.Exp, scale=-1.0),
                   r=[Bt.b], w=[S2.b], tag=f"l2eb{g}_{th}")
            P.emit("v", lambda v, A=A, S1=S1, S2=S2: v.tensor_add(A[:, :], S1[:, :], S2[:, :]),
                   r=[S1.b, S2.b], w=[A.b], tag=f"l2den{g}_{th}")
            P.emit("v", lambda v, A=A: v.tensor_scalar(A[:, :], A[:, :], mk.cst[:, C_TWO:C_TWO + 1], None, ALU.add),
                   r=[mk.cst_b], w=[A.b], tag=f"l2den2{g}_{th}")
            P.emit("v", lambda v, A=A: v.reciprocal(A[:, :], A[:, :]),
                   r=[A.b], w=[A.b], tag=f"l2rec{g}_{th}")
            P.emit("v", lambda v, S2=S2: v.tensor_scalar(S2[:, :], S2[:, :], 1.0, None, ALU.add),
                   r=[S2.b], w=[S2.b], tag=f"l2f1{g}_{th}")
            P.emit("v", lambda v, S2=S2, A=A: v.tensor_mul(S2[:, :], S2[:, :], A[:, :]),
                   r=[A.b, S2.b], w=[S2.b], tag=f"l2f2{g}_{th}")
            P.emit("v", lambda v, S1=S1: v.tensor_scalar(S1[:, :], S1[:, :], 1.0, None, ALU.add),
                   r=[S1.b], w=[S1.b], tag=f"l2i1{g}_{th}")
            P.emit("v", lambda v, S1=S1, A=A: v.tensor_mul(S1[:, :], S1[:, :], A[:, :]),
                   r=[A.b, S1.b], w=[S1.b], tag=f"l2i2{g}_{th}")
            P.emit("s", lambda s, H=H, Bt=Bt: s.activation(Bt[:, :], H[:, :], AF.Sigmoid),
                   r=[H.b], w=[Bt.b], tag=f"l2sg{g}_{th}")
            P.emit("v", lambda v, H=H, m=m: v.tensor_scalar(m[:, :], H[:, :], 0.0, None, ALU.is_ge),
                   r=[H.b], w=[m.b], tag=f"l2ge{g}_{th}")
            P.emit("v", lambda v, H=H, A=A: v.tensor_scalar(A[:, :], H[:, :], mk.cst[:, C_HALF:C_HALF + 1], None, ALU.add),
                   r=[H.b, mk.cst_b], w=[A.b], tag=f"l2hp{g}_{th}")
            P.emit("v", lambda v, m=m, A=A, Bt=Bt: v.select(Bt[:, :], m[:, :], A[:, :], Bt[:, :]),
                   r=[m.b, A.b, Bt.b], w=[Bt.b], tag=f"l2sel{g}_{th}")
            P.emit("v", lambda v, S1=S1, Bt=Bt: v.tensor_mul(S1[:, :], S1[:, :], Bt[:, :]),
                   r=[Bt.b, S1.b], w=[S1.b], tag=f"l2w{g}_{th}")
            if th == 0:
                P.emit("v", lambda v, A=A, S2=S2, S1=S1: v.tensor_tensor_scan(
                    A[:, :], S2[:, :], S1[:, :], 1.0, ALU.mult, ALU.add),
                    r=[S2.b, S1.b], w=[A.b], tag=f"l2scan{g}_{th}")
            else:
                P.emit("v", lambda v, A=A, S2=S2, S1=S1, init=init: v.tensor_tensor_scan(
                    A[:, :], S2[:, :], S1[:, :], init[:, 0:1], ALU.mult, ALU.add),
                    r=[S2.b, S1.b, init.b], w=[A.b], tag=f"l2scan{g}_{th}")
            if th == 0:
                P.emit("v", lambda v, A=A, init=init: v.tensor_copy(init[:, 0:1], A[:, 2047:2048]),
                       r=[A.b], w=[init.b], tag=f"l2sv{g}")
            hbf = mk.B8.get(f"l2hb{g}_{th}", [128, 2048], BF16)
            P.emit("v", lambda v, A=A, hbf=hbf: v.tensor_copy(hbf[:, :], A[:, :]),
                   r=[A.b], w=[hbf.b], tag=f"l2cbf{g}_{th}")
            for jh in range(2):
                pt = ps.view(6 + jh, [128, 1024], BF16)
                for j in range(8):
                    P.emit("t", lambda t, jh=jh, j=j, hbf=hbf, pt=pt: t.transpose(
                        pt[:, j * 128:(j + 1) * 128],
                        hbf[:, (jh * 8 + j) * 128:(jh * 8 + j + 1) * 128], mk.ident[:]),
                        r=[hbf.b, mk.ident_b], w=[ps.bufs[6 + jh]], tag=f"l2tr{g}_{th}_{jh}_{j}")
                ev = mk.B2.get(f"l2ev{g}_{th}_{jh}", [128, 1024], BF16)
                P.emit("s", lambda s, ev=ev, pt=pt: s.activation(ev[:, :], pt[:, :], AF.Identity),
                       r=[ps.bufs[6 + jh]], w=[ev.b], tag=f"l2tre{g}_{th}_{jh}")
                P.dma("qy", lambda y, g=g, th=th, jh=jh, ev=ev: y.dma_start(
                    out=h_d[th * 2048 + jh * 1024:th * 2048 + (jh + 1) * 1024, g * 128:(g + 1) * 128]
                        .rearrange("(j p) c -> p j c", p=128),
                    in_=ev[:].rearrange("p (j c) -> p j c", j=8)),
                    r=[ev.b], w=[h_db[th * 2 + jh]], tag=f"l2hst{g}_{th}_{jh}")
                ev.free()
            for tl in (A, Bt, H, S1, S2, m, hbf):
                tl.free()
        init.free()
    bf_p.free(); bi_p.free(); bh_p.free()
    for i in range(NT):
        ht = mk.B2.get(f"l2hf{i}", [128, D], BF16)
        P.dma("qy", lambda y, i=i, ht=ht: y.dma_start(
            out=ht[:], in_=h_d[i * 128:(i + 1) * 128, :]),
            r=h_db, w=[ht.b], tag=f"l2hfl{i}")
        xt32 = mk.B4.get(f"l2xf{i}", [128, D], F32)
        P.dma("qy", lambda y, i=i, xt32=xt32: y.dma_start(
            out=xt32[:], in_=mk.x32[i * 128:(i + 1) * 128, :]),
            r=[mk.x32_b[i]], w=[xt32.b], tag=f"l2xfl{i}")
        yt = mk.B4.get(f"l2yf{i}", [128, D], F32)
        P.emit("v", lambda v, ht=ht, xt32=xt32, yt=yt: v.tensor_add(yt[:, :], xt32[:, :], ht[:, :]),
               r=[ht.b, xt32.b], w=[yt.b], tag=f"l2add{i}")
        ht.free(); xt32.free()
        mk.ln_store(i, yt, C_EPS5)
        yt.free()


def ln512(mk, src_t, src_b, gam, bet, dst_t, dst_b, tag):
    P = mk.P
    st, st_b, cst, cst_b = mk.stats, mk.stats_b, mk.cst, mk.cst_b
    trash = mk.trash4
    P.emit("v", lambda v: v.reduce_sum(st[:, 0:1], src_t[:, :], AX.X),
           r=[src_b], w=[st_b], tag=f"{tag}sum")
    P.emit("s", lambda s: s.activation(trash[:, 0:512], src_t[:, :], AF.Square, accum_out=st[:, 1:2]),
           r=[src_b], w=[trash.b, st_b], tag=f"{tag}sq")
    P.emit("v", lambda v: v.tensor_scalar(st[:, 0:1], st[:, 0:1], cst[:, C_INV512:C_INV512 + 1], None, ALU.mult),
           r=[cst_b], w=[st_b], tag=f"{tag}mu")
    P.emit("v", lambda v: v.tensor_scalar(st[:, 2:3], st[:, 0:1], st[:, 0:1], None, ALU.mult),
           r=[st_b], w=[st_b], tag=f"{tag}mu2")
    P.emit("v", lambda v: v.tensor_scalar(st[:, 1:2], st[:, 1:2], cst[:, C_INV512:C_INV512 + 1], st[:, 2:3], ALU.mult, ALU.subtract),
           r=[cst_b], w=[st_b], tag=f"{tag}var")
    P.emit("s", lambda s: s.activation(st[:, 3:4], st[:, 1:2], AF.Sqrt, bias=cst[:, C_EPS5:C_EPS5 + 1]),
           r=[cst_b, st_b], w=[st_b], tag=f"{tag}sd")
    P.emit("v", lambda v: v.reciprocal(st[:, 4:5], st[:, 3:4]),
           r=[st_b], w=[st_b], tag=f"{tag}rc")
    P.emit("v", lambda v: v.tensor_scalar(src_t[:, :], src_t[:, :], st[:, 0:1], st[:, 4:5], ALU.subtract, ALU.mult),
           r=[st_b], w=[src_b], tag=f"{tag}ap")
    P.emit("v", lambda v: v.tensor_mul(src_t[:, :], src_t[:, :], gam[:, :]),
           r=[src_b, gam.b], w=[src_b], tag=f"{tag}g")
    P.emit("v", lambda v: v.tensor_add(dst_t[:, :], src_t[:, :], bet[:, :]),
           r=[src_b, bet.b], w=[dst_b], tag=f"{tag}b")


def l2_ffn(mk):
    P, ps = mk.P, mk.psum
    sw1t = mk.inp("l2f_sw1t", [1024, 512])
    sw2t = mk.inp("l2f_sw2t", [1024, 512])
    sw3t = mk.inp("l2f_sw3t", [512, 1024])
    sb1_r = mk.load_row128("l2fsb1", mk.inp("l2f_sb1_r", [128, 512]), 512, pool=mk.B2)
    sb2_r = mk.load_row128("l2fsb2", mk.inp("l2f_sb2_r", [128, 512]), 512, pool=mk.B2)
    sb3_r = mk.load_row128("l2fsb3", mk.inp("l2f_sb3_r", [128, 1024]), 1024)
    slng = mk.load_row128("l2fslng", mk.inp("l2f_slng_r", [128, 512]), 512, pool=mk.B2)
    slnb = mk.load_row128("l2fslnb", mk.inp("l2f_slnb_r", [128, 512]), 512, pool=mk.B2)
    so_d = mk.dram("l2f_so", [T, D], F32)
    so_db = [Buf(f"l2fso{i}") for i in range(NT)]

    xT = mk.make_xT()
    w1s = [mk.B2.get(f"l2fw1_{j}", [128, 1024], BF16) for j in range(4)]
    w2s = [mk.B2.get(f"l2fw2_{j}", [128, 1024], BF16) for j in range(4)]
    w3s = [mk.B2.get(f"l2fw3_{j}", [128, 1024], BF16) for j in range(4)]
    for j in range(4):
        P.dma("qg", lambda g, j=j, tl=w1s[j]: g.dma_start(
            out=tl[:].rearrange("p (a c) -> p a c", a=2),
            in_=sw1t[j * 256:(j + 1) * 256, :].rearrange("(a p) c -> p a c", p=128)),
            w=[w1s[j].b], tag=f"l2fw1l{j}")
        P.dma("qg", lambda g, j=j, tl=w2s[j]: g.dma_start(
            out=tl[:].rearrange("p (a c) -> p a c", a=2),
            in_=sw2t[j * 256:(j + 1) * 256, :].rearrange("(a p) c -> p a c", p=128)),
            w=[w2s[j].b], tag=f"l2fw2l{j}")
    for j in range(4):
        P.dma("qg", lambda g, j=j, tl=w3s[j]: g.dma_start(
            out=tl[:], in_=sw3t[j * 128:(j + 1) * 128, :]), w=[w3s[j].b], tag=f"l2fw3l{j}")

    for i in range(NT):
        for b, ws in ((0, w1s), (1, w2s)):
            for dc in range(8):
                P.emit("t", lambda t, i=i, dc=dc, b=b, ws=ws: t.matmul(
                    ps.banks[b][:, :], xT[dc][:, i * 128:(i + 1) * 128],
                    ws[dc // 2][:, (dc % 2) * 512:(dc % 2 + 1) * 512],
                    start=(dc == 0), stop=(dc == 7)),
                    r=[xT[dc].b, ws[dc // 2].b], w=[ps.bufs[b]], tag=f"l2fx{b}_{i}_{dc}")
        t1 = mk.B2.get(f"l2ft1_{i}", [128, 512], F32)
        P.emit("v", lambda v, t1=t1: v.tensor_add(t1[:, :], ps.banks[0][:, :], sb1_r[:, :]),
               r=[ps.bufs[0], sb1_r.b], w=[t1.b], tag=f"l2fb1{i}")
        s1 = mk.B2.get(f"l2fs1_{i}", [128, 512], F32)
        P.emit("s", lambda s, t1=t1, s1=s1: s.activation(s1[:, :], t1[:, :], AF.Silu),
               r=[t1.b], w=[s1.b], tag=f"l2fsilu{i}")
        P.emit("v", lambda v, t1=t1: v.tensor_add(t1[:, :], ps.banks[1][:, :], sb2_r[:, :]),
               r=[ps.bufs[1], sb2_r.b], w=[t1.b], tag=f"l2fb2{i}")
        P.emit("v", lambda v, t1=t1, s1=s1: v.tensor_mul(t1[:, :], t1[:, :], s1[:, :]),
               r=[s1.b, t1.b], w=[t1.b], tag=f"l2fprod{i}")
        s1.free()
        shb = mk.B1.get(f"l2fshb{i}", [128, 512], BF16)
        ln512(mk, t1, t1.b, slng, slnb, shb, shb.b, f"l2fln{i}")
        t1.free()
        pt = ps.view(6, [128, 512], BF16)
        for j in range(4):
            P.emit("t", lambda t, j=j, shb=shb, pt=pt: t.transpose(
                pt[:, j * 128:(j + 1) * 128], shb[:, j * 128:(j + 1) * 128], mk.ident[:]),
                r=[shb.b, mk.ident_b], w=[ps.bufs[6]], tag=f"l2ftr{i}_{j}")
        sT = [mk.B1.get(f"l2fsT{i}_{j}", [128, 128], BF16) for j in range(4)]
        for j in range(4):
            P.emit("s", lambda s, j=j, pt=pt, sT=sT: s.activation(sT[j][:, :], pt[:, j * 128:(j + 1) * 128], AF.Identity),
                   r=[ps.bufs[6]], w=[sT[j].b], tag=f"l2ftre{i}_{j}")
        shb.free()
        for dh in range(2):
            bank = 2 + dh
            for j in range(4):
                P.emit("t", lambda t, j=j, dh=dh, bank=bank, sT=sT: t.matmul(
                    ps.banks[bank][:, :], sT[j][:, :], w3s[j][:, dh * 512:(dh + 1) * 512],
                    start=(j == 0), stop=(j == 3)),
                    r=[sT[j].b, w3s[j].b], w=[ps.bufs[bank]], tag=f"l2fo{i}_{dh}_{j}")
        so = mk.B4.get(f"l2fsov{i}", [128, D], F32)
        for dh in range(2):
            P.emit("v", lambda v, dh=dh, so=so: v.tensor_add(
                so[:, dh * 512:(dh + 1) * 512], ps.banks[2 + dh][:, :], sb3_r[:, dh * 512:(dh + 1) * 512]),
                r=[ps.bufs[2 + dh], sb3_r.b], w=[so.b], tag=f"l2fsoe{i}_{dh}")
        P.dma("qy", lambda y, i=i, so=so: y.dma_start(
            out=so_d[i * 128:(i + 1) * 128, :], in_=so[:]),
            r=[so.b], w=[so_db[i]], tag=f"l2fsos{i}")
        for tl in sT:
            tl.free()
        so.free()
    for tl in xT + w1s + w2s + w3s:
        tl.free()
    sb1_r.free(); sb2_r.free(); sb3_r.free(); slng.free(); slnb.free()

    if DBG.get("l2f_nomoe"):
        for i in range(NT):
            so = mk.B4.get(f"l2ffin{i}", [128, D], F32)
            P.dma("qy", lambda y, i=i, so=so: y.dma_start(
                out=so[:], in_=so_d[i * 128:(i + 1) * 128, :]),
                r=so_db, w=[so.b], tag=f"l2ffl{i}")
            xt32 = mk.B4.get(f"l2ffx{i}", [128, D], F32)
            P.dma("qy", lambda y, i=i, xt32=xt32: y.dma_start(
                out=xt32[:], in_=mk.x32[i * 128:(i + 1) * 128, :]),
                r=[mk.x32_b[i]], w=[xt32.b], tag=f"l2ffxl{i}")
            P.emit("v", lambda v, so=so, xt32=xt32: v.tensor_add(so[:, :], so[:, :], xt32[:, :]),
                   r=[so.b, xt32.b], w=[so.b], tag=f"l2ffadd{i}")
            xt32.free()
            mk.ln_store(i, so, C_EPS5)
            so.free()
        return
    M = 2048
    NMT = M // 128
    gidx = mk.load_row128("l2fgidx", mk.inp("l2f_gidx", [128, 16], I32), 16, dtype=I32, pool=mk.SM)
    sidx = mk.load_row128("l2fsidx", mk.inp("l2f_sidx", [128, 16], I32), 16, dtype=I32, pool=mk.SM)
    sgw = mk.load_row128("l2fsgw", mk.inp("l2f_sgw_p", [128, 8]), 8, dtype=BF16, pool=mk.SM)
    sgb = mk.load_row128("l2fsgb", mk.inp("l2f_sgb_p", [128, 1]), 1, pool=mk.SM)
    egw = mk.load_row128("l2fegw", mk.inp("l2f_egw", [128, 64]), 64, dtype=BF16, pool=mk.SM)
    egb = mk.load_row128("l2fegb", mk.inp("l2f_egb_r", [128, 8]), 8, pool=mk.SM)
    expb = mk.load_row128("l2fexpb", mk.inp("l2f_expb_r", [128, 8]), 8, pool=mk.SM)

    xm8 = [mk.B8.get(f"l2fxmT{j}", [128, 4096], BF16) for j in range(4)]
    xmTs = lambda dc, mt: xm8[dc // 2][:, (dc % 2) * 2048 + mt * 128:(dc % 2) * 2048 + (mt + 1) * 128]
    gmeta = mk.B1.get("l2fgmeta", [128, 256], F32)
    scr = mk.B1.get("l2fgscr", [128, 64], F32)
    P.emit("v", lambda v: v.memset(scr[:, 48:56], 1.0), w=[scr.b], tag="l2fones")
    for mt in range(NMT):
        xm = mk.B2.get(f"l2fxm{mt}", [128, 1024], BF16)
        P.dma("qg", lambda g, mt=mt, xm=xm: g.indirect_dma_start(
            xm[:], None, mk.xbf[:, :], IndirectOffsetOnAxis(gidx[:, mt:mt + 1], 0)),
            r=mk.xbf_b + [gidx.b], w=[xm.b], tag=f"l2fgx{mt}")
        pt = ps.view(7, [128, 1024], BF16)
        for j in range(8):
            P.emit("t", lambda t, j=j, xm=xm, pt=pt: t.transpose(
                pt[:, j * 128:(j + 1) * 128], xm[:, j * 128:(j + 1) * 128], mk.ident[:]),
                r=[xm.b, mk.ident_b], w=[ps.bufs[7]], tag=f"l2fxtr{mt}_{j}")
        for j in range(8):
            P.emit("s", lambda s, j=j, mt=mt, pt=pt: s.activation(
                xmTs(j, mt), pt[:, j * 128:(j + 1) * 128], AF.Identity),
                r=[ps.bufs[7]], w=[xm8[j // 2].b], tag=f"l2fxte{mt}_{j}")
        xm.free()
        pg = ps.view(6, [128, 16], F32)
        for dc in range(8):
            P.emit("t", lambda t, mt=mt, dc=dc, pg=pg: t.matmul(
                pg[:, 0:1], xmTs(dc, mt), sgw[:, dc:dc + 1],
                start=(dc == 0), stop=(dc == 7)),
                r=[xm8[dc // 2].b, sgw.b], w=[ps.bufs[6]], tag=f"l2fss{mt}_{dc}")
        for dc in range(8):
            P.emit("t", lambda t, mt=mt, dc=dc, pg=pg: t.matmul(
                pg[:, 8:16], xmTs(dc, mt), egw[:, dc * 8:(dc + 1) * 8],
                start=(dc == 0), stop=(dc == 7)),
                r=[xm8[dc // 2].b, egw.b], w=[ps.bufs[6]], tag=f"l2fes{mt}_{dc}")
        P.emit("s", lambda s, pg=pg: s.activation(scr[:, 32:33], pg[:, 0:1], AF.Sigmoid, bias=sgb[:, 0:1]),
               r=[ps.bufs[6], sgb.b], w=[scr.b], tag=f"l2fssg{mt}")
        P.emit("v", lambda v, pg=pg: v.tensor_add(scr[:, 0:8], pg[:, 8:16], egb[:, :]),
               r=[ps.bufs[6], egb.b], w=[scr.b], tag=f"l2fesb{mt}")
        P.emit("s", lambda s: s.activation(scr[:, 0:8], scr[:, 0:8], AF.Sigmoid),
               r=[scr.b], w=[scr.b], tag=f"l2fesg{mt}")
        P.emit("v", lambda v: v.tensor_add(scr[:, 8:16], scr[:, 0:8], expb[:, :]),
               r=[scr.b, expb.b], w=[scr.b], tag=f"l2fe2{mt}")
        P.emit("v", lambda v: v.tensor_reduce(scr[:, 33:34], scr[:, 8:16], AX.X, ALU.max),
               r=[scr.b], w=[scr.b], tag=f"l2fmx{mt}")
        P.emit("v", lambda v: v.tensor_scalar(scr[:, 16:24], scr[:, 8:16], scr[:, 33:34], None, ALU.is_equal),
               r=[scr.b], w=[scr.b], tag=f"l2feq{mt}")
        P.emit("v", lambda v: v.tensor_tensor_scan(scr[:, 24:32], scr[:, 48:56], scr[:, 16:24], 0.0, ALU.mult, ALU.add),
               r=[scr.b], w=[scr.b], tag=f"l2fcum{mt}")
        P.emit("v", lambda v: v.tensor_scalar(scr[:, 24:32], scr[:, 24:32], 1.0, None, ALU.is_equal),
               r=[scr.b], w=[scr.b], tag=f"l2ffo{mt}")
        P.emit("v", lambda v, mt=mt: v.tensor_mul(gmeta[:, mt * 16:mt * 16 + 8], scr[:, 16:24], scr[:, 24:32]),
               r=[scr.b], w=[gmeta.b], tag=f"l2ff1{mt}")
        P.emit("v", lambda v, mt=mt: v.tensor_mul(scr[:, 16:24], scr[:, 0:8], gmeta[:, mt * 16:mt * 16 + 8]),
               r=[scr.b, gmeta.b], w=[scr.b], tag=f"l2ft1{mt}")
        P.emit("v", lambda v: v.tensor_reduce(scr[:, 34:35], scr[:, 16:24], AX.X, ALU.add),
               r=[scr.b], w=[scr.b], tag=f"l2ft1s{mt}")
        P.emit("v", lambda v: v.tensor_sub(scr[:, 35:36], scr[:, 32:33], scr[:, 34:35]),
               r=[scr.b], w=[scr.b], tag=f"l2fd{mt}")
        P.emit("s", lambda s, mt=mt: s.activation(gmeta[:, mt * 16 + 8:mt * 16 + 9], scr[:, 35:36], AF.Sigmoid),
               r=[scr.b], w=[gmeta.b], tag=f"l2fwsh{mt}")
        P.emit("s", lambda s, mt=mt: s.activation(gmeta[:, mt * 16 + 9:mt * 16 + 10], scr[:, 35:36], AF.Sigmoid, scale=-1.0),
               r=[scr.b], w=[gmeta.b], tag=f"l2fwex{mt}")
    scr.free()

    eacc_d = mk.dram("l2f_eacc", [M, D], F32)
    eacc_db = [Buf(f"l2fe{mt}") for mt in range(NMT)]
    ew1 = mk.inp("l2f_ew1t", [8, 1024, 512])
    ew2 = mk.inp("l2f_ew2t", [8, 1024, 512])
    ew3 = mk.inp("l2f_ew3t", [8, 512, 1024])
    eb1 = mk.inp("l2f_eb1_r", [8, 128, 512])
    eb2 = mk.inp("l2f_eb2_r", [8, 128, 512])
    eb3 = mk.inp("l2f_eb3_r", [8, 128, 1024])
    selg = mk.load_row128("l2fselg", mk.inp("l2f_sellng_r", [128, 512]), 512, pool=mk.B2)
    selb = mk.load_row128("l2fselb", mk.inp("l2f_sellnb_r", [128, 512]), 512, pool=mk.B2)
    for e in range(8):
        we1 = [mk.B2.get(f"l2fe1_{e}_{j}", [128, 1024], BF16) for j in range(4)]
        we2 = [mk.B2.get(f"l2fe2_{e}_{j}", [128, 1024], BF16) for j in range(4)]
        we3 = [mk.B2.get(f"l2fe3_{e}_{j}", [128, 1024], BF16) for j in range(4)]
        for j in range(4):
            P.dma("qg", lambda g, e=e, j=j, tl=we1[j]: g.dma_start(
                out=tl[:].rearrange("p (a c) -> p a c", a=2),
                in_=ew1[e, j * 256:(j + 1) * 256, :].rearrange("(a p) c -> p a c", p=128)),
                w=[we1[j].b], tag=f"l2fe1l{e}_{j}")
            P.dma("qg", lambda g, e=e, j=j, tl=we2[j]: g.dma_start(
                out=tl[:].rearrange("p (a c) -> p a c", a=2),
                in_=ew2[e, j * 256:(j + 1) * 256, :].rearrange("(a p) c -> p a c", p=128)),
                w=[we2[j].b], tag=f"l2fe2l{e}_{j}")
            P.dma("qg", lambda g, e=e, j=j, tl=we3[j]: g.dma_start(
                out=tl[:], in_=ew3[e, j * 128:(j + 1) * 128, :]),
                w=[we3[j].b], tag=f"l2fe3l{e}_{j}")
        b1r = mk.load_row128(f"l2feb1{e}", eb1[e], 512, pool=mk.B2)
        b2r = mk.load_row128(f"l2feb2{e}", eb2[e], 512, pool=mk.B2)
        b3r = mk.load_row128(f"l2feb3{e}", eb3[e], 1024)
        for mt in range(NMT):
            for b, ws in ((0, we1), (1, we2)):
                for dc in range(8):
                    P.emit("t", lambda t, mt=mt, dc=dc, b=b, ws=ws: t.matmul(
                        ps.banks[b][:, :], xmTs(dc, mt),
                        ws[dc // 2][:, (dc % 2) * 512:(dc % 2 + 1) * 512],
                        start=(dc == 0), stop=(dc == 7)),
                        r=[xm8[dc // 2].b, ws[dc // 2].b], w=[ps.bufs[b]], tag=f"l2fh{b}_{e}_{mt}_{dc}")
            t1 = mk.B2.get(f"l2fet1_{e}_{mt}", [128, 512], F32)
            P.emit("v", lambda v, t1=t1, b1r=b1r: v.tensor_add(t1[:, :], ps.banks[0][:, :], b1r[:, :]),
                   r=[ps.bufs[0], b1r.b], w=[t1.b], tag=f"l2feb1a{e}_{mt}")
            s1 = mk.B2.get(f"l2fes1_{e}_{mt}", [128, 512], F32)
            P.emit("s", lambda s, t1=t1, s1=s1: s.activation(s1[:, :], t1[:, :], AF.Silu),
                   r=[t1.b], w=[s1.b], tag=f"l2fesilu{e}_{mt}")
            P.emit("v", lambda v, t1=t1, b2r=b2r: v.tensor_add(t1[:, :], ps.banks[1][:, :], b2r[:, :]),
                   r=[ps.bufs[1], b2r.b], w=[t1.b], tag=f"l2feb2a{e}_{mt}")
            P.emit("v", lambda v, t1=t1, s1=s1: v.tensor_mul(t1[:, :], t1[:, :], s1[:, :]),
                   r=[s1.b, t1.b], w=[t1.b], tag=f"l2feprod{e}_{mt}")
            s1.free()
            if DBG.get("l2f_dump") and e == 0 and mt == 0:
                P.dma("qy", lambda y, t1=t1: y.dma_start(out=DBG["l2f_dump"]["t1"], in_=t1[:]),
                      r=[t1.b], tag="dt1")
            zb = mk.B1.get(f"l2fzb{e}_{mt}", [128, 512], BF16)
            ln512(mk, t1, t1.b, selg, selb, zb, zb.b, f"l2fzln{e}_{mt}")
            t1.free()
            if DBG.get("l2f_dump") and e == 0 and mt == 0:
                P.dma("qy", lambda y, zb=zb: y.dma_start(out=DBG["l2f_dump"]["zb"], in_=zb[:]),
                      r=[zb.b], tag="dzb")
            pt = ps.view(6, [128, 512], BF16)
            for j in range(4):
                P.emit("t", lambda t, j=j, zb=zb, pt=pt: t.transpose(
                    pt[:, j * 128:(j + 1) * 128], zb[:, j * 128:(j + 1) * 128], mk.ident[:]),
                    r=[zb.b, mk.ident_b], w=[ps.bufs[6]], tag=f"l2fztr{e}_{mt}_{j}")
            zT = [mk.B1.get(f"l2fzT{e}_{mt}_{j}", [128, 128], BF16) for j in range(4)]
            for j in range(4):
                P.emit("s", lambda s, j=j, pt=pt, zT=zT: s.activation(zT[j][:, :], pt[:, j * 128:(j + 1) * 128], AF.Identity),
                       r=[ps.bufs[6]], w=[zT[j].b], tag=f"l2fztre{e}_{mt}_{j}")
            zb.free()
            if DBG.get("l2f_dump") and e == 0 and mt == 0:
                P.dma("qy", lambda y, zT=zT: y.dma_start(out=DBG["l2f_dump"]["zT0"], in_=zT[0][:]),
                      r=[zT[0].b], tag="dzT0")
            for dh in range(2):
                bank = 2 + dh
                for j in range(4):
                    P.emit("t", lambda t, j=j, dh=dh, bank=bank, zT=zT, we3=we3: t.matmul(
                        ps.banks[bank][:, :], zT[j][:, :], we3[j][:, dh * 512:(dh + 1) * 512],
                        start=(j == 0), stop=(j == 3)),
                        r=[zT[j].b, we3[j].b], w=[ps.bufs[bank]], tag=f"l2feo{e}_{mt}_{dh}_{j}")
            acc = mk.B4.get(f"l2facc{e}_{mt}", [128, D], F32)
            if e > 0:
                P.dma("qy", lambda y, mt=mt, acc=acc: y.dma_start(
                    out=acc[:], in_=eacc_d[mt * 128:(mt + 1) * 128, :]),
                    r=[eacc_db[mt]], w=[acc.b], tag=f"l2faccl{e}_{mt}")
            tmp = mk.B4.get(f"l2ftmp{e}_{mt}", [128, D], F32)
            for dh in range(2):
                P.emit("v", lambda v, dh=dh, tmp=tmp, b3r=b3r: v.tensor_add(
                    tmp[:, dh * 512:(dh + 1) * 512], ps.banks[2 + dh][:, :], b3r[:, dh * 512:(dh + 1) * 512]),
                    r=[ps.bufs[2 + dh], b3r.b], w=[tmp.b], tag=f"l2feba{e}_{mt}_{dh}")
            if DBG.get("l2f_dump") and e == 0 and mt == 0:
                P.dma("qy", lambda y, tmp=tmp: y.dma_start(out=DBG["l2f_dump"]["tmp"], in_=tmp[:]),
                      r=[tmp.b], tag="dtmp")
            if e > 0:
                P.emit("v", lambda v, e=e, mt=mt, acc=acc, tmp=tmp: v.scalar_tensor_tensor(
                    acc[:, :], tmp[:, :], gmeta[:, mt * 16 + e:mt * 16 + e + 1], acc[:, :],
                    ALU.mult, ALU.add),
                    r=[tmp.b, gmeta.b, acc.b], w=[acc.b], tag=f"l2fsel{e}_{mt}")
            else:
                P.emit("v", lambda v, mt=mt, acc=acc, tmp=tmp: v.tensor_scalar(
                    acc[:, :], tmp[:, :], gmeta[:, mt * 16:mt * 16 + 1], None, ALU.mult),
                    r=[tmp.b, gmeta.b], w=[acc.b], tag=f"l2fsel0_{mt}")
            tmp.free()
            P.dma("qy", lambda y, mt=mt, acc=acc: y.dma_start(
                out=eacc_d[mt * 128:(mt + 1) * 128, :], in_=acc[:]),
                r=[acc.b], w=[eacc_db[mt]], tag=f"l2faccs{e}_{mt}")
            acc.free()
            for tl in zT:
                tl.free()
        for tl in we1 + we2 + we3:
            tl.free()
        b1r.free(); b2r.free(); b3r.free()
    for tl in xm8:
        tl.free()
    selg.free(); selb.free()

    for mt in range(NMT):
        sog = mk.B4.get(f"l2fsog{mt}", [128, D], F32)
        P.dma("qg", lambda g, mt=mt, sog=sog: g.indirect_dma_start(
            sog[:], None, so_d[:, :], IndirectOffsetOnAxis(gidx[:, mt:mt + 1], 0)),
            r=so_db + [gidx.b], w=[sog.b], tag=f"l2fsog{mt}")
        acc = mk.B4.get(f"l2ffacc{mt}", [128, D], F32)
        P.dma("qy", lambda y, mt=mt, acc=acc: y.dma_start(
            out=acc[:], in_=eacc_d[mt * 128:(mt + 1) * 128, :]),
            r=[eacc_db[mt]], w=[acc.b], tag=f"l2ffal{mt}")
        P.emit("v", lambda v, mt=mt, sog=sog: v.tensor_scalar(
            sog[:, :], sog[:, :], gmeta[:, mt * 16 + 8:mt * 16 + 9], None, ALU.mult),
            r=[gmeta.b], w=[sog.b], tag=f"l2fshw{mt}")
        P.emit("v", lambda v, mt=mt, sog=sog, acc=acc: v.scalar_tensor_tensor(
            sog[:, :], acc[:, :], gmeta[:, mt * 16 + 9:mt * 16 + 10], sog[:, :],
            ALU.mult, ALU.add),
            r=[acc.b, gmeta.b, sog.b], w=[sog.b], tag=f"l2fcomb{mt}")
        acc.free()
        P.dma("qg", lambda g, mt=mt, sog=sog: g.indirect_dma_start(
            so_d[:, :], IndirectOffsetOnAxis(sidx[:, mt:mt + 1], 0), sog[:], None,
            bounds_check=T - 1, oob_is_err=False),
            r=[sog.b, sidx.b], w=so_db, tag=f"l2fscat{mt}")
        sog.free()
    gmeta.free()
    gidx.free(); sidx.free(); sgw.free(); sgb.free(); egw.free(); egb.free(); expb.free()

    if DBG.get("l2f_dump"):
        P.dma("qy", lambda y: y.dma_start(out=DBG["l2f_dump"]["so"], in_=so_d), r=so_db, tag="dso")
        P.dma("qy", lambda y: y.dma_start(out=DBG["l2f_dump"]["eacc"], in_=eacc_d), r=eacc_db, tag="deacc")
        gm2 = mk.B1.get("dgm", [128, 256], F32)
        P.emit("v", lambda v, gm2=gm2: v.tensor_copy(gm2[:], gmeta[:]), r=[gmeta.b], w=[gm2.b], tag="dgmc")
        P.dma("qy", lambda y, gm2=gm2: y.dma_start(out=DBG["l2f_dump"]["gmeta"], in_=gm2[:]), r=[gm2.b], tag="dgm")
        gm2.free()
    for i in range(NT):
        so = mk.B4.get(f"l2ffin{i}", [128, D], F32)
        P.dma("qy", lambda y, i=i, so=so: y.dma_start(
            out=so[:], in_=so_d[i * 128:(i + 1) * 128, :]),
            r=so_db, w=[so.b], tag=f"l2ffl{i}")
        xt32 = mk.B4.get(f"l2ffx{i}", [128, D], F32)
        P.dma("qy", lambda y, i=i, xt32=xt32: y.dma_start(
            out=xt32[:], in_=mk.x32[i * 128:(i + 1) * 128, :]),
            r=[mk.x32_b[i]], w=[xt32.b], tag=f"l2ffxl{i}")
        P.emit("v", lambda v, so=so, xt32=xt32: v.tensor_add(so[:, :], so[:, :], xt32[:, :]),
               r=[so.b, xt32.b], w=[so.b], tag=f"l2ffadd{i}")
        xt32.free()
        mk.ln_store(i, so, C_EPS5)
        so.free()


def l3_attn_local_only(mk):
    l3_attn(mk, mode="local")


def l3_attn_global_only(mk):
    l3_attn(mk, mode="global")


def l3_attn(mk, mode="full"):
    P, ps = mk.P, mk.psum
    if mode == "global":
        scr_d = mk.inp("scr", [T, D], BF16)
        scr_db = [Buf("l3scr")]
        _l3_global(mk, scr_d, scr_db)
        return
    winl = mk.inp("l3_winl", [1024, 3072])
    binl = [mk.load_row128(f"l3binl{j}", mk.inp("l3_binl_r", [128, 3072])[:, j * 1024:(j + 1) * 1024], 1024)
            for j in range(3)]
    woutl = mk.inp("l3_woutl", [1024, 1024])
    boutl = mk.load_row128("l3boutl", mk.inp("l3_boutl_r", [128, 1024]), 1024)

    qkv_d = mk.dram("l3_qkv", [T, 3072], BF16)
    qkv_db = [Buf(f"l3qkv{i}") for i in range(NT)]
    if mode == "local":
        contrib_d = mk.out("loc", [T, D], BF16)
    else:
        contrib_d = mk.dram("l3_contrib", [T, D], BF16)
    contrib_db = [Buf(f"l3ct{i}") for i in range(NT)]
    scr_d = mk.dram("l3_scr", [T, D], BF16)
    scr_db = [Buf("l3scr")]

    xT = mk.make_xT(tag="l3xT")
    for g6 in range(6):
        wc = mk.B8.get(f"l3w{g6}", [128, 4096], BF16)
        P.dma("qg", lambda g, g6=g6, wc=wc: g.dma_start(
            out=wc[:].rearrange("p (dc c) -> p dc c", dc=8),
            in_=winl[:, g6 * 512:(g6 + 1) * 512].rearrange("(dc p) c -> p dc c", p=128)),
            w=[wc.b], tag=f"l3wl{g6}")
        for i in range(NT):
            bank = i % 2
            for dc in range(8):
                P.emit("t", lambda t, i=i, dc=dc, bank=bank, wc=wc: t.matmul(
                    ps.banks[bank][:, :], xT[dc][:, i * 128:(i + 1) * 128],
                    wc[:, dc * 512:(dc + 1) * 512],
                    start=(dc == 0), stop=(dc == 7)),
                    r=[xT[dc].b, wc.b], w=[ps.bufs[bank]], tag=f"l3qkv{i}_{g6}_{dc}")
            ot = mk.B1.get(f"l3q{i}_{g6}", [128, 512], BF16)
            P.emit("v", lambda v, g6=g6, bank=bank, ot=ot, binl=binl: v.tensor_add(
                ot[:, :], ps.banks[bank][:, :],
                binl[g6 // 2][:, (g6 % 2) * 512:(g6 % 2 + 1) * 512]),
                r=[ps.bufs[bank], binl[g6 // 2].b], w=[ot.b], tag=f"l3qkve{i}_{g6}")
            P.dma("qy", lambda y, i=i, g6=g6, ot=ot: y.dma_start(
                out=qkv_d[i * 128:(i + 1) * 128, g6 * 512:(g6 + 1) * 512], in_=ot[:]),
                r=[ot.b], w=[qkv_db[i]], tag=f"l3qkvs{i}_{g6}")
            ot.free()
        wc.free()
    for tl in xT:
        tl.free()
    for b in binl:
        b.free()

    wol8 = [mk.B8.get(f"l3wol{j}", [128, 4096], BF16) for j in range(2)]
    for j in range(2):
        P.dma("qg", lambda g, j=j, wol8=wol8: g.dma_start(
            out=wol8[j][:].rearrange("p (a c) -> p a c", a=4),
            in_=woutl[j * 512:(j + 1) * 512, :].rearrange("(a p) c -> p a c", p=128)),
            w=[wol8[j].b], tag=f"l3woll{j}")
    wols = lambda hc: wol8[hc // 4][:, (hc % 4) * 1024:(hc % 4) * 1024 + 1024]
    for pt in range(4):
        qts = [mk.B2.get(f"l3lq{pt}_{c}", [128, 1024], BF16) for c in range(8)]
        kts = [mk.B2.get(f"l3lk{pt}_{c}", [128, 1024], BF16) for c in range(8)]
        for c in range(8):
            P.dma("qy", lambda y, pt=pt, c=c, qts=qts: y.dma_start(
                out=qts[c][:], in_=qkv_d[c * 512 + pt * 128:c * 512 + (pt + 1) * 128, 0:1024]),
                r=[qkv_db[(c * 512 + pt * 128) // 128]], w=[qts[c].b], tag=f"l3lql{pt}_{c}")
            P.dma("qy", lambda y, pt=pt, c=c, kts=kts: y.dma_start(
                out=kts[c][:], in_=qkv_d[c * 512 + pt * 128:c * 512 + (pt + 1) * 128, 1024:2048]),
                r=[qkv_db[(c * 512 + pt * 128) // 128]], w=[kts[c].b], tag=f"l3lkl{pt}_{c}")
        sc = mk.B4.get(f"l3sc{pt}", [128, 1024], F32)
        prodb = mk.B2.get(f"l3pr{pt}", [128, 1024], BF16)
        for i in range(8):
            for j in range(8):
                P.emit("v", lambda v, i=i, j=j, prodb=prodb, qts=qts, kts=kts: v.tensor_mul(
                    prodb[:, :], qts[i][:, :], kts[j][:, :]),
                    r=[qts[i].b, kts[j].b], w=[prodb.b], tag=f"l3m{pt}_{i}_{j}")
                P.emit("v", lambda v, i=i, j=j, prodb=prodb, sc=sc: v.tensor_reduce(
                    sc[:].rearrange("p (h i j) -> p h i j", h=16, i=8)[:, :, i, j],
                    prodb[:].rearrange("p (h c) -> p h c", h=16), AX.X, ALU.add),
                    r=[prodb.b], w=[sc.b], tag=f"l3r{pt}_{i}_{j}")
        prodb.free()
        for c in range(8):
            kts[c].free()
            qts[c].free()
        P.emit("v", lambda v, sc=sc: v.tensor_scalar(sc[:, :], sc[:, :], mk.cst[:, C_NEG125:C_NEG125 + 1], None, ALU.mult),
               r=[mk.cst_b], w=[sc.b], tag=f"l3scl{pt}")
        mx = mk.B4.get(f"l3mx{pt}", [128, 128], F32)
        P.emit("v", lambda v, sc=sc, mx=mx: v.tensor_reduce(
            mx[:, :], sc[:].rearrange("p (hi j) -> p hi j", j=8), AX.X, ALU.min),
            r=[sc.b], w=[mx.b], tag=f"l3mx{pt}")
        for j in range(8):
            P.emit("v", lambda v, j=j, sc=sc, mx=mx: v.tensor_sub(
                sc[:].rearrange("p (hi j) -> p hi j", j=8)[:, :, j],
                sc[:].rearrange("p (hi j) -> p hi j", j=8)[:, :, j], mx[:, :]),
                r=[mx.b], w=[sc.b], tag=f"l3sub{pt}_{j}")
        P.emit("s", lambda s, sc=sc: s.activation(sc[:, :], sc[:, :], AF.Exp, scale=-1.0),
               r=[sc.b], w=[sc.b], tag=f"l3exp{pt}")
        P.emit("v", lambda v, sc=sc, mx=mx: v.tensor_reduce(
            mx[:, :], sc[:].rearrange("p (hi j) -> p hi j", j=8), AX.X, ALU.add),
            r=[sc.b], w=[mx.b], tag=f"l3z{pt}")
        P.emit("v", lambda v, mx=mx: v.reciprocal(mx[:, :], mx[:, :]),
               r=[mx.b], w=[mx.b], tag=f"l3rz{pt}")
        for j in range(8):
            P.emit("v", lambda v, j=j, sc=sc, mx=mx: v.tensor_mul(
                sc[:].rearrange("p (hi j) -> p hi j", j=8)[:, :, j],
                sc[:].rearrange("p (hi j) -> p hi j", j=8)[:, :, j], mx[:, :]),
                r=[mx.b], w=[sc.b], tag=f"l3nm{pt}_{j}")
        mx.free()
        vts = [mk.B2.get(f"l3lv{pt}_{c}", [128, 1024], BF16) for c in range(8)]
        for c in range(8):
            P.dma("qy", lambda y, pt=pt, c=c, vts=vts: y.dma_start(
                out=vts[c][:], in_=qkv_d[c * 512 + pt * 128:c * 512 + (pt + 1) * 128, 2048:3072]),
                r=[qkv_db[(c * 512 + pt * 128) // 128]], w=[vts[c].b], tag=f"l3lvl{pt}_{c}")
        for i in range(8):
            oi = mk.B4.get(f"l3oi{pt}_{i}", [128, 1024], F32)
            for j in range(8):
                for h in range(16):
                    eng = "v"
                    col = h * 64 + i * 8 + j
                    if j == 0:
                        P.emit(eng, lambda v, h=h, j=j, col=col, oi=oi, sc=sc, vts=vts: v.tensor_scalar(
                            oi[:, h * 64:(h + 1) * 64], vts[j][:, h * 64:(h + 1) * 64],
                            sc[:, col:col + 1], None, ALU.mult),
                            r=[vts[j].b, sc.b], w=[oi.b], tag=f"l3av{pt}_{i}_{j}_{h}")
                    else:
                        P.emit(eng, lambda v, h=h, j=j, col=col, oi=oi, sc=sc, vts=vts: v.scalar_tensor_tensor(
                            oi[:, h * 64:(h + 1) * 64], vts[j][:, h * 64:(h + 1) * 64],
                            sc[:, col:col + 1], oi[:, h * 64:(h + 1) * 64], ALU.mult, ALU.add),
                            r=[vts[j].b, sc.b, oi.b], w=[oi.b], tag=f"l3av{pt}_{i}_{j}_{h}")
            ob = mk.B2.get(f"l3ob{pt}_{i}", [128, 1024], BF16)
            P.emit("v", lambda v, oi=oi, ob=ob: v.tensor_copy(ob[:, :], oi[:, :]),
                   r=[oi.b], w=[ob.b], tag=f"l3obc{pt}_{i}")
            oi.free()
            ptp = ps.view(6, [128, 1024], BF16)
            for j8 in range(8):
                P.emit("t", lambda t, j8=j8, ob=ob, ptp=ptp: t.transpose(
                    ptp[:, j8 * 128:(j8 + 1) * 128], ob[:, j8 * 128:(j8 + 1) * 128], mk.ident[:]),
                    r=[ob.b, mk.ident_b], w=[ps.bufs[6]], tag=f"l3otr{pt}_{i}_{j8}")
            otb = mk.B2.get(f"l3otb{pt}_{i}", [128, 1024], BF16)
            P.emit("s", lambda s, otb=otb, ptp=ptp: s.activation(otb[:, :], ptp[:, :], AF.Identity),
                   r=[ps.bufs[6]], w=[otb.b], tag=f"l3otre{pt}_{i}")
            ob.free()
            for dh in range(2):
                bank = 4 + dh
                for hc in range(8):
                    P.emit("t", lambda t, hc=hc, dh=dh, bank=bank, otb=otb: t.matmul(
                        ps.banks[bank][:, :], otb[:, hc * 128:(hc + 1) * 128],
                        wols(hc)[:, dh * 512:(dh + 1) * 512],
                        start=(hc == 0), stop=(hc == 7)),
                        r=[otb.b, wol8[hc // 4].b], w=[ps.bufs[bank]], tag=f"l3oo{pt}_{i}_{dh}_{hc}")
            lo = mk.B2.get(f"l3lo{pt}_{i}", [128, 1024], BF16)
            for dh in range(2):
                P.emit("v", lambda v, dh=dh, lo=lo: v.tensor_add(
                    lo[:, dh * 512:(dh + 1) * 512], ps.banks[4 + dh][:, :],
                    boutl[:, dh * 512:(dh + 1) * 512]),
                    r=[ps.bufs[4 + dh], boutl.b], w=[lo.b], tag=f"l3loe{pt}_{i}_{dh}")
            otb.free()
            P.dma("qy", lambda y, pt=pt, i=i, lo=lo: y.dma_start(
                out=contrib_d[i * 512 + pt * 128:i * 512 + (pt + 1) * 128, :], in_=lo[:]),
                r=[lo.b], w=[contrib_db[(i * 512 + pt * 128) // 128]], tag=f"l3los{pt}_{i}")
            lo.free()
        for c in range(8):
            vts[c].free()
        sc.free()
    for wc in wol8:
        wc.free()
    boutl.free()

    if mode == "local":
        return
    if DBG.get("l3_noa2a"):
        for i in range(NT):
            tl = mk.B2.get(f"l3cp{i}", [128, D], BF16)
            P.dma("qy", lambda y, i=i, tl=tl: y.dma_start(
                out=tl[:], in_=contrib_d[i * 128:(i + 1) * 128, :]),
                r=[contrib_db[i]], w=[tl.b], tag=f"l3cpl{i}")
            P.dma("qy", lambda y, i=i, tl=tl: y.dma_start(
                out=scr_d[i * 128:(i + 1) * 128, :], in_=tl[:]),
                r=[tl.b], w=scr_db, tag=f"l3cps{i}")
            tl.free()
    else:
        from concourse.replica_groups import maybe_share_collective_output_space
        space = maybe_share_collective_output_space("AllGather", [[c for c in range(8)]])
        gat_d = mk.nc.dram_tensor("l3_gat", [8 * T, D], BF16, kind="Internal",
                                  addr_space=space).ap()
        gat_db = [Buf("l3gat")]
        P.coll(lambda g: g.collective_compute(
            "AllGather", ALU.bypass, [[c for c in range(8)]], [contrib_d[:, :]], [gat_d[:, :]]),
            r=contrib_db, w=gat_db, tag="l3ag")
        gidx3 = mk.load_row128("l3gidx", mk.inp("l3_gidx", [128, 32], I32), 32, dtype=I32, pool=mk.SM)
        for i in range(NT):
            tl = mk.B2.get(f"l3sl{i}", [128, D], BF16)
            P.dma("qg", lambda g, i=i, tl=tl, gat_d=gat_d, gidx3=gidx3: g.indirect_dma_start(
                tl[:], None, gat_d[:, :], IndirectOffsetOnAxis(gidx3[:, i:i + 1], 0)),
                r=gat_db + [gidx3.b], w=[tl.b], tag=f"l3slg{i}")
            P.dma("qy", lambda y, i=i, tl=tl: y.dma_start(
                out=scr_d[i * 128:(i + 1) * 128, :], in_=tl[:]),
                r=[tl.b], w=scr_db, tag=f"l3sls{i}")
            tl.free()
        gidx3.free()

    _l3_global(mk, scr_d, scr_db)


def _l3_global(mk, scr_d, scr_db):
    P, ps = mk.P, mk.psum
    wing_qk = mk.inp("l3_wing_qk", [1024, 2048])
    bing_qk = mk.load_row128("l3bingqk", mk.inp("l3_bing_qk_p", [128, 16]), 16, pool=mk.SM)
    wing_v = mk.inp("l3_wing_v", [1024, 1024])
    bing_v = mk.load_row128("l3bingv", mk.inp("l3_bing_v_r", [128, 1024]), 1024)
    woutg = mk.inp("l3_woutg", [1024, 1024])
    boutg = mk.load_row128("l3boutg", mk.inp("l3_boutg_r", [128, 1024]), 1024)
    qt_d = mk.dram("l3g_qt", [1024, T], BF16)
    kt_d = mk.dram("l3g_kt", [1024, T], BF16)
    v_d = mk.dram("l3g_v", [8, 128, T], BF16)
    ot_d = mk.dram("l3g_ot", [1024, T], BF16)
    qt_db = [Buf(f"l3gqt{g}") for g in range(8)]
    kt_db = [Buf(f"l3gkt{g}") for g in range(8)]
    v_db = [Buf(f"l3gv{i}") for i in range(NT)]
    ot_db = [Buf(f"l3got{g}") for g in range(8)]
    sT = mk.make_xT(src=scr_d, src_b=scr_db, tag="l3sT")
    proj_to_rows(mk, wing_qk, bing_qk, None, None, sT, 16, "l3gqk", dst2=(qt_d, qt_db, kt_d, kt_db))
    proj_v(mk, wing_v, bing_v, v_d, v_db, sT, "l3gv")
    for tl in sT:
        tl.free()
    bing_qk.free(); bing_v.free()
    attn_core(mk, qt_d, qt_db, kt_d, kt_db, v_d, v_db, ot_d, ot_db, "l3g")
    attn_outproj(mk, ot_d, ot_db, woutg, boutg, "l3go")
    boutg.free()



def host_consts(inputs):
    cst = np.zeros((128, N_CST), np.float32)
    cst[:, C_EPS5] = 1e-5
    cst[:, C_EPS3] = 1e-3
    cst[:, C_INV1024] = 1.0 / 1024.0
    cst[:, C_INV512] = 1.0 / 512.0
    cst[:, C_S0] = np.float32(inputs["L1_ffn_s0"])
    cst[:, C_S1] = np.float32(inputs["L1_ffn_s1"])
    cst[:, C_S2] = np.float32(inputs["L1_ffn_s2"])
    cst[:, C_HALF] = 0.5
    cst[:, C_ONE] = 1.0
    cst[:, C_NEG125] = -0.125
    cst[:, C_NBIG] = -1e30
    cst[:, C_INV64] = 1.0 / 64.0
    cst[:, C_FOUR] = 4.0
    cst[:, C_TWO] = 2.0
    cst[:, C_NEG1] = -1.0
    return cst


def host_l0a(inputs):
    rep = lambda a: np.tile(np.asarray(a, np.float32).reshape(1, -1), (128, 1))
    perp = lambda a: np.ascontiguousarray(np.asarray(a, np.float32).reshape(-1, 128).T)
    return {
        "l0_wq": np.ascontiguousarray(inputs["L0_attn_Wq"].reshape(1024, 1024)),
        "l0_wk": np.ascontiguousarray(inputs["L0_attn_Wk"].reshape(1024, 1024)),
        "l0_wv": np.ascontiguousarray(inputs["L0_attn_Wv"].reshape(1024, 1024)),
        "l0_wo": np.ascontiguousarray(inputs["L0_attn_Wo"].reshape(1024, 1024)),
        "l0_bq_p": perp(inputs["L0_attn_bq"]),
        "l0_bk_p": perp(inputs["L0_attn_bk"]),
        "l0_bv_r": rep(inputs["L0_attn_bv"]),
        "l0_bo_r": rep(inputs["L0_attn_bo"]),
        "l0_gamma_r": rep(inputs["L0_attn_gamma"]),
        "l0_beta_r": rep(inputs["L0_attn_beta"]),
    }


def host_l0f(inputs):
    return {
        "l0_gate": np.ascontiguousarray(inputs["L0_ffn_gate_w"].T),
        "l0_w1t": np.ascontiguousarray(np.transpose(inputs["L0_ffn_w1"], (0, 2, 1))),
        "l0_w2t": np.ascontiguousarray(np.transpose(inputs["L0_ffn_w2"], (0, 2, 1))),
    }


def host_l1(inputs):
    W = inputs["L1_attn_W_qkv"].T
    qcols = np.concatenate([W[:, r * 192:r * 192 + 64] for r in range(16)], axis=1)
    kcols = np.concatenate([W[:, r * 192 + 64:r * 192 + 128] for r in range(16)], axis=1)
    vcols = np.concatenate([W[:, r * 192 + 128:r * 192 + 192] for r in range(16)], axis=1)
    b = inputs["L1_attn_b_qkv"]
    bq = np.concatenate([b[r * 192:r * 192 + 64] for r in range(16)])
    bk = np.concatenate([b[r * 192 + 64:r * 192 + 128] for r in range(16)])
    bv = np.concatenate([b[r * 192 + 128:r * 192 + 192] for r in range(16)])
    bqk = np.concatenate([bq, bk]).reshape(16, 128).T
    return {
        "l1_wqk": np.ascontiguousarray(np.concatenate([qcols, kcols], axis=1)),
        "l1_bqk_p": np.ascontiguousarray(bqk),
        "l1_wv": np.ascontiguousarray(vcols),
        "l1_bv_r": np.tile(bv[None, :], (128, 1)),
        "l1_wo": np.ascontiguousarray(inputs["L1_attn_W_o"].T),
        "l1_bo_r": np.tile(inputs["L1_attn_b_o"][None, :], (128, 1)),
        "l1_k0": inputs["L1_ffn_k0"].astype(np.uint8),
        "l1_k1": inputs["L1_ffn_k1"].astype(np.uint8),
        "l1_k2": inputs["L1_ffn_k2"].astype(np.uint8),
    }


def host_l2a(inputs):
    perp = lambda a: np.ascontiguousarray(np.asarray(a, np.float32).reshape(-1, 128).T)
    return {
        "l2_wf": np.ascontiguousarray(inputs["L2_attn_W_f"].T),
        "l2_wi": np.ascontiguousarray(inputs["L2_attn_W_i"].T),
        "l2_wh": np.ascontiguousarray(inputs["L2_attn_W_h"].T),
        "l2_bf_p": perp(inputs["L2_attn_b_f"]),
        "l2_bi_p": perp(inputs["L2_attn_b_i"]),
        "l2_bh_p": perp(inputs["L2_attn_b_h"]),
    }


def host_l2f(inputs, core):
    rep = lambda a: np.tile(np.asarray(a, np.float32).reshape(1, -1), (128, 1))
    ib, it = inputs["L2_ffn_index_b"], inputs["L2_ffn_index_t"]
    sel = [int(t) for b, t in zip(ib, it) if int(b) == core]
    gi = np.zeros(2048, np.int32)
    si = np.full(2048, 1 << 20, np.int32)
    gi[:len(sel)] = sel
    si[:len(sel)] = sel
    perp16 = lambda a: np.ascontiguousarray(a.reshape(16, 128).T)
    return {
        "l2f_sw1t": np.ascontiguousarray(inputs["L2_ffn_sw1"].T),
        "l2f_sw2t": np.ascontiguousarray(inputs["L2_ffn_sw2"].T),
        "l2f_sw3t": np.ascontiguousarray(inputs["L2_ffn_sw3"].T),
        "l2f_sb1_r": rep(inputs["L2_ffn_sb1"]),
        "l2f_sb2_r": rep(inputs["L2_ffn_sb2"]),
        "l2f_sb3_r": rep(inputs["L2_ffn_sb3"]),
        "l2f_slng_r": rep(inputs["L2_ffn_s_ln_g"]),
        "l2f_slnb_r": rep(inputs["L2_ffn_s_ln_b"]),
        "l2f_gidx": perp16(gi),
        "l2f_sidx": perp16(si),
        "l2f_sgw_p": np.ascontiguousarray(inputs["L2_ffn_sg_w"].reshape(8, 128).T),
        "l2f_sgb_p": np.tile(np.asarray(inputs["L2_ffn_sg_b"], np.float32).reshape(1, 1), (128, 1)),
        "l2f_egw": np.ascontiguousarray(inputs["L2_ffn_eg_w"].T.reshape(8, 128, 8).transpose(1, 0, 2).reshape(128, 64)),
        "l2f_egb_r": rep(inputs["L2_ffn_eg_b"]),
        "l2f_expb_r": rep(inputs["L2_ffn_exp_bias"]),
        "l2f_ew1t": np.ascontiguousarray(np.transpose(inputs["L2_ffn_ew1"], (0, 2, 1))),
        "l2f_ew2t": np.ascontiguousarray(np.transpose(inputs["L2_ffn_ew2"], (0, 2, 1))),
        "l2f_ew3t": np.ascontiguousarray(np.transpose(inputs["L2_ffn_ew3"], (0, 2, 1))),
        "l2f_eb1_r": np.ascontiguousarray(np.tile(inputs["L2_ffn_eb1"][:, None, :], (1, 128, 1))),
        "l2f_eb2_r": np.ascontiguousarray(np.tile(inputs["L2_ffn_eb2"][:, None, :], (1, 128, 1))),
        "l2f_eb3_r": np.ascontiguousarray(np.tile(inputs["L2_ffn_eb3"][:, None, :], (1, 128, 1))),
        "l2f_sellng_r": rep(inputs["L2_ffn_sel_ln_g"]),
        "l2f_sellnb_r": rep(inputs["L2_ffn_sel_ln_b"]),
    }


def host_l3a(inputs):
    rep = lambda a: np.tile(np.asarray(a, np.float32).reshape(1, -1), (128, 1))
    Wg = inputs["L3_attn_w_in_g"].T
    bg = inputs["L3_attn_b_in_g"]
    return {
        "l3_winl": np.ascontiguousarray(inputs["L3_attn_w_in_l"].T),
        "l3_binl_r": rep(inputs["L3_attn_b_in_l"]),
        "l3_woutl": np.ascontiguousarray(inputs["L3_attn_w_out_l"].T),
        "l3_boutl_r": rep(inputs["L3_attn_b_out_l"]),
        "l3_wing_qk": np.ascontiguousarray(Wg[:, 0:2048]),
        "l3_bing_qk_p": np.ascontiguousarray(bg[0:2048].reshape(16, 128).T),
        "l3_wing_v": np.ascontiguousarray(Wg[:, 2048:3072]),
        "l3_bing_v_r": rep(bg[2048:3072]),
        "l3_woutg": np.ascontiguousarray(inputs["L3_attn_w_out_g"].T),
        "l3_boutg_r": rep(inputs["L3_attn_b_out_g"]),
    }


def host_l3f(inputs):
    return {
        "l3f_w1": np.ascontiguousarray(inputs["L3_ffn_W1"]),
        "l3f_b1p": np.ascontiguousarray(inputs["L3_ffn_b1"].reshape(32, 128).T),
        "l3f_w2": np.ascontiguousarray(inputs["L3_ffn_W2"]),
        "l3f_b2r": np.tile(inputs["L3_ffn_b2"][None, :], (128, 1)),
    }



def build_full():
    nc = bass.Bass("TRN2", target_bir_lowering=False)
    with nc.cleanup_on_exit():
        mk = MK(nc, set())
        mk.prologue()
        l0_ffn_prep(mk)
        l1_ffn_prep(mk)
        mk.load_x()
        l0_attn(mk)
        l0_ffn(mk)
        l1_attn(mk)
        l1_ffn(mk)
        l2_attn(mk)
        l2_ffn(mk)
        l3_attn(mk)
        l3_ffn(mk)
        mk.store_out()
        mk.P.finalize()
        nc.all_engine_barrier()
    return nc, mk


def host_all(inputs):
    m = {"cst": host_consts(inputs), "ident": np.eye(128, dtype=np.float32)}
    m.update(host_l0a(inputs))
    m.update(host_l0f(inputs))
    m.update(host_l1(inputs))
    m.update(host_l2a(inputs))
    m.update(host_l3a(inputs))
    m.update(host_l3f(inputs))
    return m


def build_part(layers, preps, store=True):
    nc = bass.Bass("TRN2", target_bir_lowering=False)
    with nc.cleanup_on_exit():
        mk = MK(nc, set())
        mk.prologue()
        for p in preps:
            p(mk)
        mk.load_x()
        for l in layers:
            l(mk)
        if store:
            mk.store_out()
        mk.P.finalize()
        nc.all_engine_barrier()
    return nc, mk


def kernel_split(**inputs):
    DBG.clear()
    common = host_all(inputs)
    x = np.ascontiguousarray(inputs["L0_attn_x"], dtype=np.float32)
    xc = [np.ascontiguousarray(x[c]) for c in range(8)]
    percore = [host_l2f(inputs, c) for c in range(8)]
    for c in range(8):
        gi3 = np.zeros((128, 32), np.int32)
        for i in range(32):
            b, jj = i // 4, i % 4
            gi3[:, i] = b * T + c * 512 + jj * 128 + np.arange(128)
        percore[c]["l3_gidx"] = gi3
    parts = [([l0_attn, l0_ffn, l1_attn, l1_ffn, l2_attn, l2_ffn, l3_attn_local_only],
              [l0_ffn_prep, l1_ffn_prep], False, "loc"),
             ([l3_attn_global_only, l3_ffn], [], True, "o")]
    scrs = None
    for layers, preps, store, outname in parts:
        nc, mk = build_part(layers, preps, store=(store or outname == "loc"))
        in_maps = []
        for c in range(8):
            m = {k: common[k] for k in mk.io_names if k in common}
            m.update({k: percore[c][k] for k in mk.io_names if k in percore[c]})
            m["x"] = xc[c]
            if "scr" in mk.io_names:
                m["scr"] = scrs[c]
            in_maps.append(m)
        res = run_bass_kernel_spmd(nc, in_maps, core_ids=list(range(8)))
        xc = [np.ascontiguousarray(np.asarray(res.results[c]["o"], np.float32)) for c in range(8)]
        if outname == "loc":
            loc = [np.asarray(res.results[c]["loc"]) for c in range(8)]
            scrs = [np.ascontiguousarray(np.concatenate(
                [loc[b][c * 512:(c + 1) * 512] for b in range(8)], axis=0)) for c in range(8)]
    return np.stack(xc, axis=0)


def kernel(**inputs):
    return kernel_split(**inputs)


def kernel_fused(**inputs):
    DBG.clear()
    nc, mk = build_full()
    common = host_all(inputs)
    x = np.ascontiguousarray(inputs["L0_attn_x"], dtype=np.float32)
    in_maps = []
    for c in range(8):
        m = dict(common)
        m["x"] = np.ascontiguousarray(x[c])
        m.update(host_l2f(inputs, c))
        gi3 = np.zeros((128, 32), np.int32)
        for i in range(32):
            b, jj = i // 4, i % 4
            gi3[:, i] = b * T + c * 512 + jj * 128 + np.arange(128)
        m["l3_gidx"] = gi3
        in_maps.append(m)
    res = run_bass_kernel_spmd(nc, in_maps, core_ids=list(range(8)))
    return np.stack([np.asarray(res.results[c]["o"], np.float32) for c in range(8)], axis=0)



# version 06 (final)
# speedup vs baseline: 1.7578x; 1.7358x over previous
import math
import numpy as np
import ml_dtypes
import concourse.bass as bass
import concourse.mybir as mybir
from concourse.bass_utils import run_bass_kernel_spmd
from concourse.bass_core_rust import IndirectOffsetOnAxis

F32 = mybir.dt.float32
BF16 = mybir.dt.bfloat16
U8 = mybir.dt.uint8
I32 = mybir.dt.int32
AF = mybir.ActivationFunctionType
ALU = mybir.AluOpType
AX = mybir.AxisListType
BF = ml_dtypes.bfloat16

DBG = {}

D = 1024
T = 4096
NT = T // 128
NH = 16
HD = 64



class Buf:
    __slots__ = ("name", "w", "rs")

    def __init__(self, name):
        self.name = name
        self.w = None
        self.rs = []


class Op:
    __slots__ = ("kind", "eng", "q", "fn", "deps", "eidx", "qidx", "inc", "tag")

    def __init__(self, kind, eng, q, fn, deps, eidx, qidx, tag):
        self.kind, self.eng, self.q, self.fn = kind, eng, q, fn
        self.deps, self.eidx, self.qidx, self.tag = deps, eidx, qidx, tag
        self.inc = False


class Prog:
    ENGINES = ("t", "v", "s", "g", "y")
    QUEUES = {"qy": "y", "qg": "g", "qs": "s"}

    def __init__(self, nc):
        self.nc = nc
        self.stream = {e: [] for e in self.ENGINES}
        self.ecount = {e: 0 for e in self.ENGINES}
        self.qcount = {q: 0 for q in self.QUEUES}
        self.qops = {q: [] for q in self.QUEUES}
        self.n_same_engine_waits = 0
        self.relax_same_engine = {"t"}
        self.trace_plan = None

    def _deps_for(self, r, w):
        deps = set()
        for b in r:
            if b.w is not None:
                deps.add(b.w)
        for b in w:
            if b.w is not None:
                deps.add(b.w)
            deps.update(b.rs)
        return deps

    def _update_bufs(self, op, r, w):
        for b in r:
            b.rs.append(op)
        for b in w:
            b.w = op
            b.rs = []

    def emit(self, eng, fn, r=(), w=(), tag=""):
        op = Op("e", eng, None, fn, self._deps_for(r, w), self.ecount[eng], None, tag)
        self.ecount[eng] += 1
        self.stream[eng].append(op)
        self._update_bufs(op, r, w)
        return op

    def dma(self, q, fn, r=(), w=(), tag=""):
        eng = self.QUEUES[q]
        op = Op("d", eng, q, fn, self._deps_for(r, w), None, self.qcount[q], tag)
        self.qcount[q] += 1
        self.qops[q].append(op)
        self.stream[eng].append(op)
        self._update_bufs(op, r, w)
        return op

    def coll(self, fn, r=(), w=(), tag=""):
        op = Op("c", "g", None, fn, self._deps_for(r, w), None, len(getattr(self, "colls", [])), tag)
        if not hasattr(self, "colls"):
            self.colls = []
        self.colls.append(op)
        self.stream["g"].append(op)
        self._update_bufs(op, r, w)
        return op

    def barrier(self, bufs, tag="barrier"):
        raise NotImplementedError

    NQSEM = 10

    def finalize(self):
        nc = self.nc
        M = self.NQSEM
        for e in self.ENGINES:
            for op in self.stream[e]:
                if op.kind == "d" and op.qidx >= M:
                    op.deps.add(self.qops[op.q][op.qidx - M])
        for e in self.ENGINES:
            for op in self.stream[e]:
                for d in op.deps:
                    if d.kind == "e":
                        d.inc = True
        self.incval = {}
        for e in self.ENGINES:
            c = 0
            for op in self.stream[e]:
                if op.kind == "e":
                    if op.inc:
                        c += 1
                    self.incval[id(op)] = c
        self.sem_e = {e: nc.alloc_semaphore(f"pe_{e}") for e in self.ENGINES}
        self.sem_q = {q: [nc.alloc_semaphore(f"pq_{q}{j}") for j in range(M)]
                      for q in self.QUEUES}
        esec = {"t": "tensor", "v": "vector", "s": "scalar", "g": "gpsimd", "y": "sync"}

        self.sem_c = {}
        for op in getattr(self, "colls", []):
            self.sem_c[id(op)] = nc.alloc_semaphore(f"pc{op.qidx}")

        def resolve(d):
            if d.kind == "e":
                return (self.sem_e[d.eng], self.incval[id(d)])
            if d.kind == "c":
                return (self.sem_c[id(d)], 16)
            return (self.sem_q[d.q][d.qidx % M], 16 * (d.qidx // M + 1))

        M = self.NQSEM
        def body_for(e):
            ops = self.stream[e]
            sems = self.sem_e
            qsems = self.sem_q

            def body(eng):
                known = {}
                for op in ops:
                    trace_waits = []
                    for d in sorted(op.deps, key=lambda d: (d.kind, d.eng, d.q or "", d.eidx or 0, d.qidx or 0)):
                        if d.kind == "e" and d.eng == e and e in self.relax_same_engine:
                            continue
                        sem, val = resolve(d)
                        if val == 0:
                            continue
                        k = id(sem)
                        if known.get(k, 0) >= val:
                            continue
                        if d.kind == "e" and d.eng == e:
                            self.n_same_engine_waits += 1
                        eng.wait_ge(sem, val)
                        trace_waits.append((d.eng if d.kind == "e" else d.q, val))
                        known[k] = val
                    inst = op.fn(eng)
                    if op.kind == "d":
                        inst.then_inc(qsems[op.q][op.qidx % M], 16)
                    elif op.kind == "c":
                        inst.then_inc(self.sem_c[id(op)], 16)
                    elif op.inc:
                        inst.then_inc(sems[e], 1)
                    if self.trace_plan is not None:
                        self.trace_plan.append((e, op.kind, op.tag, op.inc,
                                                self.incval.get(id(op)), op.qidx, trace_waits))
                if e == "y":
                    for q in self.QUEUES:
                        for j in range(M):
                            gens = (self.qcount[q] - j + M - 1) // M
                            tot = 16 * gens
                            if tot > 0 and known.get(id(qsems[q][j]), 0) < tot:
                                eng.wait_ge(qsems[q][j], tot)
                    for op2 in getattr(self, "colls", []):
                        if known.get(id(self.sem_c[id(op2)]), 0) < 16:
                            eng.wait_ge(self.sem_c[id(op2)], 16)
                    for ee in self.ENGINES:
                        tot = self.incval.get(id(self.stream[ee][-1]), 0) if self.stream[ee] else 0
                        if ee != "y" and tot and known.get(id(sems[ee]), 0) < tot:
                            eng.wait_ge(sems[ee], tot)
            return body

        with nc.Block() as block:
            for e, secname in esec.items():
                if self.stream[e] or e == "y":
                    getattr(block, secname)(body_for(e))


def ceil_div(a, b):
    return (a + b - 1) // b



class Tile:
    __slots__ = ("t", "b", "slot", "pool")

    def __init__(self, t, b, slot, pool):
        self.t, self.b, self.slot, self.pool = t, b, slot, pool

    def __getitem__(self, idx):
        return self.t[idx]

    def free(self):
        self.pool.put(self)


class Pool:

    def __init__(self, nc, name, nbytes, count, prog=None):
        self.prog = prog
        self.nc, self.name, self.nbytes = nc, name, nbytes
        backing = nc.alloc_sbuf_tensor(f"pool_{name}", [128, nbytes * count], U8)
        self.base = nc.lookup_mloc(backing).addr
        self.free_slots = list(range(count))
        self.carry = [[] for _ in range(count)]
        self.n = 0
        self.count = count
        self.watermark = 0

    def get(self, name, shape, dtype):
        assert elem_bytes(shape, dtype) <= self.nbytes, (self.name, name, shape, dtype)
        assert self.free_slots, f"pool {self.name} exhausted ({self.count} slots) at {name}"
        slot = self.free_slots.pop(0)
        self.watermark = max(self.watermark, self.count - len(self.free_slots))
        nc = self.nc
        self.n += 1
        h = nc.virtual_sbuf_tensor(f"{self.name}_{name}_{self.n}", shape, dtype)
        h.manual_base_name = f"pool_{self.name}"
        h.manual_sbuf_range = (slot * self.nbytes, elem_bytes(shape, dtype))
        nc._tensor(h.name, h.shape, h.dtype, type="SB")
        mloc = nc.lookup_mloc(h)
        mloc.allocated = True
        mloc.addr = self.base + slot * self.nbytes
        b = Buf(f"{self.name}_{name}_{self.n}")
        b.rs = list(self.carry[slot])
        tile = Tile(h, b, slot, self)
        if DBG.get("sim_init") and self.prog is not None:
            self.prog.emit("v", lambda v, h=h: v.memset(h[:], 0.0), w=[b], tag=f"simi_{name}")
        return tile

    def put(self, tile):
        ops = list(tile.b.rs)
        if tile.b.w is not None:
            ops.append(tile.b.w)
        self.carry[tile.slot] = ops
        self.free_slots.append(tile.slot)


def elem_bytes(shape, dtype):
    n = 1
    for s in shape[1:]:
        n *= s
    return n * mybir.dt.size(dtype)


class Psum:

    def __init__(self, nc):
        self.nc = nc
        self.banks = []
        self.bufs = []
        for i in range(8):
            t = nc.alloc_psum_tensor(f"bank{i}", [128, 512], F32)
            assert nc.lookup_mloc(t).bank == i
            self.banks.append(t)
            self.bufs.append(Buf(f"psum{i}"))
        self.n = 0

    def view(self, bank, shape, dtype):
        assert elem_bytes(shape, dtype) <= 2048
        nc = self.nc
        self.n += 1
        h = nc.virtual_psum_tensor(f"pb{bank}_{self.n}", shape, dtype)
        h.manual_base_name = f"bank{bank}"
        h.manual_sbuf_range = (0, elem_bytes(shape, dtype))
        nc._tensor(h.name, h.shape, h.dtype, type="PSUM")
        mloc = nc.lookup_mloc(h)
        mloc.allocated = True
        mloc.bank = bank
        return h


C_EPS5 = 0
C_EPS3 = 1
C_INV1024 = 2
C_INV512 = 3
C_S0 = 4
C_S1 = 5
C_S2 = 6
C_HALF = 7
C_ONE = 8
C_SCL0 = 9
C_SCL1 = 10
C_EPS5B = 11
N_CST = 32


class MK:

    def __init__(self, nc, io_names):
        self.nc = nc
        self.P = Prog(nc)
        self.psum = Psum(nc)
        self.io = {}
        self.io_names = io_names

        self.B8 = Pool(nc, "B8", 8448, 12, self.P)
        self.B4 = Pool(nc, "B4", 4096, 8, self.P)
        self.B2 = Pool(nc, "B2", 2048, 18, self.P)
        self.B1 = Pool(nc, "B1", 1024, 16, self.P)
        self.SM = Pool(nc, "SM", 256, 12, self.P)

        self.x32 = nc.dram_tensor("x32", [T, D], F32, kind="Internal").ap()
        self.xbf = nc.dram_tensor("xbf", [T, D], BF16, kind="Internal").ap()
        self.x32_b = [Buf(f"x32_{i}") for i in range(NT)]
        self.xbf_b = [Buf(f"xbf_{i}") for i in range(NT)]

        self.cst = nc.alloc_sbuf_tensor("cstt", [128, N_CST], F32)
        self.cst_b = Buf("cst")
        self.ones = nc.alloc_sbuf_tensor("onest", [1, 128], BF16)
        self.ones_b = Buf("ones")
        self.ones_col = nc.alloc_sbuf_tensor("onesc", [128, 1], BF16)
        self.ones_col_b = Buf("ones_col")
        self.ident = nc.alloc_sbuf_tensor("identt", [128, 128], BF16)
        self.ident_b = Buf("ident")
        self.trash4 = self.B4.get("trash", [128, 1024], F32)
        self.stats = nc.alloc_sbuf_tensor("stats", [128, 16], F32)
        self.stats_b = Buf("stats")

    def inp(self, name, shape, dtype=F32):
        if name not in self.io:
            self.io[name] = self.nc.dram_tensor(name, list(shape), dtype,
                                                kind="ExternalInput").ap()
            self.io_names.add(name)
        return self.io[name]

    def out(self, name, shape, dtype=F32):
        if name not in self.io:
            self.io[name] = self.nc.dram_tensor(name, list(shape), dtype,
                                                kind="ExternalOutput").ap()
        return self.io[name]

    def dram(self, name, shape, dtype):
        return self.nc.dram_tensor(name, list(shape), dtype, kind="Internal").ap()

    def c(self, col):
        return self.cst[:, col:col + 1]

    def prologue(self):
        P = self.P
        cst_in = self.inp("cst", [128, N_CST])
        ident_in = self.inp("ident", [128, 128])
        P.dma("qy", lambda y: y.dma_start(out=self.cst[:], in_=cst_in),
              w=[self.cst_b], tag="ld_cst")
        P.dma("qg", lambda g: g.dma_start(out=self.ident[:], in_=ident_in),
              w=[self.ident_b], tag="ld_ident")
        P.emit("v", lambda v: v.memset(self.ones[:], 1.0), w=[self.ones_b], tag="ones")
        P.emit("v", lambda v: v.memset(self.ones_col[:], 1.0), w=[self.ones_col_b], tag="onesc")

    def load_x(self):
        P = self.P
        x_in = self.inp("x", [T, D])
        for i in range(NT):
            t32 = self.B4.get(f"lx{i}", [128, D], F32)
            P.dma("qy", lambda y, i=i, t=t32: y.dma_start(out=t[:], in_=x_in[i * 128:(i + 1) * 128, :]),
                  w=[t32.b], tag=f"lx{i}")
            P.dma("qy", lambda y, i=i, t=t32: y.dma_start(out=self.x32[i * 128:(i + 1) * 128, :], in_=t[:]),
                  r=[t32.b], w=[self.x32_b[i]], tag=f"sx32_{i}")
            tbf = self.B2.get(f"lxb{i}", [128, D], BF16)
            P.emit("v", lambda v, t=t32, tb=tbf: v.tensor_copy(tb[:], t[:]),
                   r=[t32.b], w=[tbf.b], tag=f"cvt{i}")
            P.dma("qy", lambda y, i=i, tb=tbf: y.dma_start(out=self.xbf[i * 128:(i + 1) * 128, :], in_=tb[:]),
                  r=[tbf.b], w=[self.xbf_b[i]], tag=f"sxbf{i}")
            t32.free()
            tbf.free()

    def store_out(self):
        P = self.P
        o = self.out("o", [T, D])
        for i in range(NT):
            t32 = self.B4.get(f"so{i}", [128, D], F32)
            P.dma("qy", lambda y, i=i, t=t32: y.dma_start(out=t[:], in_=self.x32[i * 128:(i + 1) * 128, :]),
                  r=[self.x32_b[i]], w=[t32.b], tag=f"so_l{i}")
            P.dma("qy", lambda y, i=i, t=t32: y.dma_start(out=o[i * 128:(i + 1) * 128, :], in_=t[:]),
                  r=[t32.b], tag=f"so_s{i}")
            t32.free()

    def make_xT(self, src=None, src_b=None, tag="xT"):
        P = self.P
        if src is None:
            src, src_b = self.xbf, self.xbf_b
        xT = []
        for dc in range(8):
            tl = self.B8.get(f"{tag}{dc}", [128, T], BF16)
            P.dma("qy", lambda y, dc=dc, tl=tl, src=src: y.dma_start(
                out=tl[:], in_=src[:, dc * 128:(dc + 1) * 128], transpose=True),
                r=src_b, w=[tl.b], tag=f"{tag}{dc}")
            xT.append(tl)
        return xT

    def ln_store(self, i, yt, eps_col, gamma_row=None, beta_row=None,
                 dst32=None, dst32_b=None, dstbf=True, second=None):
        P = self.P
        st, st_b = self.stats, self.stats_b
        cst, cst_b = self.cst, self.cst_b
        trash = self.trash4

        def stats_of(src, src_b, ecol):
            P.emit("v", lambda v: v.reduce_sum(st[:, 0:1], src[:, :], AX.X),
                   r=[src_b], w=[st_b], tag=f"ln_sum{i}")
            P.emit("s", lambda s: s.activation(trash[:, :], src[:, :], AF.Square,
                                               accum_out=st[:, 1:2]),
                   r=[src_b], w=[trash.b, st_b], tag=f"ln_sq{i}")
            P.emit("v", lambda v: v.tensor_scalar(st[:, 0:1], st[:, 0:1], cst[:, C_INV1024:C_INV1024 + 1], None, ALU.mult),
                   r=[cst_b], w=[st_b], tag=f"ln_mu{i}")
            P.emit("v", lambda v: v.tensor_scalar(st[:, 2:3], st[:, 0:1], st[:, 0:1], None, ALU.mult),
                   r=[st_b], w=[st_b], tag=f"ln_mu2{i}")
            P.emit("v", lambda v: v.tensor_scalar(st[:, 1:2], st[:, 1:2], cst[:, C_INV1024:C_INV1024 + 1], st[:, 2:3], ALU.mult, ALU.subtract),
                   r=[cst_b], w=[st_b], tag=f"ln_var{i}")
            P.emit("s", lambda s, ecol=ecol: s.activation(st[:, 3:4], st[:, 1:2], AF.Sqrt, bias=cst[:, ecol:ecol + 1]),
                   r=[cst_b, st_b], w=[st_b], tag=f"ln_sd{i}")
            P.emit("v", lambda v: v.reciprocal(st[:, 4:5], st[:, 3:4]),
                   r=[st_b], w=[st_b], tag=f"ln_r{i}")

        src, src_b = yt, yt.b
        stats_of(src, src_b, eps_col)
        if gamma_row is not None:
            z = self.B4.get(f"lnz{i}", [128, D], F32)
            P.emit("v", lambda v, z=z: v.tensor_scalar(z[:, :], src[:, :], st[:, 0:1], st[:, 4:5], ALU.subtract, ALU.mult),
                   r=[src_b, st_b], w=[z.b], tag=f"ln_app{i}")
            P.emit("v", lambda v, z=z: v.tensor_mul(z[:, :], z[:, :], gamma_row[0][:, :]),
                   r=[z.b, gamma_row[1]], w=[z.b], tag=f"ln_g{i}")
            P.emit("v", lambda v, z=z: v.tensor_add(z[:, :], z[:, :], beta_row[0][:, :]),
                   r=[z.b, beta_row[1]], w=[z.b], tag=f"ln_b{i}")
            cur, cur_b = z, z.b
            owned = z
        else:
            z = self.B4.get(f"lnz{i}", [128, D], F32)
            P.emit("v", lambda v, z=z: v.tensor_scalar(z[:, :], src[:, :], st[:, 0:1], st[:, 4:5], ALU.subtract, ALU.mult),
                   r=[src_b, st_b], w=[z.b], tag=f"ln_app{i}")
            cur, cur_b = z, z.b
            owned = z

        if second is not None:
            eps2, x2t, x2b = second
            P.emit("v", lambda v, cur=cur, x2t=x2t: v.tensor_add(cur[:, :], cur[:, :], x2t[:, :]),
                   r=[cur_b, x2b], w=[cur_b], tag=f"ln2_add{i}")
            stats_of(cur, cur_b, eps2)
            z2 = self.B4.get(f"lnz2_{i}", [128, D], F32)
            P.emit("v", lambda v, z2=z2, cur=cur: v.tensor_scalar(z2[:, :], cur[:, :], st[:, 0:1], st[:, 4:5], ALU.subtract, ALU.mult),
                   r=[cur_b, st_b], w=[z2.b], tag=f"ln2_app{i}")
            owned.free()
            cur, cur_b, owned = z2, z2.b, z2

        if dst32 is None:
            P.dma("qy", lambda y, cur=cur: y.dma_start(out=self.x32[i * 128:(i + 1) * 128, :], in_=cur[:, :]),
                  r=[cur_b], w=[self.x32_b[i]], tag=f"st32_{i}")
        else:
            P.dma("qy", lambda y, cur=cur: y.dma_start(out=dst32[i * 128:(i + 1) * 128, :], in_=cur[:, :]),
                  r=[cur_b], w=([dst32_b] if dst32_b is not None else []), tag=f"stout_{i}")
        if dstbf:
            tbf = self.B2.get(f"lnbf{i}", [128, D], BF16)
            P.emit("v", lambda v, tb=tbf, cur=cur: v.tensor_copy(tb[:], cur[:, :]),
                   r=[cur_b], w=[tbf.b], tag=f"lnbfc{i}")
            P.dma("qy", lambda y, tb=tbf: y.dma_start(out=self.xbf[i * 128:(i + 1) * 128, :], in_=tb[:]),
                  r=[tbf.b], w=[self.xbf_b[i]], tag=f"stbf{i}")
            tbf.free()
        owned.free()

    def load_row128(self, name, src_ap, ncols, dtype=F32, pool=None):
        pool = pool or (self.B4 if elem_bytes([128, ncols], dtype) > 2048 else self.B2)
        tl = pool.get(name, [128, ncols], dtype)
        q = "qg" if dtype not in (F32, I32) else "qy"
        self.P.dma(q, lambda y: y.dma_start(out=tl[:], in_=src_ap), w=[tl.b], tag=f"ld_{name}")
        return tl



def mlp_4096(mk, w1_ap, b1p_ap, w2_ap, badd_row, relu_scale_col=None,
             tag="l3f"):
    P, nc, ps = mk.P, mk.nc, mk.psum
    xT = mk.make_xT()
    b1p = mk.load_row128(tag + "b1p", b1p_ap, 32, pool=mk.SM) if b1p_ap is not None else None
    for tc in range(8):
        h8 = [mk.B8.get(f"{tag}h{tc}_{j}", [128, T], BF16) for j in range(4)]
        for ic in range(32):
            w1c = mk.B2.get(f"{tag}w1_{tc}_{ic}", [128, 1024], BF16)
            P.dma("qg", lambda g, ic=ic, w1c=w1c: g.dma_start(
                out=w1c[:].rearrange("p (dc i) -> p dc i", dc=8),
                in_=w1_ap[:, ic * 128:(ic + 1) * 128].rearrange("(dc p) i -> p dc i", p=128)),
                w=[w1c.b], tag=f"{tag}w1l{tc}_{ic}")
            bank = ic % 2
            pb = ps.bufs[bank]
            pp = ps.banks[bank]
            for dc in range(8):
                P.emit("t", lambda t, dc=dc, w1c=w1c, pp=pp, tc=tc: t.matmul(
                    pp[:, :], w1c[:, dc * 128:(dc + 1) * 128],
                    xT[dc][:, tc * 512:(tc + 1) * 512],
                    start=(dc == 0), stop=(dc == 7)),
                    r=[w1c.b, xT[dc].b], w=[pb], tag=f"{tag}hmm{tc}_{ic}_{dc}")
            h_t = h8[ic // 8]
            if b1p is not None:
                P.emit("s", lambda s, ic=ic, pp=pp, h_t=h_t, b1p=b1p: s.activation(
                    h_t[:, (ic % 8) * 512:(ic % 8 + 1) * 512], pp[:, :], AF.Relu,
                    bias=b1p[:, ic:ic + 1]),
                    r=[pb, b1p.b], w=[h_t.b], tag=f"{tag}hev{tc}_{ic}")
            else:
                P.emit("s", lambda s, ic=ic, pp=pp, h_t=h_t: s.activation(
                    h_t[:, (ic % 8) * 512:(ic % 8 + 1) * 512], pp[:, :], AF.Relu),
                    r=[pb], w=[h_t.b], tag=f"{tag}hev{tc}_{ic}")
            w1c.free()
        for ic in range(32):
            w2c = mk.B2.get(f"{tag}w2_{tc}_{ic}", [128, 1024], BF16)
            P.dma("qg", lambda g, ic=ic, w2c=w2c: g.dma_start(
                out=w2c[:], in_=w2_ap[ic * 128:(ic + 1) * 128, :]),
                w=[w2c.b], tag=f"{tag}w2l{tc}_{ic}")
            for tt in range(4):
                h_t = h8[ic // 8]
                for dh in range(2):
                    bank = tt * 2 + dh
                    P.emit("t", lambda t, ic=ic, tt=tt, dh=dh, w2c=w2c, h_t=h_t, bank=bank: t.matmul(
                        ps.banks[bank][:, :],
                        h_t[:, (ic % 8) * 512 + tt * 128:(ic % 8) * 512 + (tt + 1) * 128],
                        w2c[:, dh * 512:(dh + 1) * 512],
                        start=(ic == 0), stop=(ic == 31)),
                        r=[w2c.b, h_t.b], w=[ps.bufs[bank]], tag=f"{tag}ymm{tc}_{ic}_{tt}_{dh}")
            w2c.free()
        for tt in range(4):
            i = tc * 4 + tt
            xt32 = mk.B4.get(f"{tag}x{i}", [128, D], F32)
            P.dma("qy", lambda y, i=i, xt32=xt32: y.dma_start(
                out=xt32[:], in_=mk.x32[i * 128:(i + 1) * 128, :]),
                r=[mk.x32_b[i]], w=[xt32.b], tag=f"{tag}xl{i}")
            if badd_row is not None:
                P.emit("v", lambda v, xt32=xt32: v.tensor_add(xt32[:, :], xt32[:, :], badd_row[:, :]),
                       r=[xt32.b, badd_row.b], w=[xt32.b], tag=f"{tag}xb{i}")
            yt = mk.B4.get(f"{tag}y{i}", [128, D], F32)
            for dh in range(2):
                bank = tt * 2 + dh
                P.emit("v", lambda v, dh=dh, yt=yt, xt32=xt32, bank=bank: v.scalar_tensor_tensor(
                    yt[:, dh * 512:(dh + 1) * 512], ps.banks[bank][:, :], 1.0,
                    xt32[:, dh * 512:(dh + 1) * 512], ALU.mult, ALU.add),
                    r=[ps.bufs[bank], xt32.b], w=[yt.b], tag=f"{tag}res{i}_{dh}")
            xt32.free()
            mk.ln_store(i, yt, C_EPS5)
            yt.free()
        for h_t in h8:
            h_t.free()
    for tl in xT:
        tl.free()


def l3_ffn(mk):
    w1 = mk.inp("l3f_w1", [1024, 4096])
    b1p = mk.inp("l3f_b1p", [128, 32])
    w2 = mk.inp("l3f_w2", [4096, 1024])
    b2r = mk.inp("l3f_b2r", [128, 1024])
    b2b = mk.load_row128("l3f_b2b", b2r, 1024)
    mlp_4096(mk, w1, b1p, w2, b2b, tag="l3f")
    b2b.free()


C_NEG125 = 12
C_NBIG = 13
C_INV64 = 14
C_FOUR = 15
C_TWO = 16
C_NEG1 = 17


def l0_ffn_prep(mk):
    P = mk.P
    w1t = mk.inp("l0_w1t", [8, 1024, 4096])
    w2t = mk.inp("l0_w2t", [8, 4096, 1024])
    mk.w1bf = mk.dram("w1bf", [8, 1024, 4096], BF16)
    mk.w2bf = mk.dram("w2bf", [8, 4096, 1024], BF16)
    mk.w1bf_b = [Buf(f"w1bf{e}") for e in range(8)]
    mk.w2bf_b = [Buf(f"w2bf{e}") for e in range(8)]
    for e in range(8):
        for r in range(0, 1024, 128):
            tl = mk.B8.get(f"mp1_{e}_{r}", [128, 4096], BF16)
            P.dma("qg", lambda g, e=e, r=r, tl=tl: g.dma_start(
                out=tl[:], in_=w1t[e, r:r + 128, :]), w=[tl.b], tag=f"mp1l{e}_{r}")
            P.dma("qy", lambda y, e=e, r=r, tl=tl: y.dma_start(
                out=mk.w1bf[e, r:r + 128, :], in_=tl[:]),
                r=[tl.b], w=[mk.w1bf_b[e]], tag=f"mp1s{e}_{r}")
            tl.free()
        for r in range(0, 4096, 512):
            tl = mk.B8.get(f"mp2_{e}_{r}", [128, 4096], BF16)
            P.dma("qg", lambda g, e=e, r=r, tl=tl: g.dma_start(
                out=tl[:].rearrange("p (q c) -> p q c", q=4),
                in_=w2t[e, r:r + 512, :].rearrange("(q p) c -> p q c", p=128)),
                w=[tl.b], tag=f"mp2l{e}_{r}")
            P.dma("qy", lambda y, e=e, r=r, tl=tl: y.dma_start(
                out=mk.w2bf[e, r:r + 512, :].rearrange("(q p) c -> p q c", p=128),
                in_=tl[:].rearrange("p (q c) -> p q c", q=4)),
                r=[tl.b], w=[mk.w2bf_b[e]], tag=f"mp2s{e}_{r}")
            tl.free()


def l0_ffn(mk):
    P, ps = mk.P, mk.psum
    gate = mk.inp("l0_gate", [1024, 8])
    yacc = mk.dram("l0_yacc", [T, D], F32)
    yacc_b = [Buf(f"yacc{i}") for i in range(NT)]

    xT = mk.make_xT()
    gsb = mk.B1.get("l0gw", [128, 64], BF16)
    P.dma("qg", lambda g: g.dma_start(
        out=gsb[:].rearrange("p (dc e) -> p dc e", dc=8),
        in_=gate.rearrange("(dc p) e -> p dc e", p=128)), w=[gsb.b], tag="l0gwl")
    comb = mk.B1.get("l0comb", [128, 256], F32)
    scr = mk.B1.get("l0gscr", [128, 64], F32)
    P.emit("v", lambda v: v.memset(scr[:, 48:56], 1.0), w=[scr.b], tag="l0gones")
    for i in range(NT):
        pg = ps.view(6, [128, 8], F32)
        for dc in range(8):
            P.emit("t", lambda t, i=i, dc=dc, pg=pg: t.matmul(
                pg[:, :], xT[dc][:, i * 128:(i + 1) * 128], gsb[:, dc * 8:(dc + 1) * 8],
                start=(dc == 0), stop=(dc == 7)),
                r=[xT[dc].b, gsb.b], w=[ps.bufs[6]], tag=f"l0g{i}_{dc}")
        P.emit("s", lambda s, pg=pg: s.activation(scr[:, 0:8], pg[:, :], AF.Identity),
               r=[ps.bufs[6]], w=[scr.b], tag=f"l0gv{i}")
        P.emit("v", lambda v: v.tensor_reduce(scr[:, 32:33], scr[:, 0:8], AX.X, ALU.max),
               r=[scr.b], w=[scr.b], tag=f"l0gm1{i}")
        P.emit("v", lambda v: v.tensor_scalar(scr[:, 8:16], scr[:, 0:8], scr[:, 32:33], None, ALU.is_equal),
               r=[scr.b], w=[scr.b], tag=f"l0geq{i}")
        P.emit("v", lambda v: v.tensor_tensor_scan(scr[:, 40:48], scr[:, 48:56], scr[:, 8:16], 0.0, ALU.mult, ALU.add),
               r=[scr.b], w=[scr.b], tag=f"l0gcum{i}")
        P.emit("v", lambda v: v.tensor_scalar(scr[:, 40:48], scr[:, 40:48], 1.0, None, ALU.is_equal),
               r=[scr.b], w=[scr.b], tag=f"l0gfo{i}")
        P.emit("v", lambda v: v.tensor_mul(scr[:, 8:16], scr[:, 8:16], scr[:, 40:48]),
               r=[scr.b], w=[scr.b], tag=f"l0gf1{i}")
        P.emit("v", lambda v: v.scalar_tensor_tensor(scr[:, 16:24], scr[:, 8:16], mk.cst[:, C_NBIG:C_NBIG + 1], scr[:, 0:8], ALU.mult, ALU.add),
               r=[scr.b, mk.cst_b], w=[scr.b], tag=f"l0gmask{i}")
        P.emit("v", lambda v: v.tensor_reduce(scr[:, 33:34], scr[:, 16:24], AX.X, ALU.max),
               r=[scr.b], w=[scr.b], tag=f"l0gm2{i}")
        P.emit("v", lambda v: v.tensor_scalar(scr[:, 24:32], scr[:, 16:24], scr[:, 33:34], None, ALU.is_equal),
               r=[scr.b], w=[scr.b], tag=f"l0geq2{i}")
        P.emit("v", lambda v: v.tensor_tensor_scan(scr[:, 40:48], scr[:, 48:56], scr[:, 24:32], 0.0, ALU.mult, ALU.add),
               r=[scr.b], w=[scr.b], tag=f"l0gcum2{i}")
        P.emit("v", lambda v: v.tensor_scalar(scr[:, 40:48], scr[:, 40:48], 1.0, None, ALU.is_equal),
               r=[scr.b], w=[scr.b], tag=f"l0gfo2{i}")
        P.emit("v", lambda v: v.tensor_mul(scr[:, 24:32], scr[:, 24:32], scr[:, 40:48]),
               r=[scr.b], w=[scr.b], tag=f"l0gf2{i}")
        P.emit("v", lambda v: v.tensor_sub(scr[:, 34:35], scr[:, 32:33], scr[:, 33:34]),
               r=[scr.b], w=[scr.b], tag=f"l0gd{i}")
        P.emit("s", lambda s: s.activation(scr[:, 35:36], scr[:, 34:35], AF.Sigmoid),
               r=[scr.b], w=[scr.b], tag=f"l0gg1{i}")
        P.emit("s", lambda s: s.activation(scr[:, 36:37], scr[:, 34:35], AF.Sigmoid, scale=-1.0),
               r=[scr.b], w=[scr.b], tag=f"l0gg2{i}")
        P.emit("v", lambda v, i=i: v.tensor_scalar(scr[:, 40:48], scr[:, 8:16], scr[:, 35:36], None, ALU.mult),
               r=[scr.b], w=[scr.b], tag=f"l0gc1{i}")
        P.emit("v", lambda v, i=i: v.scalar_tensor_tensor(comb[:, i * 8:(i + 1) * 8], scr[:, 24:32], scr[:, 36:37], scr[:, 40:48], ALU.mult, ALU.add),
               r=[scr.b], w=[comb.b], tag=f"l0gcomb{i}")
    gsb.free()
    scr.free()

    for e in range(8):
        for tc in range(8):
            h8 = [mk.B8.get(f"l0fh{e}_{tc}_{j}", [128, T], BF16) for j in range(4)]
            for hc in range(32):
                w1c = mk.B2.get(f"l0fw1_{e}_{tc}_{hc}", [128, 1024], BF16)
                P.dma("qg", lambda g, e=e, hc=hc, w1c=w1c: g.dma_start(
                    out=w1c[:].rearrange("p (dc i) -> p dc i", dc=8),
                    in_=mk.w1bf[e, :, hc * 128:(hc + 1) * 128].rearrange("(dc p) i -> p dc i", p=128)),
                    r=[mk.w1bf_b[e]], w=[w1c.b], tag=f"l0fw1l{e}_{tc}_{hc}")
                bank = hc % 2
                pp, pb = ps.banks[bank], ps.bufs[bank]
                for dc in range(8):
                    P.emit("t", lambda t, dc=dc, tc=tc, w1c=w1c, pp=pp: t.matmul(
                        pp[:, :], w1c[:, dc * 128:(dc + 1) * 128],
                        xT[dc][:, tc * 512:(tc + 1) * 512],
                        start=(dc == 0), stop=(dc == 7)),
                        r=[w1c.b, xT[dc].b], w=[pb], tag=f"l0fh{e}_{tc}_{hc}_{dc}")
                h_t = h8[hc // 8]
                P.emit("s", lambda s, hc=hc, pp=pp, h_t=h_t: s.activation(
                    h_t[:, (hc % 8) * 512:(hc % 8 + 1) * 512], pp[:, :], AF.Relu),
                    r=[pb], w=[h_t.b], tag=f"l0fhev{e}_{tc}_{hc}")
                w1c.free()
            for hc in range(32):
                w2c = mk.B2.get(f"l0fw2_{e}_{tc}_{hc}", [128, 1024], BF16)
                P.dma("qg", lambda g, e=e, hc=hc, w2c=w2c: g.dma_start(
                    out=w2c[:], in_=mk.w2bf[e, hc * 128:(hc + 1) * 128, :]),
                    r=[mk.w2bf_b[e]], w=[w2c.b], tag=f"l0fw2l{e}_{tc}_{hc}")
                for tt in range(4):
                    h_t = h8[hc // 8]
                    for dh in range(2):
                        bank = tt * 2 + dh
                        P.emit("t", lambda t, hc=hc, tt=tt, dh=dh, w2c=w2c, h_t=h_t, bank=bank: t.matmul(
                            ps.banks[bank][:, :],
                            h_t[:, (hc % 8) * 512 + tt * 128:(hc % 8) * 512 + (tt + 1) * 128],
                            w2c[:, dh * 512:(dh + 1) * 512],
                            start=(hc == 0), stop=(hc == 31)),
                            r=[w2c.b, h_t.b], w=[ps.bufs[bank]], tag=f"l0fy{e}_{tc}_{hc}_{tt}_{dh}")
                w2c.free()
            for tt in range(4):
                i = tc * 4 + tt
                at = mk.B4.get(f"l0fa{e}_{i}", [128, D], F32)
                if e > 0:
                    P.dma("qy", lambda y, i=i, at=at: y.dma_start(
                        out=at[:], in_=yacc[i * 128:(i + 1) * 128, :]),
                        r=[yacc_b[i]], w=[at.b], tag=f"l0fal{e}_{i}")
                for dh in range(2):
                    bank = tt * 2 + dh
                    if e > 0:
                        P.emit("v", lambda v, dh=dh, at=at, bank=bank, i=i, e=e: v.scalar_tensor_tensor(
                            at[:, dh * 512:(dh + 1) * 512], ps.banks[bank][:, :],
                            comb[:, i * 8 + e:i * 8 + e + 1], at[:, dh * 512:(dh + 1) * 512],
                            ALU.mult, ALU.add),
                            r=[ps.bufs[bank], comb.b, at.b], w=[at.b], tag=f"l0facc{e}_{i}_{dh}")
                    else:
                        P.emit("v", lambda v, dh=dh, at=at, bank=bank, i=i: v.tensor_scalar(
                            at[:, dh * 512:(dh + 1) * 512], ps.banks[bank][:, :],
                            comb[:, i * 8:i * 8 + 1], None, ALU.mult),
                            r=[ps.bufs[bank], comb.b], w=[at.b], tag=f"l0facc0_{i}_{dh}")
                P.dma("qy", lambda y, i=i, at=at: y.dma_start(
                    out=yacc[i * 128:(i + 1) * 128, :], in_=at[:]),
                    r=[at.b], w=[yacc_b[i]], tag=f"l0fas{e}_{i}")
                at.free()
            for h_t in h8:
                h_t.free()
    comb.free()
    for tl in xT:
        tl.free()
    for i in range(NT):
        at = mk.B4.get(f"l0ff{i}", [128, D], F32)
        P.dma("qy", lambda y, i=i, at=at: y.dma_start(
            out=at[:], in_=yacc[i * 128:(i + 1) * 128, :]),
            r=[yacc_b[i]], w=[at.b], tag=f"l0ffl{i}")
        xt32 = mk.B4.get(f"l0fx{i}", [128, D], F32)
        P.dma("qy", lambda y, i=i, xt32=xt32: y.dma_start(
            out=xt32[:], in_=mk.x32[i * 128:(i + 1) * 128, :]),
            r=[mk.x32_b[i]], w=[xt32.b], tag=f"l0fxl{i}")
        P.emit("v", lambda v, at=at, xt32=xt32: v.tensor_add(at[:, :], at[:, :], xt32[:, :]),
               r=[at.b, xt32.b], w=[at.b], tag=f"l0fadd{i}")
        xt32.free()
        mk.ln_store(i, at, C_EPS5)
        at.free()



def proj_to_rows(mk, w_ap, bias_p, dst_d, dst_bufs, xT, ngroups, tag, dst2=None):
    P, ps = mk.P, mk.psum
    for g in range(ngroups):
        wc = mk.B2.get(f"{tag}w{g}", [128, 1024], BF16)
        P.dma("qg", lambda q, g=g, wc=wc: q.dma_start(
            out=wc[:].rearrange("p (dc i) -> p dc i", dc=8),
            in_=w_ap[:, g * 128:(g + 1) * 128].rearrange("(dc p) i -> p dc i", p=128)),
            w=[wc.b], tag=f"{tag}wl{g}")
        ot = mk.B8.get(f"{tag}o{g}", [128, T], BF16)
        for tc in range(8):
            bank = tc % 2
            pp, pb = ps.banks[bank], ps.bufs[bank]
            for dc in range(8):
                P.emit("t", lambda t, dc=dc, tc=tc, wc=wc, pp=pp: t.matmul(
                    pp[:, :], wc[:, dc * 128:(dc + 1) * 128],
                    xT[dc][:, tc * 512:(tc + 1) * 512],
                    start=(dc == 0), stop=(dc == 7)),
                    r=[wc.b, xT[dc].b], w=[pb], tag=f"{tag}mm{g}_{tc}_{dc}")
            if bias_p is not None:
                P.emit("s", lambda s, tc=tc, g=g, pp=pp, ot=ot: s.activation(
                    ot[:, tc * 512:(tc + 1) * 512], pp[:, :], AF.Identity,
                    bias=bias_p[:, g:g + 1]),
                    r=[pb, bias_p.b], w=[ot.b], tag=f"{tag}ev{g}_{tc}")
            else:
                P.emit("s", lambda s, tc=tc, pp=pp, ot=ot: s.activation(
                    ot[:, tc * 512:(tc + 1) * 512], pp[:, :], AF.Identity),
                    r=[pb], w=[ot.b], tag=f"{tag}ev{g}_{tc}")
        if dst2 is not None:
            qd, qb, kd, kb = dst2
            dd, db, gg = (qd, qb, g) if g < 8 else (kd, kb, g - 8)
            P.dma("qy", lambda y, gg=gg, ot=ot, dd=dd: y.dma_start(
                out=dd[gg * 128:(gg + 1) * 128, :], in_=ot[:]),
                r=[ot.b], w=[db[gg]], tag=f"{tag}st{g}")
        else:
            P.dma("qy", lambda y, g=g, ot=ot: y.dma_start(
                out=dst_d[g * 128:(g + 1) * 128, :], in_=ot[:]),
                r=[ot.b], w=[dst_bufs[g]], tag=f"{tag}st{g}")
        wc.free()
        ot.free()


def proj_v(mk, w_ap, bias_row, dst_d, dst_bufs, xT, tag, ncols=1024, natural=False):
    P, ps = mk.P, mk.psum
    nh = ncols // 512
    wcs = []
    for dc in range(8):
        wc = mk.B2.get(f"{tag}vw{dc}", [128, ncols], BF16)
        P.dma("qg", lambda q, dc=dc, wc=wc: q.dma_start(
            out=wc[:], in_=w_ap[dc * 128:(dc + 1) * 128, :]),
            w=[wc.b], tag=f"{tag}vwl{dc}")
        wcs.append(wc)
    for i in range(NT):
        vt = mk.B2.get(f"{tag}v{i}", [128, ncols], BF16)
        for h in range(nh):
            bank = 2 + (i * nh + h) % 2
            pp, pb = ps.banks[bank], ps.bufs[bank]
            for dc in range(8):
                P.emit("t", lambda t, dc=dc, i=i, h=h, pp=pp: t.matmul(
                    pp[:, :], xT[dc][:, i * 128:(i + 1) * 128],
                    wcs[dc][:, h * 512:(h + 1) * 512],
                    start=(dc == 0), stop=(dc == 7)),
                    r=[wcs[dc].b, xT[dc].b], w=[pb], tag=f"{tag}vmm{i}_{h}")
            if bias_row is not None:
                P.emit("v", lambda v, h=h, pp=pp, vt=vt: v.tensor_add(
                    vt[:, h * 512:(h + 1) * 512], pp[:, :], bias_row[:, h * 512:(h + 1) * 512]),
                    r=[pb, bias_row.b], w=[vt.b], tag=f"{tag}vev{i}_{h}")
            else:
                P.emit("v", lambda v, h=h, pp=pp, vt=vt: v.tensor_copy(
                    vt[:, h * 512:(h + 1) * 512], pp[:, :]),
                    r=[pb], w=[vt.b], tag=f"{tag}vev{i}_{h}")
        if natural:
            P.dma("qy", lambda y, i=i, vt=vt: y.dma_start(
                out=dst_d[i * 128:(i + 1) * 128, :], in_=vt[:]),
                r=[vt.b], w=[dst_bufs[i]], tag=f"{tag}vst{i}")
        else:
            P.dma("qy", lambda y, i=i, vt=vt: y.dma_start(
                out=dst_d.rearrange("hp p (tt c) -> tt p hp c", c=128)[i],
                in_=vt[:].rearrange("p (hp c) -> p hp c", c=128)),
                r=[vt.b], w=[dst_bufs[i]], tag=f"{tag}vst{i}")
        vt.free()
    for wc in wcs:
        wc.free()


def neg_cbound64(mk, tile, tag):
    P, ps = mk.P, mk.psum
    sq = mk.B8.get(f"{tag}sq", [64, T], BF16)
    P.emit("s", lambda s: s.activation(sq[:], tile[0:64, :], AF.Square),
           r=[tile.b], w=[sq.b], tag=f"{tag}sq")
    pA = ps.view(6, [128, 32], F32)
    for j in range(32):
        P.emit("t", lambda t, j=j: t.matmul(
            pA[:, j:j + 1], sq[0:64, j * 128:(j + 1) * 128], mk.ones_col[0:64, 0:1],
            start=True, stop=True),
            r=[sq.b, mk.ones_col_b], w=[ps.bufs[6]], tag=f"{tag}n{j}")
    sq.free()
    mx = mk.SM.get(f"{tag}mx", [128, 1], F32)
    P.emit("v", lambda v: v.tensor_reduce(mx[:, 0:1], pA[:, :], AX.X, ALU.max),
           r=[ps.bufs[6]], w=[mx.b], tag=f"{tag}mx")
    mxb = mk.SM.get(f"{tag}mxb", [128, 1], BF16)
    P.emit("v", lambda v: v.tensor_copy(mxb[:], mx[:]), r=[mx.b], w=[mxb.b], tag=f"{tag}cv")
    mx.free()
    ptr = ps.view(7, [1, 128], BF16)
    P.emit("t", lambda t: t.transpose(ptr[:, :], mxb[:, :], mk.ident[:]),
           r=[mxb.b, mk.ident_b], w=[ps.bufs[7]], tag=f"{tag}tr")
    mxb.free()
    out = mk.SM.get(f"{tag}out", [1, 1], F32)
    P.emit("v", lambda v: v.tensor_reduce(out[:, 0:1], ptr[:, :], AX.X, ALU.max),
           r=[ps.bufs[7]], w=[out.b], tag=f"{tag}gm")
    return out


def l0_attn(mk):
    P, nc, ps = mk.P, mk.nc, mk.psum
    wq = mk.inp("l0_wq", [1024, 1024])
    wk = mk.inp("l0_wk", [1024, 1024])
    wv = mk.inp("l0_wv", [1024, 1024])
    wo = mk.inp("l0_wo", [1024, 1024])
    bq_p = mk.load_row128("l0bq", mk.inp("l0_bq_p", [128, 8]), 8, pool=mk.SM)
    bk_p = mk.load_row128("l0bk", mk.inp("l0_bk_p", [128, 8]), 8, pool=mk.SM)
    bv_r = mk.load_row128("l0bv", mk.inp("l0_bv_r", [128, 1024]), 1024)
    bo_r = mk.load_row128("l0bo", mk.inp("l0_bo_r", [128, 1024]), 1024)
    gm_r = mk.load_row128("l0gm", mk.inp("l0_gamma_r", [128, 1024]), 1024)
    bt_r = mk.load_row128("l0bt", mk.inp("l0_beta_r", [128, 1024]), 1024)

    qt_d = mk.dram("l0_qt", [1024, T], BF16)
    kt_d = mk.dram("l0_kt", [1024, T], BF16)
    v_d = mk.dram("l0_v", [8, 128, T], BF16)
    ot_d = mk.dram("l0_ot", [1024, T], BF16)
    qt_db = [Buf(f"l0qt{g}") for g in range(8)]
    kt_db = [Buf(f"l0kt{g}") for g in range(8)]
    v_db = [Buf(f"l0v{i}") for i in range(NT)]
    ot_db = [Buf(f"l0ot{g}") for g in range(8)]

    xT = mk.make_xT()
    proj_to_rows(mk, wq, bq_p, qt_d, qt_db, xT, 8, "l0q")
    proj_to_rows(mk, wk, bk_p, kt_d, kt_db, xT, 8, "l0k")
    proj_v(mk, wv, bv_r, v_d, v_db, xT, "l0v")
    for tl in xT:
        tl.free()
    bq_p.free(); bk_p.free(); bv_r.free()

    attn_core(mk, qt_d, qt_db, kt_d, kt_db, v_d, v_db, ot_d, ot_db, "l0a")

    attn_outproj(mk, ot_d, ot_db, wo, bo_r, "l0o", double=(gm_r, bt_r))
    bo_r.free(); gm_r.free(); bt_r.free()


def attn_outproj(mk, ot_d, ot_db, wo_ap, bo_row, tag, double=None):
    P, ps = mk.P, mk.psum
    wos = []
    for hc in range(8):
        wc = mk.B2.get(f"{tag}wo{hc}", [128, 1024], BF16)
        P.dma("qg", lambda q, hc=hc, wc=wc: q.dma_start(
            out=wc[:], in_=wo_ap[hc * 128:(hc + 1) * 128, :]),
            w=[wc.b], tag=f"{tag}wol{hc}")
        wos.append(wc)
    for i in range(NT):
        otc = mk.B2.get(f"{tag}otc{i}", [128, 1024], BF16)
        P.dma("qy", lambda y, i=i, otc=otc: y.dma_start(
            out=otc[:].rearrange("p (g c) -> p g c", g=8),
            in_=ot_d[:, i * 128:(i + 1) * 128].rearrange("(g p) c -> p g c", p=128)),
            r=ot_db, w=[otc.b], tag=f"{tag}otl{i}")
        for h in range(2):
            bank = 4 + (i * 2 + h) % 2
            pp, pb = ps.banks[bank], ps.bufs[bank]
            for hc in range(8):
                P.emit("t", lambda t, hc=hc, h=h, otc=otc, pp=pp: t.matmul(
                    pp[:, :], otc[:, hc * 128:(hc + 1) * 128],
                    wos[hc][:, h * 512:(h + 1) * 512],
                    start=(hc == 0), stop=(hc == 7)),
                    r=[otc.b, wos[hc].b], w=[pb], tag=f"{tag}omm{i}_{h}_{hc}")
        otc.free()
        xt32 = mk.B4.get(f"{tag}x{i}", [128, D], F32)
        P.dma("qy", lambda y, i=i, xt32=xt32: y.dma_start(
            out=xt32[:], in_=mk.x32[i * 128:(i + 1) * 128, :]),
            r=[mk.x32_b[i]], w=[xt32.b], tag=f"{tag}xl{i}")
        xb = mk.B4.get(f"{tag}xb{i}", [128, D], F32)
        P.emit("v", lambda v, xb=xb, xt32=xt32: v.tensor_add(xb[:, :], xt32[:, :], bo_row[:, :]),
               r=[xt32.b, bo_row.b], w=[xb.b], tag=f"{tag}xb{i}")
        yt = mk.B4.get(f"{tag}y{i}", [128, D], F32)
        for h in range(2):
            bank = 4 + (i * 2 + h) % 2
            P.emit("v", lambda v, h=h, yt=yt, xb=xb, bank=bank: v.scalar_tensor_tensor(
                yt[:, h * 512:(h + 1) * 512], ps.banks[bank][:, :], 1.0,
                xb[:, h * 512:(h + 1) * 512], ALU.mult, ALU.add),
                r=[ps.bufs[bank], xb.b], w=[yt.b], tag=f"{tag}res{i}_{h}")
        xb.free()
        if double is not None:
            gm_r, bt_r = double
            mk.ln_store(i, yt, C_EPS3, gamma_row=(gm_r, gm_r.b), beta_row=(bt_r, bt_r.b),
                        second=(C_EPS5, xt32, xt32.b))
        else:
            mk.ln_store(i, yt, C_EPS5)
        yt.free()
        xt32.free()
    for wc in wos:
        wc.free()


def attn_core(mk, qt_d, qt_db, kt_d, kt_db, v_d, v_db, ot_d, ot_db, tag, hps=None):
    P, ps = mk.P, mk.psum
    for hp in (hps if hps is not None else range(8)):
        qts, kts, negcs = [], [], []
        for hx in range(2):
            qt = mk.B8.get(f"{tag}qt{hp}{hx}", [128, T], BF16)
            P.emit("v", lambda v, qt=qt: v.memset(qt[64:128, :], 0.0),
                   w=[qt.b], tag=f"{tag}qtz{hp}{hx}")
            P.dma("qy", lambda y, hp=hp, hx=hx, qt=qt: y.dma_start(
                out=qt[0:64, :], in_=qt_d[hp * 128 + hx * 64:hp * 128 + (hx + 1) * 64, :]),
                r=[qt_db[hp]], w=[qt.b], tag=f"{tag}qtl{hp}{hx}")
            kt = mk.B8.get(f"{tag}kt{hp}{hx}", [128, T], BF16)
            P.emit("v", lambda v, kt=kt: v.memset(kt[64:128, :], 0.0),
                   w=[kt.b], tag=f"{tag}ktz{hp}{hx}")
            P.dma("qy", lambda y, hp=hp, hx=hx, kt=kt: y.dma_start(
                out=kt[0:64, :], in_=kt_d[hp * 128 + hx * 64:hp * 128 + (hx + 1) * 64, :]),
                r=[kt_db[hp]], w=[kt.b], tag=f"{tag}ktl{hp}{hx}")
            qts.append(qt)
            kts.append(kt)
        vimg = mk.B8.get(f"{tag}vi{hp}", [128, T], BF16)
        P.dma("qy", lambda y, hp=hp, vimg=vimg: y.dma_start(out=vimg[:], in_=v_d[hp]),
              r=v_db, w=[vimg.b], tag=f"{tag}vil{hp}")
        vt = mk.B8.get(f"{tag}vt{hp}", [128, NT * 130], BF16)
        P.emit("v", lambda v, vt=vt: v.memset(vt[:], 1.0), w=[vt.b], tag=f"{tag}vones{hp}")
        P.emit("v", lambda v, vt=vt, vimg=vimg: v.tensor_copy(
            vt[:].rearrange("p (tt c) -> p tt c", tt=NT)[:, :, 0:64],
            vimg[:].rearrange("p (tt c) -> p tt c", tt=NT)[:, :, 0:64]),
            r=[vimg.b], w=[vt.b], tag=f"{tag}vca{hp}")
        P.emit("v", lambda v, vt=vt, vimg=vimg: v.tensor_copy(
            vt[:].rearrange("p (tt c) -> p tt c", tt=NT)[:, :, 65:129],
            vimg[:].rearrange("p (tt c) -> p tt c", tt=NT)[:, :, 64:128]),
            r=[vimg.b], w=[vt.b], tag=f"{tag}vcb{hp}")

        for hx in range(2):
            qn = neg_cbound64(mk, qts[hx], f"{tag}qn{hp}{hx}")
            kn = neg_cbound64(mk, kts[hx], f"{tag}kn{hp}{hx}")
            cc = mk.SM.get(f"{tag}cc{hp}{hx}", [1, 2], F32)
            P.emit("v", lambda v, qn=qn, kn=kn, cc=cc: v.tensor_mul(cc[:, 0:1], qn[:, 0:1], kn[:, 0:1]),
                   r=[qn.b, kn.b], w=[cc.b], tag=f"{tag}cm{hp}{hx}")
            P.emit("s", lambda s, cc=cc: s.activation(cc[:, 1:2], cc[:, 0:1], AF.Sqrt),
                   r=[cc.b], w=[cc.b], tag=f"{tag}cs{hp}{hx}")
            ccb = mk.SM.get(f"{tag}ccb{hp}{hx}", [1, 1], BF16)
            P.emit("v", lambda v, cc=cc, ccb=ccb: v.tensor_scalar(
                ccb[:, 0:1], cc[:, 1:2], mk.cst[0:1, C_NEG125:C_NEG125 + 1], None, ALU.mult),
                r=[cc.b, mk.cst_b], w=[ccb.b], tag=f"{tag}cn{hp}{hx}")
            qn.free(); kn.free(); cc.free()
            negc = mk.SM.get(f"{tag}negc{hp}{hx}", [128, 1], F32)
            pbc = ps.view(6, [128, 1], F32)
            P.emit("t", lambda t, ccb=ccb, pbc=pbc: t.matmul(
                pbc[:, 0:1], mk.ones[:, :], ccb[0:1, 0:1], start=True, stop=True),
                r=[ccb.b, mk.ones_b], w=[ps.bufs[6]], tag=f"{tag}bc{hp}{hx}")
            P.emit("s", lambda s, negc=negc, pbc=pbc: s.activation(negc[:], pbc[:, :], AF.Identity),
                   r=[ps.bufs[6]], w=[negc.b], tag=f"{tag}bce{hp}{hx}")
            ccb.free()
            negcs.append(negc)

        ot = mk.B8.get(f"{tag}ot{hp}", [128, T], BF16)
        DEP = 3
        pa_t = [mk.B1.get(f"{tag}pa{hp}_{j}", [128, 512], BF16) for j in range(DEP)]
        pb_t = [mk.B1.get(f"{tag}pb{hp}_{j}", [128, 512], BF16) for j in range(DEP)]
        for qc in DBG.get("qcs", range(8)):
            oA = ps.view(6, [65, 512], F32)
            oB = ps.view(7, [65, 512], F32)

            def scores_exp(k, qc=qc):
                sA = ps.view(k % DEP, [128, 512], F32)
                sB = ps.view(DEP + k % DEP, [128, 512], F32)
                P.emit("t", lambda t, k=k, qc=qc, sA=sA, kA=kts[0], qA=qts[0]: t.matmul(
                    sA[:, :], kA[0:128, k * 128:(k + 1) * 128], qA[0:128, qc * 512:(qc + 1) * 512],
                    start=True, stop=True),
                    r=[kts[0].b, qts[0].b], w=[ps.bufs[k % DEP]], tag=f"{tag}sA{hp}_{qc}_{k}")
                P.emit("t", lambda t, k=k, qc=qc, sB=sB, kB=kts[1], qB=qts[1]: t.matmul(
                    sB[:, :], kB[0:128, k * 128:(k + 1) * 128], qB[0:128, qc * 512:(qc + 1) * 512],
                    start=True, stop=True),
                    r=[kts[1].b, qts[1].b], w=[ps.bufs[DEP + k % DEP]], tag=f"{tag}sB{hp}_{qc}_{k}")
                pa = pa_t[k % DEP]
                pb = pb_t[k % DEP]
                P.emit("s", lambda s, sA=sA, pa=pa, n0=negcs[0]: s.activation(
                    pa[:, :], sA[:, :], AF.Exp, bias=n0[:, 0:1], scale=0.125),
                    r=[ps.bufs[k % DEP], negcs[0].b], w=[pa.b], tag=f"{tag}eA{hp}_{qc}_{k}")
                P.emit("s", lambda s, sB=sB, pb=pb, n1=negcs[1]: s.activation(
                    pb[:, :], sB[:, :], AF.Exp, bias=n1[:, 0:1], scale=0.125),
                    r=[ps.bufs[DEP + k % DEP], negcs[1].b], w=[pb.b], tag=f"{tag}eB{hp}_{qc}_{k}")

            def av(k):
                pa = pa_t[k % DEP]
                pb = pb_t[k % DEP]
                P.emit("t", lambda t, k=k, pa=pa, oA=oA, vt=vt: t.matmul(
                    oA[:, :], vt[:, k * 130:k * 130 + 65], pa[:, :],
                    start=(k == 0), stop=(k == NT - 1)),
                    r=[vt.b, pa.b], w=[ps.bufs[6]], tag=f"{tag}avA{hp}_{qc}_{k}")
                P.emit("t", lambda t, k=k, pb=pb, oB=oB, vt=vt: t.matmul(
                    oB[:, :], vt[:, k * 130 + 65:k * 130 + 130], pb[:, :],
                    start=(k == 0), stop=(k == NT - 1)),
                    r=[vt.b, pb.b], w=[ps.bufs[7]], tag=f"{tag}avB{hp}_{qc}_{k}")

            scores_exp(0)
            for k in range(1, NT):
                scores_exp(k)
                av(k - 1)
            av(NT - 1)
            for hidx, obank, oview in ((0, 6, oA), (1, 7, oB)):
                zt = mk.B2.get(f"{tag}zt{hp}_{qc}_{hidx}", [1, 512], F32)
                P.emit("s", lambda s, oview=oview, zt=zt: s.activation(zt[:, :], oview[64:65, :], AF.Identity),
                       r=[ps.bufs[obank]], w=[zt.b], tag=f"{tag}zt{hp}_{qc}_{hidx}")
                rz = mk.B2.get(f"{tag}rz{hp}_{qc}_{hidx}", [1, 512], F32)
                P.emit("v", lambda v, zt=zt, rz=rz: v.reciprocal(rz[:, :], zt[:, :]),
                       r=[zt.b], w=[rz.b], tag=f"{tag}rz{hp}_{qc}_{hidx}")
                zt.free()
                rzb = mk.B1.get(f"{tag}rzb{hp}_{qc}_{hidx}", [1, 512], BF16)
                P.emit("v", lambda v, rz=rz, rzb=rzb: v.tensor_copy(rzb[:, :], rz[:, :]),
                       r=[rz.b], w=[rzb.b], tag=f"{tag}rzc{hp}_{qc}_{hidx}")
                rz.free()
                pz = ps.view(hidx, [64, 512], F32)
                P.emit("t", lambda t, rzb=rzb, pz=pz: t.matmul(
                    pz[:, :], mk.ones[:, 0:64], rzb[:, :], start=True, stop=True),
                    r=[rzb.b, mk.ones_b], w=[ps.bufs[hidx]], tag=f"{tag}rzbc{hp}_{qc}_{hidx}")
                rzf = mk.B2.get(f"{tag}rzf{hp}_{qc}_{hidx}", [64, 512], F32)
                P.emit("s", lambda s, pz=pz, rzf=rzf: s.activation(rzf[:, :], pz[:, :], AF.Identity),
                       r=[ps.bufs[hidx]], w=[rzf.b], tag=f"{tag}rze{hp}_{qc}_{hidx}")
                rzb.free()
                P.emit("v", lambda v, oview=oview, rzf=rzf, hidx=hidx, qc=qc, ot=ot: v.tensor_mul(
                    ot[hidx * 64:(hidx + 1) * 64, qc * 512:(qc + 1) * 512],
                    oview[0:64, :], rzf[:, :]),
                    r=[ps.bufs[obank], rzf.b], w=[ot.b], tag=f"{tag}onm{hp}_{qc}_{hidx}")
                rzf.free()
        P.dma("qy", lambda y, hp=hp, ot=ot: y.dma_start(
            out=ot_d[hp * 128:(hp + 1) * 128, :], in_=ot[:]),
            r=[ot.b], w=[ot_db[hp]], tag=f"{tag}ost{hp}")
        for tl in (qts[0], qts[1], kts[0], kts[1], vt, vimg, ot,
                   negcs[0], negcs[1], *pa_t, *pb_t):
            tl.free()


def l1_attn(mk):
    P, ps = mk.P, mk.psum
    wqk = mk.inp("l1_wqk", [1024, 2048])
    bqk_p = mk.load_row128("l1bqk", mk.inp("l1_bqk_p", [128, 16]), 16, pool=mk.SM)
    wv = mk.inp("l1_wv", [1024, 1024])
    bv_r = mk.load_row128("l1bv", mk.inp("l1_bv_r", [128, 1024]), 1024)
    wo = mk.inp("l1_wo", [1024, 1024])
    bo_r = mk.load_row128("l1bo", mk.inp("l1_bo_r", [128, 1024]), 1024)

    h2t_d = mk.dram("l1_h2t", [2048, T], BF16)
    v_d = mk.dram("l1_v", [T, 1024], BF16)
    ot_d = mk.dram("l1_ot", [1024, T], BF16)
    h2t_db = [Buf(f"l1h2t{g}") for g in range(16)]
    v_db = [Buf(f"l1v{i}") for i in range(NT)]
    ot_db = [Buf(f"l1ot{g}") for g in range(8)]

    xT = mk.make_xT()
    proj_to_rows(mk, wqk, bqk_p, h2t_d, h2t_db, xT, 16, "l1qk")
    proj_v(mk, wv, bv_r, v_d, v_db, xT, "l1v", natural=True)
    for tl in xT:
        tl.free()
    bqk_p.free(); bv_r.free()

    for n in range(NH):
        qt = mk.B8.get(f"l1qt{n}", [128, T], BF16)
        P.emit("v", lambda v, qt=qt: v.memset(qt[64:128, :], 0.0),
               w=[qt.b], tag=f"l1qtz{n}")
        P.dma("qy", lambda y, n=n, qt=qt: y.dma_start(
            out=qt[0:64, :].rearrange("p (r a) -> p r a", r=16),
            in_=h2t_d[:1024, n * 256:(n + 1) * 256].rearrange("(r p) a -> p r a", p=64)),
            r=h2t_db[:8], w=[qt.b], tag=f"l1qtl{n}")
        kt = mk.B8.get(f"l1kt{n}", [128, T], BF16)
        P.emit("v", lambda v, kt=kt: v.memset(kt[64:128, :], 0.0),
               w=[kt.b], tag=f"l1ktz{n}")
        P.dma("qy", lambda y, n=n, kt=kt: y.dma_start(
            out=kt[0:64, :].rearrange("p (r a) -> p r a", r=16),
            in_=h2t_d[1024:2048, n * 256:(n + 1) * 256].rearrange("(r p) a -> p r a", p=64)),
            r=h2t_db[8:], w=[kt.b], tag=f"l1ktl{n}")
        vt = mk.B8.get(f"l1vt{n}", [128, NT * 65], BF16)
        P.emit("v", lambda v, vt=vt: v.memset(vt[:], 1.0), w=[vt.b], tag=f"l1vo{n}")
        for ah in range(2):
            P.dma("qy", lambda y, n=n, ah=ah, vt=vt: y.dma_start(
                out=vt[:].rearrange("p (r ah c) -> p r ah c", r=16, ah=2)[:, :, ah, 0:64],
                in_=v_d[n * 256 + ah * 128:n * 256 + (ah + 1) * 128, :]
                    .rearrange("p (r c) -> p r c", r=16)),
                r=[v_db[(n * 256 + ah * 128) // 128]], w=[vt.b], tag=f"l1vl{n}_{ah}")

        qn = neg_cbound64(mk, qt, f"l1qn{n}")
        kn = neg_cbound64(mk, kt, f"l1kn{n}")
        cc = mk.SM.get(f"l1cc{n}", [1, 2], F32)
        P.emit("v", lambda v, qn=qn, kn=kn, cc=cc: v.tensor_mul(cc[:, 0:1], qn[:, 0:1], kn[:, 0:1]),
               r=[qn.b, kn.b], w=[cc.b], tag=f"l1cm{n}")
        P.emit("s", lambda s, cc=cc: s.activation(cc[:, 1:2], cc[:, 0:1], AF.Sqrt),
               r=[cc.b], w=[cc.b], tag=f"l1cs{n}")
        ccb = mk.SM.get(f"l1ccb{n}", [1, 1], BF16)
        P.emit("v", lambda v, cc=cc, ccb=ccb: v.tensor_scalar(
            ccb[:, 0:1], cc[:, 1:2], mk.cst[0:1, C_NEG125:C_NEG125 + 1], None, ALU.mult),
            r=[cc.b, mk.cst_b], w=[ccb.b], tag=f"l1cn{n}")
        qn.free(); kn.free(); cc.free()
        negc = mk.SM.get(f"l1negc{n}", [128, 1], F32)
        pbc = ps.view(6, [128, 1], F32)
        P.emit("t", lambda t, ccb=ccb, pbc=pbc: t.matmul(
            pbc[:, 0:1], mk.ones[:, :], ccb[0:1, 0:1], start=True, stop=True),
            r=[ccb.b, mk.ones_b], w=[ps.bufs[6]], tag=f"l1bc{n}")
        P.emit("s", lambda s, negc=negc, pbc=pbc: s.activation(negc[:], pbc[:, :], AF.Identity),
               r=[ps.bufs[6]], w=[negc.b], tag=f"l1bce{n}")
        ccb.free()

        ot = mk.B8.get(f"l1ot{n}", [64, T], BF16)
        DEP1 = 4
        pa_t = [mk.B1.get(f"l1pa{n}_{j}", [128, 512], BF16) for j in range(DEP1)]
        for qc in range(8):
            oA = ps.view(4, [65, 512], F32)

            def scores_exp(k, qc=qc):
                sA = ps.view(k % DEP1, [128, 512], F32)
                P.emit("t", lambda t, k=k, qc=qc, sA=sA, kt=kt, qt=qt: t.matmul(
                    sA[:, :], kt[0:128, k * 128:(k + 1) * 128], qt[0:128, qc * 512:(qc + 1) * 512],
                    start=True, stop=True),
                    r=[kt.b, qt.b], w=[ps.bufs[k % DEP1]], tag=f"l1s{n}_{qc}_{k}")
                pa = pa_t[k % DEP1]
                P.emit("s", lambda s, sA=sA, pa=pa, negc=negc, k=k: s.activation(
                    pa[:, :], sA[:, :], AF.Exp, bias=negc[:, 0:1], scale=0.125),
                    r=[ps.bufs[k % DEP1], negc.b], w=[pa.b], tag=f"l1e{n}_{qc}_{k}")

            def av(k):
                pa = pa_t[k % DEP1]
                P.emit("t", lambda t, k=k, pa=pa, oA=oA, vt=vt: t.matmul(
                    oA[:, :], vt[:, k * 65:k * 65 + 65], pa[:, :],
                    start=(k == 0), stop=(k == NT - 1)),
                    r=[vt.b, pa.b], w=[ps.bufs[4]], tag=f"l1av{n}_{qc}_{k}")

            scores_exp(0)
            scores_exp(1)
            for k in range(2, NT):
                scores_exp(k)
                av(k - 2)
            av(NT - 2)
            av(NT - 1)
            zt = mk.B2.get(f"l1zt{n}_{qc}", [1, 512], F32)
            P.emit("s", lambda s, oA=oA, zt=zt: s.activation(zt[:, :], oA[64:65, :], AF.Identity),
                   r=[ps.bufs[4]], w=[zt.b], tag=f"l1zt{n}_{qc}")
            rz = mk.B2.get(f"l1rz{n}_{qc}", [1, 512], F32)
            P.emit("v", lambda v, zt=zt, rz=rz: v.reciprocal(rz[:, :], zt[:, :]),
                   r=[zt.b], w=[rz.b], tag=f"l1rz{n}_{qc}")
            zt.free()
            rzb = mk.B1.get(f"l1rzb{n}_{qc}", [1, 512], BF16)
            P.emit("v", lambda v, rz=rz, rzb=rzb: v.tensor_copy(rzb[:, :], rz[:, :]),
                   r=[rz.b], w=[rzb.b], tag=f"l1rzc{n}_{qc}")
            rz.free()
            pz = ps.view(6, [64, 512], F32)
            P.emit("t", lambda t, rzb=rzb, pz=pz: t.matmul(
                pz[:, :], mk.ones[:, 0:64], rzb[:, :], start=True, stop=True),
                r=[rzb.b, mk.ones_b], w=[ps.bufs[6]], tag=f"l1rzbc{n}_{qc}")
            rzf = mk.B2.get(f"l1rzf{n}_{qc}", [64, 512], F32)
            P.emit("s", lambda s, pz=pz, rzf=rzf: s.activation(rzf[:, :], pz[:, :], AF.Identity),
                   r=[ps.bufs[6]], w=[rzf.b], tag=f"l1rze{n}_{qc}")
            rzb.free()
            P.emit("v", lambda v, oA=oA, rzf=rzf, qc=qc, ot=ot: v.tensor_mul(
                ot[0:64, :].rearrange("p (a s) -> p s a", s=16)[:, 2 * qc:2 * qc + 2, :],
                oA[0:64, :].rearrange("p (r a) -> p r a", r=2), rzf[:, :].rearrange("p (r a) -> p r a", r=2)),
                r=[ps.bufs[4], rzf.b], w=[ot.b], tag=f"l1onm{n}_{qc}")
            rzf.free()
        P.dma("qy", lambda y, n=n, ot=ot: y.dma_start(
            out=ot_d[n * 64:(n + 1) * 64, :], in_=ot[:]),
            r=[ot.b], w=[ot_db[n // 2]], tag=f"l1ost{n}")
        for tl in (qt, kt, vt, ot, negc, *pa_t):
            tl.free()

    attn_outproj(mk, ot_d, ot_db, wo, bo_r, "l1o")
    bo_r.free()


def l1_ffn_prep(mk):
    P = mk.P
    k0 = mk.inp("l1_k0", [2048, 4096], U8)
    k1 = mk.inp("l1_k1", [4096, 4096], U8)
    k2 = mk.inp("l1_k2", [4096, 1024], U8)
    mk.k0e = mk.dram("k0e", [1024, 4096], BF16)
    mk.k1pm = mk.dram("k1pm", [4096, 4096], BF16)
    mk.k2pm = mk.dram("k2pm", [4096, 1024], BF16)
    mk.k0e_b = Buf("k0e")
    mk.k1pm_b = Buf("k1pm")
    mk.k2pm_b = Buf("k2pm")
    for r in range(8):
        ta = mk.B8.get(f"k0a{r}", [128, 4096], BF16)
        tb = mk.B8.get(f"k0b{r}", [128, 4096], BF16)
        P.dma("qg", lambda g, r=r, ta=ta: g.dma_start(out=ta[:], in_=k0[r * 128:(r + 1) * 128, :]),
              w=[ta.b], tag=f"k0al{r}")
        P.dma("qg", lambda g, r=r, tb=tb: g.dma_start(out=tb[:], in_=k0[1024 + r * 128:1024 + (r + 1) * 128, :]),
              w=[tb.b], tag=f"k0bl{r}")
        P.emit("v", lambda v, ta=ta, tb=tb: v.tensor_sub(ta[:, :], ta[:, :], tb[:, :]),
               r=[ta.b, tb.b], w=[ta.b], tag=f"k0sub{r}")
        P.dma("qy", lambda y, r=r, ta=ta: y.dma_start(out=mk.k0e[r * 128:(r + 1) * 128, :], in_=ta[:]),
              r=[ta.b], w=[mk.k0e_b], tag=f"k0st{r}")
        ta.free(); tb.free()
    for nm, src_ap, dst_ap, dst_b, rows in (("k1", k1, mk.k1pm, mk.k1pm_b, 32),
                                            ("k2", k2, mk.k2pm, mk.k2pm_b, 32)):
        w = 4096 if nm == "k1" else 1024
        for r in range(rows):
            ta = mk.B8.get(f"{nm}a{r}", [128, w], BF16)
            P.dma("qg", lambda g, r=r, ta=ta, src_ap=src_ap, w=w: g.dma_start(
                out=ta[:], in_=src_ap[r * 128:(r + 1) * 128, :]), w=[ta.b], tag=f"{nm}l{r}")
            P.emit("s", lambda s, ta=ta: s.activation(ta[:, :], ta[:, :], AF.Identity,
                                                      bias=mk.cst[:, C_NEG1:C_NEG1 + 1], scale=2.0),
                   r=[ta.b, mk.cst_b], w=[ta.b], tag=f"{nm}pm{r}")
            P.dma("qy", lambda y, r=r, ta=ta, dst_ap=dst_ap: y.dma_start(
                out=dst_ap[r * 128:(r + 1) * 128, :], in_=ta[:]),
                r=[ta.b], w=[dst_b], tag=f"{nm}st{r}")
            ta.free()


def l1_ffn(mk):
    P, ps = mk.P, mk.psum
    P.emit("v", lambda v: v.tensor_scalar(mk.cst[:, C_SCL0:C_SCL0 + 1], mk.cst[:, C_S0:C_S0 + 1],
                                          mk.cst[:, C_FOUR:C_FOUR + 1], None, ALU.mult),
           r=[mk.cst_b], w=[mk.cst_b], tag="l1fscl0")
    P.emit("v", lambda v: v.tensor_scalar(mk.cst[:, C_SCL1:C_SCL1 + 1], mk.cst[:, C_S1:C_S1 + 1],
                                          mk.cst[:, C_TWO:C_TWO + 1], None, ALU.mult),
           r=[mk.cst_b], w=[mk.cst_b], tag="l1fscl1")
    for tc in range(8):
        xTc = []
        for dc in range(8):
            tl = mk.B1.get(f"l1fx{tc}_{dc}", [128, 512], BF16)
            P.dma("qy", lambda y, tc=tc, dc=dc, tl=tl: y.dma_start(
                out=tl[:], in_=mk.xbf[tc * 512:(tc + 1) * 512, dc * 128:(dc + 1) * 128], transpose=True),
                r=mk.xbf_b, w=[tl.b], tag=f"l1fxT{tc}_{dc}")
            xTc.append(tl)
        h8 = [mk.B8.get(f"l1fh{tc}_{j}", [128, T], BF16) for j in range(4)]
        for hc in range(32):
            w1c = mk.B2.get(f"l1fk0_{tc}_{hc}", [128, 1024], BF16)
            P.dma("qg", lambda g, hc=hc, w1c=w1c: g.dma_start(
                out=w1c[:].rearrange("p (dc i) -> p dc i", dc=8),
                in_=mk.k0e[:, hc * 128:(hc + 1) * 128].rearrange("(dc p) i -> p dc i", p=128)),
                r=[mk.k0e_b], w=[w1c.b], tag=f"l1fk0l{tc}_{hc}")
            bank = hc % 2
            pp, pb = ps.banks[bank], ps.bufs[bank]
            for dc in range(8):
                P.emit("t", lambda t, dc=dc, w1c=w1c, pp=pp, xTc=xTc: t.matmul(
                    pp[:, :], w1c[:, dc * 128:(dc + 1) * 128], xTc[dc][:, :],
                    start=(dc == 0), stop=(dc == 7)),
                    r=[w1c.b, xTc[dc].b], w=[pb], tag=f"l1fs1_{tc}_{hc}_{dc}")
            h_t = h8[hc // 8]
            P.emit("s", lambda s, hc=hc, pp=pp, h_t=h_t: s.activation(
                h_t[:, (hc % 8) * 512:(hc % 8 + 1) * 512], pp[:, :], AF.Relu,
                scale=mk.cst[:, C_SCL0:C_SCL0 + 1]),
                r=[pb, mk.cst_b], w=[h_t.b], tag=f"l1fs1e{tc}_{hc}")
            w1c.free()
        for tl in xTc:
            tl.free()
        b8t = [mk.B8.get(f"l1fb2_{tc}_{j}", [128, T], BF16) for j in range(4)]
        for hc2 in range(32):
            k1c = mk.B8.get(f"l1fk1_{tc}_{hc2}", [128, 4096], BF16)
            P.dma("qg", lambda g, hc2=hc2, k1c=k1c: g.dma_start(
                out=k1c[:].rearrange("p (c i) -> p c i", c=32),
                in_=mk.k1pm[:, hc2 * 128:(hc2 + 1) * 128].rearrange("(c p) i -> p c i", p=128)),
                r=[mk.k1pm_b], w=[k1c.b], tag=f"l1fk1l{tc}_{hc2}")
            bank = hc2 % 2
            pp, pb = ps.banks[bank], ps.bufs[bank]
            for c1 in range(32):
                P.emit("t", lambda t, c1=c1, k1c=k1c, pp=pp, h8=h8: t.matmul(
                    pp[:, :], k1c[:, c1 * 128:(c1 + 1) * 128],
                    h8[c1 // 8][:, (c1 % 8) * 512:(c1 % 8 + 1) * 512],
                    start=(c1 == 0), stop=(c1 == 31)),
                    r=[k1c.b, h8[c1 // 8].b], w=[pb], tag=f"l1fs2_{tc}_{hc2}_{c1}")
            b_t = b8t[hc2 // 8]
            P.emit("s", lambda s, hc2=hc2, pp=pp, b_t=b_t: s.activation(
                b_t[:, (hc2 % 8) * 512:(hc2 % 8 + 1) * 512], pp[:, :], AF.Relu,
                scale=mk.cst[:, C_SCL1:C_SCL1 + 1]),
                r=[pb, mk.cst_b], w=[b_t.b], tag=f"l1fs2e{tc}_{hc2}")
            k1c.free()
        for h_t in h8:
            h_t.free()
        for hc2 in range(32):
            k2c = mk.B2.get(f"l1fk2_{tc}_{hc2}", [128, 1024], BF16)
            P.dma("qg", lambda g, hc2=hc2, k2c=k2c: g.dma_start(
                out=k2c[:], in_=mk.k2pm[hc2 * 128:(hc2 + 1) * 128, :]),
                r=[mk.k2pm_b], w=[k2c.b], tag=f"l1fk2l{tc}_{hc2}")
            for tt in range(4):
                b_t = b8t[hc2 // 8]
                for dh in range(2):
                    bank = tt * 2 + dh
                    P.emit("t", lambda t, hc2=hc2, tt=tt, dh=dh, k2c=k2c, b_t=b_t, bank=bank: t.matmul(
                        ps.banks[bank][:, :],
                        b_t[:, (hc2 % 8) * 512 + tt * 128:(hc2 % 8) * 512 + (tt + 1) * 128],
                        k2c[:, dh * 512:(dh + 1) * 512],
                        start=(hc2 == 0), stop=(hc2 == 31)),
                        r=[k2c.b, b_t.b], w=[ps.bufs[bank]], tag=f"l1fs3_{tc}_{hc2}_{tt}_{dh}")
            k2c.free()
        for tt in range(4):
            i = tc * 4 + tt
            xt32 = mk.B4.get(f"l1fxr{i}", [128, D], F32)
            P.dma("qy", lambda y, i=i, xt32=xt32: y.dma_start(
                out=xt32[:], in_=mk.x32[i * 128:(i + 1) * 128, :]),
                r=[mk.x32_b[i]], w=[xt32.b], tag=f"l1fxl{i}")
            yt = mk.B4.get(f"l1fy{i}", [128, D], F32)
            for dh in range(2):
                bank = tt * 2 + dh
                P.emit("v", lambda v, dh=dh, yt=yt, xt32=xt32, bank=bank: v.scalar_tensor_tensor(
                    yt[:, dh * 512:(dh + 1) * 512], ps.banks[bank][:, :],
                    mk.cst[:, C_S2:C_S2 + 1], xt32[:, dh * 512:(dh + 1) * 512],
                    ALU.mult, ALU.add),
                    r=[ps.bufs[bank], xt32.b, mk.cst_b], w=[yt.b], tag=f"l1fres{i}_{dh}")
            xt32.free()
            mk.ln_store(i, yt, C_EPS5)
            yt.free()
        for b_t in b8t:
            b_t.free()


def l2_attn(mk):
    P, ps = mk.P, mk.psum
    wf = mk.inp("l2_wf", [1024, 1024])
    wi = mk.inp("l2_wi", [1024, 1024])
    wh = mk.inp("l2_wh", [1024, 1024])
    bf_p = mk.load_row128("l2bf", mk.inp("l2_bf_p", [128, 8]), 8, pool=mk.SM)
    bi_p = mk.load_row128("l2bi", mk.inp("l2_bi_p", [128, 8]), 8, pool=mk.SM)
    bh_p = mk.load_row128("l2bh", mk.inp("l2_bh_p", [128, 8]), 8, pool=mk.SM)
    h_d = mk.dram("l2_h", [T, D], BF16)
    h_db = [Buf(f"l2h{i}") for i in range(4)]

    for g in range(8):
        init = mk.SM.get(f"l2init{g}", [128, 1], F32)
        for th in range(2):
            xTh = []
            for dc in range(8):
                tl = mk.B8.get(f"l2x{g}_{th}_{dc}", [128, 2048], BF16)
                P.dma("qy", lambda y, th=th, dc=dc, tl=tl: y.dma_start(
                    out=tl[:], in_=mk.xbf[th * 2048:(th + 1) * 2048, dc * 128:(dc + 1) * 128], transpose=True),
                    r=mk.xbf_b, w=[tl.b], tag=f"l2xT{g}_{th}_{dc}")
                xTh.append(tl)
            pre = []
            for wi_ap, bp, nm in ((wf, bf_p, "f"), (wi, bi_p, "i"), (wh, bh_p, "h")):
                wc = mk.B2.get(f"l2w{nm}{g}_{th}", [128, 1024], BF16)
                P.dma("qg", lambda q, g=g, wc=wc, wi_ap=wi_ap: q.dma_start(
                    out=wc[:].rearrange("p (dc i) -> p dc i", dc=8),
                    in_=wi_ap[:, g * 128:(g + 1) * 128].rearrange("(dc p) i -> p dc i", p=128)),
                    w=[wc.b], tag=f"l2wl{nm}{g}_{th}")
                ot = mk.B8.get(f"l2p{nm}{g}_{th}", [128, 2048], F32)
                for q4 in range(4):
                    bank = q4
                    for dc in range(8):
                        P.emit("t", lambda t, q4=q4, dc=dc, wc=wc, bank=bank, xTh=xTh: t.matmul(
                            ps.banks[bank][:, :], wc[:, dc * 128:(dc + 1) * 128],
                            xTh[dc][:, q4 * 512:(q4 + 1) * 512],
                            start=(dc == 0), stop=(dc == 7)),
                            r=[wc.b, xTh[dc].b], w=[ps.bufs[bank]], tag=f"l2p{nm}{g}_{th}_{q4}_{dc}")
                    P.emit("s", lambda s, q4=q4, g=g, ot=ot, bank=bank, bp=bp: s.activation(
                        ot[:, q4 * 512:(q4 + 1) * 512], ps.banks[bank][:, :], AF.Identity,
                        bias=bp[:, g:g + 1]),
                        r=[ps.bufs[bank], bp.b], w=[ot.b], tag=f"l2pe{nm}{g}_{th}_{q4}")
                wc.free()
                pre.append(ot)
            for tl in xTh:
                tl.free()
            A, Bt, H = pre
            S1 = mk.B8.get(f"l2s1_{g}_{th}", [128, 2048], F32)
            S2 = mk.B8.get(f"l2s2_{g}_{th}", [128, 2048], F32)
            m = mk.B2.get(f"l2m{g}_{th}", [128, 2048], U8)
            P.emit("s", lambda s, A=A, S1=S1: s.activation(S1[:, :], A[:, :], AF.Exp, scale=-1.0),
                   r=[A.b], w=[S1.b], tag=f"l2ea{g}_{th}")
            P.emit("s", lambda s, Bt=Bt, S2=S2: s.activation(S2[:, :], Bt[:, :], AF.Exp, scale=-1.0),
                   r=[Bt.b], w=[S2.b], tag=f"l2eb{g}_{th}")
            P.emit("v", lambda v, A=A, S1=S1, S2=S2: v.tensor_add(A[:, :], S1[:, :], S2[:, :]),
                   r=[S1.b, S2.b], w=[A.b], tag=f"l2den{g}_{th}")
            P.emit("v", lambda v, A=A: v.tensor_scalar(A[:, :], A[:, :], mk.cst[:, C_TWO:C_TWO + 1], None, ALU.add),
                   r=[mk.cst_b], w=[A.b], tag=f"l2den2{g}_{th}")
            P.emit("v", lambda v, A=A: v.reciprocal(A[:, :], A[:, :]),
                   r=[A.b], w=[A.b], tag=f"l2rec{g}_{th}")
            P.emit("v", lambda v, S2=S2: v.tensor_scalar(S2[:, :], S2[:, :], 1.0, None, ALU.add),
                   r=[S2.b], w=[S2.b], tag=f"l2f1{g}_{th}")
            P.emit("v", lambda v, S2=S2, A=A: v.tensor_mul(S2[:, :], S2[:, :], A[:, :]),
                   r=[A.b, S2.b], w=[S2.b], tag=f"l2f2{g}_{th}")
            P.emit("v", lambda v, S1=S1: v.tensor_scalar(S1[:, :], S1[:, :], 1.0, None, ALU.add),
                   r=[S1.b], w=[S1.b], tag=f"l2i1{g}_{th}")
            P.emit("v", lambda v, S1=S1, A=A: v.tensor_mul(S1[:, :], S1[:, :], A[:, :]),
                   r=[A.b, S1.b], w=[S1.b], tag=f"l2i2{g}_{th}")
            P.emit("s", lambda s, H=H, Bt=Bt: s.activation(Bt[:, :], H[:, :], AF.Sigmoid),
                   r=[H.b], w=[Bt.b], tag=f"l2sg{g}_{th}")
            P.emit("v", lambda v, H=H, m=m: v.tensor_scalar(m[:, :], H[:, :], 0.0, None, ALU.is_ge),
                   r=[H.b], w=[m.b], tag=f"l2ge{g}_{th}")
            P.emit("v", lambda v, H=H, A=A: v.tensor_scalar(A[:, :], H[:, :], mk.cst[:, C_HALF:C_HALF + 1], None, ALU.add),
                   r=[H.b, mk.cst_b], w=[A.b], tag=f"l2hp{g}_{th}")
            P.emit("v", lambda v, m=m, A=A, Bt=Bt: v.select(Bt[:, :], m[:, :], A[:, :], Bt[:, :]),
                   r=[m.b, A.b, Bt.b], w=[Bt.b], tag=f"l2sel{g}_{th}")
            P.emit("v", lambda v, S1=S1, Bt=Bt: v.tensor_mul(S1[:, :], S1[:, :], Bt[:, :]),
                   r=[Bt.b, S1.b], w=[S1.b], tag=f"l2w{g}_{th}")
            if th == 0:
                P.emit("v", lambda v, A=A, S2=S2, S1=S1: v.tensor_tensor_scan(
                    A[:, :], S2[:, :], S1[:, :], 1.0, ALU.mult, ALU.add),
                    r=[S2.b, S1.b], w=[A.b], tag=f"l2scan{g}_{th}")
            else:
                P.emit("v", lambda v, A=A, S2=S2, S1=S1, init=init: v.tensor_tensor_scan(
                    A[:, :], S2[:, :], S1[:, :], init[:, 0:1], ALU.mult, ALU.add),
                    r=[S2.b, S1.b, init.b], w=[A.b], tag=f"l2scan{g}_{th}")
            if th == 0:
                P.emit("v", lambda v, A=A, init=init: v.tensor_copy(init[:, 0:1], A[:, 2047:2048]),
                       r=[A.b], w=[init.b], tag=f"l2sv{g}")
            hbf = mk.B8.get(f"l2hb{g}_{th}", [128, 2048], BF16)
            P.emit("v", lambda v, A=A, hbf=hbf: v.tensor_copy(hbf[:, :], A[:, :]),
                   r=[A.b], w=[hbf.b], tag=f"l2cbf{g}_{th}")
            for jh in range(2):
                pt = ps.view(6 + jh, [128, 1024], BF16)
                for j in range(8):
                    P.emit("t", lambda t, jh=jh, j=j, hbf=hbf, pt=pt: t.transpose(
                        pt[:, j * 128:(j + 1) * 128],
                        hbf[:, (jh * 8 + j) * 128:(jh * 8 + j + 1) * 128], mk.ident[:]),
                        r=[hbf.b, mk.ident_b], w=[ps.bufs[6 + jh]], tag=f"l2tr{g}_{th}_{jh}_{j}")
                ev = mk.B2.get(f"l2ev{g}_{th}_{jh}", [128, 1024], BF16)
                P.emit("s", lambda s, ev=ev, pt=pt: s.activation(ev[:, :], pt[:, :], AF.Identity),
                       r=[ps.bufs[6 + jh]], w=[ev.b], tag=f"l2tre{g}_{th}_{jh}")
                P.dma("qy", lambda y, g=g, th=th, jh=jh, ev=ev: y.dma_start(
                    out=h_d[th * 2048 + jh * 1024:th * 2048 + (jh + 1) * 1024, g * 128:(g + 1) * 128]
                        .rearrange("(j p) c -> p j c", p=128),
                    in_=ev[:].rearrange("p (j c) -> p j c", j=8)),
                    r=[ev.b], w=[h_db[th * 2 + jh]], tag=f"l2hst{g}_{th}_{jh}")
                ev.free()
            for tl in (A, Bt, H, S1, S2, m, hbf):
                tl.free()
        init.free()
    bf_p.free(); bi_p.free(); bh_p.free()
    for i in range(NT):
        ht = mk.B2.get(f"l2hf{i}", [128, D], BF16)
        P.dma("qy", lambda y, i=i, ht=ht: y.dma_start(
            out=ht[:], in_=h_d[i * 128:(i + 1) * 128, :]),
            r=h_db, w=[ht.b], tag=f"l2hfl{i}")
        xt32 = mk.B4.get(f"l2xf{i}", [128, D], F32)
        P.dma("qy", lambda y, i=i, xt32=xt32: y.dma_start(
            out=xt32[:], in_=mk.x32[i * 128:(i + 1) * 128, :]),
            r=[mk.x32_b[i]], w=[xt32.b], tag=f"l2xfl{i}")
        yt = mk.B4.get(f"l2yf{i}", [128, D], F32)
        P.emit("v", lambda v, ht=ht, xt32=xt32, yt=yt: v.tensor_add(yt[:, :], xt32[:, :], ht[:, :]),
               r=[ht.b, xt32.b], w=[yt.b], tag=f"l2add{i}")
        ht.free(); xt32.free()
        mk.ln_store(i, yt, C_EPS5)
        yt.free()


def ln512(mk, src_t, src_b, gam, bet, dst_t, dst_b, tag):
    P = mk.P
    st, st_b, cst, cst_b = mk.stats, mk.stats_b, mk.cst, mk.cst_b
    trash = mk.trash4
    P.emit("v", lambda v: v.reduce_sum(st[:, 0:1], src_t[:, :], AX.X),
           r=[src_b], w=[st_b], tag=f"{tag}sum")
    P.emit("s", lambda s: s.activation(trash[:, 0:512], src_t[:, :], AF.Square, accum_out=st[:, 1:2]),
           r=[src_b], w=[trash.b, st_b], tag=f"{tag}sq")
    P.emit("v", lambda v: v.tensor_scalar(st[:, 0:1], st[:, 0:1], cst[:, C_INV512:C_INV512 + 1], None, ALU.mult),
           r=[cst_b], w=[st_b], tag=f"{tag}mu")
    P.emit("v", lambda v: v.tensor_scalar(st[:, 2:3], st[:, 0:1], st[:, 0:1], None, ALU.mult),
           r=[st_b], w=[st_b], tag=f"{tag}mu2")
    P.emit("v", lambda v: v.tensor_scalar(st[:, 1:2], st[:, 1:2], cst[:, C_INV512:C_INV512 + 1], st[:, 2:3], ALU.mult, ALU.subtract),
           r=[cst_b], w=[st_b], tag=f"{tag}var")
    P.emit("s", lambda s: s.activation(st[:, 3:4], st[:, 1:2], AF.Sqrt, bias=cst[:, C_EPS5:C_EPS5 + 1]),
           r=[cst_b, st_b], w=[st_b], tag=f"{tag}sd")
    P.emit("v", lambda v: v.reciprocal(st[:, 4:5], st[:, 3:4]),
           r=[st_b], w=[st_b], tag=f"{tag}rc")
    P.emit("v", lambda v: v.tensor_scalar(src_t[:, :], src_t[:, :], st[:, 0:1], st[:, 4:5], ALU.subtract, ALU.mult),
           r=[st_b], w=[src_b], tag=f"{tag}ap")
    P.emit("v", lambda v: v.tensor_mul(src_t[:, :], src_t[:, :], gam[:, :]),
           r=[src_b, gam.b], w=[src_b], tag=f"{tag}g")
    P.emit("v", lambda v: v.tensor_add(dst_t[:, :], src_t[:, :], bet[:, :]),
           r=[src_b, bet.b], w=[dst_b], tag=f"{tag}b")


def l2_ffn(mk):
    P, ps = mk.P, mk.psum
    sw1t = mk.inp("l2f_sw1t", [1024, 512])
    sw2t = mk.inp("l2f_sw2t", [1024, 512])
    sw3t = mk.inp("l2f_sw3t", [512, 1024])
    sb1_r = mk.load_row128("l2fsb1", mk.inp("l2f_sb1_r", [128, 512]), 512, pool=mk.B2)
    sb2_r = mk.load_row128("l2fsb2", mk.inp("l2f_sb2_r", [128, 512]), 512, pool=mk.B2)
    sb3_r = mk.load_row128("l2fsb3", mk.inp("l2f_sb3_r", [128, 1024]), 1024)
    slng = mk.load_row128("l2fslng", mk.inp("l2f_slng_r", [128, 512]), 512, pool=mk.B2)
    slnb = mk.load_row128("l2fslnb", mk.inp("l2f_slnb_r", [128, 512]), 512, pool=mk.B2)
    so_d = mk.dram("l2f_so", [T, D], F32)
    so_db = [Buf(f"l2fso{i}") for i in range(NT)]

    xT = mk.make_xT()
    w1s = [mk.B2.get(f"l2fw1_{j}", [128, 1024], BF16) for j in range(4)]
    w2s = [mk.B2.get(f"l2fw2_{j}", [128, 1024], BF16) for j in range(4)]
    w3s = [mk.B2.get(f"l2fw3_{j}", [128, 1024], BF16) for j in range(4)]
    for j in range(4):
        P.dma("qg", lambda g, j=j, tl=w1s[j]: g.dma_start(
            out=tl[:].rearrange("p (a c) -> p a c", a=2),
            in_=sw1t[j * 256:(j + 1) * 256, :].rearrange("(a p) c -> p a c", p=128)),
            w=[w1s[j].b], tag=f"l2fw1l{j}")
        P.dma("qg", lambda g, j=j, tl=w2s[j]: g.dma_start(
            out=tl[:].rearrange("p (a c) -> p a c", a=2),
            in_=sw2t[j * 256:(j + 1) * 256, :].rearrange("(a p) c -> p a c", p=128)),
            w=[w2s[j].b], tag=f"l2fw2l{j}")
    for j in range(4):
        P.dma("qg", lambda g, j=j, tl=w3s[j]: g.dma_start(
            out=tl[:], in_=sw3t[j * 128:(j + 1) * 128, :]), w=[w3s[j].b], tag=f"l2fw3l{j}")

    for i in range(NT):
        for b, ws in ((0, w1s), (1, w2s)):
            for dc in range(8):
                P.emit("t", lambda t, i=i, dc=dc, b=b, ws=ws: t.matmul(
                    ps.banks[b][:, :], xT[dc][:, i * 128:(i + 1) * 128],
                    ws[dc // 2][:, (dc % 2) * 512:(dc % 2 + 1) * 512],
                    start=(dc == 0), stop=(dc == 7)),
                    r=[xT[dc].b, ws[dc // 2].b], w=[ps.bufs[b]], tag=f"l2fx{b}_{i}_{dc}")
        t1 = mk.B2.get(f"l2ft1_{i}", [128, 512], F32)
        P.emit("v", lambda v, t1=t1: v.tensor_add(t1[:, :], ps.banks[0][:, :], sb1_r[:, :]),
               r=[ps.bufs[0], sb1_r.b], w=[t1.b], tag=f"l2fb1{i}")
        s1 = mk.B2.get(f"l2fs1_{i}", [128, 512], F32)
        P.emit("s", lambda s, t1=t1, s1=s1: s.activation(s1[:, :], t1[:, :], AF.Silu),
               r=[t1.b], w=[s1.b], tag=f"l2fsilu{i}")
        P.emit("v", lambda v, t1=t1: v.tensor_add(t1[:, :], ps.banks[1][:, :], sb2_r[:, :]),
               r=[ps.bufs[1], sb2_r.b], w=[t1.b], tag=f"l2fb2{i}")
        P.emit("v", lambda v, t1=t1, s1=s1: v.tensor_mul(t1[:, :], t1[:, :], s1[:, :]),
               r=[s1.b, t1.b], w=[t1.b], tag=f"l2fprod{i}")
        s1.free()
        shb = mk.B1.get(f"l2fshb{i}", [128, 512], BF16)
        ln512(mk, t1, t1.b, slng, slnb, shb, shb.b, f"l2fln{i}")
        t1.free()
        pt = ps.view(6, [128, 512], BF16)
        for j in range(4):
            P.emit("t", lambda t, j=j, shb=shb, pt=pt: t.transpose(
                pt[:, j * 128:(j + 1) * 128], shb[:, j * 128:(j + 1) * 128], mk.ident[:]),
                r=[shb.b, mk.ident_b], w=[ps.bufs[6]], tag=f"l2ftr{i}_{j}")
        sT = [mk.B1.get(f"l2fsT{i}_{j}", [128, 128], BF16) for j in range(4)]
        for j in range(4):
            P.emit("s", lambda s, j=j, pt=pt, sT=sT: s.activation(sT[j][:, :], pt[:, j * 128:(j + 1) * 128], AF.Identity),
                   r=[ps.bufs[6]], w=[sT[j].b], tag=f"l2ftre{i}_{j}")
        shb.free()
        for dh in range(2):
            bank = 2 + dh
            for j in range(4):
                P.emit("t", lambda t, j=j, dh=dh, bank=bank, sT=sT: t.matmul(
                    ps.banks[bank][:, :], sT[j][:, :], w3s[j][:, dh * 512:(dh + 1) * 512],
                    start=(j == 0), stop=(j == 3)),
                    r=[sT[j].b, w3s[j].b], w=[ps.bufs[bank]], tag=f"l2fo{i}_{dh}_{j}")
        so = mk.B4.get(f"l2fsov{i}", [128, D], F32)
        for dh in range(2):
            P.emit("v", lambda v, dh=dh, so=so: v.tensor_add(
                so[:, dh * 512:(dh + 1) * 512], ps.banks[2 + dh][:, :], sb3_r[:, dh * 512:(dh + 1) * 512]),
                r=[ps.bufs[2 + dh], sb3_r.b], w=[so.b], tag=f"l2fsoe{i}_{dh}")
        P.dma("qy", lambda y, i=i, so=so: y.dma_start(
            out=so_d[i * 128:(i + 1) * 128, :], in_=so[:]),
            r=[so.b], w=[so_db[i]], tag=f"l2fsos{i}")
        for tl in sT:
            tl.free()
        so.free()
    for tl in xT + w1s + w2s + w3s:
        tl.free()
    sb1_r.free(); sb2_r.free(); sb3_r.free(); slng.free(); slnb.free()

    if DBG.get("l2f_nomoe"):
        for i in range(NT):
            so = mk.B4.get(f"l2ffin{i}", [128, D], F32)
            P.dma("qy", lambda y, i=i, so=so: y.dma_start(
                out=so[:], in_=so_d[i * 128:(i + 1) * 128, :]),
                r=so_db, w=[so.b], tag=f"l2ffl{i}")
            xt32 = mk.B4.get(f"l2ffx{i}", [128, D], F32)
            P.dma("qy", lambda y, i=i, xt32=xt32: y.dma_start(
                out=xt32[:], in_=mk.x32[i * 128:(i + 1) * 128, :]),
                r=[mk.x32_b[i]], w=[xt32.b], tag=f"l2ffxl{i}")
            P.emit("v", lambda v, so=so, xt32=xt32: v.tensor_add(so[:, :], so[:, :], xt32[:, :]),
                   r=[so.b, xt32.b], w=[so.b], tag=f"l2ffadd{i}")
            xt32.free()
            mk.ln_store(i, so, C_EPS5)
            so.free()
        return
    M = 2048
    NMT = M // 128
    gidx = mk.load_row128("l2fgidx", mk.inp("l2f_gidx", [128, 16], I32), 16, dtype=I32, pool=mk.SM)
    sidx = mk.load_row128("l2fsidx", mk.inp("l2f_sidx", [128, 16], I32), 16, dtype=I32, pool=mk.SM)
    sgw = mk.load_row128("l2fsgw", mk.inp("l2f_sgw_p", [128, 8]), 8, dtype=BF16, pool=mk.SM)
    sgb = mk.load_row128("l2fsgb", mk.inp("l2f_sgb_p", [128, 1]), 1, pool=mk.SM)
    egw = mk.load_row128("l2fegw", mk.inp("l2f_egw", [128, 64]), 64, dtype=BF16, pool=mk.SM)
    egb = mk.load_row128("l2fegb", mk.inp("l2f_egb_r", [128, 8]), 8, pool=mk.SM)
    expb = mk.load_row128("l2fexpb", mk.inp("l2f_expb_r", [128, 8]), 8, pool=mk.SM)

    xm8 = [mk.B8.get(f"l2fxmT{j}", [128, 4096], BF16) for j in range(4)]
    xmTs = lambda dc, mt: xm8[dc // 2][:, (dc % 2) * 2048 + mt * 128:(dc % 2) * 2048 + (mt + 1) * 128]
    gmeta = mk.B1.get("l2fgmeta", [128, 256], F32)
    scr = mk.B1.get("l2fgscr", [128, 64], F32)
    P.emit("v", lambda v: v.memset(scr[:, 48:56], 1.0), w=[scr.b], tag="l2fones")
    for mt in range(NMT):
        xm = mk.B2.get(f"l2fxm{mt}", [128, 1024], BF16)
        P.dma("qg", lambda g, mt=mt, xm=xm: g.indirect_dma_start(
            xm[:], None, mk.xbf[:, :], IndirectOffsetOnAxis(gidx[:, mt:mt + 1], 0)),
            r=mk.xbf_b + [gidx.b], w=[xm.b], tag=f"l2fgx{mt}")
        pt = ps.view(7, [128, 1024], BF16)
        for j in range(8):
            P.emit("t", lambda t, j=j, xm=xm, pt=pt: t.transpose(
                pt[:, j * 128:(j + 1) * 128], xm[:, j * 128:(j + 1) * 128], mk.ident[:]),
                r=[xm.b, mk.ident_b], w=[ps.bufs[7]], tag=f"l2fxtr{mt}_{j}")
        for j in range(8):
            P.emit("s", lambda s, j=j, mt=mt, pt=pt: s.activation(
                xmTs(j, mt), pt[:, j * 128:(j + 1) * 128], AF.Identity),
                r=[ps.bufs[7]], w=[xm8[j // 2].b], tag=f"l2fxte{mt}_{j}")
        xm.free()
        pg = ps.view(6, [128, 16], F32)
        for dc in range(8):
            P.emit("t", lambda t, mt=mt, dc=dc, pg=pg: t.matmul(
                pg[:, 0:1], xmTs(dc, mt), sgw[:, dc:dc + 1],
                start=(dc == 0), stop=(dc == 7)),
                r=[xm8[dc // 2].b, sgw.b], w=[ps.bufs[6]], tag=f"l2fss{mt}_{dc}")
        for dc in range(8):
            P.emit("t", lambda t, mt=mt, dc=dc, pg=pg: t.matmul(
                pg[:, 8:16], xmTs(dc, mt), egw[:, dc * 8:(dc + 1) * 8],
                start=(dc == 0), stop=(dc == 7)),
                r=[xm8[dc // 2].b, egw.b], w=[ps.bufs[6]], tag=f"l2fes{mt}_{dc}")
        P.emit("s", lambda s, pg=pg: s.activation(scr[:, 32:33], pg[:, 0:1], AF.Sigmoid, bias=sgb[:, 0:1]),
               r=[ps.bufs[6], sgb.b], w=[scr.b], tag=f"l2fssg{mt}")
        P.emit("v", lambda v, pg=pg: v.tensor_add(scr[:, 0:8], pg[:, 8:16], egb[:, :]),
               r=[ps.bufs[6], egb.b], w=[scr.b], tag=f"l2fesb{mt}")
        P.emit("s", lambda s: s.activation(scr[:, 0:8], scr[:, 0:8], AF.Sigmoid),
               r=[scr.b], w=[scr.b], tag=f"l2fesg{mt}")
        P.emit("v", lambda v: v.tensor_add(scr[:, 8:16], scr[:, 0:8], expb[:, :]),
               r=[scr.b, expb.b], w=[scr.b], tag=f"l2fe2{mt}")
        P.emit("v", lambda v: v.tensor_reduce(scr[:, 33:34], scr[:, 8:16], AX.X, ALU.max),
               r=[scr.b], w=[scr.b], tag=f"l2fmx{mt}")
        P.emit("v", lambda v: v.tensor_scalar(scr[:, 16:24], scr[:, 8:16], scr[:, 33:34], None, ALU.is_equal),
               r=[scr.b], w=[scr.b], tag=f"l2feq{mt}")
        P.emit("v", lambda v: v.tensor_tensor_scan(scr[:, 24:32], scr[:, 48:56], scr[:, 16:24], 0.0, ALU.mult, ALU.add),
               r=[scr.b], w=[scr.b], tag=f"l2fcum{mt}")
        P.emit("v", lambda v: v.tensor_scalar(scr[:, 24:32], scr[:, 24:32], 1.0, None, ALU.is_equal),
               r=[scr.b], w=[scr.b], tag=f"l2ffo{mt}")
        P.emit("v", lambda v, mt=mt: v.tensor_mul(gmeta[:, mt * 16:mt * 16 + 8], scr[:, 16:24], scr[:, 24:32]),
               r=[scr.b], w=[gmeta.b], tag=f"l2ff1{mt}")
        P.emit("v", lambda v, mt=mt: v.tensor_mul(scr[:, 16:24], scr[:, 0:8], gmeta[:, mt * 16:mt * 16 + 8]),
               r=[scr.b, gmeta.b], w=[scr.b], tag=f"l2ft1{mt}")
        P.emit("v", lambda v: v.tensor_reduce(scr[:, 34:35], scr[:, 16:24], AX.X, ALU.add),
               r=[scr.b], w=[scr.b], tag=f"l2ft1s{mt}")
        P.emit("v", lambda v: v.tensor_sub(scr[:, 35:36], scr[:, 32:33], scr[:, 34:35]),
               r=[scr.b], w=[scr.b], tag=f"l2fd{mt}")
        P.emit("s", lambda s, mt=mt: s.activation(gmeta[:, mt * 16 + 8:mt * 16 + 9], scr[:, 35:36], AF.Sigmoid),
               r=[scr.b], w=[gmeta.b], tag=f"l2fwsh{mt}")
        P.emit("s", lambda s, mt=mt: s.activation(gmeta[:, mt * 16 + 9:mt * 16 + 10], scr[:, 35:36], AF.Sigmoid, scale=-1.0),
               r=[scr.b], w=[gmeta.b], tag=f"l2fwex{mt}")
    scr.free()

    eacc_d = mk.dram("l2f_eacc", [M, D], F32)
    eacc_db = [Buf(f"l2fe{mt}") for mt in range(NMT)]
    ew1 = mk.inp("l2f_ew1t", [8, 1024, 512])
    ew2 = mk.inp("l2f_ew2t", [8, 1024, 512])
    ew3 = mk.inp("l2f_ew3t", [8, 512, 1024])
    eb1 = mk.inp("l2f_eb1_r", [8, 128, 512])
    eb2 = mk.inp("l2f_eb2_r", [8, 128, 512])
    eb3 = mk.inp("l2f_eb3_r", [8, 128, 1024])
    selg = mk.load_row128("l2fselg", mk.inp("l2f_sellng_r", [128, 512]), 512, pool=mk.B2)
    selb = mk.load_row128("l2fselb", mk.inp("l2f_sellnb_r", [128, 512]), 512, pool=mk.B2)
    for e in range(8):
        we1 = [mk.B2.get(f"l2fe1_{e}_{j}", [128, 1024], BF16) for j in range(4)]
        we2 = [mk.B2.get(f"l2fe2_{e}_{j}", [128, 1024], BF16) for j in range(4)]
        we3 = [mk.B2.get(f"l2fe3_{e}_{j}", [128, 1024], BF16) for j in range(4)]
        for j in range(4):
            P.dma("qg", lambda g, e=e, j=j, tl=we1[j]: g.dma_start(
                out=tl[:].rearrange("p (a c) -> p a c", a=2),
                in_=ew1[e, j * 256:(j + 1) * 256, :].rearrange("(a p) c -> p a c", p=128)),
                w=[we1[j].b], tag=f"l2fe1l{e}_{j}")
            P.dma("qg", lambda g, e=e, j=j, tl=we2[j]: g.dma_start(
                out=tl[:].rearrange("p (a c) -> p a c", a=2),
                in_=ew2[e, j * 256:(j + 1) * 256, :].rearrange("(a p) c -> p a c", p=128)),
                w=[we2[j].b], tag=f"l2fe2l{e}_{j}")
            P.dma("qg", lambda g, e=e, j=j, tl=we3[j]: g.dma_start(
                out=tl[:], in_=ew3[e, j * 128:(j + 1) * 128, :]),
                w=[we3[j].b], tag=f"l2fe3l{e}_{j}")
        b1r = mk.load_row128(f"l2feb1{e}", eb1[e], 512, pool=mk.B2)
        b2r = mk.load_row128(f"l2feb2{e}", eb2[e], 512, pool=mk.B2)
        b3r = mk.load_row128(f"l2feb3{e}", eb3[e], 1024)
        for mt in range(NMT):
            for b, ws in ((0, we1), (1, we2)):
                for dc in range(8):
                    P.emit("t", lambda t, mt=mt, dc=dc, b=b, ws=ws: t.matmul(
                        ps.banks[b][:, :], xmTs(dc, mt),
                        ws[dc // 2][:, (dc % 2) * 512:(dc % 2 + 1) * 512],
                        start=(dc == 0), stop=(dc == 7)),
                        r=[xm8[dc // 2].b, ws[dc // 2].b], w=[ps.bufs[b]], tag=f"l2fh{b}_{e}_{mt}_{dc}")
            t1 = mk.B2.get(f"l2fet1_{e}_{mt}", [128, 512], F32)
            P.emit("v", lambda v, t1=t1, b1r=b1r: v.tensor_add(t1[:, :], ps.banks[0][:, :], b1r[:, :]),
                   r=[ps.bufs[0], b1r.b], w=[t1.b], tag=f"l2feb1a{e}_{mt}")
            s1 = mk.B2.get(f"l2fes1_{e}_{mt}", [128, 512], F32)
            P.emit("s", lambda s, t1=t1, s1=s1: s.activation(s1[:, :], t1[:, :], AF.Silu),
                   r=[t1.b], w=[s1.b], tag=f"l2fesilu{e}_{mt}")
            P.emit("v", lambda v, t1=t1, b2r=b2r: v.tensor_add(t1[:, :], ps.banks[1][:, :], b2r[:, :]),
                   r=[ps.bufs[1], b2r.b], w=[t1.b], tag=f"l2feb2a{e}_{mt}")
            P.emit("v", lambda v, t1=t1, s1=s1: v.tensor_mul(t1[:, :], t1[:, :], s1[:, :]),
                   r=[s1.b, t1.b], w=[t1.b], tag=f"l2feprod{e}_{mt}")
            s1.free()
            if DBG.get("l2f_dump") and e == 0 and mt == 0:
                P.dma("qy", lambda y, t1=t1: y.dma_start(out=DBG["l2f_dump"]["t1"], in_=t1[:]),
                      r=[t1.b], tag="dt1")
            zb = mk.B1.get(f"l2fzb{e}_{mt}", [128, 512], BF16)
            ln512(mk, t1, t1.b, selg, selb, zb, zb.b, f"l2fzln{e}_{mt}")
            t1.free()
            if DBG.get("l2f_dump") and e == 0 and mt == 0:
                P.dma("qy", lambda y, zb=zb: y.dma_start(out=DBG["l2f_dump"]["zb"], in_=zb[:]),
                      r=[zb.b], tag="dzb")
            pt = ps.view(6, [128, 512], BF16)
            for j in range(4):
                P.emit("t", lambda t, j=j, zb=zb, pt=pt: t.transpose(
                    pt[:, j * 128:(j + 1) * 128], zb[:, j * 128:(j + 1) * 128], mk.ident[:]),
                    r=[zb.b, mk.ident_b], w=[ps.bufs[6]], tag=f"l2fztr{e}_{mt}_{j}")
            zT = [mk.B1.get(f"l2fzT{e}_{mt}_{j}", [128, 128], BF16) for j in range(4)]
            for j in range(4):
                P.emit("s", lambda s, j=j, pt=pt, zT=zT: s.activation(zT[j][:, :], pt[:, j * 128:(j + 1) * 128], AF.Identity),
                       r=[ps.bufs[6]], w=[zT[j].b], tag=f"l2fztre{e}_{mt}_{j}")
            zb.free()
            if DBG.get("l2f_dump") and e == 0 and mt == 0:
                P.dma("qy", lambda y, zT=zT: y.dma_start(out=DBG["l2f_dump"]["zT0"], in_=zT[0][:]),
                      r=[zT[0].b], tag="dzT0")
            for dh in range(2):
                bank = 2 + dh
                for j in range(4):
                    P.emit("t", lambda t, j=j, dh=dh, bank=bank, zT=zT, we3=we3: t.matmul(
                        ps.banks[bank][:, :], zT[j][:, :], we3[j][:, dh * 512:(dh + 1) * 512],
                        start=(j == 0), stop=(j == 3)),
                        r=[zT[j].b, we3[j].b], w=[ps.bufs[bank]], tag=f"l2feo{e}_{mt}_{dh}_{j}")
            acc = mk.B4.get(f"l2facc{e}_{mt}", [128, D], F32)
            if e > 0:
                P.dma("qy", lambda y, mt=mt, acc=acc: y.dma_start(
                    out=acc[:], in_=eacc_d[mt * 128:(mt + 1) * 128, :]),
                    r=[eacc_db[mt]], w=[acc.b], tag=f"l2faccl{e}_{mt}")
            tmp = mk.B4.get(f"l2ftmp{e}_{mt}", [128, D], F32)
            for dh in range(2):
                P.emit("v", lambda v, dh=dh, tmp=tmp, b3r=b3r: v.tensor_add(
                    tmp[:, dh * 512:(dh + 1) * 512], ps.banks[2 + dh][:, :], b3r[:, dh * 512:(dh + 1) * 512]),
                    r=[ps.bufs[2 + dh], b3r.b], w=[tmp.b], tag=f"l2feba{e}_{mt}_{dh}")
            if DBG.get("l2f_dump") and e == 0 and mt == 0:
                P.dma("qy", lambda y, tmp=tmp: y.dma_start(out=DBG["l2f_dump"]["tmp"], in_=tmp[:]),
                      r=[tmp.b], tag="dtmp")
            if e > 0:
                P.emit("v", lambda v, e=e, mt=mt, acc=acc, tmp=tmp: v.scalar_tensor_tensor(
                    acc[:, :], tmp[:, :], gmeta[:, mt * 16 + e:mt * 16 + e + 1], acc[:, :],
                    ALU.mult, ALU.add),
                    r=[tmp.b, gmeta.b, acc.b], w=[acc.b], tag=f"l2fsel{e}_{mt}")
            else:
                P.emit("v", lambda v, mt=mt, acc=acc, tmp=tmp: v.tensor_scalar(
                    acc[:, :], tmp[:, :], gmeta[:, mt * 16:mt * 16 + 1], None, ALU.mult),
                    r=[tmp.b, gmeta.b], w=[acc.b], tag=f"l2fsel0_{mt}")
            tmp.free()
            P.dma("qy", lambda y, mt=mt, acc=acc: y.dma_start(
                out=eacc_d[mt * 128:(mt + 1) * 128, :], in_=acc[:]),
                r=[acc.b], w=[eacc_db[mt]], tag=f"l2faccs{e}_{mt}")
            acc.free()
            for tl in zT:
                tl.free()
        for tl in we1 + we2 + we3:
            tl.free()
        b1r.free(); b2r.free(); b3r.free()
    for tl in xm8:
        tl.free()
    selg.free(); selb.free()

    for mt in range(NMT):
        sog = mk.B4.get(f"l2fsog{mt}", [128, D], F32)
        P.dma("qg", lambda g, mt=mt, sog=sog: g.indirect_dma_start(
            sog[:], None, so_d[:, :], IndirectOffsetOnAxis(gidx[:, mt:mt + 1], 0)),
            r=so_db + [gidx.b], w=[sog.b], tag=f"l2fsog{mt}")
        acc = mk.B4.get(f"l2ffacc{mt}", [128, D], F32)
        P.dma("qy", lambda y, mt=mt, acc=acc: y.dma_start(
            out=acc[:], in_=eacc_d[mt * 128:(mt + 1) * 128, :]),
            r=[eacc_db[mt]], w=[acc.b], tag=f"l2ffal{mt}")
        P.emit("v", lambda v, mt=mt, sog=sog: v.tensor_scalar(
            sog[:, :], sog[:, :], gmeta[:, mt * 16 + 8:mt * 16 + 9], None, ALU.mult),
            r=[gmeta.b], w=[sog.b], tag=f"l2fshw{mt}")
        P.emit("v", lambda v, mt=mt, sog=sog, acc=acc: v.scalar_tensor_tensor(
            sog[:, :], acc[:, :], gmeta[:, mt * 16 + 9:mt * 16 + 10], sog[:, :],
            ALU.mult, ALU.add),
            r=[acc.b, gmeta.b, sog.b], w=[sog.b], tag=f"l2fcomb{mt}")
        acc.free()
        P.dma("qg", lambda g, mt=mt, sog=sog: g.indirect_dma_start(
            so_d[:, :], IndirectOffsetOnAxis(sidx[:, mt:mt + 1], 0), sog[:], None,
            bounds_check=T - 1, oob_is_err=False),
            r=[sog.b, sidx.b], w=so_db, tag=f"l2fscat{mt}")
        sog.free()
    gmeta.free()
    gidx.free(); sidx.free(); sgw.free(); sgb.free(); egw.free(); egb.free(); expb.free()

    if DBG.get("l2f_dump"):
        P.dma("qy", lambda y: y.dma_start(out=DBG["l2f_dump"]["so"], in_=so_d), r=so_db, tag="dso")
        P.dma("qy", lambda y: y.dma_start(out=DBG["l2f_dump"]["eacc"], in_=eacc_d), r=eacc_db, tag="deacc")
        gm2 = mk.B1.get("dgm", [128, 256], F32)
        P.emit("v", lambda v, gm2=gm2: v.tensor_copy(gm2[:], gmeta[:]), r=[gmeta.b], w=[gm2.b], tag="dgmc")
        P.dma("qy", lambda y, gm2=gm2: y.dma_start(out=DBG["l2f_dump"]["gmeta"], in_=gm2[:]), r=[gm2.b], tag="dgm")
        gm2.free()
    for i in range(NT):
        so = mk.B4.get(f"l2ffin{i}", [128, D], F32)
        P.dma("qy", lambda y, i=i, so=so: y.dma_start(
            out=so[:], in_=so_d[i * 128:(i + 1) * 128, :]),
            r=so_db, w=[so.b], tag=f"l2ffl{i}")
        xt32 = mk.B4.get(f"l2ffx{i}", [128, D], F32)
        P.dma("qy", lambda y, i=i, xt32=xt32: y.dma_start(
            out=xt32[:], in_=mk.x32[i * 128:(i + 1) * 128, :]),
            r=[mk.x32_b[i]], w=[xt32.b], tag=f"l2ffxl{i}")
        P.emit("v", lambda v, so=so, xt32=xt32: v.tensor_add(so[:, :], so[:, :], xt32[:, :]),
               r=[so.b, xt32.b], w=[so.b], tag=f"l2ffadd{i}")
        xt32.free()
        mk.ln_store(i, so, C_EPS5)
        so.free()


def l3_attn_local_only(mk):
    l3_attn(mk, mode="local")


def l3_attn_global_only(mk):
    l3_attn(mk, mode="global")


def l3_attn(mk, mode="full"):
    P, ps = mk.P, mk.psum
    if mode == "global":
        scr_d = mk.inp("scr", [T, D], BF16)
        scr_db = [Buf("l3scr")]
        _l3_global(mk, scr_d, scr_db)
        return
    winl = mk.inp("l3_winl", [1024, 3072])
    binl = [mk.load_row128(f"l3binl{j}", mk.inp("l3_binl_r", [128, 3072])[:, j * 1024:(j + 1) * 1024], 1024)
            for j in range(3)]
    woutl = mk.inp("l3_woutl", [1024, 1024])
    boutl = mk.load_row128("l3boutl", mk.inp("l3_boutl_r", [128, 1024]), 1024)

    qkv_d = mk.dram("l3_qkv", [T, 3072], BF16)
    qkv_db = [Buf(f"l3qkv{i}") for i in range(NT)]
    if mode == "local":
        contrib_d = mk.out("loc", [T, D], BF16)
    else:
        contrib_d = mk.dram("l3_contrib", [T, D], BF16)
    contrib_db = [Buf(f"l3ct{i}") for i in range(NT)]
    scr_d = mk.dram("l3_scr", [T, D], BF16)
    scr_db = [Buf("l3scr")]

    xT = mk.make_xT(tag="l3xT")
    for g6 in range(6):
        wc = mk.B8.get(f"l3w{g6}", [128, 4096], BF16)
        P.dma("qg", lambda g, g6=g6, wc=wc: g.dma_start(
            out=wc[:].rearrange("p (dc c) -> p dc c", dc=8),
            in_=winl[:, g6 * 512:(g6 + 1) * 512].rearrange("(dc p) c -> p dc c", p=128)),
            w=[wc.b], tag=f"l3wl{g6}")
        for i in range(NT):
            bank = i % 2
            for dc in range(8):
                P.emit("t", lambda t, i=i, dc=dc, bank=bank, wc=wc: t.matmul(
                    ps.banks[bank][:, :], xT[dc][:, i * 128:(i + 1) * 128],
                    wc[:, dc * 512:(dc + 1) * 512],
                    start=(dc == 0), stop=(dc == 7)),
                    r=[xT[dc].b, wc.b], w=[ps.bufs[bank]], tag=f"l3qkv{i}_{g6}_{dc}")
            ot = mk.B1.get(f"l3q{i}_{g6}", [128, 512], BF16)
            P.emit("v", lambda v, g6=g6, bank=bank, ot=ot, binl=binl: v.tensor_add(
                ot[:, :], ps.banks[bank][:, :],
                binl[g6 // 2][:, (g6 % 2) * 512:(g6 % 2 + 1) * 512]),
                r=[ps.bufs[bank], binl[g6 // 2].b], w=[ot.b], tag=f"l3qkve{i}_{g6}")
            P.dma("qy", lambda y, i=i, g6=g6, ot=ot: y.dma_start(
                out=qkv_d[i * 128:(i + 1) * 128, g6 * 512:(g6 + 1) * 512], in_=ot[:]),
                r=[ot.b], w=[qkv_db[i]], tag=f"l3qkvs{i}_{g6}")
            ot.free()
        wc.free()
    for tl in xT:
        tl.free()
    for b in binl:
        b.free()

    wol8 = [mk.B8.get(f"l3wol{j}", [128, 4096], BF16) for j in range(2)]
    for j in range(2):
        P.dma("qg", lambda g, j=j, wol8=wol8: g.dma_start(
            out=wol8[j][:].rearrange("p (a c) -> p a c", a=4),
            in_=woutl[j * 512:(j + 1) * 512, :].rearrange("(a p) c -> p a c", p=128)),
            w=[wol8[j].b], tag=f"l3woll{j}")
    wols = lambda hc: wol8[hc // 4][:, (hc % 4) * 1024:(hc % 4) * 1024 + 1024]
    for pt in range(4):
        qts = [mk.B2.get(f"l3lq{pt}_{c}", [128, 1024], BF16) for c in range(8)]
        kts = [mk.B2.get(f"l3lk{pt}_{c}", [128, 1024], BF16) for c in range(8)]
        for c in range(8):
            P.dma("qy", lambda y, pt=pt, c=c, qts=qts: y.dma_start(
                out=qts[c][:], in_=qkv_d[c * 512 + pt * 128:c * 512 + (pt + 1) * 128, 0:1024]),
                r=[qkv_db[(c * 512 + pt * 128) // 128]], w=[qts[c].b], tag=f"l3lql{pt}_{c}")
            P.dma("qy", lambda y, pt=pt, c=c, kts=kts: y.dma_start(
                out=kts[c][:], in_=qkv_d[c * 512 + pt * 128:c * 512 + (pt + 1) * 128, 1024:2048]),
                r=[qkv_db[(c * 512 + pt * 128) // 128]], w=[kts[c].b], tag=f"l3lkl{pt}_{c}")
        sc = mk.B4.get(f"l3sc{pt}", [128, 1024], F32)
        prodb = mk.B2.get(f"l3pr{pt}", [128, 1024], BF16)
        for i in range(8):
            for j in range(8):
                P.emit("v", lambda v, i=i, j=j, prodb=prodb, qts=qts, kts=kts: v.tensor_mul(
                    prodb[:, :], qts[i][:, :], kts[j][:, :]),
                    r=[qts[i].b, kts[j].b], w=[prodb.b], tag=f"l3m{pt}_{i}_{j}")
                P.emit("v", lambda v, i=i, j=j, prodb=prodb, sc=sc: v.tensor_reduce(
                    sc[:].rearrange("p (h i j) -> p h i j", h=16, i=8)[:, :, i, j],
                    prodb[:].rearrange("p (h c) -> p h c", h=16), AX.X, ALU.add),
                    r=[prodb.b], w=[sc.b], tag=f"l3r{pt}_{i}_{j}")
        prodb.free()
        for c in range(8):
            kts[c].free()
            qts[c].free()
        P.emit("v", lambda v, sc=sc: v.tensor_scalar(sc[:, :], sc[:, :], mk.cst[:, C_NEG125:C_NEG125 + 1], None, ALU.mult),
               r=[mk.cst_b], w=[sc.b], tag=f"l3scl{pt}")
        mx = mk.B4.get(f"l3mx{pt}", [128, 128], F32)
        P.emit("v", lambda v, sc=sc, mx=mx: v.tensor_reduce(
            mx[:, :], sc[:].rearrange("p (hi j) -> p hi j", j=8), AX.X, ALU.min),
            r=[sc.b], w=[mx.b], tag=f"l3mx{pt}")
        for j in range(8):
            P.emit("v", lambda v, j=j, sc=sc, mx=mx: v.tensor_sub(
                sc[:].rearrange("p (hi j) -> p hi j", j=8)[:, :, j],
                sc[:].rearrange("p (hi j) -> p hi j", j=8)[:, :, j], mx[:, :]),
                r=[mx.b], w=[sc.b], tag=f"l3sub{pt}_{j}")
        P.emit("s", lambda s, sc=sc: s.activation(sc[:, :], sc[:, :], AF.Exp, scale=-1.0),
               r=[sc.b], w=[sc.b], tag=f"l3exp{pt}")
        P.emit("v", lambda v, sc=sc, mx=mx: v.tensor_reduce(
            mx[:, :], sc[:].rearrange("p (hi j) -> p hi j", j=8), AX.X, ALU.add),
            r=[sc.b], w=[mx.b], tag=f"l3z{pt}")
        P.emit("v", lambda v, mx=mx: v.reciprocal(mx[:, :], mx[:, :]),
               r=[mx.b], w=[mx.b], tag=f"l3rz{pt}")
        for j in range(8):
            P.emit("v", lambda v, j=j, sc=sc, mx=mx: v.tensor_mul(
                sc[:].rearrange("p (hi j) -> p hi j", j=8)[:, :, j],
                sc[:].rearrange("p (hi j) -> p hi j", j=8)[:, :, j], mx[:, :]),
                r=[mx.b], w=[sc.b], tag=f"l3nm{pt}_{j}")
        mx.free()
        vts = [mk.B2.get(f"l3lv{pt}_{c}", [128, 1024], BF16) for c in range(8)]
        for c in range(8):
            P.dma("qy", lambda y, pt=pt, c=c, vts=vts: y.dma_start(
                out=vts[c][:], in_=qkv_d[c * 512 + pt * 128:c * 512 + (pt + 1) * 128, 2048:3072]),
                r=[qkv_db[(c * 512 + pt * 128) // 128]], w=[vts[c].b], tag=f"l3lvl{pt}_{c}")
        for i in range(8):
            oi = mk.B4.get(f"l3oi{pt}_{i}", [128, 1024], F32)
            for j in range(8):
                for h in range(16):
                    eng = "v"
                    col = h * 64 + i * 8 + j
                    if j == 0:
                        P.emit(eng, lambda v, h=h, j=j, col=col, oi=oi, sc=sc, vts=vts: v.tensor_scalar(
                            oi[:, h * 64:(h + 1) * 64], vts[j][:, h * 64:(h + 1) * 64],
                            sc[:, col:col + 1], None, ALU.mult),
                            r=[vts[j].b, sc.b], w=[oi.b], tag=f"l3av{pt}_{i}_{j}_{h}")
                    else:
                        P.emit(eng, lambda v, h=h, j=j, col=col, oi=oi, sc=sc, vts=vts: v.scalar_tensor_tensor(
                            oi[:, h * 64:(h + 1) * 64], vts[j][:, h * 64:(h + 1) * 64],
                            sc[:, col:col + 1], oi[:, h * 64:(h + 1) * 64], ALU.mult, ALU.add),
                            r=[vts[j].b, sc.b, oi.b], w=[oi.b], tag=f"l3av{pt}_{i}_{j}_{h}")
            ob = mk.B2.get(f"l3ob{pt}_{i}", [128, 1024], BF16)
            P.emit("v", lambda v, oi=oi, ob=ob: v.tensor_copy(ob[:, :], oi[:, :]),
                   r=[oi.b], w=[ob.b], tag=f"l3obc{pt}_{i}")
            oi.free()
            ptp = ps.view(6, [128, 1024], BF16)
            for j8 in range(8):
                P.emit("t", lambda t, j8=j8, ob=ob, ptp=ptp: t.transpose(
                    ptp[:, j8 * 128:(j8 + 1) * 128], ob[:, j8 * 128:(j8 + 1) * 128], mk.ident[:]),
                    r=[ob.b, mk.ident_b], w=[ps.bufs[6]], tag=f"l3otr{pt}_{i}_{j8}")
            otb = mk.B2.get(f"l3otb{pt}_{i}", [128, 1024], BF16)
            P.emit("s", lambda s, otb=otb, ptp=ptp: s.activation(otb[:, :], ptp[:, :], AF.Identity),
                   r=[ps.bufs[6]], w=[otb.b], tag=f"l3otre{pt}_{i}")
            ob.free()
            for dh in range(2):
                bank = 4 + dh
                for hc in range(8):
                    P.emit("t", lambda t, hc=hc, dh=dh, bank=bank, otb=otb: t.matmul(
                        ps.banks[bank][:, :], otb[:, hc * 128:(hc + 1) * 128],
                        wols(hc)[:, dh * 512:(dh + 1) * 512],
                        start=(hc == 0), stop=(hc == 7)),
                        r=[otb.b, wol8[hc // 4].b], w=[ps.bufs[bank]], tag=f"l3oo{pt}_{i}_{dh}_{hc}")
            lo = mk.B2.get(f"l3lo{pt}_{i}", [128, 1024], BF16)
            for dh in range(2):
                P.emit("v", lambda v, dh=dh, lo=lo: v.tensor_add(
                    lo[:, dh * 512:(dh + 1) * 512], ps.banks[4 + dh][:, :],
                    boutl[:, dh * 512:(dh + 1) * 512]),
                    r=[ps.bufs[4 + dh], boutl.b], w=[lo.b], tag=f"l3loe{pt}_{i}_{dh}")
            otb.free()
            P.dma("qy", lambda y, pt=pt, i=i, lo=lo: y.dma_start(
                out=contrib_d[i * 512 + pt * 128:i * 512 + (pt + 1) * 128, :], in_=lo[:]),
                r=[lo.b], w=[contrib_db[(i * 512 + pt * 128) // 128]], tag=f"l3los{pt}_{i}")
            lo.free()
        for c in range(8):
            vts[c].free()
        sc.free()
    for wc in wol8:
        wc.free()
    boutl.free()

    if mode == "local":
        return
    if DBG.get("l3_noa2a"):
        for i in range(NT):
            tl = mk.B2.get(f"l3cp{i}", [128, D], BF16)
            P.dma("qy", lambda y, i=i, tl=tl: y.dma_start(
                out=tl[:], in_=contrib_d[i * 128:(i + 1) * 128, :]),
                r=[contrib_db[i]], w=[tl.b], tag=f"l3cpl{i}")
            P.dma("qy", lambda y, i=i, tl=tl: y.dma_start(
                out=scr_d[i * 128:(i + 1) * 128, :], in_=tl[:]),
                r=[tl.b], w=scr_db, tag=f"l3cps{i}")
            tl.free()
    else:
        from concourse.replica_groups import maybe_share_collective_output_space
        space = maybe_share_collective_output_space("AllGather", [[c for c in range(8)]])
        gat_d = mk.nc.dram_tensor("l3_gat", [8 * T, D], BF16, kind="Internal",
                                  addr_space=space).ap()
        gat_db = [Buf("l3gat")]
        P.coll(lambda g: g.collective_compute(
            "AllGather", ALU.bypass, [[c for c in range(8)]], [contrib_d[:, :]], [gat_d[:, :]]),
            r=contrib_db, w=gat_db, tag="l3ag")
        gidx3 = mk.load_row128("l3gidx", mk.inp("l3_gidx", [128, 32], I32), 32, dtype=I32, pool=mk.SM)
        for i in range(NT):
            tl = mk.B2.get(f"l3sl{i}", [128, D], BF16)
            P.dma("qg", lambda g, i=i, tl=tl, gat_d=gat_d, gidx3=gidx3: g.indirect_dma_start(
                tl[:], None, gat_d[:, :], IndirectOffsetOnAxis(gidx3[:, i:i + 1], 0)),
                r=gat_db + [gidx3.b], w=[tl.b], tag=f"l3slg{i}")
            P.dma("qy", lambda y, i=i, tl=tl: y.dma_start(
                out=scr_d[i * 128:(i + 1) * 128, :], in_=tl[:]),
                r=[tl.b], w=scr_db, tag=f"l3sls{i}")
            tl.free()
        gidx3.free()

    _l3_global(mk, scr_d, scr_db)


def _l3_global(mk, scr_d, scr_db):
    P, ps = mk.P, mk.psum
    wing_qk = mk.inp("l3_wing_qk", [1024, 2048])
    bing_qk = mk.load_row128("l3bingqk", mk.inp("l3_bing_qk_p", [128, 16]), 16, pool=mk.SM)
    wing_v = mk.inp("l3_wing_v", [1024, 1024])
    bing_v = mk.load_row128("l3bingv", mk.inp("l3_bing_v_r", [128, 1024]), 1024)
    woutg = mk.inp("l3_woutg", [1024, 1024])
    boutg = mk.load_row128("l3boutg", mk.inp("l3_boutg_r", [128, 1024]), 1024)
    qt_d = mk.dram("l3g_qt", [1024, T], BF16)
    kt_d = mk.dram("l3g_kt", [1024, T], BF16)
    v_d = mk.dram("l3g_v", [8, 128, T], BF16)
    ot_d = mk.dram("l3g_ot", [1024, T], BF16)
    qt_db = [Buf(f"l3gqt{g}") for g in range(8)]
    kt_db = [Buf(f"l3gkt{g}") for g in range(8)]
    v_db = [Buf(f"l3gv{i}") for i in range(NT)]
    ot_db = [Buf(f"l3got{g}") for g in range(8)]
    sT = mk.make_xT(src=scr_d, src_b=scr_db, tag="l3sT")
    proj_to_rows(mk, wing_qk, bing_qk, None, None, sT, 16, "l3gqk", dst2=(qt_d, qt_db, kt_d, kt_db))
    proj_v(mk, wing_v, bing_v, v_d, v_db, sT, "l3gv")
    for tl in sT:
        tl.free()
    bing_qk.free(); bing_v.free()
    attn_core(mk, qt_d, qt_db, kt_d, kt_db, v_d, v_db, ot_d, ot_db, "l3g")
    attn_outproj(mk, ot_d, ot_db, woutg, boutg, "l3go")
    boutg.free()



def host_consts(inputs):
    cst = np.zeros((128, N_CST), np.float32)
    cst[:, C_EPS5] = 1e-5
    cst[:, C_EPS3] = 1e-3
    cst[:, C_INV1024] = 1.0 / 1024.0
    cst[:, C_INV512] = 1.0 / 512.0
    cst[:, C_S0] = np.float32(inputs["L1_ffn_s0"])
    cst[:, C_S1] = np.float32(inputs["L1_ffn_s1"])
    cst[:, C_S2] = np.float32(inputs["L1_ffn_s2"])
    cst[:, C_HALF] = 0.5
    cst[:, C_ONE] = 1.0
    cst[:, C_NEG125] = -0.125
    cst[:, C_NBIG] = -1e30
    cst[:, C_INV64] = 1.0 / 64.0
    cst[:, C_FOUR] = 4.0
    cst[:, C_TWO] = 2.0
    cst[:, C_NEG1] = -1.0
    return cst


def host_l0a(inputs):
    rep = lambda a: np.tile(np.asarray(a, np.float32).reshape(1, -1), (128, 1))
    perp = lambda a: np.ascontiguousarray(np.asarray(a, np.float32).reshape(-1, 128).T)
    return {
        "l0_wq": np.ascontiguousarray(inputs["L0_attn_Wq"].reshape(1024, 1024)),
        "l0_wk": np.ascontiguousarray(inputs["L0_attn_Wk"].reshape(1024, 1024)),
        "l0_wv": np.ascontiguousarray(inputs["L0_attn_Wv"].reshape(1024, 1024)),
        "l0_wo": np.ascontiguousarray(inputs["L0_attn_Wo"].reshape(1024, 1024)),
        "l0_bq_p": perp(inputs["L0_attn_bq"]),
        "l0_bk_p": perp(inputs["L0_attn_bk"]),
        "l0_bv_r": rep(inputs["L0_attn_bv"]),
        "l0_bo_r": rep(inputs["L0_attn_bo"]),
        "l0_gamma_r": rep(inputs["L0_attn_gamma"]),
        "l0_beta_r": rep(inputs["L0_attn_beta"]),
    }


def host_l0f(inputs):
    return {
        "l0_gate": np.ascontiguousarray(inputs["L0_ffn_gate_w"].T),
        "l0_w1t": np.ascontiguousarray(np.transpose(inputs["L0_ffn_w1"], (0, 2, 1))),
        "l0_w2t": np.ascontiguousarray(np.transpose(inputs["L0_ffn_w2"], (0, 2, 1))),
    }


def host_l1(inputs):
    W = inputs["L1_attn_W_qkv"].T
    qcols = np.concatenate([W[:, r * 192:r * 192 + 64] for r in range(16)], axis=1)
    kcols = np.concatenate([W[:, r * 192 + 64:r * 192 + 128] for r in range(16)], axis=1)
    vcols = np.concatenate([W[:, r * 192 + 128:r * 192 + 192] for r in range(16)], axis=1)
    b = inputs["L1_attn_b_qkv"]
    bq = np.concatenate([b[r * 192:r * 192 + 64] for r in range(16)])
    bk = np.concatenate([b[r * 192 + 64:r * 192 + 128] for r in range(16)])
    bv = np.concatenate([b[r * 192 + 128:r * 192 + 192] for r in range(16)])
    bqk = np.concatenate([bq, bk]).reshape(16, 128).T
    return {
        "l1_wqk": np.ascontiguousarray(np.concatenate([qcols, kcols], axis=1)),
        "l1_bqk_p": np.ascontiguousarray(bqk),
        "l1_wv": np.ascontiguousarray(vcols),
        "l1_bv_r": np.tile(bv[None, :], (128, 1)),
        "l1_wo": np.ascontiguousarray(inputs["L1_attn_W_o"].T),
        "l1_bo_r": np.tile(inputs["L1_attn_b_o"][None, :], (128, 1)),
        "l1_k0": inputs["L1_ffn_k0"].astype(np.uint8),
        "l1_k1": inputs["L1_ffn_k1"].astype(np.uint8),
        "l1_k2": inputs["L1_ffn_k2"].astype(np.uint8),
    }


def host_l2a(inputs):
    perp = lambda a: np.ascontiguousarray(np.asarray(a, np.float32).reshape(-1, 128).T)
    return {
        "l2_wf": np.ascontiguousarray(inputs["L2_attn_W_f"].T),
        "l2_wi": np.ascontiguousarray(inputs["L2_attn_W_i"].T),
        "l2_wh": np.ascontiguousarray(inputs["L2_attn_W_h"].T),
        "l2_bf_p": perp(inputs["L2_attn_b_f"]),
        "l2_bi_p": perp(inputs["L2_attn_b_i"]),
        "l2_bh_p": perp(inputs["L2_attn_b_h"]),
    }


def host_l2f(inputs, core):
    rep = lambda a: np.tile(np.asarray(a, np.float32).reshape(1, -1), (128, 1))
    ib, it = inputs["L2_ffn_index_b"], inputs["L2_ffn_index_t"]
    sel = [int(t) for b, t in zip(ib, it) if int(b) == core]
    gi = np.zeros(2048, np.int32)
    si = np.full(2048, 1 << 20, np.int32)
    gi[:len(sel)] = sel
    si[:len(sel)] = sel
    perp16 = lambda a: np.ascontiguousarray(a.reshape(16, 128).T)
    return {
        "l2f_sw1t": np.ascontiguousarray(inputs["L2_ffn_sw1"].T),
        "l2f_sw2t": np.ascontiguousarray(inputs["L2_ffn_sw2"].T),
        "l2f_sw3t": np.ascontiguousarray(inputs["L2_ffn_sw3"].T),
        "l2f_sb1_r": rep(inputs["L2_ffn_sb1"]),
        "l2f_sb2_r": rep(inputs["L2_ffn_sb2"]),
        "l2f_sb3_r": rep(inputs["L2_ffn_sb3"]),
        "l2f_slng_r": rep(inputs["L2_ffn_s_ln_g"]),
        "l2f_slnb_r": rep(inputs["L2_ffn_s_ln_b"]),
        "l2f_gidx": perp16(gi),
        "l2f_sidx": perp16(si),
        "l2f_sgw_p": np.ascontiguousarray(inputs["L2_ffn_sg_w"].reshape(8, 128).T),
        "l2f_sgb_p": np.tile(np.asarray(inputs["L2_ffn_sg_b"], np.float32).reshape(1, 1), (128, 1)),
        "l2f_egw": np.ascontiguousarray(inputs["L2_ffn_eg_w"].T.reshape(8, 128, 8).transpose(1, 0, 2).reshape(128, 64)),
        "l2f_egb_r": rep(inputs["L2_ffn_eg_b"]),
        "l2f_expb_r": rep(inputs["L2_ffn_exp_bias"]),
        "l2f_ew1t": np.ascontiguousarray(np.transpose(inputs["L2_ffn_ew1"], (0, 2, 1))),
        "l2f_ew2t": np.ascontiguousarray(np.transpose(inputs["L2_ffn_ew2"], (0, 2, 1))),
        "l2f_ew3t": np.ascontiguousarray(np.transpose(inputs["L2_ffn_ew3"], (0, 2, 1))),
        "l2f_eb1_r": np.ascontiguousarray(np.tile(inputs["L2_ffn_eb1"][:, None, :], (1, 128, 1))),
        "l2f_eb2_r": np.ascontiguousarray(np.tile(inputs["L2_ffn_eb2"][:, None, :], (1, 128, 1))),
        "l2f_eb3_r": np.ascontiguousarray(np.tile(inputs["L2_ffn_eb3"][:, None, :], (1, 128, 1))),
        "l2f_sellng_r": rep(inputs["L2_ffn_sel_ln_g"]),
        "l2f_sellnb_r": rep(inputs["L2_ffn_sel_ln_b"]),
    }


def host_l3a(inputs):
    rep = lambda a: np.tile(np.asarray(a, np.float32).reshape(1, -1), (128, 1))
    Wg = inputs["L3_attn_w_in_g"].T
    bg = inputs["L3_attn_b_in_g"]
    return {
        "l3_winl": np.ascontiguousarray(inputs["L3_attn_w_in_l"].T),
        "l3_binl_r": rep(inputs["L3_attn_b_in_l"]),
        "l3_woutl": np.ascontiguousarray(inputs["L3_attn_w_out_l"].T),
        "l3_boutl_r": rep(inputs["L3_attn_b_out_l"]),
        "l3_wing_qk": np.ascontiguousarray(Wg[:, 0:2048]),
        "l3_bing_qk_p": np.ascontiguousarray(bg[0:2048].reshape(16, 128).T),
        "l3_wing_v": np.ascontiguousarray(Wg[:, 2048:3072]),
        "l3_bing_v_r": rep(bg[2048:3072]),
        "l3_woutg": np.ascontiguousarray(inputs["L3_attn_w_out_g"].T),
        "l3_boutg_r": rep(inputs["L3_attn_b_out_g"]),
    }


def host_l3f(inputs):
    return {
        "l3f_w1": np.ascontiguousarray(inputs["L3_ffn_W1"]),
        "l3f_b1p": np.ascontiguousarray(inputs["L3_ffn_b1"].reshape(32, 128).T),
        "l3f_w2": np.ascontiguousarray(inputs["L3_ffn_W2"]),
        "l3f_b2r": np.tile(inputs["L3_ffn_b2"][None, :], (128, 1)),
    }



def build_full():
    nc = bass.Bass("TRN2", target_bir_lowering=False)
    with nc.cleanup_on_exit():
        mk = MK(nc, set())
        mk.prologue()
        l0_ffn_prep(mk)
        l1_ffn_prep(mk)
        mk.load_x()
        l0_attn(mk)
        l0_ffn(mk)
        l1_attn(mk)
        l1_ffn(mk)
        l2_attn(mk)
        l2_ffn(mk)
        l3_attn(mk)
        l3_ffn(mk)
        mk.store_out()
        mk.P.finalize()
        nc.all_engine_barrier()
    return nc, mk


def host_all(inputs):
    m = {"cst": host_consts(inputs), "ident": np.eye(128, dtype=np.float32)}
    m.update(host_l0a(inputs))
    m.update(host_l0f(inputs))
    m.update(host_l1(inputs))
    m.update(host_l2a(inputs))
    m.update(host_l3a(inputs))
    m.update(host_l3f(inputs))
    return m


def build_part(layers, preps, store=True):
    nc = bass.Bass("TRN2", target_bir_lowering=False)
    with nc.cleanup_on_exit():
        mk = MK(nc, set())
        mk.prologue()
        for p in preps:
            p(mk)
        mk.load_x()
        for l in layers:
            l(mk)
        if store:
            mk.store_out()
        mk.P.finalize()
        nc.all_engine_barrier()
    return nc, mk


def kernel_split(**inputs):
    DBG.clear()
    common = host_all(inputs)
    x = np.ascontiguousarray(inputs["L0_attn_x"], dtype=np.float32)
    xc = [np.ascontiguousarray(x[c]) for c in range(8)]
    percore = [host_l2f(inputs, c) for c in range(8)]
    for c in range(8):
        gi3 = np.zeros((128, 32), np.int32)
        for i in range(32):
            b, jj = i // 4, i % 4
            gi3[:, i] = b * T + c * 512 + jj * 128 + np.arange(128)
        percore[c]["l3_gidx"] = gi3
    parts = [([l0_attn, l0_ffn, l1_attn, l1_ffn, l2_attn, l2_ffn, l3_attn_local_only],
              [l0_ffn_prep, l1_ffn_prep], False, "loc"),
             ([l3_attn_global_only, l3_ffn], [], True, "o")]
    scrs = None
    for layers, preps, store, outname in parts:
        nc, mk = build_part(layers, preps, store=(store or outname == "loc"))
        in_maps = []
        for c in range(8):
            m = {k: common[k] for k in mk.io_names if k in common}
            m.update({k: percore[c][k] for k in mk.io_names if k in percore[c]})
            m["x"] = xc[c]
            if "scr" in mk.io_names:
                m["scr"] = scrs[c]
            in_maps.append(m)
        res = run_bass_kernel_spmd(nc, in_maps, core_ids=list(range(8)))
        xc = [np.ascontiguousarray(np.asarray(res.results[c]["o"], np.float32)) for c in range(8)]
        if outname == "loc":
            loc = [np.asarray(res.results[c]["loc"]) for c in range(8)]
            scrs = [np.ascontiguousarray(np.concatenate(
                [loc[b][c * 512:(c + 1) * 512] for b in range(8)], axis=0)) for c in range(8)]
    return np.stack(xc, axis=0)


def kernel(**inputs):
    return kernel_split(**inputs)


def kernel_fused(**inputs):
    DBG.clear()
    nc, mk = build_full()
    common = host_all(inputs)
    x = np.ascontiguousarray(inputs["L0_attn_x"], dtype=np.float32)
    in_maps = []
    for c in range(8):
        m = dict(common)
        m["x"] = np.ascontiguousarray(x[c])
        m.update(host_l2f(inputs, c))
        gi3 = np.zeros((128, 32), np.int32)
        for i in range(32):
            b, jj = i // 4, i % 4
            gi3[:, i] = b * T + c * 512 + jj * 128 + np.arange(128)
        m["l3_gidx"] = gi3
        in_maps.append(m)
    res = run_bass_kernel_spmd(nc, in_maps, core_ids=list(range(8)))
    return np.stack([np.asarray(res.results[c]["o"], np.float32) for c in range(8)], axis=0)

